# version 18
# baseline (speedup 1.0000x reference)
import numpy as np
import ml_dtypes
from contextlib import ExitStack

import concourse.bass as bass
import concourse.mybir as mybir
from concourse.bass_utils import run_bass_kernel_spmd

F32 = mybir.dt.float32
BF16 = mybir.dt.bfloat16
I32 = mybir.dt.int32
U8 = mybir.dt.uint8
AF = mybir.ActivationFunctionType
ALU = mybir.AluOpType
AX = mybir.AxisListType
BFNP = ml_dtypes.bfloat16

NCORES = 8
B, S, D = 2, 8192, 1024
TPC = 2048
NTG = 4
TG = 512
NCH = 8
H, HD = 16, 64
DC, DR = 256, 128
EPS_LN = 1e-5
EPS_F32 = float(np.finfo(np.float32).eps)

ENGINES = ("sync", "gpsimd", "scalar", "vector", "tensor")
DBG_A = False
FDBG = False
USE_DRAINS = False


class Sem:
    __slots__ = ("h", "v")

    def __init__(self, h):
        self.h = h
        self.v = 0


class Prog:
    CHAINED = ("vector", "scalar")

    def __init__(self, name):
        self.nc = bass.Bass("TRN2", target_bir_lowering=False, name=name)
        self.es = ExitStack()
        self.scopes = [self.es]
        self.q = {e: [] for e in ENGINES}
        self.chain = {e: Sem(self.es.enter_context(self.nc.semaphore(e + "_chain")))
                      for e in self.CHAINED}
        self.last = {e: None for e in ENGINES}

    def push_scope(self):
        self.scopes.append(ExitStack())

    def pop_scope(self):
        self.scopes.pop().close()

    def sem(self, name):
        return Sem(self.es.enter_context(self.nc.semaphore(name)))

    def sb(self, name, shape, dt):
        return self.scopes[-1].enter_context(self.nc.sbuf_tensor(name, shape, dt))

    def ps(self, name, shape, dt=F32):
        return self.scopes[-1].enter_context(self.nc.psum_tensor(name, shape, dt))

    def o(self, eng, fn, wait=(), inc=None, incn=1):
        w = list(wait)
        if eng in self.CHAINED:
            if self.last[eng] is not None:
                w.append(self.last[eng])
            if inc is None:
                inc = self.chain[eng]
                incn = 1
        w = tuple((s.h, int(v)) for (s, v) in w)
        tgt = None
        if inc is not None:
            inc.v += incn
            tgt = (inc.h, incn)
            if eng in self.CHAINED:
                self.last[eng] = (inc, inc.v)

        drain = eng in self.CHAINED and USE_DRAINS

        def op(E, fn=fn, w=w, tgt=tgt, drain=drain):
            for h, v in w:
                E.wait_ge(h, v)
            r = fn(E)
            if tgt is not None:
                if drain:
                    E.drain().then_inc(*tgt)
                else:
                    if r is None:
                        r = E.nop()
                    r.then_inc(*tgt)
        self.q[eng].append(op)
        return inc.v if inc is not None else None

    def dma(self, eng, out_ap, in_ap, sem, wait=(), slow=False, serial=True):
        w = list(wait)
        if serial and sem.v > 0:
            w.append((sem, sem.v))
        return self.o(eng, lambda E, o=out_ap, i=in_ap, s=slow: E.dma_start(
            out=o, in_=i, allow_slow_non_contiguous=s),
            wait=w, inc=sem, incn=16)

    def build(self):
        with self.nc.Block() as blk:
            for e in ENGINES:
                ops = self.q[e]
                if ops:
                    getattr(blk, e)(lambda E, ops=ops: [op(E) for op in ops])
        return self.nc


class Pool:

    def __init__(self, tiles):
        self.t = tiles
        self.last = [None] * len(tiles)
        self.i = 0

    def acquire(self):
        slot = self.i % len(self.t)
        self.i += 1
        prev = self.last[slot]
        return slot, self.t[slot], ([prev] if prev is not None else [])

    def release(self, slot, sem, val):
        self.last[slot] = (sem, val)


def MM(out, lhsT, rhs, start, stop):
    return lambda E, o=out, l=lhsT, r=rhs, st=start, sp=stop: E.matmul(
        o, l, r, start=st, stop=sp, skip_group_check=True)


def _rsqrt_strip(P, ap, s_pre, v_pre, s_post):
    P.o("scalar", lambda E, a=ap: E.activation(out=a, in_=a, func=AF.Ln),
        wait=[(s_pre, v_pre)])
    P.o("scalar", lambda E, a=ap: E.activation(out=a, in_=a, func=AF.Exp, scale=-0.5),
        inc=s_post)


def emit_ln(P, pfx, pool, pbig, h_slice, ones_col_bf, ones_row_f, ones_row64,
            cw, scr_bf, scr_sq, st1, st2, st3, rstd_b, mst_b, src_dram=None,
            src_sb=None, xc=None, src_ready=(), nbuf=2, sems=None):
    if sems is None:
        sems = [P.sem(pfx + n) for n in ("cast", "sq", "mm", "str", "bc", "app", "x")]
    s_cast, s_sq, s_mm, s_str, s_bc, s_app, s_x = sems
    cast0, sq0, mm0, str0, app0 = s_cast.v, s_sq.v, s_mm.v, s_str.v, s_app.v

    def src_chunk(c, phase):
        if src_sb is not None:
            return src_sb[:, TPC * c:TPC * (c + 1)], list(src_ready)
        buf = c % 2
        ap = xc[:, TPC * buf:TPC * (buf + 1)]
        w = [(s_x, s_x.v)]
        if phase == 0:
            if c >= 2:
                w.append((s_cast, cast0 + c - 1))
        else:
            w.append((s_mm, mm0 + NCH))
            if c >= 2:
                w.append((s_app, app0 + 4 * (c - 1)))
        v = P.dma("sync", ap, src_dram[128 * c:128 * (c + 1), :], s_x, wait=w)
        return ap, [(s_x, v)]

    for c in range(NCH):
        src, waits = src_chunk(c, 0)
        buf = c % nbuf
        cb = scr_bf[:, TPC * buf:TPC * (buf + 1)]
        sq = scr_sq[:, TPC * buf:TPC * (buf + 1)]
        w = list(waits)
        if c >= nbuf:
            w.append((s_mm, mm0 + c - nbuf + 1))
        P.o("vector", lambda E, o=cb, i=src: E.tensor_copy(o, i), wait=w, inc=s_cast)
        P.o("scalar", lambda E, o=sq, i=cb: E.activation(out=o, in_=i, func=AF.Square),
            wait=[(s_cast, cast0 + c + 1)] +
            ([(s_mm, mm0 + c - nbuf + 1)] if c >= nbuf else []), inc=s_sq)
        for tg in range(NTG):
            first = (c == 0)
            last = (c == NCH - 1)
            P.o("tensor", MM(pbig[0:1, TG * tg:TG * (tg + 1)], ones_col_bf,
                             cb[:, TG * tg:TG * (tg + 1)], first, last),
                wait=([(s_sq, sq0 + c + 1)] + (list(cw) if c == 0 else []))
                if tg == 0 else [])
            P.o("tensor", MM(pbig[32:33, TG * tg:TG * (tg + 1)], ones_col_bf,
                             sq[:, TG * tg:TG * (tg + 1)], first, last),
                inc=(s_mm if tg == NTG - 1 else None))
    P.o("vector", lambda E: E.tensor_scalar_mul(st1[:, :], pbig[0:1, 0:TPC], 1.0 / D),
        wait=[(s_mm, s_mm.v)])
    P.o("vector", lambda E: E.tensor_scalar_mul(st2[:, :], pbig[32:33, 0:TPC], 1.0 / D),
        inc=s_str)
    P.o("scalar", lambda E: E.activation(out=st3, in_=st1[:, :], func=AF.Square),
        wait=[(s_str, str0 + 1)], inc=s_str)
    P.o("vector", lambda E: E.tensor_sub(st2[:, :], st2[:, :], st3),
        wait=[(s_str, str0 + 2)])
    P.o("vector", lambda E: E.tensor_scalar_add(st2[:, :], st2[:, :], EPS_LN),
        inc=s_str)
    _rsqrt_strip(P, st2[:, :], s_str, str0 + 3, s_str)
    P.o("vector", lambda E: E.tensor_mul(st1[:, :], st1[:, :], st2[:, :]),
        wait=[(s_str, str0 + 4)])
    P.o("vector", lambda E: E.tensor_scalar_mul(st1[:, :], st1[:, :], -1.0),
        inc=s_str)
    for tg in range(NTG):
        for si, (strip, orow, bdst) in enumerate(((st2, ones_row64, rstd_b),
                                                  (st1, ones_row_f, mst_b))):
            slot, pt, pw = pool.acquire()
            mmw = list(pw) + ([(s_str, str0 + 5)] if (tg == 0 and si == 0) else [])
            vmm = P.o("tensor", MM(pt[:, 0:TG], orow,
                                   strip[0:1, TG * tg:TG * (tg + 1)], True, True),
                      wait=mmw, inc=s_bc)
            P.o("scalar", lambda E, o=bdst[:, TG * tg:TG * (tg + 1)], p=pt:
                E.activation(out=o, in_=p[:, 0:TG], func=AF.Copy),
                wait=[(s_bc, vmm)], inc=s_bc)
            pool.release(slot, s_bc, s_bc.v)
    bcv = s_bc.v
    for c in range(NCH):
        src, waits = src_chunk(c, 1)
        for tg in range(NTG):
            xs = src[:, TG * tg:TG * (tg + 1)]
            tmp = pbig[:, TG * tg:TG * (tg + 1)]
            P.o("vector", lambda E, t=tmp, x=xs,
                r=rstd_b[:, TG * tg:TG * (tg + 1)]: E.tensor_mul(t, x, r),
                wait=[(s_bc, bcv)] + list(waits))
            hs = h_slice(c, tg)
            P.o("vector", lambda E, o=hs, t=tmp,
                m=mst_b[:, TG * tg:TG * (tg + 1)]: E.tensor_add(o, t, m),
                inc=s_app)
    return st2, st1, s_app


GROUPS = [[0, 1, 2, 3], [4, 5, 6, 7]]


def build_fused():
    P = Prog("mkfull")
    nc = P.nc

    def inp(name, shape, dt=F32):
        return nc.dram_tensor(name, shape, dt, kind="ExternalInput").ap()

    xT = inp("xT", [D, TPC])
    wq = inp("wq", [D, DC]); wqc = inp("wqc", [DC, 896]); wqr = inp("wqr", [D, DR])
    wkv = inp("wkv", [D, DC]); wkr = inp("wkr", [D, DR]); wkc = inp("wkc", [DC, 896])
    wv = inp("wv", [DC, D])
    bqx = inp("bqx", [128, 2]); bqcx = inp("bqcx", [128, 7]); bqrx = inp("bqrx", [128, 1])
    bkvx = inp("bkvx", [128, 2]); bkrx = inp("bkrx", [128, 1]); bkcx = inp("bkcx", [128, 7])
    bvx = inp("bvx", [1, D])
    cosT = inp("cosT", [DR, TPC]); sinT = inp("sinT", [DR, TPC])
    onesc = inp("onesc", [128, 1]); onesr = inp("onesr", [1, 128])
    tri = inp("tri", [128, 128], BF16); identb = inp("identb", [128, 128], BF16)
    identf = inp("identf", [128, 128])
    wo = inp("wo", [D, D]); box = inp("box", [128, 8])
    wrt = inp("wrt", [D, 128]); centT = inp("centT", [128, 16])
    tempx = inp("tempx", [1, 1])
    iexp = inp("iexp", [16, D, 512]); oexp = inp("oexp", [16, 512, 64])
    selc = inp("selc", [16, 8 * 128], BF16)
    wqkvT = inp("wqkvT", [D, 3 * D])
    wpT = inp("wpT", [D, D]); bpx = inp("bpx", [128, 8])
    gamx = inp("gamx", [128, 8]); betx = inp("betx", [128, 8])
    fc1q = inp("fc1q", [D, 4096], U8); sc1x = inp("sc1x", [32, D])
    b1x = inp("b1x", [128, 32])
    fc2q = inp("fc2q", [4096, D], U8); sc2x = inp("sc2x", [32, D])
    b2x = inp("b2x", [128, 8])

    x4T = nc.dram_tensor("x4T", [D, TPC], F32, kind="ExternalOutput").ap()
    if FDBG:
        dkq = nc.dram_tensor("dkq", [2 * D, TPC], BF16, kind="ExternalOutput").ap()
        dvi = nc.dram_tensor("dvi", [TPC, H * 65], BF16, kind="ExternalOutput").ap()
        dao = nc.dram_tensor("dao", [256, S], BF16, kind="ExternalOutput").ap()
        dq2 = nc.dram_tensor("dq2", [D, TPC], BF16, kind="ExternalOutput").ap()
        dctx = nc.dram_tensor("dctx", [65, H * 65], F32, kind="ExternalOutput").ap()
        dao2 = nc.dram_tensor("dao2", [256, S], BF16, kind="ExternalOutput").ap()
        dx3 = nc.dram_tensor("dx3", [D, TPC], F32, kind="ExternalOutput").ap()
        dx2 = nc.dram_tensor("dx2", [D, TPC], F32, kind="ExternalOutput").ap()

    k_in = nc.dram_tensor("k_in", [D, TPC], BF16).ap()
    q_in = nc.dram_tensor("q_in", [D, TPC], BF16).ap()
    vA_in = nc.dram_tensor("vA_in", [D, H * 65], BF16).ap()
    vB_in = nc.dram_tensor("vB_in", [D, H * 65], BF16).ap()
    kA_out = nc.dram_tensor("kA_out", [4 * D, TPC], BF16).ap()
    qB_out = nc.dram_tensor("qB_out", [4 * D, TPC], BF16).ap()
    vA_out = nc.dram_tensor("vA_out", [4 * D, H * 65], BF16).ap()
    vB_out = nc.dram_tensor("vB_out", [4 * D, H * 65], BF16).ap()
    ao_in = nc.dram_tensor("ao_in", [256, S], BF16).ap()
    ao_out = nc.dram_tensor("ao_out", [D, S], BF16).ap()
    q2_in = nc.dram_tensor("q2_in", [D, TPC], BF16).ap()
    q2_out = nc.dram_tensor("q2_out", [4 * D, TPC], BF16).ap()
    ctx_in = nc.dram_tensor("ctx_in", [65, H * 65], F32).ap()
    ctx_out = nc.dram_tensor("ctx_out", [65, H * 65], F32).ap()
    ao2_in = nc.dram_tensor("ao2_in", [256, S], BF16).ap()
    ao2_out = nc.dram_tensor("ao2_out", [D, S], BF16).ap()
    y1d = nc.dram_tensor("y1d", [4096, TPC], BF16).ap()
    kmy = nc.dram_tensor("kmy", [4 * 256, TPC], BF16).ap()
    qmy = nc.dram_tensor("qmy", [4 * 256, TPC], BF16).ap()
    vmy = nc.dram_tensor("vmy", [S, 260], BF16).ap()
    aomy = nc.dram_tensor("aomy", [D, TPC], BF16).ap()
    ctxmy = nc.dram_tensor("ctxmy", [65, 260], F32).ap()
    q2my = nc.dram_tensor("q2my", [4 * 256, TPC], BF16).ap()
    ao2my = nc.dram_tensor("ao2my", [D, TPC], BF16).ap()

    ones_col_bf = P.sb("ones_col_bf", [128, 1], BF16)
    ones_row_f = P.sb("ones_row_f", [1, 128], F32)
    ones_row64 = ones_row_f[:, :]
    idb = P.sb("idb", [128, 128], BF16)
    trib = P.sb("trib", [128, 128], BF16)
    bq_s = P.sb("bq_s", [128, 2], F32); bqc_s = P.sb("bqc_s", [128, 7], F32)
    bqr_s = P.sb("bqr_s", [128, 1], F32); bkv_s = P.sb("bkv_s", [128, 2], F32)
    bkr_s = P.sb("bkr_s", [128, 1], F32); bkc_s = P.sb("bkc_s", [128, 7], F32)
    bo_s = P.sb("bo_s", [128, 8], F32)
    cent_s = P.sb("cent_s", [128, 16], F32)
    selcs = P.sb("selcs", [16, 8 * 128], BF16)
    bp_s = P.sb("bp_s", [128, 8], F32)
    gam_s = P.sb("gam_s", [128, 8], F32); bet_s = P.sb("bet_s", [128, 8], F32)
    b1_s = P.sb("b1_s", [128, 32], F32)
    b2_s = P.sb("b2_s", [128, 8], F32); b2h_s = P.sb("b2h_s", [128, 8], F32)
    tstrip = P.sb("tstrip", [1, 16], F32)
    stp = P.sb("stp", [1, 2 * TPC], F32)
    st1 = stp[0:1, 0:TPC]
    st2 = stp[0:1, TPC:2 * TPC]
    rstd_b = P.sb("rstd_b", [128, TPC], F32)
    mst_b = P.sb("mst_b", [128, TPC], F32)

    pbig = P.ps("pbig", [128, 4 * TG], F32)
    pool = Pool([P.ps(f"pp{i}", [128, TG], F32) for i in range(4)])
    ln_sems = [P.sem("ln_" + n) for n in ("cast", "sq", "mm", "str", "bc", "app", "x")]

    s_c = P.sem("s_c")
    s_w = P.sem("s_w")
    s_cc = P.sem("s_cc")

    for dst, src in ((bq_s, bqx), (bqc_s, bqcx), (bqr_s, bqrx), (bkv_s, bkvx),
                     (bkr_s, bkrx), (bkc_s, bkcx),
                     (ones_row_f, onesr),
                     (idb, identb), (trib, tri), (bo_s, box), (cent_s, centT),
                     (selcs, selc), (bp_s, bpx), (gam_s, gamx), (bet_s, betx),
                     (b1_s, b1x), (b2_s, b2x)):
        P.dma("sync", dst[:, :], src[:, :], s_c, serial=False)
    P.dma("sync", tstrip[0:1, 0:1], tempx[:, :], s_c, serial=False)
    P.dma("gpsimd", ones_col_bf[:, :], onesc[:, :], s_w, serial=False)
    CW = [(s_c, s_c.v), (s_w, s_w.v)]

    dynv = {}

    def DV(E, key):
        eng = E.engine
        if eng not in dynv:
            pid = E.partition_id()
            hg = pid % 4
            dynv[eng] = {"hg": hg, "hg2": hg * 2}
        return dynv[eng][key]

    def dyn_dma(out_ap, mk_in, sem, wait, q="sync"):
        w = list(wait)
        if sem.v > 0:
            w.append((sem, sem.v))
        sem.v += 16
        tgt = (sem.h, 16)
        wt = tuple((x.h, int(v)) for (x, v) in w)

        def op(E, o=out_ap, mk=mk_in, wt=wt, tgt=tgt):
            for h, v in wt:
                E.wait_ge(h, v)
            E.dma_start(out=o, in_=mk(E)).then_inc(*tgt)
        P.q[q].append(op)
        return sem.v

    P.push_scope()
    xc = P.sb("a_xc", [128, 2 * TPC], F32)
    scr_bf = P.sb("a_scrbf", [128, 2 * TPC], BF16)
    scr_sq = P.sb("a_scrsq", [128, 2 * TPC], BF16)
    hbf = P.sb("a_h", [128, NCH * TPC], BF16)
    wqb = P.sb("a_wqb", [128, 8 * DC], BF16)
    wqcb = P.sb("a_wqcb", [128, 2 * 896], BF16)
    wqrb = P.sb("a_wqrb", [128, 8 * DR], BF16)
    wkvb = P.sb("a_wkvb", [128, 8 * DC], BF16)
    wkrb = P.sb("a_wkrb", [128, 8 * DR], BF16)
    wkcb = P.sb("a_wkcb", [128, 2 * 896], BF16)
    wvb = P.sb("a_wvb", [128, 2 * D], BF16)
    cos_s = P.sb("a_cos", [128, TPC], F32)
    sin_s = P.sb("a_sin", [128, TPC], F32)
    cqb = P.sb("a_cq", [128, 2 * TPC], BF16)
    ckvb = P.sb("a_ckv", [128, 2 * TPC], BF16)
    qch = P.sb("a_qch", [128, 2 * TPC], BF16)
    kch = P.sb("a_kch", [128, 2 * TPC], BF16)
    qrf = P.sb("a_qrf", [128, TPC], F32)
    krf = P.sb("a_krf", [128, TPC], F32)
    vat = P.sb("a_vat", [128, 2 * H * 65], BF16)
    bv_b = P.sb("a_bvb", [128, D], F32)
    bv_row = P.sb("a_bv_row", [1, D], F32)

    s_aw = P.sem("s_aw")
    P.dma("sync", bv_row[:, :], bvx[:, :], s_aw, serial=False)
    P.dma("gpsimd", wqb.rearrange("p (kc n) -> p kc n", n=DC),
          wq.rearrange("(kc p) n -> p kc n", p=128), s_aw, serial=False)
    P.dma("gpsimd", wqcb.rearrange("p (kc n) -> p kc n", n=896),
          wqc.rearrange("(kc p) n -> p kc n", p=128), s_aw, serial=False)
    P.dma("gpsimd", wqrb.rearrange("p (kc n) -> p kc n", n=DR),
          wqr.rearrange("(kc p) n -> p kc n", p=128), s_aw, serial=False)
    P.dma("gpsimd", wkvb.rearrange("p (kc n) -> p kc n", n=DC),
          wkv.rearrange("(kc p) n -> p kc n", p=128), s_aw, serial=False)
    P.dma("gpsimd", wkrb.rearrange("p (kc n) -> p kc n", n=DR),
          wkr.rearrange("(kc p) n -> p kc n", p=128), s_aw, serial=False)
    P.dma("gpsimd", wkcb.rearrange("p (kc n) -> p kc n", n=896),
          wkc.rearrange("(kc p) n -> p kc n", p=128), s_aw, serial=False)
    P.dma("gpsimd", wvb.rearrange("p (kc n) -> p kc n", n=D),
          wv.rearrange("(kc p) n -> p kc n", p=128), s_aw, serial=False)
    P.dma("sync", cos_s[:, :], cosT[:, :], s_aw, serial=False)
    P.dma("sync", sin_s[:, :], sinT[:, :], s_aw, serial=False)
    cw = CW + [(s_aw, s_aw.v)]

    def h_slice(c, tg):
        return hbf[:, TPC * c + TG * tg: TPC * c + TG * (tg + 1)]

    a_st, b_st, s_app = emit_ln(P, "aln", pool, pbig, h_slice,
                                ones_col_bf[:, :], ones_row_f[:, :],
                                ones_row64, cw,
                                scr_bf, scr_sq, st1, st2, pbig[64:65, 0:TPC],
                                rstd_b, mst_b, src_dram=xT, xc=xc, sems=ln_sems)
    HREADY = [(s_app, s_app.v)]

    s_pmm = P.sem("s_pmm")
    s_pev = P.sem("s_pev")

    def proj(n_oc, n_kc, lhsT_fn, rhs_fn, evac, first_wait):
        for oc in range(n_oc):
            for tg in range(NTG):
                slot, pt, pw = pool.acquire()
                w = list(pw) + (first_wait if (oc == 0 and tg == 0) else [])
                for kc in range(n_kc):
                    P.o("tensor", MM(pt[:, 0:TG], lhsT_fn(kc, oc), rhs_fn(kc, tg),
                                     kc == 0, kc == n_kc - 1),
                        wait=w if kc == 0 else [],
                        inc=(s_pmm if kc == n_kc - 1 else None))
                evac(oc, tg, pt, [(s_pmm, s_pmm.v)], s_pev)
                pool.release(slot, s_pev, s_pev.v)

    def mk_gelu_evac(dst, bias):
        def evac(oc, tg, pt, w, s_ev):
            o = dst[:, TPC * oc + TG * tg: TPC * oc + TG * (tg + 1)]
            P.o("scalar", lambda E, o=o, p=pt, b=bias[:, oc:oc + 1]: E.activation(
                out=o, in_=p[:, 0:TG], func=AF.Gelu, bias=b), wait=w, inc=s_ev)
        return evac

    proj(2, NCH,
         lambda kc, oc: wkvb[:, DC * kc + 128 * oc: DC * kc + 128 * (oc + 1)],
         lambda kc, tg: hbf[:, TPC * kc + TG * tg: TPC * kc + TG * (tg + 1)],
         mk_gelu_evac(ckvb, bkv_s), HREADY + cw)
    CKVR = [(s_pev, s_pev.v)]

    s_ag = P.sem("s_ag")
    s_ast = P.sem("s_ast")

    hc_sems = [P.sem("hc_" + n) for n in ("mm", "ev", "rmm", "rev", "rdma", "rope")]

    def emit_head_chunks(which, wsb, bsb, csrc, cready, kqd, rope_srcs,
                         rope_extra_wait=()):
        ch = qch if which == "q" else kch
        s_mm, s_ev = hc_sems[0], hc_sems[1]
        for oc in range(7):
            buf = oc % 2
            dst = ch[:, TPC * buf:TPC * (buf + 1)]
            for tg in range(NTG):
                slot, pt, pw = pool.acquire()
                w = list(pw) + (cready if (oc == 0 and tg == 0) else [])
                if tg == 0 and oc >= 2:
                    w.append((s_ast, s_ast.v - 16))
                for kc in range(2):
                    P.o("tensor", MM(pt[:, 0:TG],
                                     wsb[:, 896 * kc + 128 * oc: 896 * kc + 128 * (oc + 1)],
                                     csrc[:, TPC * kc + TG * tg: TPC * kc + TG * (tg + 1)],
                                     kc == 0, kc == 1),
                        wait=w if kc == 0 else [], inc=(s_mm if kc == 1 else None))
                P.o("vector", lambda E, o=dst[:, TG * tg:TG * (tg + 1)], p=pt,
                    b=bsb[:, oc:oc + 1]: E.tensor_single_scalar(o, p[:, 0:TG], b, ALU.add),
                    wait=[(s_mm, s_mm.v)], inc=s_ev)
                pool.release(slot, s_ev, s_ev.v)
            P.o("scalar", lambda E, a=dst: E.activation(out=a, in_=a, func=AF.Gelu),
                wait=[(s_ev, s_ev.v)], inc=s_ag)
            P.dma("sync", kqd[128 * oc:128 * (oc + 1), :], dst,
                  s_ast, wait=[(s_ag, s_ag.v)])
        wrb, brb, rf = rope_srcs
        s_rmm, s_rev = hc_sems[2], hc_sems[3]
        for tg in range(NTG):
            slot, pt, pw = pool.acquire()
            for kc in range(NCH):
                P.o("tensor", MM(pt[:, 0:TG], wrb[:, DR * kc:DR * (kc + 1)],
                                 hbf[:, TPC * kc + TG * tg: TPC * kc + TG * (tg + 1)],
                                 kc == 0, kc == NCH - 1),
                    wait=(list(pw) + HREADY) if kc == 0 else [],
                    inc=(s_rmm if kc == NCH - 1 else None))
            P.o("vector", lambda E, o=rf[:, TG * tg:TG * (tg + 1)], p=pt,
                b=brb[:, 0:1]: E.tensor_single_scalar(o, p[:, 0:TG], b, ALU.add),
                wait=[(s_rmm, s_rmm.v)], inc=s_rev)
            pool.release(slot, s_rev, s_rev.v)
        buf = 1
        dst = ch[:, TPC * buf:TPC * (buf + 1)]
        rot = xc[:, 0:TPC]
        tmp2 = xc[:, TPC:2 * TPC]
        s_rdma, s_rope = hc_sems[4], hc_sems[5]
        P.dma("sync", rot[0:64, :], rf[64:128, :], s_rdma,
              wait=[(s_rev, s_rev.v)] + list(rope_extra_wait))
        P.dma("sync", rot[64:128, :], rf[0:64, :], s_rdma)
        P.o("vector", lambda E: E.tensor_scalar_mul(rot[0:64, :], rot[0:64, :], -1.0),
            wait=[(s_rdma, s_rdma.v), (s_ast, s_ast.v - 16)])
        P.o("vector", lambda E: E.tensor_mul(tmp2[:, :], rot[:, :], sin_s[:, :]))
        P.o("vector", lambda E: E.tensor_mul(rot[:, :], rf[:, :], cos_s[:, :]))
        P.o("vector", lambda E, o=dst: E.tensor_add(o, rot[:, :], tmp2[:, :]),
            inc=s_rope)
        P.o("scalar", lambda E, a=dst: E.activation(out=a, in_=a, func=AF.Gelu),
            wait=[(s_rope, s_rope.v)], inc=s_ag)
        P.dma("sync", kqd[128 * 7:128 * 8, :], dst, s_ast,
              wait=[(s_ag, s_ag.v)])
        return s_rope

    s_vbc = P.sem("s_vbc")
    s_vbe = P.sem("s_vbe")
    for half in range(2):
        slot, pt, pw = pool.acquire()
        P.o("tensor", MM(pt[:, 0:TG], ones_row_f[:, :],
                         bv_row[0:1, TG * half:TG * (half + 1)], True, True),
            wait=list(pw) + cw, inc=s_vbc)
        P.o("vector", lambda E, o=bv_b[:, TG * half:TG * (half + 1)], p=pt:
            E.tensor_copy(o, p[:, 0:TG]), wait=[(s_vbc, s_vbc.v)], inc=s_vbe)
        pool.release(slot, s_vbe, s_vbe.v)
    s_vmm = P.sem("s_vmm")
    s_vev = P.sem("s_vev")
    NT = TPC // 128

    def emit_v_tiles(lo, hi):
        for tt in range(lo, hi):
            buf = tt % 2
            tile = vat[:, (H * 65) * buf:(H * 65) * (buf + 1)]
            t3 = tile.rearrange("p (h c) -> p h c", c=65)
            w = [(s_ast, s_ast.v - 16)] if tt >= 2 else []
            P.o("vector", lambda E, o=t3[:, :, 64:65]: E.memset(o, 1.0),
                wait=w + ([(s_vbe, s_vbe.v)] if tt == 0 else []))
            for vc in range(2):
                slot, pt, pw = pool.acquire()
                for kc in range(2):
                    P.o("tensor", MM(pt[:, 0:TG],
                                     ckvb[:, TPC * kc + 128 * tt: TPC * kc + 128 * (tt + 1)],
                                     wvb[:, D * kc + TG * vc: D * kc + TG * (vc + 1)],
                                     kc == 0, kc == 1),
                        wait=(list(pw) + CKVR) if kc == 0 else [],
                        inc=(s_vmm if kc == 1 else None))
                P.o("vector", lambda E, o=t3[:, 8 * vc:8 * (vc + 1), 0:64], p=pt,
                    b=bv_b[:, TG * vc:TG * (vc + 1)].rearrange("p (h c) -> p h c", c=64):
                    E.scalar_tensor_tensor(o, p[:, 0:TG].rearrange("p (h c) -> p h c", c=64),
                                           0.0, b, ALU.bypass, ALU.add),
                    wait=[(s_vmm, s_vmm.v)], inc=s_vev)
                pool.release(slot, s_vev, s_vev.v)
            vdst = vA_in if tt < NT // 2 else vB_in
            P.dma("sync", vdst[128 * (tt % (NT // 2)):128 * (tt % (NT // 2) + 1), :],
                  tile, s_ast, wait=[(s_vev, s_vev.v)])

    k_rope = emit_head_chunks("k", wkcb, bkc_s, ckvb, CKVR, k_in,
                              (wkrb, bkr_s, krf))
    AST_KO = s_ast.v
    P.o("gpsimd", lambda E: E.collective_compute(
        "AllGather", ALU.bypass, GROUPS, ins=[k_in[:, :]], outs=[kA_out[:, :]]),
        wait=[(s_ast, AST_KO)], inc=s_cc)
    CCK = s_cc.v
    emit_v_tiles(0, NT // 2)
    AST_VA = s_ast.v
    P.o("gpsimd", lambda E: E.collective_compute(
        "AllGather", ALU.bypass, GROUPS, ins=[vA_in[:, :]], outs=[vA_out[:, :]]),
        wait=[(s_ast, AST_VA)], inc=s_cc)
    CC_VA = s_cc.v

    proj(2, NCH,
         lambda kc, oc: wqb[:, DC * kc + 128 * oc: DC * kc + 128 * (oc + 1)],
         lambda kc, tg: hbf[:, TPC * kc + TG * tg: TPC * kc + TG * (tg + 1)],
         mk_gelu_evac(cqb, bq_s), [])
    CQR = [(s_pev, s_pev.v)]
    emit_head_chunks("q", wqcb, bqc_s, cqb, CQR, q_in, (wqrb, bqr_s, qrf),
                     rope_extra_wait=[(k_rope, k_rope.v)])
    AST_QO = s_ast.v
    P.o("gpsimd", lambda E: E.collective_compute(
        "AllGather", ALU.bypass, GROUPS, ins=[q_in[:, :]], outs=[qB_out[:, :]]),
        wait=[(s_ast, AST_QO)], inc=s_cc)
    CCQ = s_cc.v
    emit_v_tiles(NT // 2, NT)
    AST_ALL = s_ast.v
    P.pop_scope()
    P.o("gpsimd", lambda E: E.collective_compute(
        "AllGather", ALU.bypass, GROUPS, ins=[vB_in[:, :]], outs=[vB_out[:, :]]),
        wait=[(s_ast, AST_ALL)], inc=s_cc)
    CC_VB = s_cc.v

    P.push_scope()
    kp = P.sb("b_kp", [128, 2 * S], BF16)
    qp = P.sb("b_qp", [128, 2 * S], BF16)
    vh = P.sb("b_vh", [128, (S // 128) * 260], BF16)
    qsq = P.sb("b_qsq", [128, S], BF16)
    Pt = P.sb("b_Pt", [128, 4 * TG], BF16)
    aot = P.sb("b_aot", [64, 2 * TG], BF16)
    pvsbB = P.sb("b_pvsb", [65, 2 * TG], F32)
    rbB = P.sb("b_rb", [64, TG], F32)
    rstripB = P.sb("b_rstrip", [1, TG], F32)
    nstr = P.sb("b_nstr", [1, 16], F32)
    bias_col = P.sb("b_bias", [128, 1], F32)

    NKT = S // 128
    NQG = S // TG
    s_bl = P.sem("s_bl")
    kqA3 = kA_out.rearrange("(r t) c -> r t c", t=D)
    kqB3 = qB_out.rearrange("(r t) c -> r t c", t=D)
    vA3 = vA_out.rearrange("(r t) c -> r t c", t=D)
    vB3 = vB_out.rearrange("(r t) c -> r t c", t=D)
    vmy4 = vmy.rearrange("(g u) c -> g u c", u=TPC)
    dyn_dma(kmy.rearrange("(r t) c -> r t c", t=256),
            (lambda E: kqA3[:, bass.ts(DV(E, "hg"), 256), :][:, 0:256, :]),
            s_bl, [(s_cc, CCK)])
    dyn_dma(vmy4[:, 0:TPC // 2, :],
            (lambda E: vA3[:, :, bass.ts(DV(E, "hg"), 260)]), s_bl,
            [(s_cc, CC_VA)])
    VMA = s_bl.v
    for hp in range(2):
        for r in range(4):
            P.dma("sync", kp[:, S * hp + TPC * r:S * hp + TPC * (r + 1)],
                  kmy[256 * r + 128 * hp:256 * r + 128 * (hp + 1), :], s_bl,
                  wait=([(s_ast, AST_ALL)] if (hp == 0 and r == 0) else []))
    LWK = [(s_bl, s_bl.v)]
    dyn_dma(qmy.rearrange("(r t) c -> r t c", t=256),
            (lambda E: kqB3[:, bass.ts(DV(E, "hg"), 256), :][:, 0:256, :]),
            s_bl, [(s_cc, CCQ)])
    dyn_dma(vmy4[:, TPC // 2:TPC, :],
            (lambda E: vB3[:, :, bass.ts(DV(E, "hg"), 260)]), s_bl,
            [(s_cc, CC_VB)])
    for hp in range(2):
        for r in range(4):
            P.dma("sync", qp[:, S * hp + TPC * r:S * hp + TPC * (r + 1)],
                  qmy[256 * r + 128 * hp:256 * r + 128 * (hp + 1), :], s_bl)
    P.dma("sync", vh.rearrange("p (t c) -> p t c", c=260),
          vmy.rearrange("(t p) c -> p t c", p=128), s_bl)
    LWB = [(s_bl, s_bl.v)]
    VHR = LWB

    s_nrm = P.sem("s_nrm")
    s_nmm = P.sem("s_nmm")
    s_sqs = P.sem("s_sqs")
    s_bias = P.sem("s_bias")
    s_smm = P.sem("s_smm")
    s_exp = P.sem("s_exp")
    s_pvm = P.sem("s_pvm")
    s_pvc = P.sem("s_pvc")
    s_v1 = P.sem("s_v1")
    s_bmm = P.sem("s_bmm")
    s_aoev = P.sem("s_aoev")
    s_ast2 = P.sem("s_ast2")

    slot_pv_, ppvA, pw_pv = pool.acquire()
    slot_pv2_, ppvB, pw_pv2 = pool.acquire()
    slot_pb_, pb1, pw_pb = pool.acquire()
    PV_FREE = pw_pv + pw_pv2 + pw_pb

    slot_exp = [None] * 4
    slot_pv = [None] * 4
    first_scores_extra = list(PV_FREE)
    nstore = 0

    for hp in range(2):
        kslab = kp[:, S * hp:S * (hp + 1)]
        qslab = qp[:, S * hp:S * (hp + 1)]
        for which, slab, acc in (("k", kslab, 4), ("q", qslab, 0)):
            P.o("scalar", lambda E, i=slab[:, :]: E.activation(
                out=qsq[:, :], in_=i, func=AF.Square),
                wait=(LWK if which == "k" else LWB) + [(s_nrm, s_nrm.v)],
                inc=s_sqs)
            for r in range(4):
                for j in range(4):
                    cg = 4 * r + j
                    w = [(s_sqs, s_sqs.v), (s_nrm, s_nrm.v)] if j == 0 else []
                    if r == 0 and j == 0:
                        for si in range(4):
                            if slot_exp[si] is not None:
                                w.append(slot_exp[si])
                    P.o("tensor", MM(pbig[0:1, TG * j:TG * (j + 1)],
                                     ones_col_bf[:, 0:1],
                                     qsq[:, TG * cg:TG * (cg + 1)], True, True),
                        wait=w, inc=(s_nmm if j == 3 else None))
                P.o("vector", lambda E, o=nstr[0:1, acc + r:acc + r + 1]:
                    E.tensor_reduce(o, pbig[0:1, 0:4 * TG], AX.X, ALU.max),
                    wait=[(s_nmm, s_nmm.v)], inc=s_nrm)
            P.o("vector", lambda E, o=nstr[0:1, 8 + (0 if which == "q" else 1):
                                           9 + (0 if which == "q" else 1)],
                i=nstr[0:1, acc:acc + 4]: E.tensor_reduce(o, i, AX.X, ALU.max),
                wait=[(s_nrm, s_nrm.v)], inc=s_nrm)
        P.o("vector", lambda E: E.tensor_mul(nstr[0:1, 10:11], nstr[0:1, 8:9],
                                             nstr[0:1, 9:10]),
            wait=[(s_nrm, s_nrm.v)], inc=s_nrm)
        P.o("scalar", lambda E: E.activation(out=nstr[0:1, 10:11], in_=nstr[0:1, 10:11],
                                             func=AF.Ln), wait=[(s_nrm, s_nrm.v)])
        P.o("scalar", lambda E: E.activation(out=nstr[0:1, 10:11], in_=nstr[0:1, 10:11],
                                             func=AF.Exp, scale=0.5), inc=s_sqs)
        P.o("vector", lambda E: E.tensor_scalar_mul(nstr[0:1, 10:11], nstr[0:1, 10:11],
                                                    -0.125),
            wait=[(s_sqs, s_sqs.v)], inc=s_nrm)
        P.o("tensor", MM(pb1[:, 0:1], ones_row_f[:, :], nstr[0:1, 10:11], True, True),
            wait=[(s_nrm, s_nrm.v), (s_aoev, s_aoev.v)] +
            (list(PV_FREE) if hp == 0 else []), inc=s_bmm)
        P.o("vector", lambda E: E.tensor_copy(bias_col[:, :], pb1[:, 0:1]),
            wait=[(s_bmm, s_bmm.v), (s_exp, s_exp.v)], inc=s_bias)

        for g in range(NQG):
            kts = 4 * g + 4

            def emit_scores(kt):
                sa = 2 * (kt % 2)
                diag = kt >= 4 * g
                j = kt - 4 * g if diag else 0
                for ih in range(2):
                    slot = sa + ih
                    w = []
                    if slot_exp[slot] is not None:
                        w.append(slot_exp[slot])
                        slot_exp[slot] = None
                    if kt == 0 and ih == 0:
                        w += [(s_bias, s_bias.v)]
                        if hp == 0 and g == 0:
                            w += first_scores_extra
                    P.o("tensor", MM(pbig[:, TG * slot:TG * (slot + 1)],
                                     kslab[64 * ih:64 * (ih + 1),
                                           128 * kt:128 * (kt + 1)],
                                     qslab[64 * ih:64 * (ih + 1),
                                           TG * g:TG * (g + 1)],
                                     True, not diag),
                        wait=w, inc=(None if diag else (s_smm if ih == 1 else None)))
                if diag:
                    for ih in range(2):
                        slot = sa + ih
                        P.o("tensor", MM(pbig[:, TG * slot + 128 * j:
                                              TG * slot + 128 * (j + 1)],
                                         idb[:, :], trib[:, :], False, True),
                            inc=(s_smm if ih == 1 else None))

            def emit_exp(kt):
                sa = 2 * (kt % 2)
                w = [(s_smm, s_smm.v), (s_bias, s_bias.v)]
                for ih in range(2):
                    if slot_pv[sa + ih] is not None:
                        w.append(slot_pv[sa + ih])
                        slot_pv[sa + ih] = None
                P.o("scalar", lambda E, o=Pt[:, TG * sa:TG * (sa + 2)],
                    i=pbig[:, TG * sa:TG * (sa + 2)]: E.activation(
                        out=o, in_=i, func=AF.Exp, scale=0.125, bias=bias_col[:, 0:1]),
                    wait=w, inc=s_exp)
                for ih in range(2):
                    slot_exp[sa + ih] = (s_exp, s_exp.v)

            def emit_pv(kt):
                sa = 2 * (kt % 2)
                ev = (s_exp, s_exp.v)
                diag = kt >= 4 * g
                j = kt - 4 * g if diag else 0
                jm = 128 * j
                for ih, pv in ((0, ppvA), (1, ppvB)):
                    hl = 2 * hp + ih
                    slot = sa + ih
                    w = []
                    if ih == 0:
                        w.append(ev)
                    if kt == 0:
                        w.append((s_pvc, s_pvc.v))
                        if hp == 0 and g == 0:
                            w += VHR
                    P.o("tensor", MM(pv[0:65, jm:TG],
                                     vh[:, 260 * kt + 65 * hl:260 * kt + 65 * (hl + 1)],
                                     Pt[:, TG * slot + jm:TG * (slot + 1)],
                                     kt == 0, kt == kts - 1),
                        wait=w, inc=(s_pvm if ih == 1 else None))
                for ih in range(2):
                    slot_pv[sa + ih] = (s_pvm, s_pvm.v)

            emit_scores(0)
            emit_exp(0)
            for kt in range(kts):
                if kt + 1 < kts:
                    emit_scores(kt + 1)
                emit_pv(kt)
                if kt + 1 < kts:
                    emit_exp(kt + 1)

            pvcv = {}
            for ih, pv in ((0, ppvA), (1, ppvB)):
                P.o("scalar", lambda E, o=pvsbB[0:64, TG * ih:TG * (ih + 1)], p=pv:
                    E.activation(out=o, in_=p[0:64, 0:TG], func=AF.Copy),
                    wait=[(s_pvm, s_pvm.v)] +
                    ([(s_aoev, s_aoev.v)] if ih == 0 else []))
                pvcv[ih] = P.o("scalar", lambda E,
                               o=pvsbB[64:65, TG * ih:TG * (ih + 1)], p=pv:
                               E.activation(out=o, in_=p[64:65, 0:TG],
                                            func=AF.Copy), inc=s_pvc)
            for ih in range(2):
                hl = 2 * hp + ih
                pvs = pvsbB[:, TG * ih:TG * (ih + 1)]
                P.o("vector", lambda E, p=pvs: E.reciprocal(rstripB[:, :],
                                                            p[64:65, :]),
                    wait=[(s_bmm, s_bmm.v), (s_pvc, pvcv[ih])], inc=s_v1)
                P.o("tensor", MM(pb1[0:64, 0:TG], ones_row_f[0:1, 0:64],
                                 rstripB[:, :], True, True),
                    wait=[(s_v1, s_v1.v), (s_bias, s_bias.v)], inc=s_bmm)
                P.o("vector", lambda E: E.tensor_copy(rbB[:, :], pb1[0:64, 0:TG]),
                    wait=[(s_bmm, s_bmm.v)])
                slot_o = aot[:, TG * (nstore % 2):TG * (nstore % 2 + 1)]
                w2 = [(s_ast2, s_ast2.v - 16)] if nstore >= 2 else []
                P.o("vector", lambda E, o=slot_o, p=pvs: E.tensor_mul(
                    o, p[0:64, :], rbB[:, :]), wait=w2, inc=s_aoev)
                P.dma("sync", ao_in[64 * hl:64 * (hl + 1), TG * g:TG * (g + 1)],
                      slot_o, s_ast2, wait=[(s_aoev, s_aoev.v)])
                nstore += 1
    pool.release(slot_pv_, s_aoev, s_aoev.v)
    pool.release(slot_pv2_, s_aoev, s_aoev.v)
    pool.release(slot_pb_, s_aoev, s_aoev.v)
    PBIG_LAST_B = [(s_exp, s_exp.v)]
    AST2_ALL = s_ast2.v
    P.pop_scope()

    P.o("gpsimd", lambda E: E.collective_compute(
        "AllGather", ALU.bypass, GROUPS, ins=[ao_in[:, :]], outs=[ao_out[:, :]]),
        wait=[(s_ast2, AST2_ALL)], inc=s_cc)
    CC3 = s_cc.v

    P.push_scope()
    x1sb = P.sb("x1sb", [128, NCH * TPC], F32)
    scrC = P.sb("c_scr", [128, TPC], BF16)
    scrC2 = P.sb("c_scr2", [128, TPC], BF16)
    P.push_scope()
    arena = P.sb("c_arena", [128, 16384], BF16)
    aob = arena[:, 0:16384]
    h2b = P.sb("c_h2", [128, NCH * TPC], BF16)
    gfm = P.sb("c_gfm", [16, TPC], BF16)
    idf = P.sb("idf", [128, 128], F32)
    gbs = P.sb("c_gbs", [128, TPC], BF16)
    oexb = P.sb("c_oexb", [128, 2 * 512], BF16)
    wrtb = P.sb("c_wrtb", [128, 8 * 128], BF16)
    csqb = P.sb("c_csqb", [128, 16], BF16)
    cnb = P.sb("c_cnb", [128, 16], BF16)
    gcol = P.sb("c_gcol", [128, 8], F32)
    gw = P.sb("c_gw", [128, 32], F32)
    tmpC = P.sb("c_tmp", [128, TG], F32)

    s_cl = P.sem("s_cl")
    s_x1 = P.sem("s_x1")
    s_wmm = P.sem("s_wmm")
    s_wol = P.sem("s_wol")
    P.dma("gpsimd", wrtb.rearrange("p (c n) -> p c n", n=128),
          wrt.rearrange("(c p) n -> p c n", p=128), s_cl, serial=False,
          wait=[(s_ast2, AST2_ALL)])
    s_idf = P.sem("s_idf")
    P.dma("gpsimd", idf[:, :], identf[:, :], s_idf, serial=False)
    dyn_dma(aomy[:, :], (lambda E: ao_out[:, bass.ts(DV(E, "hg"), TPC)]),
            s_cl, [(s_cc, CC3)])
    for c in range(NCH):
        P.dma("sync", aob[:, TPC * c:TPC * (c + 1)], aomy[128 * c:128 * (c + 1), :],
              s_cl)
    AOL = [(s_cl, s_cl.v)]

    P.push_scope()
    wos = P.sb("c_wos", [128, 2 * 1024], BF16)
    zf = P.sb("c_zf", [128, TPC], F32)
    for oc in range(NCH):
        wbuf = wos[:, 1024 * (oc % 2):1024 * (oc % 2 + 1)]
        wlv = P.dma("gpsimd", wbuf.rearrange("p (kc n) -> p kc n", n=128),
                    wo.rearrange("(kc p) n -> p kc n", p=128)
                    [:, :, 128 * oc:128 * (oc + 1)], s_wol,
                    wait=[(s_wmm, 4 * (oc - 1))] if oc >= 2
                    else [(s_ast2, AST2_ALL)])
        for tg in range(NTG):
            xsl = x1sb[:, TPC * oc + TG * tg:TPC * oc + TG * (tg + 1)]
            lv = P.dma("sync", xsl, xT[128 * oc:128 * (oc + 1), TG * tg:TG * (tg + 1)],
                       s_x1)
            slot, pt, pw = pool.acquire()
            for kc in range(NCH):
                P.o("tensor", MM(pt[:, 0:TG],
                                 wbuf[:, 128 * kc:128 * (kc + 1)],
                                 aob[:, TPC * kc + TG * tg:TPC * kc + TG * (tg + 1)],
                                 kc == 0, kc == NCH - 1),
                    wait=(list(pw) + CW + AOL + PBIG_LAST_B + [(s_wol, wlv)])
                    if kc == 0 else [],
                    inc=(s_wmm if kc == NCH - 1 else None))
            P.o("vector", lambda E, o=xsl, p=pt, b=bo_s[:, oc:oc + 1]:
                E.scalar_tensor_tensor(o, p[:, 0:TG], b, o, ALU.add, ALU.add),
                wait=[(s_wmm, s_wmm.v), (s_x1, lv)], inc=s_x1)
            pool.release(slot, s_x1, s_x1.v)
    X1R = [(s_x1, s_x1.v)]
    WMMF = s_wmm.v

    def h2_slice(c, tg):
        return h2b[:, TPC * c + TG * tg:TPC * c + TG * (tg + 1)]
    st3p = pbig[64:65, 0:TPC]
    _, _, s_app2 = emit_ln(P, "cln", pool, pbig, h2_slice,
                           ones_col_bf[:, :], ones_row_f[:, :], ones_row64,
                           CW + PBIG_LAST_B,
                           scrC, scrC2, st1, st2, st3p, rstd_b, mst_b,
                           src_sb=x1sb, src_ready=X1R, nbuf=1, sems=ln_sems)
    H2R = [(s_app2, s_app2.v)]

    s_zmm = P.sem("s_zmm")
    s_zev = P.sem("s_zev")
    s_zsq = P.sem("s_zsq")
    s_nmmC = P.sem("s_nmmC")
    s_strC = P.sem("s_strC")
    s_tmp = P.sem("s_tmp")
    s_zsc = P.sem("s_zsc")
    s_cn = P.sem("s_cn")
    for tg in range(NTG):
        slot, pt, pw = pool.acquire()
        for kc in range(NCH):
            P.o("tensor", MM(pt[:, 0:TG], wrtb[:, 128 * kc:128 * (kc + 1)],
                             h2b[:, TPC * kc + TG * tg:TPC * kc + TG * (tg + 1)],
                             kc == 0, kc == NCH - 1),
                wait=(list(pw) + H2R) if kc == 0 else [],
                inc=(s_zmm if kc == NCH - 1 else None))
        P.o("vector", lambda E, o=zf[:, TG * tg:TG * (tg + 1)], p=pt:
            E.tensor_copy(o, p[:, 0:TG]), wait=[(s_zmm, s_zmm.v)], inc=s_zev)
        pool.release(slot, s_zev, s_zev.v)
    P.o("scalar", lambda E: E.activation(out=scrC2[:, :], in_=zf[:, :], func=AF.Square),
        wait=[(s_zev, s_zev.v)], inc=s_zsq)
    for tg in range(NTG):
        P.o("tensor", MM(pbig[0:1, TG * tg:TG * (tg + 1)], ones_col_bf[:, :],
                         scrC2[:, TG * tg:TG * (tg + 1)], True, True),
            wait=[(s_zsq, s_zsq.v)] if tg == 0 else [],
            inc=(s_nmmC if tg == NTG - 1 else None))
    P.o("vector", lambda E: E.tensor_copy(st1[:, :], pbig[0:1, 0:TPC]),
        wait=[(s_nmmC, s_nmmC.v)], inc=s_strC)
    P.o("scalar", lambda E: E.activation(out=st1[:, :], in_=st1[:, :], func=AF.Ln),
        wait=[(s_strC, 1)])
    P.o("scalar", lambda E: E.activation(out=st1[:, :], in_=st1[:, :], func=AF.Exp,
                                         scale=0.5), inc=s_strC)
    P.o("vector", lambda E: E.tensor_scalar_max(st1[:, :], st1[:, :], 1e-12),
        wait=[(s_strC, 2)], inc=s_strC)
    P.o("vector", lambda E: E.reciprocal(st1[:, :], st1[:, :]),
        wait=[(s_strC, 3)], inc=s_strC)
    P.o("scalar", lambda E: E.activation(out=tstrip[0:1, 0:1], in_=tstrip[0:1, 0:1],
                                         func=AF.Exp), wait=CW, inc=s_tmp)
    P.o("vector", lambda E: E.tensor_scalar_mul(st1[:, :], st1[:, :],
                                                float(1.0 / np.sqrt(128.0))),
        wait=[(s_strC, 4), (s_tmp, 1)])
    P.o("vector", lambda E: E.tensor_single_scalar(st1[:, :], st1[:, :],
                                                   tstrip[0:1, 0:1], ALU.mult),
        inc=s_strC)
    for tg in range(NTG):
        slot, pt, pw = pool.acquire()
        P.o("tensor", MM(pt[:, 0:TG], ones_row_f[:, :], st1[0:1, TG * tg:TG * (tg + 1)],
                         True, True), wait=list(pw) + [(s_strC, 5)], inc=s_zmm)
        P.o("vector", lambda E, o=scrC[:, TG * tg:TG * (tg + 1)],
            z=zf[:, TG * tg:TG * (tg + 1)], p=pt: E.tensor_mul(o, z, p[:, 0:TG]),
            wait=[(s_zmm, s_zmm.v)], inc=s_zsc)
        pool.release(slot, s_zsc, s_zsc.v)
    ZSCF = s_zsc.v
    P.pop_scope()
    hidb = P.sb("c_hid", [128, 4 * TPC], BF16)
    P.o("scalar", lambda E: E.activation(out=csqb[:, :], in_=cent_s[:, :],
                                         func=AF.Square), wait=CW, inc=s_zsq)
    slot, pt, pw = pool.acquire()
    P.o("tensor", MM(pt[0:1, 0:16], ones_col_bf[:, :], csqb[:, :], True, True),
        wait=list(pw) + [(s_zsq, s_zsq.v)], inc=s_nmmC)
    P.o("vector", lambda E, p=pt: E.tensor_copy(gw[0:1, 0:16], p[0:1, 0:16]),
        wait=[(s_nmmC, s_nmmC.v)], inc=s_cn)
    P.o("scalar", lambda E: E.activation(out=gw[0:1, 0:16], in_=gw[0:1, 0:16],
                                         func=AF.Ln), wait=[(s_cn, s_cn.v)])
    P.o("scalar", lambda E: E.activation(out=gw[0:1, 0:16], in_=gw[0:1, 0:16],
                                         func=AF.Exp, scale=0.5), inc=s_zsq)
    P.o("vector", lambda E: E.tensor_scalar_max(gw[0:1, 0:16], gw[0:1, 0:16], 1e-12),
        wait=[(s_zsq, s_zsq.v)], inc=s_cn)
    P.o("vector", lambda E: E.reciprocal(gw[0:1, 0:16], gw[0:1, 0:16]),
        wait=[(s_cn, s_cn.v)], inc=s_cn)
    pool.release(slot, s_cn, s_cn.v)
    slot, pt, pw = pool.acquire()
    P.o("tensor", MM(pt[:, 0:16], ones_row_f[:, :], gw[0:1, 0:16], True, True),
        wait=list(pw) + [(s_cn, s_cn.v)], inc=s_nmmC)
    P.o("vector", lambda E, p=pt: E.tensor_mul(cnb[:, :], cent_s[:, :], p[:, 0:16]),
        wait=[(s_nmmC, s_nmmC.v)], inc=s_cn)
    pool.release(slot, s_cn, s_cn.v)
    CNR = (s_cn, s_cn.v)

    s_lg = P.sem("s_lg")
    s_g1 = P.sem("s_g1")
    s_g2 = P.sem("s_g2")
    s_g3 = P.sem("s_g3")
    s_trC = P.sem("s_trC")
    s_gf = P.sem("s_gf")
    for tt in range(TPC // 128):
        slot, pt, pw = pool.acquire()
        P.o("tensor", MM(pt[:, 0:16], scrC[:, 128 * tt:128 * (tt + 1)], cnb[:, :],
                         True, True),
            wait=list(pw) + ([CNR, (s_zsc, s_zsc.v)] if tt == 0 else []), inc=s_lg)
        P.o("vector", lambda E, p=pt: E.tensor_reduce(gcol[:, 0:1], p[:, 0:16], AX.X,
                                                      ALU.max),
            wait=[(s_lg, s_lg.v), (s_g3, s_g3.v)], inc=s_g1)
        P.o("vector", lambda E: E.tensor_scalar_mul(gcol[:, 1:2], gcol[:, 0:1], -1.0),
            wait=[(s_g1, s_g1.v)], inc=s_g1)
        P.o("scalar", lambda E, p=pt: E.activation(out=gw[:, 0:16], in_=p[:, 0:16],
                                                   func=AF.Exp, bias=gcol[:, 1:2]),
            wait=[(s_g1, s_g1.v)], inc=s_g2)
        pool.release(slot, s_g2, s_g2.v)
        P.o("vector", lambda E: E.tensor_reduce(gcol[:, 2:3], gw[:, 0:16], AX.X,
                                                ALU.add),
            wait=[(s_g2, s_g2.v)], inc=s_g1)
        P.o("vector", lambda E: E.reciprocal(gcol[:, 3:4], gcol[:, 2:3]),
            wait=[(s_g1, s_g1.v)], inc=s_g1)
        P.o("vector", lambda E: E.tensor_reduce(gcol[:, 4:5], gw[:, 0:16], AX.X,
                                                ALU.max),
            wait=[(s_g1, s_g1.v)], inc=s_g1)
        P.o("vector", lambda E: E.tensor_single_scalar(gw[:, 16:32], gw[:, 0:16],
                                                       gcol[:, 4:5], ALU.is_equal),
            wait=[(s_g1, s_g1.v)], inc=s_g1)
        P.o("vector", lambda E: E.scalar_tensor_tensor(gw[:, 16:32], gw[:, 16:32],
                                                       -2.0, gw[:, 0:16], ALU.mult,
                                                       ALU.add),
            wait=[(s_g1, s_g1.v)], inc=s_g1)
        P.o("vector", lambda E: E.tensor_reduce(gcol[:, 5:6], gw[:, 16:32], AX.X,
                                                ALU.max), wait=[(s_g1, s_g1.v)],
            inc=s_g1)
        P.o("vector", lambda E: E.tensor_single_scalar(gw[:, 16:32], gw[:, 0:16],
                                                       gcol[:, 5:6], ALU.is_ge),
            wait=[(s_g1, s_g1.v)], inc=s_g1)
        P.o("vector", lambda E: E.tensor_mul(gw[:, 16:32], gw[:, 16:32], gw[:, 0:16]),
            wait=[(s_g1, s_g1.v)], inc=s_g1)
        P.o("vector", lambda E: E.tensor_single_scalar(gw[:, 16:32], gw[:, 16:32],
                                                       gcol[:, 3:4], ALU.mult),
            wait=[(s_g1, s_g1.v)], inc=s_g3)
        slot2, pt2, pw2 = pool.acquire()
        P.o("tensor", lambda E, o=pt2[0:16, 0:128]: E.transpose(o, gw[:, 16:32],
                                                                idf[:, :]),
            wait=list(pw2) + [(s_g3, s_g3.v), (s_idf, 16)], inc=s_trC)
        P.o("vector", lambda E, o=gfm[:, 128 * tt:128 * (tt + 1)], p=pt2:
            E.tensor_copy(o, p[0:16, 0:128]), wait=[(s_trC, s_trC.v)], inc=s_gf)
        pool.release(slot2, s_gf, s_gf.v)
    GFR = [(s_gf, s_gf.v)]

    s_ie = P.sem("s_ie")
    s_oe = P.sem("s_oe")
    s_gbe = P.sem("s_gbe")
    s_hmm = P.sem("s_hmm")
    s_hev = P.sem("s_hev")
    s_omm = P.sem("s_omm")
    s_moev = P.sem("s_moev")
    gelu_val = {}
    for c in range(8):
        buf = c % 2
        iex = [arena[:, 8192 * buf + 4096 * le:8192 * buf + 4096 * (le + 1)]
               for le in range(2)]
        oex = oexb[:, 512 * buf:512 * (buf + 1)]
        wld = ([(s_hmm, 32 * (c - 1))] if c >= 2 else []) + \
            ([(s_wmm, WMMF)] if c < 2 else [])
        iev = []
        for le in range(2):
            iev.append(P.dma("gpsimd", iex[le].rearrange("p (kc n) -> p kc n", n=512),
                             iexp[2 * c + le].rearrange("(kc p) n -> p kc n", p=128),
                             s_ie, wait=wld if le == 0 else []))
        ow = [(s_omm, 8 * (c - 1))] if c >= 2 else [(s_wmm, WMMF)]
        oev = P.dma("gpsimd", oex.rearrange("p (e hc n) -> p e hc n", hc=4, n=64),
                    oexp.rearrange("e (hc p) n -> p e hc n", p=128)
                    [:, 2 * c:2 * (c + 1), :, :], s_oe, wait=ow)
        for tg in range(NTG):
            slot, pt, pw = pool.acquire()
            P.o("tensor", MM(pt[:, 0:TG], selcs[:, 128 * c:128 * (c + 1)],
                             gfm[:, TG * tg:TG * (tg + 1)], True, True),
                wait=list(pw) + (GFR if (c == 0 and tg == 0) else []), inc=s_zmm)
            P.o("vector", lambda E, o=gbs[:, TG * tg:TG * (tg + 1)], p=pt:
                E.tensor_copy(o, p[:, 0:TG]),
                wait=[(s_zmm, s_zmm.v)] + ([(s_moev, 4 * c)] if c >= 1 and tg == 0
                                           else []), inc=s_gbe)
            pool.release(slot, s_gbe, s_gbe.v)
        gbv = s_gbe.v
        for le in range(2):
            E_idx = 2 * c + le
            for hc in range(4):
                for tg in range(NTG):
                    slot, pt, pw = pool.acquire()
                    w = list(pw)
                    if hc == 0 and tg == 0:
                        w += [(s_ie, iev[le])] + (H2R if c == 0 and le == 0 else [])
                    for kc in range(NCH):
                        P.o("tensor", MM(pt[:, 0:TG],
                                         iex[le][:, 512 * kc + 128 * hc:
                                                 512 * kc + 128 * (hc + 1)],
                                         h2b[:, TPC * kc + TG * tg:
                                             TPC * kc + TG * (tg + 1)],
                                         kc == 0, kc == NCH - 1),
                            wait=w if kc == 0 else [],
                            inc=(s_hmm if kc == NCH - 1 else None))
                    hw = [(s_hmm, s_hmm.v)]
                    if hc == 0 and tg == 0:
                        hw.append((s_omm, 4 * E_idx))
                        if c == 0 and le == 0:
                            hw.append((s_zsc, ZSCF))
                    P.o("scalar", lambda E, o=hidb[:, TPC * hc + TG * tg:
                                                   TPC * hc + TG * (tg + 1)], p=pt:
                        E.activation(out=o, in_=p[:, 0:TG], func=AF.Gelu),
                        wait=hw, inc=s_hev)
                    gelu_val[(hc, tg)] = s_hev.v
                    pool.release(slot, s_hev, s_hev.v)
            for tg in range(NTG):
                for hc in range(4):
                    w = [(s_hev, gelu_val[(hc, tg)])]
                    if hc == 0:
                        w.append((s_oe, oev))
                        if le == 0 and c >= 1:
                            w.append((s_moev, 4 * (c - 1)))
                    P.o("tensor", lambda E, o=pbig[64 * le:64 * (le + 1),
                                                   TG * tg:TG * (tg + 1)],
                        l=oex.rearrange("p (e hc n) -> p e hc n", hc=4, n=64)
                        [:, le, hc, :],
                        r=hidb[:, TPC * hc + TG * tg:TPC * hc + TG * (tg + 1)],
                        st=(hc == 0), sp=(hc == 3),
                        tp=(0, 64 * le): E.matmul(o, l, r, start=st, stop=sp,
                                                  tile_position=tp,
                                                  skip_group_check=True),
                        wait=w, inc=(s_omm if hc == 3 else None))
        for tg in range(NTG):
            P.o("vector", lambda E, t=tmpC, p=pbig[:, TG * tg:TG * (tg + 1)],
                g=gbs[:, TG * tg:TG * (tg + 1)]: E.tensor_mul(t[:, :], p, g),
                wait=[(s_omm, s_omm.v), (s_gbe, gbv)])
            P.o("vector", lambda E, o=x1sb[:, TPC * c + TG * tg:
                                           TPC * c + TG * (tg + 1)], t=tmpC:
                E.tensor_add(o, o, t[:, :]), inc=s_moev)
    X2R = [(s_moev, s_moev.v)]
    PBIG_LAST_C = [(s_moev, s_moev.v)]
    if FDBG:
        s_dx2 = P.sem("s_dx2")
        for c in range(NCH):
            P.dma("sync", dx2[128 * c:128 * (c + 1), :],
                  x1sb[:, TPC * c:TPC * (c + 1)], s_dx2, wait=X2R)
    P.pop_scope()

    P.push_scope()
    h3b = P.sb("d_h3", [128, NCH * TPC], BF16)
    wqb2 = P.sb("d_wq", [128, 8 * D], BF16)
    wkvb2 = P.sb("d_wkv", [128, 8 * 2048], BF16)
    qch2 = P.sb("d_qch", [128, 2 * TPC], BF16)
    ktm = P.sb("d_ktm", [128, 2 * 1024], BF16)
    vtm = P.sb("d_vtm", [128, 2 * 1024], BF16)
    ctxsb = P.sb("d_ctx", [65, H * 65], F32)
    scrD = scrC
    scrD2 = scrC2

    s_dw = s_aw
    P.dma("gpsimd", wqb2.rearrange("p (kc n) -> p kc n", n=D),
          wqkvT.rearrange("(kc p) n -> p kc n", p=128)[:, :, 0:D], s_dw, serial=False,
          wait=X2R)
    P.dma("gpsimd", wkvb2.rearrange("p (kc n) -> p kc n", n=2048),
          wqkvT.rearrange("(kc p) n -> p kc n", p=128)[:, :, D:3 * D], s_dw,
          serial=False, wait=X2R)
    dwv = [(s_dw, s_dw.v)]

    def h3_slice(c, tg):
        return h3b[:, TPC * c + TG * tg:TPC * c + TG * (tg + 1)]
    _, _, s_app3 = emit_ln(P, "dln", pool, pbig, h3_slice,
                           ones_col_bf[:, :], ones_row_f[:, :], ones_row64,
                           CW + PBIG_LAST_C + dwv,
                           scrD, scrD2, st1, st2, st3p, rstd_b, mst_b,
                           src_sb=x1sb, src_ready=X2R, nbuf=1, sems=ln_sems)
    H3R = [(s_app3, s_app3.v)]

    s_qmm = P.sem("s_qmm")
    s_qg = P.sem("s_qg")
    s_stq2 = P.sem("s_stq2")
    for oc in range(NCH):
        buf = oc % 2
        dst = qch2[:, TPC * buf:TPC * (buf + 1)]
        for tg in range(NTG):
            slot, pt, pw = pool.acquire()
            w = list(pw) + (H3R if (oc == 0 and tg == 0) else [])
            if tg == 0 and oc >= 2:
                w.append((s_stq2, s_stq2.v - 16))
            for kc in range(NCH):
                P.o("tensor", MM(pt[:, 0:TG],
                                 wqb2[:, D * kc + 128 * oc:D * kc + 128 * (oc + 1)],
                                 h3b[:, TPC * kc + TG * tg:TPC * kc + TG * (tg + 1)],
                                 kc == 0, kc == NCH - 1),
                    wait=w if kc == 0 else [], inc=(s_qmm if kc == NCH - 1 else None))
            P.o("scalar", lambda E, o=dst[:, TG * tg:TG * (tg + 1)], p=pt:
                E.activation(out=o, in_=p[:, 0:TG], func=AF.Relu),
                wait=[(s_qmm, s_qmm.v)], inc=s_qg)
            pool.release(slot, s_qg, s_qg.v)
        P.dma("sync", q2_in[128 * oc:128 * (oc + 1), :], dst, s_stq2,
              wait=[(s_qg, s_qg.v)])

    s_kvm = P.sem("s_kvm")
    s_kve = P.sem("s_kve")
    s_cmm = P.sem("s_cmm")
    pcD = [pbig[0:64, 0:7 * 65], pbig[0:64, TG:TG + 7 * 65],
           pbig[0:64, 2 * TG:2 * TG + 2 * 65]]
    NT = TPC // 128
    for tt in range(NT):
        buf = tt % 2
        kslot = ktm[:, 1024 * buf:1024 * (buf + 1)]
        vslot = vtm[:, 1024 * buf:1024 * (buf + 1)]
        for pi, dst in ((0, kslot), (1, vslot)):
            for vc in range(2):
                slot, pt, pw = pool.acquire()
                w = list(pw)
                if pi == 0 and vc == 0:
                    if tt >= 2:
                        w.append((s_cmm, tt - 1))
                    if tt == 0:
                        w += H3R
                for kc in range(NCH):
                    P.o("tensor", MM(pt[:, 0:TG],
                                     h3b[:, TPC * kc + 128 * tt:TPC * kc + 128 * (tt + 1)],
                                     wkvb2[:, 2048 * kc + 1024 * pi + TG * vc:
                                           2048 * kc + 1024 * pi + TG * (vc + 1)],
                                     kc == 0, kc == NCH - 1),
                        wait=w if kc == 0 else [],
                        inc=(s_kvm if kc == NCH - 1 else None))
                if pi == 0:
                    P.o("vector", lambda E, o=dst[:, TG * vc:TG * (vc + 1)], p=pt:
                        E.tensor_scalar_max(o, p[:, 0:TG], 0.0),
                        wait=[(s_kvm, s_kvm.v)], inc=s_kve)
                else:
                    P.o("vector", lambda E, o=dst[:, TG * vc:TG * (vc + 1)], p=pt:
                        E.tensor_copy(o, p[:, 0:TG]),
                        wait=[(s_kvm, s_kvm.v)], inc=s_kve)
                pool.release(slot, s_kve, s_kve.v)
        kvv = s_kve.v
        for h in range(H):
            bank, off = (0, h) if h < 7 else ((1, h - 7) if h < 14 else (2, h - 14))
            P.o("tensor", lambda E, o=pcD[bank][:, 65 * off:65 * off + 64],
                l=kslot[:, 64 * h:64 * (h + 1)], r=vslot[:, 64 * h:64 * (h + 1)],
                st=(tt == 0), sp=(tt == NT - 1): E.matmul(
                    o, l, r, start=st, stop=sp, skip_group_check=True),
                wait=([(s_kve, kvv)] if h == 0 else []))
            P.o("tensor", lambda E, o=pcD[bank][:, 65 * off + 64:65 * off + 65],
                l=kslot[:, 64 * h:64 * (h + 1)],
                st=(tt == 0), sp=(tt == NT - 1): E.matmul(
                    o, l, ones_col_bf[:, 0:1], start=st, stop=sp,
                    skip_group_check=True),
                inc=(s_cmm if h == H - 1 else None))
    s_cev = P.sem("s_cev")
    s_stc = P.sem("s_stc")
    P.o("vector", lambda E: E.memset(ctxsb[64:65, :], 0.0), inc=s_cev)
    for h in range(H):
        bank, off = (0, h) if h < 7 else ((1, h - 7) if h < 14 else (2, h - 14))
        P.o("vector", lambda E, o=ctxsb[0:64, 65 * h:65 * (h + 1)],
            p=pcD[bank][:, 65 * off:65 * (off + 1)]: E.tensor_copy(o, p),
            wait=[(s_cmm, s_cmm.v)] if h == 0 else [], inc=s_cev)
    P.dma("sync", ctx_in[:, :], ctxsb[:, :], s_stc, wait=[(s_cev, s_cev.v)])
    PBIG_LAST_D = [(s_cev, s_cev.v)]
    STQ2 = s_stq2.v
    STC = s_stc.v
    P.pop_scope()

    P.o("gpsimd", lambda E: E.collective_compute(
        "AllGather", ALU.bypass, GROUPS, ins=[q2_in[:, :]], outs=[q2_out[:, :]]),
        wait=[(s_stq2, STQ2)], inc=s_cc)
    P.o("gpsimd", lambda E: E.collective_compute(
        "AllReduce", ALU.add, GROUPS, ins=[ctx_in[:, :]], outs=[ctx_out[:, :]]),
        wait=[(s_stc, STC)], inc=s_cc)
    CC45 = s_cc.v

    P.push_scope()
    qp2 = P.sb("e_qp", [128, 2 * S], BF16)
    ctxf = P.sb("e_ctxf", [65, 4 * 65], F32)
    ctxb2 = P.sb("e_ctxb", [128, 4 * 65], BF16)
    aot2 = P.sb("e_aot", [64, 2 * TG], BF16)
    pvsbE = P.sb("e_pvsb", [64, 2 * TG], F32)
    drowE = P.sb("e_drow", [1, 2 * 4 * TG], F32)
    dcolE = P.sb("e_dcol", [4, 2 * TG], F32)
    rcolE = P.sb("e_rcol", [4, 2 * TG], F32)
    rrowE = P.sb("e_rrow", [1, 2 * 4 * TG], F32)

    s_el = s_bl
    dyn_dma(ctxmy[:, :], lambda E: ctx_out[:, bass.ts(DV(E, "hg"), 4 * 65)],
            s_el, [(s_cc, CC45)])
    q23 = q2_out.rearrange("(r t) c -> r t c", t=D)
    dyn_dma(q2my.rearrange("(r t) c -> r t c", t=256),
            (lambda E: q23[:, bass.ts(DV(E, "hg"), 256), :][:, 0:256, :]),
            s_el, [])
    P.dma("sync", ctxf[:, :], ctxmy[:, :], s_el)
    for hp in range(2):
        for r in range(4):
            P.dma("sync", qp2[:, S * hp + TPC * r:S * hp + TPC * (r + 1)],
                  q2my[256 * r + 128 * hp:256 * r + 128 * (hp + 1), :], s_el)
    ELW = [(s_el, s_el.v)]
    s_sumE = s_cn
    P.o("vector", lambda E: E.tensor_copy(ctxb2[0:65, :], ctxf[:, :]),
        wait=ELW, inc=s_sumE)
    P.dma("sync", ctxb2[64:128, :], ctxb2[0:64, :], s_sumE,
          wait=[(s_sumE, s_sumE.v)])
    CTXR = [(s_sumE, s_sumE.v)]

    s_emm = P.sem("s_emm")
    s_ev1 = P.sem("s_ev1")
    s_ebm = P.sem("s_ebm")
    s_eev = P.sem("s_eev")
    s_ast3 = P.sem("s_ast3")
    s_ecp2 = P.sem("s_ecp2")
    s_edm = P.sem("s_edm")
    s_ecpd = P.sem("s_ecpd")
    s_edma = P.sem("s_edma")
    s_erc = P.sem("s_erc")
    slots_e = [pool.acquire() for _ in range(4)]
    ppv2L = [slots_e[0][1], slots_e[1][1]]
    pb2L = [slots_e[2][1], slots_e[3][1]]
    EPF = [w for sl in slots_e for w in sl[2]]
    NQ = S // TG

    def emit_dchain(m):
        par = m % 2
        pp = 64 * par
        hl_, ch = m // 4, m % 4
        hp_ = hl_ // 2
        base_ = 64 * (hl_ % 2)
        w = [(s_ecpd, m - 1)] if m >= 2 else (list(CTXR) + PBIG_LAST_D)
        for q4 in range(4):
            P.o("tensor", lambda E, o=pbig[pp:pp + 1, TG * q4:TG * (q4 + 1)],
                l=ctxb2[base_:base_ + 64, 65 * hl_ + 64:65 * hl_ + 65],
                r=qp2[base_:base_ + 64, S * hp_ + 4 * TG * ch + TG * q4:
                      S * hp_ + 4 * TG * ch + TG * (q4 + 1)]:
                E.matmul(o, l, r, start=True, stop=True, skip_group_check=True),
                wait=w if q4 == 0 else [], inc=(s_edm if q4 == 3 else None))
        P.o("scalar", lambda E, o=drowE[0:1, 4 * TG * par:4 * TG * (par + 1)],
            i=pbig[pp:pp + 1, 0:4 * TG]: E.activation(
                out=o, in_=i, func=AF.Copy, bias=EPS_F32),
            wait=[(s_edm, m + 1)] +
            ([(s_edma, 128 * (m - 2) + 64)] if m >= 2 else []),
            inc=s_ecpd)
        for j in range(4):
            P.dma("sync", dcolE[j:j + 1, TG * par:TG * (par + 1)],
                  drowE[0:1, 4 * TG * par + TG * j:4 * TG * par + TG * (j + 1)],
                  s_edma,
                  wait=([(s_ecpd, m + 1)] + ([(s_erc, m - 1)] if m >= 2 else []))
                  if j == 0 else [])
        P.o("vector", lambda E, o=rcolE[:, TG * par:TG * (par + 1)],
            i=dcolE[:, TG * par:TG * (par + 1)]: E.reciprocal(o, i),
            wait=[(s_edma, 128 * m + 64)], inc=s_erc)
        for j in range(4):
            P.dma("sync", rrowE[0:1, 4 * TG * par + TG * j:4 * TG * par + TG * (j + 1)],
                  rcolE[j:j + 1, TG * par:TG * (par + 1)], s_edma,
                  wait=([(s_erc, m + 1)] + ([(s_ebm, 4 * (m - 1))] if m >= 2 else []))
                  if j == 0 else [])

    emit_dchain(0)
    for hl in range(4):
        hp, base = hl // 2, 64 * (hl % 2)
        for g in range(NQ):
            n = NQ * hl + g
            m = n // 4
            if n % 4 == 0 and m + 1 < 16:
                emit_dchain(m + 1)
            buf = n % 2
            j = n % 4
            par = m % 2
            ppv2 = ppv2L[buf]
            pb2 = pb2L[buf]
            pvb = pvsbE[:, TG * buf:TG * (buf + 1)]
            w = (list(CTXR) + EPF) if n < 2 else [(s_ecp2, n - 1)]
            P.o("tensor", lambda E, o=ppv2[0:64, 0:TG],
                l=ctxb2[base:base + 64, 65 * hl:65 * hl + 64],
                r=qp2[base:base + 64, S * hp + TG * g:S * hp + TG * (g + 1)]:
                E.matmul(o, l, r, start=True, stop=True, skip_group_check=True),
                wait=w, inc=s_emm)
            P.o("scalar", lambda E, o=pvb, i=ppv2[0:64, 0:TG]: E.activation(
                out=o, in_=i, func=AF.Copy),
                wait=[(s_emm, n + 1)] + ([(s_eev, n - 1)] if n >= 2 else []),
                inc=s_ecp2)
            P.o("tensor", MM(pb2[0:64, 0:TG], ones_row_f[0:1, 0:64],
                             rrowE[0:1, 4 * TG * par + TG * j:
                                   4 * TG * par + TG * (j + 1)], True, True),
                wait=[(s_edma, 128 * (m + 1))] +
                ([(s_eev, n - 1)] if n >= 2 else []),
                inc=s_ebm)
            slot_o = aot2[:, TG * (n % 2):TG * (n % 2 + 1)]
            w2 = [(s_ast3, s_ast3.v - 16)] if n >= 2 else []
            P.o("vector", lambda E, o=slot_o, p=pvb, b=pb2: E.tensor_mul(
                o, p, b[0:64, 0:TG]),
                wait=[(s_ecp2, n + 1), (s_ebm, n + 1)] + w2, inc=s_eev)
            P.dma("sync", ao2_in[64 * hl:64 * (hl + 1), TG * g:TG * (g + 1)], slot_o,
                  s_ast3, wait=[(s_eev, s_eev.v)])
    for sl in slots_e:
        pool.release(sl[0], s_eev, s_eev.v)
    AST3 = s_ast3.v
    P.pop_scope()

    P.o("gpsimd", lambda E: E.collective_compute(
        "AllGather", ALU.bypass, GROUPS, ins=[ao2_in[:, :]], outs=[ao2_out[:, :]]),
        wait=[(s_ast3, AST3)], inc=s_cc)
    CC6 = s_cc.v

    x3d = nc.dram_tensor("x3d", [D, TPC], F32).ap()
    P.push_scope()
    ao2b = P.sb("f_ao2", [128, NCH * TPC], BF16)
    wos2 = P.sb("f_wos", [128, 2 * 1024], BF16)
    s_fl = s_bl
    dyn_dma(ao2my[:, :], (lambda E: ao2_out[:, bass.ts(DV(E, "hg"), TPC)]),
            s_fl, [(s_cc, CC6)])
    for c in range(NCH):
        P.dma("sync", ao2b[:, TPC * c:TPC * (c + 1)], ao2my[128 * c:128 * (c + 1), :],
              s_fl)
    AO2L = [(s_fl, s_fl.v)]
    s_pm = P.sem("s_pm")
    s_pe = P.sem("s_pe")
    s_st3 = P.sem("s_st3")
    s_wol2 = s_wol
    for oc in range(NCH):
        wbuf = wos2[:, 1024 * (oc % 2):1024 * (oc % 2 + 1)]
        wlv = P.dma("gpsimd", wbuf.rearrange("p (kc n) -> p kc n", n=128),
                    wpT.rearrange("(kc p) n -> p kc n", p=128)
                    [:, :, 128 * oc:128 * (oc + 1)], s_wol2,
                    wait=[(s_pm, 4 * (oc - 1))] if oc >= 2
                    else [(s_ast3, AST3)])
        for tg in range(NTG):
            slot, pt, pw = pool.acquire()
            for kc in range(NCH):
                P.o("tensor", MM(pt[:, 0:TG],
                                 wbuf[:, 128 * kc:128 * (kc + 1)],
                                 ao2b[:, TPC * kc + TG * tg:TPC * kc + TG * (tg + 1)],
                                 kc == 0, kc == NCH - 1),
                    wait=(list(pw) + AO2L + [(s_wol2, wlv)] + PBIG_LAST_D)
                    if kc == 0 else [],
                    inc=(s_pm if kc == NCH - 1 else None))
            P.o("vector", lambda E, o=x1sb[:, TPC * oc + TG * tg:
                                           TPC * oc + TG * (tg + 1)], p=pt,
                b=bp_s[:, oc:oc + 1]: E.scalar_tensor_tensor(
                    o, p[:, 0:TG], b, o, ALU.add, ALU.add),
                wait=[(s_pm, s_pm.v)] + (X2R if oc == 0 and tg == 0 else []),
                inc=s_pe)
            pool.release(slot, s_pe, s_pe.v)
        P.dma("sync", x3d[128 * oc:128 * (oc + 1), :], x1sb[:, TPC * oc:TPC * (oc + 1)],
              s_st3, wait=[(s_pe, s_pe.v)])
    X3ALL = [(s_st3, s_st3.v)]
    P.pop_scope()
    P.pop_scope()

    P.push_scope()
    xcF = P.sb("f_xc", [128, 2 * TPC], F32)
    scrF = P.sb("f_scr", [128, TPC], BF16)
    scrF2 = P.sb("f_scr2", [128, TPC], BF16)
    str_ = P.sb("f_str", [1, TPC], F32)
    sts = P.sb("f_sts", [1, TPC], F32)
    hhy = P.sb("f_hhy", [128, NCH * TPC], BF16)
    srowF = P.sb("f_srow", [1, 2 * D], F32)
    qsb = P.sb("f_qsb", [128, 2 * D], U8)
    w1c = P.sb("f_w1c", [128, 2 * D], BF16)
    y1c = P.sb("f_y1c", [128, 2 * TPC], BF16)
    w2b = P.sb("f_w2b", [128, 32 * D], BF16)
    tmpvF = P.sb("f_tmpv", [128, TG], F32)
    tmpx3 = P.sb("f_tmpx3", [128, 2 * TG], F32)
    stm = st1
    stv = st2

    s_misc = s_tmp
    b2hv = P.o("vector", lambda E: E.tensor_scalar_mul(b2h_s[:, :], b2_s[:, :], 0.5),
               wait=CW, inc=s_misc)

    s_castF, s_sqF, s_mmF, s_strF, _, s_appF, s_xsF = ln_sems
    castF0, sqF0, mmF0, appF0 = s_castF.v, s_sqF.v, s_mmF.v, s_appF.v
    for c in range(NCH):
        buf = c % 2
        ap = xcF[:, TPC * buf:TPC * (buf + 1)]
        w = list(X3ALL) + [(s_xsF, s_xsF.v)]
        if c >= 2:
            w.append((s_castF, castF0 + c - 1))
        lv = P.dma("sync", ap, x3d[128 * c:128 * (c + 1), :], s_xsF, wait=w)
        wv = [(s_xsF, lv)]
        if c >= 1:
            wv.append((s_mmF, mmF0 + c))
        P.o("vector", lambda E, o=scrF[:, :], i=ap: E.tensor_copy(o, i),
            wait=wv, inc=s_castF)
        P.o("scalar", lambda E: E.activation(out=scrF2[:, :], in_=scrF[:, :],
                                             func=AF.Square),
            wait=[(s_castF, castF0 + c + 1)] + ([(s_mmF, mmF0 + c)] if c >= 1 else []),
            inc=s_sqF)
        for tg in range(NTG):
            P.o("tensor", MM(pbig[0:1, TG * tg:TG * (tg + 1)], ones_col_bf[:, :],
                             scrF[:, TG * tg:TG * (tg + 1)], c == 0, c == NCH - 1),
                wait=([(s_sqF, sqF0 + c + 1)] + (PBIG_LAST_D if c == 0 else []))
                if tg == 0 else [])
            P.o("tensor", MM(pbig[32:33, TG * tg:TG * (tg + 1)], ones_col_bf[:, :],
                             scrF2[:, TG * tg:TG * (tg + 1)], c == 0, c == NCH - 1),
                inc=(s_mmF if tg == NTG - 1 else None))
    st3pF = pbig[64:65, 0:TPC]
    P.o("vector", lambda E: E.tensor_scalar_mul(stm[:, :], pbig[0:1, 0:TPC], 1.0 / D),
        wait=[(s_mmF, s_mmF.v)])
    P.o("vector", lambda E: E.tensor_scalar_mul(stv[:, :], pbig[32:33, 0:TPC], 1.0 / D),
        inc=s_strF)
    P.o("scalar", lambda E: E.activation(out=st3pF, in_=stm[:, :], func=AF.Square),
        wait=[(s_strF, s_strF.v)], inc=s_strF)
    P.o("vector", lambda E: E.tensor_sub(stv[:, :], stv[:, :], st3pF),
        wait=[(s_strF, s_strF.v)])
    P.o("vector", lambda E: E.tensor_scalar_add(str_[:, :], stv[:, :], EPS_LN),
        inc=s_strF)
    P.o("scalar", lambda E: E.activation(out=str_[:, :], in_=str_[:, :], func=AF.Ln),
        wait=[(s_strF, s_strF.v)])
    P.o("scalar", lambda E: E.activation(out=str_[:, :], in_=str_[:, :], func=AF.Exp,
                                         scale=-0.5), inc=s_strF)
    P.o("vector", lambda E: E.tensor_scalar_add(sts[:, :], stv[:, :], EPS_LN),
        wait=[(s_strF, s_strF.v)], inc=s_strF)
    P.o("vector", lambda E: E.reciprocal(sts[:, :], sts[:, :]),
        wait=[(s_strF, s_strF.v)], inc=s_strF)
    P.o("vector", lambda E: E.tensor_mul(sts[:, :], sts[:, :], stv[:, :]),
        wait=[(s_strF, s_strF.v)], inc=s_strF)
    P.o("vector", lambda E: E.tensor_scalar_add(sts[:, :], sts[:, :], EPS_LN),
        inc=s_strF)
    P.o("scalar", lambda E: E.activation(out=sts[:, :], in_=sts[:, :], func=AF.Ln),
        wait=[(s_strF, s_strF.v)])
    P.o("scalar", lambda E: E.activation(out=sts[:, :], in_=sts[:, :], func=AF.Exp,
                                         scale=-0.5), inc=s_strF)
    P.o("vector", lambda E: E.tensor_mul(sts[:, :], sts[:, :], str_[:, :]),
        wait=[(s_strF, s_strF.v)])
    P.o("vector", lambda E: E.tensor_mul(stv[:, :], stm[:, :], sts[:, :]))
    P.o("vector", lambda E: E.tensor_scalar_mul(stv[:, :], stv[:, :], -1.0))
    P.o("vector", lambda E: E.tensor_mul(stm[:, :], stm[:, :], str_[:, :]))
    P.o("vector", lambda E: E.tensor_scalar_mul(stm[:, :], stm[:, :], -1.0))
    STRF = P.sem("f_strfin")
    P.o("vector", lambda E: E.tensor_scalar_add(str_[:, :], str_[:, :], 1.0),
        inc=STRF)

    s_bcF = ln_sems[4]
    for tg in range(NTG):
        for strip, orow, bdst in ((sts, ones_row_f[:, :], rstd_b),
                                  (stv, ones_row64, mst_b)):
            slot, pt, pw = pool.acquire()
            vmm = P.o("tensor", MM(pt[:, 0:TG], orow,
                                   strip[0:1, TG * tg:TG * (tg + 1)], True, True),
                      wait=list(pw) + [(STRF, 1)], inc=s_bcF)
            P.o("scalar", lambda E, o=bdst[:, TG * tg:TG * (tg + 1)], p=pt:
                E.activation(out=o, in_=p[:, 0:TG], func=AF.Copy),
                wait=[(s_bcF, vmm)], inc=s_bcF)
            pool.release(slot, s_bcF, s_bcF.v)
    bcFv = s_bcF.v
    for c in range(NCH):
        buf = c % 2
        ap = xcF[:, TPC * buf:TPC * (buf + 1)]
        w = [(s_mmF, mmF0 + NCH), (s_xsF, s_xsF.v)]
        if c >= 2:
            w.append((s_appF, appF0 + 4 * (c - 1)))
        lv = P.dma("sync", ap, x3d[128 * c:128 * (c + 1), :], s_xsF, wait=w)
        for tg in range(NTG):
            P.o("vector", lambda E, t=tmpvF, x=ap[:, TG * tg:TG * (tg + 1)],
                r=rstd_b[:, TG * tg:TG * (tg + 1)]: E.tensor_mul(t[:, :], x, r),
                wait=[(s_bcF, bcFv), (s_xsF, lv)])
            P.o("vector", lambda E, t=tmpvF, m=mst_b[:, TG * tg:TG * (tg + 1)]:
                E.tensor_add(t[:, :], t[:, :], m))
            P.o("vector", lambda E, o=hhy[:, TPC * c + TG * tg:TPC * c + TG * (tg + 1)],
                t=tmpvF, g=gam_s[:, c:c + 1], b=bet_s[:, c:c + 1]:
                E.tensor_scalar(o, t[:, :], g, b, ALU.mult, ALU.add), inc=s_appF)
    HHR = [(s_appF, s_appF.v)]

    s_q1 = P.sem("f_q1")
    s_dq = P.sem("f_dq")
    s_fmm = P.sem("f_fmm")
    s_sil = P.sem("f_sil")
    s_sty = P.sem("f_sty")
    s_sbm = P.sem("f_sbm")
    for yc in range(32):
        buf = yc % 2
        qb = qsb[:, D * buf:D * (buf + 1)]
        sr = srowF[0:1, D * buf:D * (buf + 1)]
        wc = w1c[:, D * buf:D * (buf + 1)]
        yb = y1c[:, TPC * buf:TPC * (buf + 1)]
        w = ([(s_dq, 2 * (yc - 1))] if yc >= 2 else list(X3ALL)) + [(s_q1, s_q1.v)]
        l1 = P.dma("sync", qb.rearrange("p (kc n) -> p kc n", n=128),
                   fc1q.rearrange("(kc p) n -> p kc n", p=128)
                   [:, :, 128 * yc:128 * (yc + 1)], s_q1, wait=w)
        l2 = P.dma("gpsimd", sr, sc1x[yc:yc + 1, :], s_q1)
        for hb in range(2):
            slot_s, pts, pws = pool.acquire()
            P.o("tensor", MM(pts[:, 0:TG], ones_row_f[0:1, :],
                             sr[0:1, TG * hb:TG * (hb + 1)], True, True),
                wait=list(pws) + ([(s_q1, l2)] if hb == 0 else []), inc=s_sbm)
            P.o("vector", lambda E, o=wc[:, TG * hb:TG * (hb + 1)],
                q=qb[:, TG * hb:TG * (hb + 1)], s=pts: E.scalar_tensor_tensor(
                    o, q, -128.0, s[:, 0:TG], ALU.add, ALU.mult),
                wait=[(s_sbm, s_sbm.v)] + ([(s_fmm, 4 * (yc - 1))]
                                           if (yc >= 2 and hb == 0) else []),
                inc=s_dq)
            pool.release(slot_s, s_dq, s_dq.v)
        for tg in range(NTG):
            slot, pt, pw = pool.acquire()
            for kc in range(NCH):
                P.o("tensor", MM(pt[:, 0:TG], wc[:, 128 * kc:128 * (kc + 1)],
                                 hhy[:, TPC * kc + TG * tg:TPC * kc + TG * (tg + 1)],
                                 kc == 0, kc == NCH - 1),
                    wait=(list(pw) + [(s_dq, s_dq.v)] +
                          (HHR if yc == 0 and tg == 0 else [])) if kc == 0 else [],
                    inc=(s_fmm if kc == NCH - 1 else None))
            P.o("scalar", lambda E, o=yb[:, TG * tg:TG * (tg + 1)], p=pt,
                b=b1_s[:, yc:yc + 1]: E.activation(out=o, in_=p[:, 0:TG], func=AF.Silu,
                                                   bias=b),
                wait=[(s_fmm, s_fmm.v)] + ([(s_sty, 16 * (yc - 1))] if yc >= 2 and
                                           tg == 0 else []), inc=s_sil)
            pool.release(slot, s_sil, s_sil.v)
        P.dma("sync", y1d[128 * yc:128 * (yc + 1), :], yb, s_sty,
              wait=[(s_sil, s_sil.v)])
    FC1ALL = [(s_sty, s_sty.v)]

    s_q2 = s_q1
    s_dq2 = P.sem("f_dq2")
    for yc in range(32):
        buf = yc % 2
        qb = qsb[:, D * buf:D * (buf + 1)]
        sr = srowF[0:1, D * buf:D * (buf + 1)]
        w = [(s_q2, s_q2.v)]
        if yc < 2:
            w.append((s_dq, 64))
        else:
            w.append((s_dq2, 2 * (yc - 1)))
        l1 = P.dma("sync", qb, fc2q[128 * yc:128 * (yc + 1), :], s_q2, wait=w)
        l2 = P.dma("gpsimd", sr, sc2x[yc:yc + 1, :], s_q2)
        for hb in range(2):
            slot_s, pts, pws = pool.acquire()
            P.o("tensor", MM(pts[:, 0:TG], ones_row_f[0:1, :],
                             sr[0:1, TG * hb:TG * (hb + 1)], True, True),
                wait=list(pws) + ([(s_q2, l2)] if hb == 0 else []), inc=s_sbm)
            P.o("vector", lambda E,
                o=w2b[:, D * yc + TG * hb:D * yc + TG * (hb + 1)],
                q=qb[:, TG * hb:TG * (hb + 1)], s=pts: E.scalar_tensor_tensor(
                    o, q, -128.0, s[:, 0:TG], ALU.add, ALU.mult),
                wait=[(s_sbm, s_sbm.v)], inc=s_dq2)
            pool.release(slot_s, s_dq2, s_dq2.v)
    W2R = [(s_dq2, s_dq2.v)]

    s_y1 = P.sem("f_y1")
    s_2mm = P.sem("f_2mm")
    s_x3l = P.sem("f_x3l")
    s_fin = P.sem("f_fin")
    s_st4 = P.sem("f_st4")
    for tg in range(NTG):
        for strip, orow, bdst in ((str_, ones_row_f[:, :], rstd_b),
                                  (stm, ones_row_f[:, :], mst_b)):
            slot, pt, pw = pool.acquire()
            vmm = P.o("tensor", MM(pt[:, 0:TG], orow,
                                   strip[0:1, TG * tg:TG * (tg + 1)], True, True),
                      wait=list(pw) + [(STRF, 1)] + (HHR if tg == 0 else []),
                      inc=s_bcF)
            P.o("scalar", lambda E, o=bdst[:, TG * tg:TG * (tg + 1)], p=pt:
                E.activation(out=o, in_=p[:, 0:TG], func=AF.Copy),
                wait=[(s_bcF, vmm)], inc=s_bcF)
            pool.release(slot, s_bcF, s_bcF.v)
    bcF4v = s_bcF.v
    for tg in range(NTG):
        wv = (list(W2R) + FC1ALL if tg == 0 else [(s_fin, s_fin.v)]) + \
            [(s_y1, s_y1.v)]
        l1 = P.dma("sync", hhy.rearrange("p (yc t) -> p yc t", t=TG),
                   y1d.rearrange("(yc p) t -> p yc t", p=128)
                   [:, :, TG * tg:TG * (tg + 1)], s_y1, wait=wv)
        for oc in range(NCH):
            n = NCH * tg + oc
            slot, pt, pw = pool.acquire()
            for yc in range(32):
                P.o("tensor", MM(pt[:, 0:TG],
                                 w2b[:, D * yc + 128 * oc:D * yc + 128 * (oc + 1)],
                                 hhy[:, TG * yc:TG * (yc + 1)],
                                 yc == 0, yc == 31),
                    wait=(list(pw) + [(s_y1, l1)] + (W2R if n == 0 else []))
                    if yc == 0 else [],
                    inc=(s_2mm if yc == 31 else None))
            gv = s_2mm.v
            xb3 = tmpx3[:, TG * (n % 2):TG * (n % 2 + 1)]
            lx = P.dma("sync", xb3,
                       x3d[128 * oc:128 * (oc + 1), TG * tg:TG * (tg + 1)], s_x3l,
                       wait=[(s_fin, s_fin.v - 1)] if n >= 2 else [])
            P.o("vector", lambda E, t=tmpvF, x=xb3,
                r=rstd_b[:, TG * tg:TG * (tg + 1)]: E.tensor_mul(t[:, :], x, r),
                wait=[(s_2mm, gv), (s_bcF, bcF4v), (s_x3l, lx), (s_misc, b2hv)] +
                ([(s_st4, 16 * n)] if n >= 1 else []))
            P.o("vector", lambda E, t=tmpvF,
                m=mst_b[:, TG * tg:TG * (tg + 1)]: E.tensor_add(t[:, :], t[:, :], m))
            P.o("vector", lambda E, t=tmpvF, b=b2h_s[:, oc:oc + 1]:
                E.tensor_single_scalar(t[:, :], t[:, :], b, ALU.add))
            P.o("vector", lambda E, t=tmpvF, p=pt: E.scalar_tensor_tensor(
                t[:, :], p[:, 0:TG], 0.5, t[:, :], ALU.mult, ALU.add), inc=s_fin)
            pool.release(slot, s_fin, s_fin.v)
            P.dma("sync", x4T[128 * oc:128 * (oc + 1), TG * tg:TG * (tg + 1)],
                  tmpvF[:, :], s_st4, wait=[(s_fin, s_fin.v)])
    P.pop_scope()
    if FDBG:
        s_dbg = P.sem("s_fdbg")
        for dst, src in ((dkq, kq_in), (dvi, v_in), (dao, ao_in), (dq2, q2_in),
                         (dctx, ctx_in), (dao2, ao2_in), (dx3, x3d)):
            P.dma("sync", dst[:, :], src[:, :], s_dbg, wait=[(s_st4, s_st4.v)])
        P.o("sync", lambda E: E.nop(), wait=[(s_st4, s_st4.v), (s_dbg, s_dbg.v)])
    else:
        P.o("sync", lambda E: E.nop(), wait=[(s_st4, s_st4.v)])
    P.build()
    return P


def stage_fused(inputs):
    x = inputs["L0_attn_x"]
    pos_all = np.arange(S, dtype=np.float32)
    inv = (1.0 / (10000.0 ** (np.arange(0, DR, 2, dtype=np.float32) / DR))).astype(np.float32)
    selcv = np.zeros((16, 8 * 128), BFNP)
    for c in range(8):
        selcv[2 * c, 128 * c:128 * c + 64] = 1.0
        selcv[2 * c + 1, 128 * c + 64:128 * (c + 1)] = 1.0
    sc1 = np.asarray(inputs["L1_ffn_fc1_scales"], dtype=np.float32)
    sc2 = np.asarray(inputs["L1_ffn_fc2_scales"], dtype=np.float32)
    sc1x = np.ascontiguousarray(
        np.stack([sc1[:, 128 * yc:128 * (yc + 1)].reshape(8 * 128)
                  for yc in range(32)]))
    sc2x = np.ascontiguousarray(sc2)
    common = {
        "wq": inputs["L0_attn_wq"], "wqc": inputs["L0_attn_wqc"],
        "wqr": inputs["L0_attn_wqr"], "wkv": inputs["L0_attn_wkv"],
        "wkr": inputs["L0_attn_wkr"], "wkc": inputs["L0_attn_wkc"],
        "wv": inputs["L0_attn_wv"],
        "bqx": np.ascontiguousarray(np.asarray(inputs["L0_attn_bq"]).reshape(2, 128).T),
        "bqcx": np.ascontiguousarray(np.asarray(inputs["L0_attn_bqc"]).reshape(7, 128).T),
        "bqrx": np.ascontiguousarray(np.asarray(inputs["L0_attn_bqr"]).reshape(1, 128).T),
        "bkvx": np.ascontiguousarray(np.asarray(inputs["L0_attn_bkv"]).reshape(2, 128).T),
        "bkrx": np.ascontiguousarray(np.asarray(inputs["L0_attn_bkr"]).reshape(1, 128).T),
        "bkcx": np.ascontiguousarray(np.asarray(inputs["L0_attn_bkc"]).reshape(7, 128).T),
        "bvx": np.asarray(inputs["L0_attn_bv"]).reshape(1, D),
        "onesc": np.ones((128, 1), np.float32),
        "onesr": np.ones((1, 128), np.float32),
        "tri": np.where(np.arange(128)[:, None] <= np.arange(128)[None, :],
                        0.0, -65536.0).astype(BFNP),
        "identb": np.eye(128, dtype=BFNP),
        "identf": np.eye(128, dtype=np.float32),
        "wo": inputs["L0_attn_wo"],
        "box": np.ascontiguousarray(np.asarray(inputs["L0_attn_bo"]).reshape(8, 128).T),
        "wrt": np.ascontiguousarray(np.asarray(inputs["L0_ffn_w_router"]).T),
        "centT": np.ascontiguousarray(np.asarray(inputs["L0_ffn_head_centroids"]).T),
        "tempx": np.asarray(inputs["L0_ffn_temperature"]).reshape(1, 1),
        "iexp": inputs["L0_ffn_input_experts"],
        "oexp": inputs["L0_ffn_output_experts"],
        "selc": selcv,
        "wqkvT": np.ascontiguousarray(np.asarray(inputs["L1_attn_w_qkv"]).T),
        "wpT": np.ascontiguousarray(np.asarray(inputs["L1_attn_w_proj"]).T),
        "bpx": np.ascontiguousarray(np.asarray(inputs["L1_attn_b_proj"]).reshape(8, 128).T),
        "gamx": np.ascontiguousarray(np.asarray(inputs["L1_ffn_ln_gamma"]).reshape(8, 128).T),
        "betx": np.ascontiguousarray(np.asarray(inputs["L1_ffn_ln_beta"]).reshape(8, 128).T),
        "fc1q": np.asarray(inputs["L1_ffn_fc1_q"]).astype(np.uint8),
        "sc1x": sc1x,
        "b1x": np.ascontiguousarray(np.asarray(inputs["L1_ffn_fc1_bias"]).reshape(32, 128).T),
        "fc2q": np.asarray(inputs["L1_ffn_fc2_q"]).astype(np.uint8),
        "sc2x": sc2x,
        "b2x": np.ascontiguousarray(np.asarray(inputs["L1_ffn_fc2_bias"]).reshape(8, 128).T),
    }
    maps = []
    for c in range(NCORES):
        b, i = c // 4, c % 4
        m = dict(common)
        m["xT"] = np.ascontiguousarray(np.asarray(x[b, TPC * i:TPC * (i + 1)]).T)
        pos = pos_all[TPC * i:TPC * (i + 1)]
        f = pos[:, None] * inv[None, :]
        emb = np.concatenate([f, f], axis=1)
        m["cosT"] = np.ascontiguousarray(np.cos(emb).T.astype(np.float32))
        m["sinT"] = np.ascontiguousarray(np.sin(emb).T.astype(np.float32))
        maps.append(m)
    return maps


def kernel(**inputs):
    inputs = {k: np.asarray(v) for k, v in inputs.items()}
    P = build_fused()
    maps = stage_fused(inputs)
    res = run_bass_kernel_spmd(P.nc, maps, core_ids=list(range(NCORES)))
    P.es.close()
    out = np.empty((B, S, D), np.float32)
    for c in range(NCORES):
        b, i = c // 4, c % 4
        out[b, TPC * i:TPC * (i + 1), :] = res.results[c]["x4T"].T
    return out


# version 19
# speedup vs baseline: 1.0098x; 1.0098x over previous
import numpy as np
import ml_dtypes
from contextlib import ExitStack

import concourse.bass as bass
import concourse.mybir as mybir
from concourse.bass_utils import run_bass_kernel_spmd

F32 = mybir.dt.float32
BF16 = mybir.dt.bfloat16
I32 = mybir.dt.int32
U8 = mybir.dt.uint8
AF = mybir.ActivationFunctionType
ALU = mybir.AluOpType
AX = mybir.AxisListType
BFNP = ml_dtypes.bfloat16

NCORES = 8
B, S, D = 2, 8192, 1024
TPC = 2048
NTG = 4
TG = 512
NCH = 8
H, HD = 16, 64
DC, DR = 256, 128
EPS_LN = 1e-5
EPS_F32 = float(np.finfo(np.float32).eps)

ENGINES = ("sync", "gpsimd", "scalar", "vector", "tensor")
DBG_A = False
FDBG = False
USE_DRAINS = False


class Sem:
    __slots__ = ("h", "v")

    def __init__(self, h):
        self.h = h
        self.v = 0


class Prog:
    CHAINED = ("vector", "scalar")

    def __init__(self, name):
        self.nc = bass.Bass("TRN2", target_bir_lowering=False, name=name)
        self.es = ExitStack()
        self.scopes = [self.es]
        self.q = {e: [] for e in ENGINES}
        self.chain = {e: Sem(self.es.enter_context(self.nc.semaphore(e + "_chain")))
                      for e in self.CHAINED}
        self.last = {e: None for e in ENGINES}

    def push_scope(self):
        self.scopes.append(ExitStack())

    def pop_scope(self):
        self.scopes.pop().close()

    def sem(self, name):
        return Sem(self.es.enter_context(self.nc.semaphore(name)))

    def sb(self, name, shape, dt):
        return self.scopes[-1].enter_context(self.nc.sbuf_tensor(name, shape, dt))

    def ps(self, name, shape, dt=F32):
        return self.scopes[-1].enter_context(self.nc.psum_tensor(name, shape, dt))

    def o(self, eng, fn, wait=(), inc=None, incn=1):
        w = list(wait)
        if eng in self.CHAINED:
            if self.last[eng] is not None:
                w.append(self.last[eng])
            if inc is None:
                inc = self.chain[eng]
                incn = 1
        w = tuple((s.h, int(v)) for (s, v) in w)
        tgt = None
        if inc is not None:
            inc.v += incn
            tgt = (inc.h, incn)
            if eng in self.CHAINED:
                self.last[eng] = (inc, inc.v)

        drain = eng in self.CHAINED and USE_DRAINS

        def op(E, fn=fn, w=w, tgt=tgt, drain=drain):
            for h, v in w:
                E.wait_ge(h, v)
            r = fn(E)
            if tgt is not None:
                if drain:
                    E.drain().then_inc(*tgt)
                else:
                    if r is None:
                        r = E.nop()
                    r.then_inc(*tgt)
        self.q[eng].append(op)
        return inc.v if inc is not None else None

    def dma(self, eng, out_ap, in_ap, sem, wait=(), slow=False, serial=True):
        w = list(wait)
        if serial and sem.v > 0:
            w.append((sem, sem.v))
        return self.o(eng, lambda E, o=out_ap, i=in_ap, s=slow: E.dma_start(
            out=o, in_=i, allow_slow_non_contiguous=s),
            wait=w, inc=sem, incn=16)

    def build(self):
        with self.nc.Block() as blk:
            for e in ENGINES:
                ops = self.q[e]
                if ops:
                    getattr(blk, e)(lambda E, ops=ops: [op(E) for op in ops])
        return self.nc


class Pool:

    def __init__(self, tiles):
        self.t = tiles
        self.last = [None] * len(tiles)
        self.i = 0

    def acquire(self):
        slot = self.i % len(self.t)
        self.i += 1
        prev = self.last[slot]
        return slot, self.t[slot], ([prev] if prev is not None else [])

    def release(self, slot, sem, val):
        self.last[slot] = (sem, val)


def MM(out, lhsT, rhs, start, stop):
    return lambda E, o=out, l=lhsT, r=rhs, st=start, sp=stop: E.matmul(
        o, l, r, start=st, stop=sp, skip_group_check=True)


def _rsqrt_strip(P, ap, s_pre, v_pre, s_post):
    P.o("scalar", lambda E, a=ap: E.activation(out=a, in_=a, func=AF.Ln),
        wait=[(s_pre, v_pre)])
    P.o("scalar", lambda E, a=ap: E.activation(out=a, in_=a, func=AF.Exp, scale=-0.5),
        inc=s_post)


def emit_ln(P, pfx, pool, pbig, h_slice, ones_col_bf, ones_row_f, ones_row64,
            cw, scr_bf, scr_sq, st1, st2, st3, rstd_b, mst_b, src_dram=None,
            src_sb=None, xc=None, src_ready=(), nbuf=2, sems=None):
    if sems is None:
        sems = [P.sem(pfx + n) for n in ("cast", "sq", "mm", "str", "bc", "app", "x")]
    s_cast, s_sq, s_mm, s_str, s_bc, s_app, s_x = sems
    cast0, sq0, mm0, str0, app0 = s_cast.v, s_sq.v, s_mm.v, s_str.v, s_app.v

    def src_chunk(c, phase):
        if src_sb is not None:
            return src_sb[:, TPC * c:TPC * (c + 1)], list(src_ready)
        buf = c % 2
        ap = xc[:, TPC * buf:TPC * (buf + 1)]
        w = [(s_x, s_x.v)]
        if phase == 0:
            if c >= 2:
                w.append((s_cast, cast0 + c - 1))
        else:
            w.append((s_mm, mm0 + NCH))
            if c >= 2:
                w.append((s_app, app0 + 4 * (c - 1)))
        v = P.dma("sync", ap, src_dram[128 * c:128 * (c + 1), :], s_x, wait=w)
        return ap, [(s_x, v)]

    for c in range(NCH):
        src, waits = src_chunk(c, 0)
        buf = c % nbuf
        cb = scr_bf[:, TPC * buf:TPC * (buf + 1)]
        sq = scr_sq[:, TPC * buf:TPC * (buf + 1)]
        w = list(waits)
        if c >= nbuf:
            w.append((s_mm, mm0 + c - nbuf + 1))
        P.o("vector", lambda E, o=cb, i=src: E.tensor_copy(o, i), wait=w, inc=s_cast)
        P.o("scalar", lambda E, o=sq, i=cb: E.activation(out=o, in_=i, func=AF.Square),
            wait=[(s_cast, cast0 + c + 1)] +
            ([(s_mm, mm0 + c - nbuf + 1)] if c >= nbuf else []), inc=s_sq)
        for tg in range(NTG):
            first = (c == 0)
            last = (c == NCH - 1)
            P.o("tensor", MM(pbig[0:1, TG * tg:TG * (tg + 1)], ones_col_bf,
                             cb[:, TG * tg:TG * (tg + 1)], first, last),
                wait=([(s_sq, sq0 + c + 1)] + (list(cw) if c == 0 else []))
                if tg == 0 else [])
            P.o("tensor", MM(pbig[32:33, TG * tg:TG * (tg + 1)], ones_col_bf,
                             sq[:, TG * tg:TG * (tg + 1)], first, last),
                inc=(s_mm if tg == NTG - 1 else None))
    P.o("vector", lambda E: E.tensor_scalar_mul(st1[:, :], pbig[0:1, 0:TPC], 1.0 / D),
        wait=[(s_mm, s_mm.v)])
    P.o("vector", lambda E: E.tensor_scalar_mul(st2[:, :], pbig[32:33, 0:TPC], 1.0 / D),
        inc=s_str)
    P.o("scalar", lambda E: E.activation(out=st3, in_=st1[:, :], func=AF.Square),
        wait=[(s_str, str0 + 1)], inc=s_str)
    P.o("vector", lambda E: E.tensor_sub(st2[:, :], st2[:, :], st3),
        wait=[(s_str, str0 + 2)])
    P.o("vector", lambda E: E.tensor_scalar_add(st2[:, :], st2[:, :], EPS_LN),
        inc=s_str)
    _rsqrt_strip(P, st2[:, :], s_str, str0 + 3, s_str)
    P.o("vector", lambda E: E.tensor_mul(st1[:, :], st1[:, :], st2[:, :]),
        wait=[(s_str, str0 + 4)])
    P.o("vector", lambda E: E.tensor_scalar_mul(st1[:, :], st1[:, :], -1.0),
        inc=s_str)
    for tg in range(NTG):
        for si, (strip, orow, bdst) in enumerate(((st2, ones_row64, rstd_b),
                                                  (st1, ones_row_f, mst_b))):
            slot, pt, pw = pool.acquire()
            mmw = list(pw) + ([(s_str, str0 + 5)] if (tg == 0 and si == 0) else [])
            vmm = P.o("tensor", MM(pt[:, 0:TG], orow,
                                   strip[0:1, TG * tg:TG * (tg + 1)], True, True),
                      wait=mmw, inc=s_bc)
            P.o("scalar", lambda E, o=bdst[:, TG * tg:TG * (tg + 1)], p=pt:
                E.activation(out=o, in_=p[:, 0:TG], func=AF.Copy),
                wait=[(s_bc, vmm)], inc=s_bc)
            pool.release(slot, s_bc, s_bc.v)
    bcv = s_bc.v
    for c in range(NCH):
        src, waits = src_chunk(c, 1)
        for tg in range(NTG):
            xs = src[:, TG * tg:TG * (tg + 1)]
            tmp = pbig[:, TG * tg:TG * (tg + 1)]
            P.o("vector", lambda E, t=tmp, x=xs,
                r=rstd_b[:, TG * tg:TG * (tg + 1)]: E.tensor_mul(t, x, r),
                wait=[(s_bc, bcv)] + list(waits))
            hs = h_slice(c, tg)
            P.o("vector", lambda E, o=hs, t=tmp,
                m=mst_b[:, TG * tg:TG * (tg + 1)]: E.tensor_add(o, t, m),
                inc=s_app)
    return st2, st1, s_app


GROUPS = [[0, 1, 2, 3], [4, 5, 6, 7]]


def build_fused():
    P = Prog("mkfull")
    nc = P.nc

    def inp(name, shape, dt=F32):
        return nc.dram_tensor(name, shape, dt, kind="ExternalInput").ap()

    xT = inp("xT", [D, TPC])
    wq = inp("wq", [D, DC]); wqc = inp("wqc", [DC, 896]); wqr = inp("wqr", [D, DR])
    wkv = inp("wkv", [D, DC]); wkr = inp("wkr", [D, DR]); wkc = inp("wkc", [DC, 896])
    wv = inp("wv", [DC, D])
    bqx = inp("bqx", [128, 2]); bqcx = inp("bqcx", [128, 7]); bqrx = inp("bqrx", [128, 1])
    bkvx = inp("bkvx", [128, 2]); bkrx = inp("bkrx", [128, 1]); bkcx = inp("bkcx", [128, 7])
    bvx = inp("bvx", [1, D])
    cosT = inp("cosT", [DR, TPC]); sinT = inp("sinT", [DR, TPC])
    onesc = inp("onesc", [128, 1]); onesr = inp("onesr", [1, 128])
    tri = inp("tri", [128, 128], BF16); identb = inp("identb", [128, 128], BF16)
    identf = inp("identf", [128, 128])
    wo = inp("wo", [D, D]); box = inp("box", [128, 8])
    wrt = inp("wrt", [D, 128]); centT = inp("centT", [128, 16])
    tempx = inp("tempx", [1, 1])
    iexp = inp("iexp", [16, D, 512]); oexp = inp("oexp", [16, 512, 64])
    selc = inp("selc", [16, 8 * 128], BF16)
    wqkvT = inp("wqkvT", [D, 3 * D])
    wpT = inp("wpT", [D, D]); bpx = inp("bpx", [128, 8])
    gamx = inp("gamx", [128, 8]); betx = inp("betx", [128, 8])
    fc1q = inp("fc1q", [D, 4096], U8); sc1x = inp("sc1x", [32, D])
    b1x = inp("b1x", [128, 32])
    fc2q = inp("fc2q", [4096, D], U8); sc2x = inp("sc2x", [32, D])
    b2x = inp("b2x", [128, 8])

    x4T = nc.dram_tensor("x4T", [D, TPC], F32, kind="ExternalOutput").ap()
    if FDBG:
        dkq = nc.dram_tensor("dkq", [2 * D, TPC], BF16, kind="ExternalOutput").ap()
        dvi = nc.dram_tensor("dvi", [TPC, H * 65], BF16, kind="ExternalOutput").ap()
        dao = nc.dram_tensor("dao", [256, S], BF16, kind="ExternalOutput").ap()
        dq2 = nc.dram_tensor("dq2", [D, TPC], BF16, kind="ExternalOutput").ap()
        dctx = nc.dram_tensor("dctx", [65, H * 65], F32, kind="ExternalOutput").ap()
        dao2 = nc.dram_tensor("dao2", [256, S], BF16, kind="ExternalOutput").ap()
        dx3 = nc.dram_tensor("dx3", [D, TPC], F32, kind="ExternalOutput").ap()
        dx2 = nc.dram_tensor("dx2", [D, TPC], F32, kind="ExternalOutput").ap()

    k_in = nc.dram_tensor("k_in", [D, TPC], BF16).ap()
    q_in = nc.dram_tensor("q_in", [D, TPC], BF16).ap()
    vA_in = nc.dram_tensor("vA_in", [D, H * 65], BF16).ap()
    vB_in = nc.dram_tensor("vB_in", [D, H * 65], BF16).ap()
    kA_out = nc.dram_tensor("kA_out", [4 * D, TPC], BF16).ap()
    qB_out = nc.dram_tensor("qB_out", [4 * D, TPC], BF16).ap()
    vA_out = nc.dram_tensor("vA_out", [4 * D, H * 65], BF16).ap()
    vB_out = nc.dram_tensor("vB_out", [4 * D, H * 65], BF16).ap()
    ao_in = nc.dram_tensor("ao_in", [256, S], BF16).ap()
    ao_out = nc.dram_tensor("ao_out", [D, S], BF16).ap()
    q2_in = nc.dram_tensor("q2_in", [D, TPC], BF16).ap()
    q2_out = nc.dram_tensor("q2_out", [4 * D, TPC], BF16).ap()
    ctx_in = nc.dram_tensor("ctx_in", [65, H * 65], F32).ap()
    ctx_out = nc.dram_tensor("ctx_out", [65, H * 65], F32).ap()
    ao2_in = nc.dram_tensor("ao2_in", [256, S], BF16).ap()
    ao2_out = nc.dram_tensor("ao2_out", [D, S], BF16).ap()
    y1d = nc.dram_tensor("y1d", [4096, TPC], BF16).ap()
    kmy = nc.dram_tensor("kmy", [4 * 256, TPC], BF16).ap()
    qmy = nc.dram_tensor("qmy", [4 * 256, TPC], BF16).ap()
    vmy = nc.dram_tensor("vmy", [S, 260], BF16).ap()
    aomy = nc.dram_tensor("aomy", [D, TPC], BF16).ap()
    ctxmy = nc.dram_tensor("ctxmy", [65, 260], F32).ap()
    q2my = nc.dram_tensor("q2my", [4 * 256, TPC], BF16).ap()
    ao2my = nc.dram_tensor("ao2my", [D, TPC], BF16).ap()

    ones_col_bf = P.sb("ones_col_bf", [128, 1], BF16)
    ones_row_f = P.sb("ones_row_f", [1, 128], F32)
    ones_row64 = ones_row_f[:, :]
    idb = P.sb("idb", [128, 128], BF16)
    trib = P.sb("trib", [128, 128], BF16)
    bq_s = P.sb("bq_s", [128, 2], F32); bqc_s = P.sb("bqc_s", [128, 7], F32)
    bqr_s = P.sb("bqr_s", [128, 1], F32); bkv_s = P.sb("bkv_s", [128, 2], F32)
    bkr_s = P.sb("bkr_s", [128, 1], F32); bkc_s = P.sb("bkc_s", [128, 7], F32)
    bo_s = P.sb("bo_s", [128, 8], F32)
    cent_s = P.sb("cent_s", [128, 16], F32)
    selcs = P.sb("selcs", [16, 8 * 128], BF16)
    bp_s = P.sb("bp_s", [128, 8], F32)
    gam_s = P.sb("gam_s", [128, 8], F32); bet_s = P.sb("bet_s", [128, 8], F32)
    b1_s = P.sb("b1_s", [128, 32], F32)
    b2_s = P.sb("b2_s", [128, 8], F32); b2h_s = P.sb("b2h_s", [128, 8], F32)
    tstrip = P.sb("tstrip", [1, 16], F32)
    stp = P.sb("stp", [1, 2 * TPC], F32)
    st1 = stp[0:1, 0:TPC]
    st2 = stp[0:1, TPC:2 * TPC]
    rstd_b = P.sb("rstd_b", [128, TPC], F32)
    mst_b = P.sb("mst_b", [128, TPC], F32)

    pbig = P.ps("pbig", [128, 4 * TG], F32)
    pool = Pool([P.ps(f"pp{i}", [128, TG], F32) for i in range(4)])
    ln_sems = [P.sem("ln_" + n) for n in ("cast", "sq", "mm", "str", "bc", "app", "x")]

    s_c = P.sem("s_c")
    s_w = P.sem("s_w")
    s_cc = P.sem("s_cc")

    for dst, src in ((bq_s, bqx), (bqc_s, bqcx), (bqr_s, bqrx), (bkv_s, bkvx),
                     (bkr_s, bkrx), (bkc_s, bkcx),
                     (ones_row_f, onesr),
                     (idb, identb), (trib, tri), (bo_s, box), (cent_s, centT),
                     (selcs, selc), (bp_s, bpx), (gam_s, gamx), (bet_s, betx),
                     (b1_s, b1x), (b2_s, b2x)):
        P.dma("sync", dst[:, :], src[:, :], s_c, serial=False)
    P.dma("sync", tstrip[0:1, 0:1], tempx[:, :], s_c, serial=False)
    P.dma("gpsimd", ones_col_bf[:, :], onesc[:, :], s_w, serial=False)
    CW = [(s_c, s_c.v), (s_w, s_w.v)]

    dynv = {}

    def DV(E, key):
        eng = E.engine
        if eng not in dynv:
            pid = E.partition_id()
            hg = pid % 4
            dynv[eng] = {"hg": hg, "hg2": hg * 2}
        return dynv[eng][key]

    def dyn_dma(out_ap, mk_in, sem, wait, q="sync"):
        w = list(wait)
        if sem.v > 0:
            w.append((sem, sem.v))
        sem.v += 16
        tgt = (sem.h, 16)
        wt = tuple((x.h, int(v)) for (x, v) in w)

        def op(E, o=out_ap, mk=mk_in, wt=wt, tgt=tgt):
            for h, v in wt:
                E.wait_ge(h, v)
            E.dma_start(out=o, in_=mk(E)).then_inc(*tgt)
        P.q[q].append(op)
        return sem.v

    P.push_scope()
    xc = P.sb("a_xc", [128, 2 * TPC], F32)
    scr_bf = P.sb("a_scrbf", [128, 2 * TPC], BF16)
    scr_sq = P.sb("a_scrsq", [128, 2 * TPC], BF16)
    hbf = P.sb("a_h", [128, NCH * TPC], BF16)
    wqb = P.sb("a_wqb", [128, 8 * DC], BF16)
    wqcb = P.sb("a_wqcb", [128, 2 * 896], BF16)
    wqrb = P.sb("a_wqrb", [128, 8 * DR], BF16)
    wkvb = P.sb("a_wkvb", [128, 8 * DC], BF16)
    wkrb = P.sb("a_wkrb", [128, 8 * DR], BF16)
    wkcb = P.sb("a_wkcb", [128, 2 * 896], BF16)
    wvb = P.sb("a_wvb", [128, 2 * D], BF16)
    cos_s = P.sb("a_cos", [128, TPC], F32)
    sin_s = P.sb("a_sin", [128, TPC], F32)
    cqb = P.sb("a_cq", [128, 2 * TPC], BF16)
    ckvb = P.sb("a_ckv", [128, 2 * TPC], BF16)
    qch = P.sb("a_qch", [128, 2 * TPC], BF16)
    kch = P.sb("a_kch", [128, 2 * TPC], BF16)
    qrf = P.sb("a_qrf", [128, TPC], F32)
    krf = P.sb("a_krf", [128, TPC], F32)
    vat = P.sb("a_vat", [128, 2 * H * 65], BF16)
    bv_b = P.sb("a_bvb", [128, D], F32)
    bv_row = P.sb("a_bv_row", [1, D], F32)

    s_aw = P.sem("s_aw")
    P.dma("sync", bv_row[:, :], bvx[:, :], s_aw, serial=False)
    P.dma("gpsimd", wqb.rearrange("p (kc n) -> p kc n", n=DC),
          wq.rearrange("(kc p) n -> p kc n", p=128), s_aw, serial=False)
    P.dma("gpsimd", wqcb.rearrange("p (kc n) -> p kc n", n=896),
          wqc.rearrange("(kc p) n -> p kc n", p=128), s_aw, serial=False)
    P.dma("gpsimd", wqrb.rearrange("p (kc n) -> p kc n", n=DR),
          wqr.rearrange("(kc p) n -> p kc n", p=128), s_aw, serial=False)
    P.dma("gpsimd", wkvb.rearrange("p (kc n) -> p kc n", n=DC),
          wkv.rearrange("(kc p) n -> p kc n", p=128), s_aw, serial=False)
    P.dma("gpsimd", wkrb.rearrange("p (kc n) -> p kc n", n=DR),
          wkr.rearrange("(kc p) n -> p kc n", p=128), s_aw, serial=False)
    P.dma("gpsimd", wkcb.rearrange("p (kc n) -> p kc n", n=896),
          wkc.rearrange("(kc p) n -> p kc n", p=128), s_aw, serial=False)
    P.dma("gpsimd", wvb.rearrange("p (kc n) -> p kc n", n=D),
          wv.rearrange("(kc p) n -> p kc n", p=128), s_aw, serial=False)
    P.dma("sync", cos_s[:, :], cosT[:, :], s_aw, serial=False)
    P.dma("sync", sin_s[:, :], sinT[:, :], s_aw, serial=False)
    cw = CW + [(s_aw, s_aw.v)]

    def h_slice(c, tg):
        return hbf[:, TPC * c + TG * tg: TPC * c + TG * (tg + 1)]

    a_st, b_st, s_app = emit_ln(P, "aln", pool, pbig, h_slice,
                                ones_col_bf[:, :], ones_row_f[:, :],
                                ones_row64, cw,
                                scr_bf, scr_sq, st1, st2, pbig[64:65, 0:TPC],
                                rstd_b, mst_b, src_dram=xT, xc=xc, sems=ln_sems)
    HREADY = [(s_app, s_app.v)]

    s_pmm = P.sem("s_pmm")
    s_pev = P.sem("s_pev")

    def proj(n_oc, n_kc, lhsT_fn, rhs_fn, evac, first_wait):
        for oc in range(n_oc):
            for tg in range(NTG):
                slot, pt, pw = pool.acquire()
                w = list(pw) + (first_wait if (oc == 0 and tg == 0) else [])
                for kc in range(n_kc):
                    P.o("tensor", MM(pt[:, 0:TG], lhsT_fn(kc, oc), rhs_fn(kc, tg),
                                     kc == 0, kc == n_kc - 1),
                        wait=w if kc == 0 else [],
                        inc=(s_pmm if kc == n_kc - 1 else None))
                evac(oc, tg, pt, [(s_pmm, s_pmm.v)], s_pev)
                pool.release(slot, s_pev, s_pev.v)

    def mk_gelu_evac(dst, bias):
        def evac(oc, tg, pt, w, s_ev):
            o = dst[:, TPC * oc + TG * tg: TPC * oc + TG * (tg + 1)]
            P.o("scalar", lambda E, o=o, p=pt, b=bias[:, oc:oc + 1]: E.activation(
                out=o, in_=p[:, 0:TG], func=AF.Gelu, bias=b), wait=w, inc=s_ev)
        return evac

    proj(2, NCH,
         lambda kc, oc: wkvb[:, DC * kc + 128 * oc: DC * kc + 128 * (oc + 1)],
         lambda kc, tg: hbf[:, TPC * kc + TG * tg: TPC * kc + TG * (tg + 1)],
         mk_gelu_evac(ckvb, bkv_s), HREADY + cw)
    CKVR = [(s_pev, s_pev.v)]

    s_ag = P.sem("s_ag")
    s_ast = P.sem("s_ast")

    hc_sems = [P.sem("hc_" + n) for n in ("mm", "ev", "rmm", "rev", "rdma", "rope")]

    def emit_head_chunks(which, wsb, bsb, csrc, cready, kqd, rope_srcs,
                         rope_extra_wait=()):
        ch = qch if which == "q" else kch
        s_mm, s_ev = hc_sems[0], hc_sems[1]
        for oc in range(7):
            buf = oc % 2
            dst = ch[:, TPC * buf:TPC * (buf + 1)]
            for tg in range(NTG):
                slot, pt, pw = pool.acquire()
                w = list(pw) + (cready if (oc == 0 and tg == 0) else [])
                if tg == 0 and oc >= 2:
                    w.append((s_ast, s_ast.v - 16))
                for kc in range(2):
                    P.o("tensor", MM(pt[:, 0:TG],
                                     wsb[:, 896 * kc + 128 * oc: 896 * kc + 128 * (oc + 1)],
                                     csrc[:, TPC * kc + TG * tg: TPC * kc + TG * (tg + 1)],
                                     kc == 0, kc == 1),
                        wait=w if kc == 0 else [], inc=(s_mm if kc == 1 else None))
                P.o("vector", lambda E, o=dst[:, TG * tg:TG * (tg + 1)], p=pt,
                    b=bsb[:, oc:oc + 1]: E.tensor_single_scalar(o, p[:, 0:TG], b, ALU.add),
                    wait=[(s_mm, s_mm.v)], inc=s_ev)
                pool.release(slot, s_ev, s_ev.v)
            P.o("scalar", lambda E, a=dst: E.activation(out=a, in_=a, func=AF.Gelu),
                wait=[(s_ev, s_ev.v)], inc=s_ag)
            P.dma("sync", kqd[128 * oc:128 * (oc + 1), :], dst,
                  s_ast, wait=[(s_ag, s_ag.v)])
        wrb, brb, rf = rope_srcs
        s_rmm, s_rev = hc_sems[2], hc_sems[3]
        for tg in range(NTG):
            slot, pt, pw = pool.acquire()
            for kc in range(NCH):
                P.o("tensor", MM(pt[:, 0:TG], wrb[:, DR * kc:DR * (kc + 1)],
                                 hbf[:, TPC * kc + TG * tg: TPC * kc + TG * (tg + 1)],
                                 kc == 0, kc == NCH - 1),
                    wait=(list(pw) + HREADY) if kc == 0 else [],
                    inc=(s_rmm if kc == NCH - 1 else None))
            P.o("vector", lambda E, o=rf[:, TG * tg:TG * (tg + 1)], p=pt,
                b=brb[:, 0:1]: E.tensor_single_scalar(o, p[:, 0:TG], b, ALU.add),
                wait=[(s_rmm, s_rmm.v)], inc=s_rev)
            pool.release(slot, s_rev, s_rev.v)
        buf = 1
        dst = ch[:, TPC * buf:TPC * (buf + 1)]
        rot = xc[:, 0:TPC]
        tmp2 = xc[:, TPC:2 * TPC]
        s_rdma, s_rope = hc_sems[4], hc_sems[5]
        P.dma("sync", rot[0:64, :], rf[64:128, :], s_rdma,
              wait=[(s_rev, s_rev.v)] + list(rope_extra_wait))
        P.dma("sync", rot[64:128, :], rf[0:64, :], s_rdma)
        P.o("vector", lambda E: E.tensor_scalar_mul(rot[0:64, :], rot[0:64, :], -1.0),
            wait=[(s_rdma, s_rdma.v), (s_ast, s_ast.v - 16)])
        P.o("vector", lambda E: E.tensor_mul(tmp2[:, :], rot[:, :], sin_s[:, :]))
        P.o("vector", lambda E: E.tensor_mul(rot[:, :], rf[:, :], cos_s[:, :]))
        P.o("vector", lambda E, o=dst: E.tensor_add(o, rot[:, :], tmp2[:, :]),
            inc=s_rope)
        P.o("scalar", lambda E, a=dst: E.activation(out=a, in_=a, func=AF.Gelu),
            wait=[(s_rope, s_rope.v)], inc=s_ag)
        P.dma("sync", kqd[128 * 7:128 * 8, :], dst, s_ast,
              wait=[(s_ag, s_ag.v)])
        return s_rope

    s_vbc = P.sem("s_vbc")
    s_vbe = P.sem("s_vbe")
    for half in range(2):
        slot, pt, pw = pool.acquire()
        P.o("tensor", MM(pt[:, 0:TG], ones_row_f[:, :],
                         bv_row[0:1, TG * half:TG * (half + 1)], True, True),
            wait=list(pw) + cw, inc=s_vbc)
        P.o("vector", lambda E, o=bv_b[:, TG * half:TG * (half + 1)], p=pt:
            E.tensor_copy(o, p[:, 0:TG]), wait=[(s_vbc, s_vbc.v)], inc=s_vbe)
        pool.release(slot, s_vbe, s_vbe.v)
    s_vmm = P.sem("s_vmm")
    s_vev = P.sem("s_vev")
    NT = TPC // 128

    def emit_v_tiles(lo, hi):
        for tt in range(lo, hi):
            buf = tt % 2
            tile = vat[:, (H * 65) * buf:(H * 65) * (buf + 1)]
            t3 = tile.rearrange("p (h c) -> p h c", c=65)
            w = [(s_ast, s_ast.v - 16)] if tt >= 2 else []
            P.o("vector", lambda E, o=t3[:, :, 64:65]: E.memset(o, 1.0),
                wait=w + ([(s_vbe, s_vbe.v)] if tt == 0 else []))
            for vc in range(2):
                slot, pt, pw = pool.acquire()
                for kc in range(2):
                    P.o("tensor", MM(pt[:, 0:TG],
                                     ckvb[:, TPC * kc + 128 * tt: TPC * kc + 128 * (tt + 1)],
                                     wvb[:, D * kc + TG * vc: D * kc + TG * (vc + 1)],
                                     kc == 0, kc == 1),
                        wait=(list(pw) + CKVR) if kc == 0 else [],
                        inc=(s_vmm if kc == 1 else None))
                P.o("vector", lambda E, o=t3[:, 8 * vc:8 * (vc + 1), 0:64], p=pt,
                    b=bv_b[:, TG * vc:TG * (vc + 1)].rearrange("p (h c) -> p h c", c=64):
                    E.scalar_tensor_tensor(o, p[:, 0:TG].rearrange("p (h c) -> p h c", c=64),
                                           0.0, b, ALU.bypass, ALU.add),
                    wait=[(s_vmm, s_vmm.v)], inc=s_vev)
                pool.release(slot, s_vev, s_vev.v)
            vdst = vA_in if tt < NT // 2 else vB_in
            P.dma("sync", vdst[128 * (tt % (NT // 2)):128 * (tt % (NT // 2) + 1), :],
                  tile, s_ast, wait=[(s_vev, s_vev.v)])

    k_rope = emit_head_chunks("k", wkcb, bkc_s, ckvb, CKVR, k_in,
                              (wkrb, bkr_s, krf))
    AST_KO = s_ast.v
    P.o("gpsimd", lambda E: E.collective_compute(
        "AllGather", ALU.bypass, GROUPS, ins=[k_in[:, :]], outs=[kA_out[:, :]]),
        wait=[(s_ast, AST_KO)], inc=s_cc)
    CCK = s_cc.v
    emit_v_tiles(0, NT // 2)
    AST_VA = s_ast.v

    proj(2, NCH,
         lambda kc, oc: wqb[:, DC * kc + 128 * oc: DC * kc + 128 * (oc + 1)],
         lambda kc, tg: hbf[:, TPC * kc + TG * tg: TPC * kc + TG * (tg + 1)],
         mk_gelu_evac(cqb, bq_s), [])
    CQR = [(s_pev, s_pev.v)]
    emit_head_chunks("q", wqcb, bqc_s, cqb, CQR, q_in, (wqrb, bqr_s, qrf),
                     rope_extra_wait=[(k_rope, k_rope.v)])
    AST_QO = s_ast.v
    P.o("gpsimd", lambda E: E.collective_compute(
        "AllGather", ALU.bypass, GROUPS, ins=[q_in[:, :]], outs=[qB_out[:, :]]),
        wait=[(s_ast, AST_QO)], inc=s_cc)
    CCQ = s_cc.v
    P.o("gpsimd", lambda E: E.collective_compute(
        "AllGather", ALU.bypass, GROUPS, ins=[vA_in[:, :]], outs=[vA_out[:, :]]),
        wait=[(s_ast, AST_VA)], inc=s_cc)
    CC_VA = s_cc.v
    emit_v_tiles(NT // 2, NT)
    AST_ALL = s_ast.v
    P.pop_scope()
    P.o("gpsimd", lambda E: E.collective_compute(
        "AllGather", ALU.bypass, GROUPS, ins=[vB_in[:, :]], outs=[vB_out[:, :]]),
        wait=[(s_ast, AST_ALL)], inc=s_cc)
    CC_VB = s_cc.v

    P.push_scope()
    kp = P.sb("b_kp", [128, 2 * S], BF16)
    qp = P.sb("b_qp", [128, 2 * S], BF16)
    vh = P.sb("b_vh", [128, (S // 128) * 260], BF16)
    qsq = P.sb("b_qsq", [128, S], BF16)
    Pt = P.sb("b_Pt", [128, 4 * TG], BF16)
    aot = P.sb("b_aot", [64, 2 * TG], BF16)
    pvsbB = P.sb("b_pvsb", [65, 2 * TG], F32)
    rbB = P.sb("b_rb", [64, TG], F32)
    rstripB = P.sb("b_rstrip", [1, TG], F32)
    nstr = P.sb("b_nstr", [1, 16], F32)
    bias_col = P.sb("b_bias", [128, 1], F32)

    NKT = S // 128
    NQG = S // TG
    s_bl = P.sem("s_bl")
    kqA3 = kA_out.rearrange("(r t) c -> r t c", t=D)
    kqB3 = qB_out.rearrange("(r t) c -> r t c", t=D)
    vA3 = vA_out.rearrange("(r t) c -> r t c", t=D)
    vB3 = vB_out.rearrange("(r t) c -> r t c", t=D)
    vmy4 = vmy.rearrange("(g u) c -> g u c", u=TPC)
    dyn_dma(kmy.rearrange("(r t) c -> r t c", t=256),
            (lambda E: kqA3[:, bass.ts(DV(E, "hg"), 256), :][:, 0:256, :]),
            s_bl, [(s_cc, CCK)])
    for hp in range(2):
        for r in range(4):
            P.dma("sync", kp[:, S * hp + TPC * r:S * hp + TPC * (r + 1)],
                  kmy[256 * r + 128 * hp:256 * r + 128 * (hp + 1), :], s_bl,
                  wait=([(s_ast, AST_ALL)] if (hp == 0 and r == 0) else []))
    LWK = [(s_bl, s_bl.v)]
    dyn_dma(qmy.rearrange("(r t) c -> r t c", t=256),
            (lambda E: kqB3[:, bass.ts(DV(E, "hg"), 256), :][:, 0:256, :]),
            s_bl, [(s_cc, CCQ)])
    for hp in range(2):
        for r in range(4):
            P.dma("sync", qp[:, S * hp + TPC * r:S * hp + TPC * (r + 1)],
                  qmy[256 * r + 128 * hp:256 * r + 128 * (hp + 1), :], s_bl)
    QPL = s_bl.v
    dyn_dma(vmy4[:, 0:TPC // 2, :],
            (lambda E: vA3[:, :, bass.ts(DV(E, "hg"), 260)]), s_bl,
            [(s_cc, CC_VA)])
    dyn_dma(vmy4[:, TPC // 2:TPC, :],
            (lambda E: vB3[:, :, bass.ts(DV(E, "hg"), 260)]), s_bl,
            [(s_cc, CC_VB)])
    P.dma("sync", vh.rearrange("p (t c) -> p t c", c=260),
          vmy.rearrange("(t p) c -> p t c", p=128), s_bl)
    LWB = [(s_bl, QPL)]
    VHR = [(s_bl, s_bl.v)]

    s_nrm = P.sem("s_nrm")
    s_nmm = P.sem("s_nmm")
    s_sqs = P.sem("s_sqs")
    s_bias = P.sem("s_bias")
    s_smm = P.sem("s_smm")
    s_exp = P.sem("s_exp")
    s_pvm = P.sem("s_pvm")
    s_pvc = P.sem("s_pvc")
    s_v1 = P.sem("s_v1")
    s_bmm = P.sem("s_bmm")
    s_aoev = P.sem("s_aoev")
    s_ast2 = P.sem("s_ast2")

    slot_pv_, ppvA, pw_pv = pool.acquire()
    slot_pv2_, ppvB, pw_pv2 = pool.acquire()
    slot_pb_, pb1, pw_pb = pool.acquire()
    PV_FREE = pw_pv + pw_pv2 + pw_pb

    slot_exp = [None] * 4
    slot_pv = [None] * 4
    first_scores_extra = list(PV_FREE)
    nstore = 0

    for hp in range(2):
        kslab = kp[:, S * hp:S * (hp + 1)]
        qslab = qp[:, S * hp:S * (hp + 1)]
        for which, slab, acc in (("k", kslab, 4), ("q", qslab, 0)):
            P.o("scalar", lambda E, i=slab[:, :]: E.activation(
                out=qsq[:, :], in_=i, func=AF.Square),
                wait=(LWK if which == "k" else LWB) + [(s_nrm, s_nrm.v)],
                inc=s_sqs)
            for r in range(4):
                for j in range(4):
                    cg = 4 * r + j
                    w = [(s_sqs, s_sqs.v), (s_nrm, s_nrm.v)] if j == 0 else []
                    if r == 0 and j == 0:
                        for si in range(4):
                            if slot_exp[si] is not None:
                                w.append(slot_exp[si])
                    P.o("tensor", MM(pbig[0:1, TG * j:TG * (j + 1)],
                                     ones_col_bf[:, 0:1],
                                     qsq[:, TG * cg:TG * (cg + 1)], True, True),
                        wait=w, inc=(s_nmm if j == 3 else None))
                P.o("vector", lambda E, o=nstr[0:1, acc + r:acc + r + 1]:
                    E.tensor_reduce(o, pbig[0:1, 0:4 * TG], AX.X, ALU.max),
                    wait=[(s_nmm, s_nmm.v)], inc=s_nrm)
            P.o("vector", lambda E, o=nstr[0:1, 8 + (0 if which == "q" else 1):
                                           9 + (0 if which == "q" else 1)],
                i=nstr[0:1, acc:acc + 4]: E.tensor_reduce(o, i, AX.X, ALU.max),
                wait=[(s_nrm, s_nrm.v)], inc=s_nrm)
        P.o("vector", lambda E: E.tensor_mul(nstr[0:1, 10:11], nstr[0:1, 8:9],
                                             nstr[0:1, 9:10]),
            wait=[(s_nrm, s_nrm.v)], inc=s_nrm)
        P.o("scalar", lambda E: E.activation(out=nstr[0:1, 10:11], in_=nstr[0:1, 10:11],
                                             func=AF.Ln), wait=[(s_nrm, s_nrm.v)])
        P.o("scalar", lambda E: E.activation(out=nstr[0:1, 10:11], in_=nstr[0:1, 10:11],
                                             func=AF.Exp, scale=0.5), inc=s_sqs)
        P.o("vector", lambda E: E.tensor_scalar_mul(nstr[0:1, 10:11], nstr[0:1, 10:11],
                                                    -0.125),
            wait=[(s_sqs, s_sqs.v)], inc=s_nrm)
        P.o("tensor", MM(pb1[:, 0:1], ones_row_f[:, :], nstr[0:1, 10:11], True, True),
            wait=[(s_nrm, s_nrm.v), (s_aoev, s_aoev.v)] +
            (list(PV_FREE) if hp == 0 else []), inc=s_bmm)
        P.o("vector", lambda E: E.tensor_copy(bias_col[:, :], pb1[:, 0:1]),
            wait=[(s_bmm, s_bmm.v), (s_exp, s_exp.v)], inc=s_bias)

        for g in range(NQG):
            kts = 4 * g + 4

            def emit_scores(kt):
                sa = 2 * (kt % 2)
                diag = kt >= 4 * g
                j = kt - 4 * g if diag else 0
                for ih in range(2):
                    slot = sa + ih
                    w = []
                    if slot_exp[slot] is not None:
                        w.append(slot_exp[slot])
                        slot_exp[slot] = None
                    if kt == 0 and ih == 0:
                        w += [(s_bias, s_bias.v)]
                        if hp == 0 and g == 0:
                            w += first_scores_extra
                    P.o("tensor", MM(pbig[:, TG * slot:TG * (slot + 1)],
                                     kslab[64 * ih:64 * (ih + 1),
                                           128 * kt:128 * (kt + 1)],
                                     qslab[64 * ih:64 * (ih + 1),
                                           TG * g:TG * (g + 1)],
                                     True, not diag),
                        wait=w, inc=(None if diag else (s_smm if ih == 1 else None)))
                if diag:
                    for ih in range(2):
                        slot = sa + ih
                        P.o("tensor", MM(pbig[:, TG * slot + 128 * j:
                                              TG * slot + 128 * (j + 1)],
                                         idb[:, :], trib[:, :], False, True),
                            inc=(s_smm if ih == 1 else None))

            def emit_exp(kt):
                sa = 2 * (kt % 2)
                w = [(s_smm, s_smm.v), (s_bias, s_bias.v)]
                for ih in range(2):
                    if slot_pv[sa + ih] is not None:
                        w.append(slot_pv[sa + ih])
                        slot_pv[sa + ih] = None
                P.o("scalar", lambda E, o=Pt[:, TG * sa:TG * (sa + 2)],
                    i=pbig[:, TG * sa:TG * (sa + 2)]: E.activation(
                        out=o, in_=i, func=AF.Exp, scale=0.125, bias=bias_col[:, 0:1]),
                    wait=w, inc=s_exp)
                for ih in range(2):
                    slot_exp[sa + ih] = (s_exp, s_exp.v)

            def emit_pv(kt):
                sa = 2 * (kt % 2)
                ev = (s_exp, s_exp.v)
                diag = kt >= 4 * g
                j = kt - 4 * g if diag else 0
                jm = 128 * j
                for ih, pv in ((0, ppvA), (1, ppvB)):
                    hl = 2 * hp + ih
                    slot = sa + ih
                    w = []
                    if ih == 0:
                        w.append(ev)
                    if kt == 0:
                        w.append((s_pvc, s_pvc.v))
                        if hp == 0 and g == 0:
                            w += VHR
                    P.o("tensor", MM(pv[0:65, jm:TG],
                                     vh[:, 260 * kt + 65 * hl:260 * kt + 65 * (hl + 1)],
                                     Pt[:, TG * slot + jm:TG * (slot + 1)],
                                     kt == 0, kt == kts - 1),
                        wait=w, inc=(s_pvm if ih == 1 else None))
                for ih in range(2):
                    slot_pv[sa + ih] = (s_pvm, s_pvm.v)

            emit_scores(0)
            emit_exp(0)
            for kt in range(kts):
                if kt + 1 < kts:
                    emit_scores(kt + 1)
                emit_pv(kt)
                if kt + 1 < kts:
                    emit_exp(kt + 1)

            pvcv = {}
            for ih, pv in ((0, ppvA), (1, ppvB)):
                P.o("scalar", lambda E, o=pvsbB[0:64, TG * ih:TG * (ih + 1)], p=pv:
                    E.activation(out=o, in_=p[0:64, 0:TG], func=AF.Copy),
                    wait=[(s_pvm, s_pvm.v)] +
                    ([(s_aoev, s_aoev.v)] if ih == 0 else []))
                pvcv[ih] = P.o("scalar", lambda E,
                               o=pvsbB[64:65, TG * ih:TG * (ih + 1)], p=pv:
                               E.activation(out=o, in_=p[64:65, 0:TG],
                                            func=AF.Copy), inc=s_pvc)
            for ih in range(2):
                hl = 2 * hp + ih
                pvs = pvsbB[:, TG * ih:TG * (ih + 1)]
                P.o("vector", lambda E, p=pvs: E.reciprocal(rstripB[:, :],
                                                            p[64:65, :]),
                    wait=[(s_bmm, s_bmm.v), (s_pvc, pvcv[ih])], inc=s_v1)
                P.o("tensor", MM(pb1[0:64, 0:TG], ones_row_f[0:1, 0:64],
                                 rstripB[:, :], True, True),
                    wait=[(s_v1, s_v1.v), (s_bias, s_bias.v)], inc=s_bmm)
                P.o("vector", lambda E: E.tensor_copy(rbB[:, :], pb1[0:64, 0:TG]),
                    wait=[(s_bmm, s_bmm.v)])
                slot_o = aot[:, TG * (nstore % 2):TG * (nstore % 2 + 1)]
                w2 = [(s_ast2, s_ast2.v - 16)] if nstore >= 2 else []
                P.o("vector", lambda E, o=slot_o, p=pvs: E.tensor_mul(
                    o, p[0:64, :], rbB[:, :]), wait=w2, inc=s_aoev)
                P.dma("sync", ao_in[64 * hl:64 * (hl + 1), TG * g:TG * (g + 1)],
                      slot_o, s_ast2, wait=[(s_aoev, s_aoev.v)])
                nstore += 1
    pool.release(slot_pv_, s_aoev, s_aoev.v)
    pool.release(slot_pv2_, s_aoev, s_aoev.v)
    pool.release(slot_pb_, s_aoev, s_aoev.v)
    PBIG_LAST_B = [(s_exp, s_exp.v)]
    AST2_ALL = s_ast2.v
    P.pop_scope()

    P.o("gpsimd", lambda E: E.collective_compute(
        "AllGather", ALU.bypass, GROUPS, ins=[ao_in[:, :]], outs=[ao_out[:, :]]),
        wait=[(s_ast2, AST2_ALL)], inc=s_cc)
    CC3 = s_cc.v

    P.push_scope()
    x1sb = P.sb("x1sb", [128, NCH * TPC], F32)
    scrC = P.sb("c_scr", [128, TPC], BF16)
    scrC2 = P.sb("c_scr2", [128, TPC], BF16)
    P.push_scope()
    arena = P.sb("c_arena", [128, 16384], BF16)
    aob = arena[:, 0:16384]
    h2b = P.sb("c_h2", [128, NCH * TPC], BF16)
    gfm = P.sb("c_gfm", [16, TPC], BF16)
    idf = P.sb("idf", [128, 128], F32)
    gbs = P.sb("c_gbs", [128, TPC], BF16)
    oexb = P.sb("c_oexb", [128, 2 * 512], BF16)
    wrtb = P.sb("c_wrtb", [128, 8 * 128], BF16)
    csqb = P.sb("c_csqb", [128, 16], BF16)
    cnb = P.sb("c_cnb", [128, 16], BF16)
    gcol = P.sb("c_gcol", [128, 8], F32)
    gw = P.sb("c_gw", [128, 32], F32)
    tmpC = P.sb("c_tmp", [128, TG], F32)

    s_cl = P.sem("s_cl")
    s_x1 = P.sem("s_x1")
    s_wmm = P.sem("s_wmm")
    s_wol = P.sem("s_wol")
    P.dma("gpsimd", wrtb.rearrange("p (c n) -> p c n", n=128),
          wrt.rearrange("(c p) n -> p c n", p=128), s_cl, serial=False,
          wait=[(s_ast2, AST2_ALL)])
    s_idf = P.sem("s_idf")
    P.dma("gpsimd", idf[:, :], identf[:, :], s_idf, serial=False)
    dyn_dma(aomy[:, :], (lambda E: ao_out[:, bass.ts(DV(E, "hg"), TPC)]),
            s_cl, [(s_cc, CC3)])
    for c in range(NCH):
        P.dma("sync", aob[:, TPC * c:TPC * (c + 1)], aomy[128 * c:128 * (c + 1), :],
              s_cl)
    AOL = [(s_cl, s_cl.v)]

    P.push_scope()
    wos = P.sb("c_wos", [128, 2 * 1024], BF16)
    zf = P.sb("c_zf", [128, TPC], F32)
    for oc in range(NCH):
        wbuf = wos[:, 1024 * (oc % 2):1024 * (oc % 2 + 1)]
        wlv = P.dma("gpsimd", wbuf.rearrange("p (kc n) -> p kc n", n=128),
                    wo.rearrange("(kc p) n -> p kc n", p=128)
                    [:, :, 128 * oc:128 * (oc + 1)], s_wol,
                    wait=[(s_wmm, 4 * (oc - 1))] if oc >= 2
                    else [(s_ast2, AST2_ALL)])
        for tg in range(NTG):
            xsl = x1sb[:, TPC * oc + TG * tg:TPC * oc + TG * (tg + 1)]
            lv = P.dma("sync", xsl, xT[128 * oc:128 * (oc + 1), TG * tg:TG * (tg + 1)],
                       s_x1)
            slot, pt, pw = pool.acquire()
            for kc in range(NCH):
                P.o("tensor", MM(pt[:, 0:TG],
                                 wbuf[:, 128 * kc:128 * (kc + 1)],
                                 aob[:, TPC * kc + TG * tg:TPC * kc + TG * (tg + 1)],
                                 kc == 0, kc == NCH - 1),
                    wait=(list(pw) + CW + AOL + PBIG_LAST_B + [(s_wol, wlv)])
                    if kc == 0 else [],
                    inc=(s_wmm if kc == NCH - 1 else None))
            P.o("vector", lambda E, o=xsl, p=pt, b=bo_s[:, oc:oc + 1]:
                E.scalar_tensor_tensor(o, p[:, 0:TG], b, o, ALU.add, ALU.add),
                wait=[(s_wmm, s_wmm.v), (s_x1, lv)], inc=s_x1)
            pool.release(slot, s_x1, s_x1.v)
    X1R = [(s_x1, s_x1.v)]
    WMMF = s_wmm.v

    def h2_slice(c, tg):
        return h2b[:, TPC * c + TG * tg:TPC * c + TG * (tg + 1)]
    st3p = pbig[64:65, 0:TPC]
    _, _, s_app2 = emit_ln(P, "cln", pool, pbig, h2_slice,
                           ones_col_bf[:, :], ones_row_f[:, :], ones_row64,
                           CW + PBIG_LAST_B,
                           scrC, scrC2, st1, st2, st3p, rstd_b, mst_b,
                           src_sb=x1sb, src_ready=X1R, nbuf=1, sems=ln_sems)
    H2R = [(s_app2, s_app2.v)]

    s_zmm = P.sem("s_zmm")
    s_zev = P.sem("s_zev")
    s_zsq = P.sem("s_zsq")
    s_nmmC = P.sem("s_nmmC")
    s_strC = P.sem("s_strC")
    s_tmp = P.sem("s_tmp")
    s_zsc = P.sem("s_zsc")
    s_cn = P.sem("s_cn")
    for tg in range(NTG):
        slot, pt, pw = pool.acquire()
        for kc in range(NCH):
            P.o("tensor", MM(pt[:, 0:TG], wrtb[:, 128 * kc:128 * (kc + 1)],
                             h2b[:, TPC * kc + TG * tg:TPC * kc + TG * (tg + 1)],
                             kc == 0, kc == NCH - 1),
                wait=(list(pw) + H2R) if kc == 0 else [],
                inc=(s_zmm if kc == NCH - 1 else None))
        P.o("vector", lambda E, o=zf[:, TG * tg:TG * (tg + 1)], p=pt:
            E.tensor_copy(o, p[:, 0:TG]), wait=[(s_zmm, s_zmm.v)], inc=s_zev)
        pool.release(slot, s_zev, s_zev.v)
    P.o("scalar", lambda E: E.activation(out=scrC2[:, :], in_=zf[:, :], func=AF.Square),
        wait=[(s_zev, s_zev.v)], inc=s_zsq)
    for tg in range(NTG):
        P.o("tensor", MM(pbig[0:1, TG * tg:TG * (tg + 1)], ones_col_bf[:, :],
                         scrC2[:, TG * tg:TG * (tg + 1)], True, True),
            wait=[(s_zsq, s_zsq.v)] if tg == 0 else [],
            inc=(s_nmmC if tg == NTG - 1 else None))
    P.o("vector", lambda E: E.tensor_copy(st1[:, :], pbig[0:1, 0:TPC]),
        wait=[(s_nmmC, s_nmmC.v)], inc=s_strC)
    P.o("scalar", lambda E: E.activation(out=st1[:, :], in_=st1[:, :], func=AF.Ln),
        wait=[(s_strC, 1)])
    P.o("scalar", lambda E: E.activation(out=st1[:, :], in_=st1[:, :], func=AF.Exp,
                                         scale=0.5), inc=s_strC)
    P.o("vector", lambda E: E.tensor_scalar_max(st1[:, :], st1[:, :], 1e-12),
        wait=[(s_strC, 2)], inc=s_strC)
    P.o("vector", lambda E: E.reciprocal(st1[:, :], st1[:, :]),
        wait=[(s_strC, 3)], inc=s_strC)
    P.o("scalar", lambda E: E.activation(out=tstrip[0:1, 0:1], in_=tstrip[0:1, 0:1],
                                         func=AF.Exp), wait=CW, inc=s_tmp)
    P.o("vector", lambda E: E.tensor_scalar_mul(st1[:, :], st1[:, :],
                                                float(1.0 / np.sqrt(128.0))),
        wait=[(s_strC, 4), (s_tmp, 1)])
    P.o("vector", lambda E: E.tensor_single_scalar(st1[:, :], st1[:, :],
                                                   tstrip[0:1, 0:1], ALU.mult),
        inc=s_strC)
    for tg in range(NTG):
        slot, pt, pw = pool.acquire()
        P.o("tensor", MM(pt[:, 0:TG], ones_row_f[:, :], st1[0:1, TG * tg:TG * (tg + 1)],
                         True, True), wait=list(pw) + [(s_strC, 5)], inc=s_zmm)
        P.o("vector", lambda E, o=scrC[:, TG * tg:TG * (tg + 1)],
            z=zf[:, TG * tg:TG * (tg + 1)], p=pt: E.tensor_mul(o, z, p[:, 0:TG]),
            wait=[(s_zmm, s_zmm.v)], inc=s_zsc)
        pool.release(slot, s_zsc, s_zsc.v)
    ZSCF = s_zsc.v
    P.pop_scope()
    hidb = P.sb("c_hid", [128, 4 * TPC], BF16)
    P.o("scalar", lambda E: E.activation(out=csqb[:, :], in_=cent_s[:, :],
                                         func=AF.Square), wait=CW, inc=s_zsq)
    slot, pt, pw = pool.acquire()
    P.o("tensor", MM(pt[0:1, 0:16], ones_col_bf[:, :], csqb[:, :], True, True),
        wait=list(pw) + [(s_zsq, s_zsq.v)], inc=s_nmmC)
    P.o("vector", lambda E, p=pt: E.tensor_copy(gw[0:1, 0:16], p[0:1, 0:16]),
        wait=[(s_nmmC, s_nmmC.v)], inc=s_cn)
    P.o("scalar", lambda E: E.activation(out=gw[0:1, 0:16], in_=gw[0:1, 0:16],
                                         func=AF.Ln), wait=[(s_cn, s_cn.v)])
    P.o("scalar", lambda E: E.activation(out=gw[0:1, 0:16], in_=gw[0:1, 0:16],
                                         func=AF.Exp, scale=0.5), inc=s_zsq)
    P.o("vector", lambda E: E.tensor_scalar_max(gw[0:1, 0:16], gw[0:1, 0:16], 1e-12),
        wait=[(s_zsq, s_zsq.v)], inc=s_cn)
    P.o("vector", lambda E: E.reciprocal(gw[0:1, 0:16], gw[0:1, 0:16]),
        wait=[(s_cn, s_cn.v)], inc=s_cn)
    pool.release(slot, s_cn, s_cn.v)
    slot, pt, pw = pool.acquire()
    P.o("tensor", MM(pt[:, 0:16], ones_row_f[:, :], gw[0:1, 0:16], True, True),
        wait=list(pw) + [(s_cn, s_cn.v)], inc=s_nmmC)
    P.o("vector", lambda E, p=pt: E.tensor_mul(cnb[:, :], cent_s[:, :], p[:, 0:16]),
        wait=[(s_nmmC, s_nmmC.v)], inc=s_cn)
    pool.release(slot, s_cn, s_cn.v)
    CNR = (s_cn, s_cn.v)

    s_lg = P.sem("s_lg")
    s_g1 = P.sem("s_g1")
    s_g2 = P.sem("s_g2")
    s_g3 = P.sem("s_g3")
    s_trC = P.sem("s_trC")
    s_gf = P.sem("s_gf")
    for tt in range(TPC // 128):
        slot, pt, pw = pool.acquire()
        P.o("tensor", MM(pt[:, 0:16], scrC[:, 128 * tt:128 * (tt + 1)], cnb[:, :],
                         True, True),
            wait=list(pw) + ([CNR, (s_zsc, s_zsc.v)] if tt == 0 else []), inc=s_lg)
        P.o("vector", lambda E, p=pt: E.tensor_reduce(gcol[:, 0:1], p[:, 0:16], AX.X,
                                                      ALU.max),
            wait=[(s_lg, s_lg.v), (s_g3, s_g3.v)], inc=s_g1)
        P.o("vector", lambda E: E.tensor_scalar_mul(gcol[:, 1:2], gcol[:, 0:1], -1.0),
            wait=[(s_g1, s_g1.v)], inc=s_g1)
        P.o("scalar", lambda E, p=pt: E.activation(out=gw[:, 0:16], in_=p[:, 0:16],
                                                   func=AF.Exp, bias=gcol[:, 1:2]),
            wait=[(s_g1, s_g1.v)], inc=s_g2)
        pool.release(slot, s_g2, s_g2.v)
        P.o("vector", lambda E: E.tensor_reduce(gcol[:, 2:3], gw[:, 0:16], AX.X,
                                                ALU.add),
            wait=[(s_g2, s_g2.v)], inc=s_g1)
        P.o("vector", lambda E: E.reciprocal(gcol[:, 3:4], gcol[:, 2:3]),
            wait=[(s_g1, s_g1.v)], inc=s_g1)
        P.o("vector", lambda E: E.tensor_reduce(gcol[:, 4:5], gw[:, 0:16], AX.X,
                                                ALU.max),
            wait=[(s_g1, s_g1.v)], inc=s_g1)
        P.o("vector", lambda E: E.tensor_single_scalar(gw[:, 16:32], gw[:, 0:16],
                                                       gcol[:, 4:5], ALU.is_equal),
            wait=[(s_g1, s_g1.v)], inc=s_g1)
        P.o("vector", lambda E: E.scalar_tensor_tensor(gw[:, 16:32], gw[:, 16:32],
                                                       -2.0, gw[:, 0:16], ALU.mult,
                                                       ALU.add),
            wait=[(s_g1, s_g1.v)], inc=s_g1)
        P.o("vector", lambda E: E.tensor_reduce(gcol[:, 5:6], gw[:, 16:32], AX.X,
                                                ALU.max), wait=[(s_g1, s_g1.v)],
            inc=s_g1)
        P.o("vector", lambda E: E.tensor_single_scalar(gw[:, 16:32], gw[:, 0:16],
                                                       gcol[:, 5:6], ALU.is_ge),
            wait=[(s_g1, s_g1.v)], inc=s_g1)
        P.o("vector", lambda E: E.tensor_mul(gw[:, 16:32], gw[:, 16:32], gw[:, 0:16]),
            wait=[(s_g1, s_g1.v)], inc=s_g1)
        P.o("vector", lambda E: E.tensor_single_scalar(gw[:, 16:32], gw[:, 16:32],
                                                       gcol[:, 3:4], ALU.mult),
            wait=[(s_g1, s_g1.v)], inc=s_g3)
        slot2, pt2, pw2 = pool.acquire()
        P.o("tensor", lambda E, o=pt2[0:16, 0:128]: E.transpose(o, gw[:, 16:32],
                                                                idf[:, :]),
            wait=list(pw2) + [(s_g3, s_g3.v), (s_idf, 16)], inc=s_trC)
        P.o("vector", lambda E, o=gfm[:, 128 * tt:128 * (tt + 1)], p=pt2:
            E.tensor_copy(o, p[0:16, 0:128]), wait=[(s_trC, s_trC.v)], inc=s_gf)
        pool.release(slot2, s_gf, s_gf.v)
    GFR = [(s_gf, s_gf.v)]

    s_ie = P.sem("s_ie")
    s_oe = P.sem("s_oe")
    s_gbe = P.sem("s_gbe")
    s_hmm = P.sem("s_hmm")
    s_hev = P.sem("s_hev")
    s_omm = P.sem("s_omm")
    s_moev = P.sem("s_moev")
    gelu_val = {}
    for c in range(8):
        buf = c % 2
        iex = [arena[:, 8192 * buf + 4096 * le:8192 * buf + 4096 * (le + 1)]
               for le in range(2)]
        oex = oexb[:, 512 * buf:512 * (buf + 1)]
        wld = ([(s_hmm, 32 * (c - 1))] if c >= 2 else []) + \
            ([(s_wmm, WMMF)] if c < 2 else [])
        iev = []
        for le in range(2):
            iev.append(P.dma("gpsimd", iex[le].rearrange("p (kc n) -> p kc n", n=512),
                             iexp[2 * c + le].rearrange("(kc p) n -> p kc n", p=128),
                             s_ie, wait=wld if le == 0 else []))
        ow = [(s_omm, 8 * (c - 1))] if c >= 2 else [(s_wmm, WMMF)]
        oev = P.dma("gpsimd", oex.rearrange("p (e hc n) -> p e hc n", hc=4, n=64),
                    oexp.rearrange("e (hc p) n -> p e hc n", p=128)
                    [:, 2 * c:2 * (c + 1), :, :], s_oe, wait=ow)
        for tg in range(NTG):
            slot, pt, pw = pool.acquire()
            P.o("tensor", MM(pt[:, 0:TG], selcs[:, 128 * c:128 * (c + 1)],
                             gfm[:, TG * tg:TG * (tg + 1)], True, True),
                wait=list(pw) + (GFR if (c == 0 and tg == 0) else []), inc=s_zmm)
            P.o("vector", lambda E, o=gbs[:, TG * tg:TG * (tg + 1)], p=pt:
                E.tensor_copy(o, p[:, 0:TG]),
                wait=[(s_zmm, s_zmm.v)] + ([(s_moev, 4 * c)] if c >= 1 and tg == 0
                                           else []), inc=s_gbe)
            pool.release(slot, s_gbe, s_gbe.v)
        gbv = s_gbe.v
        for le in range(2):
            E_idx = 2 * c + le
            for hc in range(4):
                for tg in range(NTG):
                    slot, pt, pw = pool.acquire()
                    w = list(pw)
                    if hc == 0 and tg == 0:
                        w += [(s_ie, iev[le])] + (H2R if c == 0 and le == 0 else [])
                    for kc in range(NCH):
                        P.o("tensor", MM(pt[:, 0:TG],
                                         iex[le][:, 512 * kc + 128 * hc:
                                                 512 * kc + 128 * (hc + 1)],
                                         h2b[:, TPC * kc + TG * tg:
                                             TPC * kc + TG * (tg + 1)],
                                         kc == 0, kc == NCH - 1),
                            wait=w if kc == 0 else [],
                            inc=(s_hmm if kc == NCH - 1 else None))
                    hw = [(s_hmm, s_hmm.v)]
                    if hc == 0 and tg == 0:
                        hw.append((s_omm, 4 * E_idx))
                        if c == 0 and le == 0:
                            hw.append((s_zsc, ZSCF))
                    P.o("scalar", lambda E, o=hidb[:, TPC * hc + TG * tg:
                                                   TPC * hc + TG * (tg + 1)], p=pt:
                        E.activation(out=o, in_=p[:, 0:TG], func=AF.Gelu),
                        wait=hw, inc=s_hev)
                    gelu_val[(hc, tg)] = s_hev.v
                    pool.release(slot, s_hev, s_hev.v)
            for tg in range(NTG):
                for hc in range(4):
                    w = [(s_hev, gelu_val[(hc, tg)])]
                    if hc == 0:
                        w.append((s_oe, oev))
                        if le == 0 and c >= 1:
                            w.append((s_moev, 4 * (c - 1)))
                    P.o("tensor", lambda E, o=pbig[64 * le:64 * (le + 1),
                                                   TG * tg:TG * (tg + 1)],
                        l=oex.rearrange("p (e hc n) -> p e hc n", hc=4, n=64)
                        [:, le, hc, :],
                        r=hidb[:, TPC * hc + TG * tg:TPC * hc + TG * (tg + 1)],
                        st=(hc == 0), sp=(hc == 3),
                        tp=(0, 64 * le): E.matmul(o, l, r, start=st, stop=sp,
                                                  tile_position=tp,
                                                  skip_group_check=True),
                        wait=w, inc=(s_omm if hc == 3 else None))
        for tg in range(NTG):
            P.o("vector", lambda E, t=tmpC, p=pbig[:, TG * tg:TG * (tg + 1)],
                g=gbs[:, TG * tg:TG * (tg + 1)]: E.tensor_mul(t[:, :], p, g),
                wait=[(s_omm, s_omm.v), (s_gbe, gbv)])
            P.o("vector", lambda E, o=x1sb[:, TPC * c + TG * tg:
                                           TPC * c + TG * (tg + 1)], t=tmpC:
                E.tensor_add(o, o, t[:, :]), inc=s_moev)
    X2R = [(s_moev, s_moev.v)]
    PBIG_LAST_C = [(s_moev, s_moev.v)]
    if FDBG:
        s_dx2 = P.sem("s_dx2")
        for c in range(NCH):
            P.dma("sync", dx2[128 * c:128 * (c + 1), :],
                  x1sb[:, TPC * c:TPC * (c + 1)], s_dx2, wait=X2R)
    P.pop_scope()

    P.push_scope()
    h3b = P.sb("d_h3", [128, NCH * TPC], BF16)
    wqb2 = P.sb("d_wq", [128, 8 * D], BF16)
    wkvb2 = P.sb("d_wkv", [128, 8 * 2048], BF16)
    qch2 = P.sb("d_qch", [128, 2 * TPC], BF16)
    ktm = P.sb("d_ktm", [128, 2 * 1024], BF16)
    vtm = P.sb("d_vtm", [128, 2 * 1024], BF16)
    ctxsb = P.sb("d_ctx", [65, H * 65], F32)
    scrD = scrC
    scrD2 = scrC2

    s_dw = s_aw
    P.dma("gpsimd", wqb2.rearrange("p (kc n) -> p kc n", n=D),
          wqkvT.rearrange("(kc p) n -> p kc n", p=128)[:, :, 0:D], s_dw, serial=False,
          wait=X2R)
    P.dma("gpsimd", wkvb2.rearrange("p (kc n) -> p kc n", n=2048),
          wqkvT.rearrange("(kc p) n -> p kc n", p=128)[:, :, D:3 * D], s_dw,
          serial=False, wait=X2R)
    dwv = [(s_dw, s_dw.v)]

    def h3_slice(c, tg):
        return h3b[:, TPC * c + TG * tg:TPC * c + TG * (tg + 1)]
    _, _, s_app3 = emit_ln(P, "dln", pool, pbig, h3_slice,
                           ones_col_bf[:, :], ones_row_f[:, :], ones_row64,
                           CW + PBIG_LAST_C + dwv,
                           scrD, scrD2, st1, st2, st3p, rstd_b, mst_b,
                           src_sb=x1sb, src_ready=X2R, nbuf=1, sems=ln_sems)
    H3R = [(s_app3, s_app3.v)]

    s_qmm = P.sem("s_qmm")
    s_qg = P.sem("s_qg")
    s_stq2 = P.sem("s_stq2")
    for oc in range(NCH):
        buf = oc % 2
        dst = qch2[:, TPC * buf:TPC * (buf + 1)]
        for tg in range(NTG):
            slot, pt, pw = pool.acquire()
            w = list(pw) + (H3R if (oc == 0 and tg == 0) else [])
            if tg == 0 and oc >= 2:
                w.append((s_stq2, s_stq2.v - 16))
            for kc in range(NCH):
                P.o("tensor", MM(pt[:, 0:TG],
                                 wqb2[:, D * kc + 128 * oc:D * kc + 128 * (oc + 1)],
                                 h3b[:, TPC * kc + TG * tg:TPC * kc + TG * (tg + 1)],
                                 kc == 0, kc == NCH - 1),
                    wait=w if kc == 0 else [], inc=(s_qmm if kc == NCH - 1 else None))
            P.o("scalar", lambda E, o=dst[:, TG * tg:TG * (tg + 1)], p=pt:
                E.activation(out=o, in_=p[:, 0:TG], func=AF.Relu),
                wait=[(s_qmm, s_qmm.v)], inc=s_qg)
            pool.release(slot, s_qg, s_qg.v)
        P.dma("sync", q2_in[128 * oc:128 * (oc + 1), :], dst, s_stq2,
              wait=[(s_qg, s_qg.v)])

    s_kvm = P.sem("s_kvm")
    s_kve = P.sem("s_kve")
    s_cmm = P.sem("s_cmm")
    pcD = [pbig[0:64, 0:7 * 65], pbig[0:64, TG:TG + 7 * 65],
           pbig[0:64, 2 * TG:2 * TG + 2 * 65]]
    NT = TPC // 128
    for tt in range(NT):
        buf = tt % 2
        kslot = ktm[:, 1024 * buf:1024 * (buf + 1)]
        vslot = vtm[:, 1024 * buf:1024 * (buf + 1)]
        for pi, dst in ((0, kslot), (1, vslot)):
            for vc in range(2):
                slot, pt, pw = pool.acquire()
                w = list(pw)
                if pi == 0 and vc == 0:
                    if tt >= 2:
                        w.append((s_cmm, tt - 1))
                    if tt == 0:
                        w += H3R
                for kc in range(NCH):
                    P.o("tensor", MM(pt[:, 0:TG],
                                     h3b[:, TPC * kc + 128 * tt:TPC * kc + 128 * (tt + 1)],
                                     wkvb2[:, 2048 * kc + 1024 * pi + TG * vc:
                                           2048 * kc + 1024 * pi + TG * (vc + 1)],
                                     kc == 0, kc == NCH - 1),
                        wait=w if kc == 0 else [],
                        inc=(s_kvm if kc == NCH - 1 else None))
                if pi == 0:
                    P.o("vector", lambda E, o=dst[:, TG * vc:TG * (vc + 1)], p=pt:
                        E.tensor_scalar_max(o, p[:, 0:TG], 0.0),
                        wait=[(s_kvm, s_kvm.v)], inc=s_kve)
                else:
                    P.o("vector", lambda E, o=dst[:, TG * vc:TG * (vc + 1)], p=pt:
                        E.tensor_copy(o, p[:, 0:TG]),
                        wait=[(s_kvm, s_kvm.v)], inc=s_kve)
                pool.release(slot, s_kve, s_kve.v)
        kvv = s_kve.v
        for h in range(H):
            bank, off = (0, h) if h < 7 else ((1, h - 7) if h < 14 else (2, h - 14))
            P.o("tensor", lambda E, o=pcD[bank][:, 65 * off:65 * off + 64],
                l=kslot[:, 64 * h:64 * (h + 1)], r=vslot[:, 64 * h:64 * (h + 1)],
                st=(tt == 0), sp=(tt == NT - 1): E.matmul(
                    o, l, r, start=st, stop=sp, skip_group_check=True),
                wait=([(s_kve, kvv)] if h == 0 else []))
            P.o("tensor", lambda E, o=pcD[bank][:, 65 * off + 64:65 * off + 65],
                l=kslot[:, 64 * h:64 * (h + 1)],
                st=(tt == 0), sp=(tt == NT - 1): E.matmul(
                    o, l, ones_col_bf[:, 0:1], start=st, stop=sp,
                    skip_group_check=True),
                inc=(s_cmm if h == H - 1 else None))
    s_cev = P.sem("s_cev")
    s_stc = P.sem("s_stc")
    P.o("vector", lambda E: E.memset(ctxsb[64:65, :], 0.0), inc=s_cev)
    for h in range(H):
        bank, off = (0, h) if h < 7 else ((1, h - 7) if h < 14 else (2, h - 14))
        P.o("vector", lambda E, o=ctxsb[0:64, 65 * h:65 * (h + 1)],
            p=pcD[bank][:, 65 * off:65 * (off + 1)]: E.tensor_copy(o, p),
            wait=[(s_cmm, s_cmm.v)] if h == 0 else [], inc=s_cev)
    P.dma("sync", ctx_in[:, :], ctxsb[:, :], s_stc, wait=[(s_cev, s_cev.v)])
    PBIG_LAST_D = [(s_cev, s_cev.v)]
    STQ2 = s_stq2.v
    STC = s_stc.v
    P.pop_scope()

    P.o("gpsimd", lambda E: E.collective_compute(
        "AllGather", ALU.bypass, GROUPS, ins=[q2_in[:, :]], outs=[q2_out[:, :]]),
        wait=[(s_stq2, STQ2)], inc=s_cc)
    P.o("gpsimd", lambda E: E.collective_compute(
        "AllReduce", ALU.add, GROUPS, ins=[ctx_in[:, :]], outs=[ctx_out[:, :]]),
        wait=[(s_stc, STC)], inc=s_cc)
    CC45 = s_cc.v

    P.push_scope()
    qp2 = P.sb("e_qp", [128, 2 * S], BF16)
    ctxf = P.sb("e_ctxf", [65, 4 * 65], F32)
    ctxb2 = P.sb("e_ctxb", [128, 4 * 65], BF16)
    aot2 = P.sb("e_aot", [64, 2 * TG], BF16)
    pvsbE = P.sb("e_pvsb", [64, 2 * TG], F32)
    drowE = P.sb("e_drow", [1, 2 * 4 * TG], F32)
    dcolE = P.sb("e_dcol", [4, 2 * TG], F32)
    rcolE = P.sb("e_rcol", [4, 2 * TG], F32)
    rrowE = P.sb("e_rrow", [1, 2 * 4 * TG], F32)

    s_el = s_bl
    dyn_dma(ctxmy[:, :], lambda E: ctx_out[:, bass.ts(DV(E, "hg"), 4 * 65)],
            s_el, [(s_cc, CC45)])
    q23 = q2_out.rearrange("(r t) c -> r t c", t=D)
    dyn_dma(q2my.rearrange("(r t) c -> r t c", t=256),
            (lambda E: q23[:, bass.ts(DV(E, "hg"), 256), :][:, 0:256, :]),
            s_el, [])
    P.dma("sync", ctxf[:, :], ctxmy[:, :], s_el)
    for hp in range(2):
        for r in range(4):
            P.dma("sync", qp2[:, S * hp + TPC * r:S * hp + TPC * (r + 1)],
                  q2my[256 * r + 128 * hp:256 * r + 128 * (hp + 1), :], s_el)
    ELW = [(s_el, s_el.v)]
    s_sumE = s_cn
    P.o("vector", lambda E: E.tensor_copy(ctxb2[0:65, :], ctxf[:, :]),
        wait=ELW, inc=s_sumE)
    P.dma("sync", ctxb2[64:128, :], ctxb2[0:64, :], s_sumE,
          wait=[(s_sumE, s_sumE.v)])
    CTXR = [(s_sumE, s_sumE.v)]

    s_emm = P.sem("s_emm")
    s_ev1 = P.sem("s_ev1")
    s_ebm = P.sem("s_ebm")
    s_eev = P.sem("s_eev")
    s_ast3 = P.sem("s_ast3")
    s_ecp2 = P.sem("s_ecp2")
    s_edm = P.sem("s_edm")
    s_ecpd = P.sem("s_ecpd")
    s_edma = P.sem("s_edma")
    s_erc = P.sem("s_erc")
    slots_e = [pool.acquire() for _ in range(4)]
    ppv2L = [slots_e[0][1], slots_e[1][1]]
    pb2L = [slots_e[2][1], slots_e[3][1]]
    EPF = [w for sl in slots_e for w in sl[2]]
    NQ = S // TG

    def emit_dchain(m):
        par = m % 2
        pp = 64 * par
        hl_, ch = m // 4, m % 4
        hp_ = hl_ // 2
        base_ = 64 * (hl_ % 2)
        w = [(s_ecpd, m - 1)] if m >= 2 else (list(CTXR) + PBIG_LAST_D)
        for q4 in range(4):
            P.o("tensor", lambda E, o=pbig[pp:pp + 1, TG * q4:TG * (q4 + 1)],
                l=ctxb2[base_:base_ + 64, 65 * hl_ + 64:65 * hl_ + 65],
                r=qp2[base_:base_ + 64, S * hp_ + 4 * TG * ch + TG * q4:
                      S * hp_ + 4 * TG * ch + TG * (q4 + 1)]:
                E.matmul(o, l, r, start=True, stop=True, skip_group_check=True),
                wait=w if q4 == 0 else [], inc=(s_edm if q4 == 3 else None))
        P.o("scalar", lambda E, o=drowE[0:1, 4 * TG * par:4 * TG * (par + 1)],
            i=pbig[pp:pp + 1, 0:4 * TG]: E.activation(
                out=o, in_=i, func=AF.Copy, bias=EPS_F32),
            wait=[(s_edm, m + 1)] +
            ([(s_edma, 128 * (m - 2) + 64)] if m >= 2 else []),
            inc=s_ecpd)
        for j in range(4):
            P.dma("sync", dcolE[j:j + 1, TG * par:TG * (par + 1)],
                  drowE[0:1, 4 * TG * par + TG * j:4 * TG * par + TG * (j + 1)],
                  s_edma,
                  wait=([(s_ecpd, m + 1)] + ([(s_erc, m - 1)] if m >= 2 else []))
                  if j == 0 else [])
        P.o("vector", lambda E, o=rcolE[:, TG * par:TG * (par + 1)],
            i=dcolE[:, TG * par:TG * (par + 1)]: E.reciprocal(o, i),
            wait=[(s_edma, 128 * m + 64)], inc=s_erc)
        for j in range(4):
            P.dma("sync", rrowE[0:1, 4 * TG * par + TG * j:4 * TG * par + TG * (j + 1)],
                  rcolE[j:j + 1, TG * par:TG * (par + 1)], s_edma,
                  wait=([(s_erc, m + 1)] + ([(s_ebm, 4 * (m - 1))] if m >= 2 else []))
                  if j == 0 else [])

    emit_dchain(0)
    for hl in range(4):
        hp, base = hl // 2, 64 * (hl % 2)
        for g in range(NQ):
            n = NQ * hl + g
            m = n // 4
            if n % 4 == 0 and m + 1 < 16:
                emit_dchain(m + 1)
            buf = n % 2
            j = n % 4
            par = m % 2
            ppv2 = ppv2L[buf]
            pb2 = pb2L[buf]
            pvb = pvsbE[:, TG * buf:TG * (buf + 1)]
            w = (list(CTXR) + EPF) if n < 2 else [(s_ecp2, n - 1)]
            P.o("tensor", lambda E, o=ppv2[0:64, 0:TG],
                l=ctxb2[base:base + 64, 65 * hl:65 * hl + 64],
                r=qp2[base:base + 64, S * hp + TG * g:S * hp + TG * (g + 1)]:
                E.matmul(o, l, r, start=True, stop=True, skip_group_check=True),
                wait=w, inc=s_emm)
            P.o("scalar", lambda E, o=pvb, i=ppv2[0:64, 0:TG]: E.activation(
                out=o, in_=i, func=AF.Copy),
                wait=[(s_emm, n + 1)] + ([(s_eev, n - 1)] if n >= 2 else []),
                inc=s_ecp2)
            P.o("tensor", MM(pb2[0:64, 0:TG], ones_row_f[0:1, 0:64],
                             rrowE[0:1, 4 * TG * par + TG * j:
                                   4 * TG * par + TG * (j + 1)], True, True),
                wait=[(s_edma, 128 * (m + 1))] +
                ([(s_eev, n - 1)] if n >= 2 else []),
                inc=s_ebm)
            slot_o = aot2[:, TG * (n % 2):TG * (n % 2 + 1)]
            w2 = [(s_ast3, s_ast3.v - 16)] if n >= 2 else []
            P.o("vector", lambda E, o=slot_o, p=pvb, b=pb2: E.tensor_mul(
                o, p, b[0:64, 0:TG]),
                wait=[(s_ecp2, n + 1), (s_ebm, n + 1)] + w2, inc=s_eev)
            P.dma("sync", ao2_in[64 * hl:64 * (hl + 1), TG * g:TG * (g + 1)], slot_o,
                  s_ast3, wait=[(s_eev, s_eev.v)])
    for sl in slots_e:
        pool.release(sl[0], s_eev, s_eev.v)
    AST3 = s_ast3.v
    P.pop_scope()

    P.o("gpsimd", lambda E: E.collective_compute(
        "AllGather", ALU.bypass, GROUPS, ins=[ao2_in[:, :]], outs=[ao2_out[:, :]]),
        wait=[(s_ast3, AST3)], inc=s_cc)
    CC6 = s_cc.v

    x3d = nc.dram_tensor("x3d", [D, TPC], F32).ap()
    P.push_scope()
    ao2b = P.sb("f_ao2", [128, NCH * TPC], BF16)
    wos2 = P.sb("f_wos", [128, 2 * 1024], BF16)
    s_fl = s_bl
    dyn_dma(ao2my[:, :], (lambda E: ao2_out[:, bass.ts(DV(E, "hg"), TPC)]),
            s_fl, [(s_cc, CC6)])
    for c in range(NCH):
        P.dma("sync", ao2b[:, TPC * c:TPC * (c + 1)], ao2my[128 * c:128 * (c + 1), :],
              s_fl)
    AO2L = [(s_fl, s_fl.v)]
    s_pm = P.sem("s_pm")
    s_pe = P.sem("s_pe")
    s_st3 = P.sem("s_st3")
    s_wol2 = s_wol
    for oc in range(NCH):
        wbuf = wos2[:, 1024 * (oc % 2):1024 * (oc % 2 + 1)]
        wlv = P.dma("gpsimd", wbuf.rearrange("p (kc n) -> p kc n", n=128),
                    wpT.rearrange("(kc p) n -> p kc n", p=128)
                    [:, :, 128 * oc:128 * (oc + 1)], s_wol2,
                    wait=[(s_pm, 4 * (oc - 1))] if oc >= 2
                    else [(s_ast3, AST3)])
        for tg in range(NTG):
            slot, pt, pw = pool.acquire()
            for kc in range(NCH):
                P.o("tensor", MM(pt[:, 0:TG],
                                 wbuf[:, 128 * kc:128 * (kc + 1)],
                                 ao2b[:, TPC * kc + TG * tg:TPC * kc + TG * (tg + 1)],
                                 kc == 0, kc == NCH - 1),
                    wait=(list(pw) + AO2L + [(s_wol2, wlv)] + PBIG_LAST_D)
                    if kc == 0 else [],
                    inc=(s_pm if kc == NCH - 1 else None))
            P.o("vector", lambda E, o=x1sb[:, TPC * oc + TG * tg:
                                           TPC * oc + TG * (tg + 1)], p=pt,
                b=bp_s[:, oc:oc + 1]: E.scalar_tensor_tensor(
                    o, p[:, 0:TG], b, o, ALU.add, ALU.add),
                wait=[(s_pm, s_pm.v)] + (X2R if oc == 0 and tg == 0 else []),
                inc=s_pe)
            pool.release(slot, s_pe, s_pe.v)
        P.dma("sync", x3d[128 * oc:128 * (oc + 1), :], x1sb[:, TPC * oc:TPC * (oc + 1)],
              s_st3, wait=[(s_pe, s_pe.v)])
    X3ALL = [(s_st3, s_st3.v)]
    P.pop_scope()
    P.pop_scope()

    P.push_scope()
    xcF = P.sb("f_xc", [128, 2 * TPC], F32)
    scrF = P.sb("f_scr", [128, TPC], BF16)
    scrF2 = P.sb("f_scr2", [128, TPC], BF16)
    str_ = P.sb("f_str", [1, TPC], F32)
    sts = P.sb("f_sts", [1, TPC], F32)
    hhy = P.sb("f_hhy", [128, NCH * TPC], BF16)
    srowF = P.sb("f_srow", [1, 2 * D], F32)
    qsb = P.sb("f_qsb", [128, 2 * D], U8)
    w1c = P.sb("f_w1c", [128, 2 * D], BF16)
    y1c = P.sb("f_y1c", [128, 2 * TPC], BF16)
    w2b = P.sb("f_w2b", [128, 32 * D], BF16)
    tmpvF = P.sb("f_tmpv", [128, TG], F32)
    tmpx3 = P.sb("f_tmpx3", [128, 2 * TG], F32)
    stm = st1
    stv = st2

    s_misc = s_tmp
    b2hv = P.o("vector", lambda E: E.tensor_scalar_mul(b2h_s[:, :], b2_s[:, :], 0.5),
               wait=CW, inc=s_misc)

    s_castF, s_sqF, s_mmF, s_strF, _, s_appF, s_xsF = ln_sems
    castF0, sqF0, mmF0, appF0 = s_castF.v, s_sqF.v, s_mmF.v, s_appF.v
    for c in range(NCH):
        buf = c % 2
        ap = xcF[:, TPC * buf:TPC * (buf + 1)]
        w = list(X3ALL) + [(s_xsF, s_xsF.v)]
        if c >= 2:
            w.append((s_castF, castF0 + c - 1))
        lv = P.dma("sync", ap, x3d[128 * c:128 * (c + 1), :], s_xsF, wait=w)
        wv = [(s_xsF, lv)]
        if c >= 1:
            wv.append((s_mmF, mmF0 + c))
        P.o("vector", lambda E, o=scrF[:, :], i=ap: E.tensor_copy(o, i),
            wait=wv, inc=s_castF)
        P.o("scalar", lambda E: E.activation(out=scrF2[:, :], in_=scrF[:, :],
                                             func=AF.Square),
            wait=[(s_castF, castF0 + c + 1)] + ([(s_mmF, mmF0 + c)] if c >= 1 else []),
            inc=s_sqF)
        for tg in range(NTG):
            P.o("tensor", MM(pbig[0:1, TG * tg:TG * (tg + 1)], ones_col_bf[:, :],
                             scrF[:, TG * tg:TG * (tg + 1)], c == 0, c == NCH - 1),
                wait=([(s_sqF, sqF0 + c + 1)] + (PBIG_LAST_D if c == 0 else []))
                if tg == 0 else [])
            P.o("tensor", MM(pbig[32:33, TG * tg:TG * (tg + 1)], ones_col_bf[:, :],
                             scrF2[:, TG * tg:TG * (tg + 1)], c == 0, c == NCH - 1),
                inc=(s_mmF if tg == NTG - 1 else None))
    st3pF = pbig[64:65, 0:TPC]
    P.o("vector", lambda E: E.tensor_scalar_mul(stm[:, :], pbig[0:1, 0:TPC], 1.0 / D),
        wait=[(s_mmF, s_mmF.v)])
    P.o("vector", lambda E: E.tensor_scalar_mul(stv[:, :], pbig[32:33, 0:TPC], 1.0 / D),
        inc=s_strF)
    P.o("scalar", lambda E: E.activation(out=st3pF, in_=stm[:, :], func=AF.Square),
        wait=[(s_strF, s_strF.v)], inc=s_strF)
    P.o("vector", lambda E: E.tensor_sub(stv[:, :], stv[:, :], st3pF),
        wait=[(s_strF, s_strF.v)])
    P.o("vector", lambda E: E.tensor_scalar_add(str_[:, :], stv[:, :], EPS_LN),
        inc=s_strF)
    P.o("scalar", lambda E: E.activation(out=str_[:, :], in_=str_[:, :], func=AF.Ln),
        wait=[(s_strF, s_strF.v)])
    P.o("scalar", lambda E: E.activation(out=str_[:, :], in_=str_[:, :], func=AF.Exp,
                                         scale=-0.5), inc=s_strF)
    P.o("vector", lambda E: E.tensor_scalar_add(sts[:, :], stv[:, :], EPS_LN),
        wait=[(s_strF, s_strF.v)], inc=s_strF)
    P.o("vector", lambda E: E.reciprocal(sts[:, :], sts[:, :]),
        wait=[(s_strF, s_strF.v)], inc=s_strF)
    P.o("vector", lambda E: E.tensor_mul(sts[:, :], sts[:, :], stv[:, :]),
        wait=[(s_strF, s_strF.v)], inc=s_strF)
    P.o("vector", lambda E: E.tensor_scalar_add(sts[:, :], sts[:, :], EPS_LN),
        inc=s_strF)
    P.o("scalar", lambda E: E.activation(out=sts[:, :], in_=sts[:, :], func=AF.Ln),
        wait=[(s_strF, s_strF.v)])
    P.o("scalar", lambda E: E.activation(out=sts[:, :], in_=sts[:, :], func=AF.Exp,
                                         scale=-0.5), inc=s_strF)
    P.o("vector", lambda E: E.tensor_mul(sts[:, :], sts[:, :], str_[:, :]),
        wait=[(s_strF, s_strF.v)])
    P.o("vector", lambda E: E.tensor_mul(stv[:, :], stm[:, :], sts[:, :]))
    P.o("vector", lambda E: E.tensor_scalar_mul(stv[:, :], stv[:, :], -1.0))
    P.o("vector", lambda E: E.tensor_mul(stm[:, :], stm[:, :], str_[:, :]))
    P.o("vector", lambda E: E.tensor_scalar_mul(stm[:, :], stm[:, :], -1.0))
    STRF = P.sem("f_strfin")
    P.o("vector", lambda E: E.tensor_scalar_add(str_[:, :], str_[:, :], 1.0),
        inc=STRF)

    s_bcF = ln_sems[4]
    for tg in range(NTG):
        for strip, orow, bdst in ((sts, ones_row_f[:, :], rstd_b),
                                  (stv, ones_row64, mst_b)):
            slot, pt, pw = pool.acquire()
            vmm = P.o("tensor", MM(pt[:, 0:TG], orow,
                                   strip[0:1, TG * tg:TG * (tg + 1)], True, True),
                      wait=list(pw) + [(STRF, 1)], inc=s_bcF)
            P.o("scalar", lambda E, o=bdst[:, TG * tg:TG * (tg + 1)], p=pt:
                E.activation(out=o, in_=p[:, 0:TG], func=AF.Copy),
                wait=[(s_bcF, vmm)], inc=s_bcF)
            pool.release(slot, s_bcF, s_bcF.v)
    bcFv = s_bcF.v
    for c in range(NCH):
        buf = c % 2
        ap = xcF[:, TPC * buf:TPC * (buf + 1)]
        w = [(s_mmF, mmF0 + NCH), (s_xsF, s_xsF.v)]
        if c >= 2:
            w.append((s_appF, appF0 + 4 * (c - 1)))
        lv = P.dma("sync", ap, x3d[128 * c:128 * (c + 1), :], s_xsF, wait=w)
        for tg in range(NTG):
            P.o("vector", lambda E, t=tmpvF, x=ap[:, TG * tg:TG * (tg + 1)],
                r=rstd_b[:, TG * tg:TG * (tg + 1)]: E.tensor_mul(t[:, :], x, r),
                wait=[(s_bcF, bcFv), (s_xsF, lv)])
            P.o("vector", lambda E, t=tmpvF, m=mst_b[:, TG * tg:TG * (tg + 1)]:
                E.tensor_add(t[:, :], t[:, :], m))
            P.o("vector", lambda E, o=hhy[:, TPC * c + TG * tg:TPC * c + TG * (tg + 1)],
                t=tmpvF, g=gam_s[:, c:c + 1], b=bet_s[:, c:c + 1]:
                E.tensor_scalar(o, t[:, :], g, b, ALU.mult, ALU.add), inc=s_appF)
    HHR = [(s_appF, s_appF.v)]

    s_q1 = P.sem("f_q1")
    s_dq = P.sem("f_dq")
    s_fmm = P.sem("f_fmm")
    s_sil = P.sem("f_sil")
    s_sty = P.sem("f_sty")
    s_sbm = P.sem("f_sbm")
    for yc in range(32):
        buf = yc % 2
        qb = qsb[:, D * buf:D * (buf + 1)]
        sr = srowF[0:1, D * buf:D * (buf + 1)]
        wc = w1c[:, D * buf:D * (buf + 1)]
        yb = y1c[:, TPC * buf:TPC * (buf + 1)]
        w = ([(s_dq, 2 * (yc - 1))] if yc >= 2 else list(X3ALL)) + [(s_q1, s_q1.v)]
        l1 = P.dma("sync", qb.rearrange("p (kc n) -> p kc n", n=128),
                   fc1q.rearrange("(kc p) n -> p kc n", p=128)
                   [:, :, 128 * yc:128 * (yc + 1)], s_q1, wait=w)
        l2 = P.dma("gpsimd", sr, sc1x[yc:yc + 1, :], s_q1)
        for hb in range(2):
            slot_s, pts, pws = pool.acquire()
            P.o("tensor", MM(pts[:, 0:TG], ones_row_f[0:1, :],
                             sr[0:1, TG * hb:TG * (hb + 1)], True, True),
                wait=list(pws) + ([(s_q1, l2)] if hb == 0 else []), inc=s_sbm)
            P.o("vector", lambda E, o=wc[:, TG * hb:TG * (hb + 1)],
                q=qb[:, TG * hb:TG * (hb + 1)], s=pts: E.scalar_tensor_tensor(
                    o, q, -128.0, s[:, 0:TG], ALU.add, ALU.mult),
                wait=[(s_sbm, s_sbm.v)] + ([(s_fmm, 4 * (yc - 1))]
                                           if (yc >= 2 and hb == 0) else []),
                inc=s_dq)
            pool.release(slot_s, s_dq, s_dq.v)
        for tg in range(NTG):
            slot, pt, pw = pool.acquire()
            for kc in range(NCH):
                P.o("tensor", MM(pt[:, 0:TG], wc[:, 128 * kc:128 * (kc + 1)],
                                 hhy[:, TPC * kc + TG * tg:TPC * kc + TG * (tg + 1)],
                                 kc == 0, kc == NCH - 1),
                    wait=(list(pw) + [(s_dq, s_dq.v)] +
                          (HHR if yc == 0 and tg == 0 else [])) if kc == 0 else [],
                    inc=(s_fmm if kc == NCH - 1 else None))
            P.o("scalar", lambda E, o=yb[:, TG * tg:TG * (tg + 1)], p=pt,
                b=b1_s[:, yc:yc + 1]: E.activation(out=o, in_=p[:, 0:TG], func=AF.Silu,
                                                   bias=b),
                wait=[(s_fmm, s_fmm.v)] + ([(s_sty, 16 * (yc - 1))] if yc >= 2 and
                                           tg == 0 else []), inc=s_sil)
            pool.release(slot, s_sil, s_sil.v)
        P.dma("sync", y1d[128 * yc:128 * (yc + 1), :], yb, s_sty,
              wait=[(s_sil, s_sil.v)])
    FC1ALL = [(s_sty, s_sty.v)]

    s_q2 = s_q1
    s_dq2 = P.sem("f_dq2")
    for yc in range(32):
        buf = yc % 2
        qb = qsb[:, D * buf:D * (buf + 1)]
        sr = srowF[0:1, D * buf:D * (buf + 1)]
        w = [(s_q2, s_q2.v)]
        if yc < 2:
            w.append((s_dq, 64))
        else:
            w.append((s_dq2, 2 * (yc - 1)))
        l1 = P.dma("sync", qb, fc2q[128 * yc:128 * (yc + 1), :], s_q2, wait=w)
        l2 = P.dma("gpsimd", sr, sc2x[yc:yc + 1, :], s_q2)
        for hb in range(2):
            slot_s, pts, pws = pool.acquire()
            P.o("tensor", MM(pts[:, 0:TG], ones_row_f[0:1, :],
                             sr[0:1, TG * hb:TG * (hb + 1)], True, True),
                wait=list(pws) + ([(s_q2, l2)] if hb == 0 else []), inc=s_sbm)
            P.o("vector", lambda E,
                o=w2b[:, D * yc + TG * hb:D * yc + TG * (hb + 1)],
                q=qb[:, TG * hb:TG * (hb + 1)], s=pts: E.scalar_tensor_tensor(
                    o, q, -128.0, s[:, 0:TG], ALU.add, ALU.mult),
                wait=[(s_sbm, s_sbm.v)], inc=s_dq2)
            pool.release(slot_s, s_dq2, s_dq2.v)
    W2R = [(s_dq2, s_dq2.v)]

    s_y1 = P.sem("f_y1")
    s_2mm = P.sem("f_2mm")
    s_x3l = P.sem("f_x3l")
    s_fin = P.sem("f_fin")
    s_st4 = P.sem("f_st4")
    for tg in range(NTG):
        for strip, orow, bdst in ((str_, ones_row_f[:, :], rstd_b),
                                  (stm, ones_row_f[:, :], mst_b)):
            slot, pt, pw = pool.acquire()
            vmm = P.o("tensor", MM(pt[:, 0:TG], orow,
                                   strip[0:1, TG * tg:TG * (tg + 1)], True, True),
                      wait=list(pw) + [(STRF, 1)] + (HHR if tg == 0 else []),
                      inc=s_bcF)
            P.o("scalar", lambda E, o=bdst[:, TG * tg:TG * (tg + 1)], p=pt:
                E.activation(out=o, in_=p[:, 0:TG], func=AF.Copy),
                wait=[(s_bcF, vmm)], inc=s_bcF)
            pool.release(slot, s_bcF, s_bcF.v)
    bcF4v = s_bcF.v
    for tg in range(NTG):
        wv = (list(W2R) + FC1ALL if tg == 0 else [(s_fin, s_fin.v)]) + \
            [(s_y1, s_y1.v)]
        l1 = P.dma("sync", hhy.rearrange("p (yc t) -> p yc t", t=TG),
                   y1d.rearrange("(yc p) t -> p yc t", p=128)
                   [:, :, TG * tg:TG * (tg + 1)], s_y1, wait=wv)
        for oc in range(NCH):
            n = NCH * tg + oc
            slot, pt, pw = pool.acquire()
            for yc in range(32):
                P.o("tensor", MM(pt[:, 0:TG],
                                 w2b[:, D * yc + 128 * oc:D * yc + 128 * (oc + 1)],
                                 hhy[:, TG * yc:TG * (yc + 1)],
                                 yc == 0, yc == 31),
                    wait=(list(pw) + [(s_y1, l1)] + (W2R if n == 0 else []))
                    if yc == 0 else [],
                    inc=(s_2mm if yc == 31 else None))
            gv = s_2mm.v
            xb3 = tmpx3[:, TG * (n % 2):TG * (n % 2 + 1)]
            lx = P.dma("sync", xb3,
                       x3d[128 * oc:128 * (oc + 1), TG * tg:TG * (tg + 1)], s_x3l,
                       wait=[(s_fin, s_fin.v - 1)] if n >= 2 else [])
            P.o("vector", lambda E, t=tmpvF, x=xb3,
                r=rstd_b[:, TG * tg:TG * (tg + 1)]: E.tensor_mul(t[:, :], x, r),
                wait=[(s_2mm, gv), (s_bcF, bcF4v), (s_x3l, lx), (s_misc, b2hv)] +
                ([(s_st4, 16 * n)] if n >= 1 else []))
            P.o("vector", lambda E, t=tmpvF,
                m=mst_b[:, TG * tg:TG * (tg + 1)]: E.tensor_add(t[:, :], t[:, :], m))
            P.o("vector", lambda E, t=tmpvF, b=b2h_s[:, oc:oc + 1]:
                E.tensor_single_scalar(t[:, :], t[:, :], b, ALU.add))
            P.o("vector", lambda E, t=tmpvF, p=pt: E.scalar_tensor_tensor(
                t[:, :], p[:, 0:TG], 0.5, t[:, :], ALU.mult, ALU.add), inc=s_fin)
            pool.release(slot, s_fin, s_fin.v)
            P.dma("sync", x4T[128 * oc:128 * (oc + 1), TG * tg:TG * (tg + 1)],
                  tmpvF[:, :], s_st4, wait=[(s_fin, s_fin.v)])
    P.pop_scope()
    if FDBG:
        s_dbg = P.sem("s_fdbg")
        for dst, src in ((dkq, kq_in), (dvi, v_in), (dao, ao_in), (dq2, q2_in),
                         (dctx, ctx_in), (dao2, ao2_in), (dx3, x3d)):
            P.dma("sync", dst[:, :], src[:, :], s_dbg, wait=[(s_st4, s_st4.v)])
        P.o("sync", lambda E: E.nop(), wait=[(s_st4, s_st4.v), (s_dbg, s_dbg.v)])
    else:
        P.o("sync", lambda E: E.nop(), wait=[(s_st4, s_st4.v)])
    P.build()
    return P


def stage_fused(inputs):
    x = inputs["L0_attn_x"]
    pos_all = np.arange(S, dtype=np.float32)
    inv = (1.0 / (10000.0 ** (np.arange(0, DR, 2, dtype=np.float32) / DR))).astype(np.float32)
    selcv = np.zeros((16, 8 * 128), BFNP)
    for c in range(8):
        selcv[2 * c, 128 * c:128 * c + 64] = 1.0
        selcv[2 * c + 1, 128 * c + 64:128 * (c + 1)] = 1.0
    sc1 = np.asarray(inputs["L1_ffn_fc1_scales"], dtype=np.float32)
    sc2 = np.asarray(inputs["L1_ffn_fc2_scales"], dtype=np.float32)
    sc1x = np.ascontiguousarray(
        np.stack([sc1[:, 128 * yc:128 * (yc + 1)].reshape(8 * 128)
                  for yc in range(32)]))
    sc2x = np.ascontiguousarray(sc2)
    common = {
        "wq": inputs["L0_attn_wq"], "wqc": inputs["L0_attn_wqc"],
        "wqr": inputs["L0_attn_wqr"], "wkv": inputs["L0_attn_wkv"],
        "wkr": inputs["L0_attn_wkr"], "wkc": inputs["L0_attn_wkc"],
        "wv": inputs["L0_attn_wv"],
        "bqx": np.ascontiguousarray(np.asarray(inputs["L0_attn_bq"]).reshape(2, 128).T),
        "bqcx": np.ascontiguousarray(np.asarray(inputs["L0_attn_bqc"]).reshape(7, 128).T),
        "bqrx": np.ascontiguousarray(np.asarray(inputs["L0_attn_bqr"]).reshape(1, 128).T),
        "bkvx": np.ascontiguousarray(np.asarray(inputs["L0_attn_bkv"]).reshape(2, 128).T),
        "bkrx": np.ascontiguousarray(np.asarray(inputs["L0_attn_bkr"]).reshape(1, 128).T),
        "bkcx": np.ascontiguousarray(np.asarray(inputs["L0_attn_bkc"]).reshape(7, 128).T),
        "bvx": np.asarray(inputs["L0_attn_bv"]).reshape(1, D),
        "onesc": np.ones((128, 1), np.float32),
        "onesr": np.ones((1, 128), np.float32),
        "tri": np.where(np.arange(128)[:, None] <= np.arange(128)[None, :],
                        0.0, -65536.0).astype(BFNP),
        "identb": np.eye(128, dtype=BFNP),
        "identf": np.eye(128, dtype=np.float32),
        "wo": inputs["L0_attn_wo"],
        "box": np.ascontiguousarray(np.asarray(inputs["L0_attn_bo"]).reshape(8, 128).T),
        "wrt": np.ascontiguousarray(np.asarray(inputs["L0_ffn_w_router"]).T),
        "centT": np.ascontiguousarray(np.asarray(inputs["L0_ffn_head_centroids"]).T),
        "tempx": np.asarray(inputs["L0_ffn_temperature"]).reshape(1, 1),
        "iexp": inputs["L0_ffn_input_experts"],
        "oexp": inputs["L0_ffn_output_experts"],
        "selc": selcv,
        "wqkvT": np.ascontiguousarray(np.asarray(inputs["L1_attn_w_qkv"]).T),
        "wpT": np.ascontiguousarray(np.asarray(inputs["L1_attn_w_proj"]).T),
        "bpx": np.ascontiguousarray(np.asarray(inputs["L1_attn_b_proj"]).reshape(8, 128).T),
        "gamx": np.ascontiguousarray(np.asarray(inputs["L1_ffn_ln_gamma"]).reshape(8, 128).T),
        "betx": np.ascontiguousarray(np.asarray(inputs["L1_ffn_ln_beta"]).reshape(8, 128).T),
        "fc1q": np.asarray(inputs["L1_ffn_fc1_q"]).astype(np.uint8),
        "sc1x": sc1x,
        "b1x": np.ascontiguousarray(np.asarray(inputs["L1_ffn_fc1_bias"]).reshape(32, 128).T),
        "fc2q": np.asarray(inputs["L1_ffn_fc2_q"]).astype(np.uint8),
        "sc2x": sc2x,
        "b2x": np.ascontiguousarray(np.asarray(inputs["L1_ffn_fc2_bias"]).reshape(8, 128).T),
    }
    maps = []
    for c in range(NCORES):
        b, i = c // 4, c % 4
        m = dict(common)
        m["xT"] = np.ascontiguousarray(np.asarray(x[b, TPC * i:TPC * (i + 1)]).T)
        pos = pos_all[TPC * i:TPC * (i + 1)]
        f = pos[:, None] * inv[None, :]
        emb = np.concatenate([f, f], axis=1)
        m["cosT"] = np.ascontiguousarray(np.cos(emb).T.astype(np.float32))
        m["sinT"] = np.ascontiguousarray(np.sin(emb).T.astype(np.float32))
        maps.append(m)
    return maps


def kernel(**inputs):
    inputs = {k: np.asarray(v) for k, v in inputs.items()}
    P = build_fused()
    maps = stage_fused(inputs)
    res = run_bass_kernel_spmd(P.nc, maps, core_ids=list(range(NCORES)))
    P.es.close()
    out = np.empty((B, S, D), np.float32)
    for c in range(NCORES):
        b, i = c // 4, c % 4
        out[b, TPC * i:TPC * (i + 1), :] = res.results[c]["x4T"].T
    return out


# version 20
# speedup vs baseline: 1.1330x; 1.1220x over previous
import numpy as np
import ml_dtypes
from contextlib import ExitStack

import concourse.bass as bass
import concourse.mybir as mybir
from concourse.bass_utils import run_bass_kernel_spmd

F32 = mybir.dt.float32
BF16 = mybir.dt.bfloat16
I32 = mybir.dt.int32
U8 = mybir.dt.uint8
AF = mybir.ActivationFunctionType
ALU = mybir.AluOpType
AX = mybir.AxisListType
BFNP = ml_dtypes.bfloat16

NCORES = 8
B, S, D = 2, 8192, 1024
TPC = 2048
NTG = 4
TG = 512
NCH = 8
H, HD = 16, 64
DC, DR = 256, 128
EPS_LN = 1e-5
EPS_F32 = float(np.finfo(np.float32).eps)

ENGINES = ("sync", "gpsimd", "scalar", "vector", "tensor")
DBG_A = False
FDBG = False
USE_DRAINS = False


class Sem:
    __slots__ = ("h", "v")

    def __init__(self, h):
        self.h = h
        self.v = 0


class Prog:
    CHAINED = ("vector", "scalar")

    def __init__(self, name):
        self.nc = bass.Bass("TRN2", target_bir_lowering=False, name=name)
        self.es = ExitStack()
        self.scopes = [self.es]
        self.q = {e: [] for e in ENGINES}
        self.chain = {e: Sem(self.es.enter_context(self.nc.semaphore(e + "_chain")))
                      for e in self.CHAINED}
        self.last = {e: None for e in ENGINES}

    def push_scope(self):
        self.scopes.append(ExitStack())

    def pop_scope(self):
        self.scopes.pop().close()

    def sem(self, name):
        return Sem(self.es.enter_context(self.nc.semaphore(name)))

    def sb(self, name, shape, dt):
        return self.scopes[-1].enter_context(self.nc.sbuf_tensor(name, shape, dt))

    def ps(self, name, shape, dt=F32):
        return self.scopes[-1].enter_context(self.nc.psum_tensor(name, shape, dt))

    def o(self, eng, fn, wait=(), inc=None, incn=1):
        w = list(wait)
        if eng in self.CHAINED:
            if self.last[eng] is not None:
                w.append(self.last[eng])
            if inc is None:
                inc = self.chain[eng]
                incn = 1
        w = tuple((s.h, int(v)) for (s, v) in w)
        tgt = None
        if inc is not None:
            inc.v += incn
            tgt = (inc.h, incn)
            if eng in self.CHAINED:
                self.last[eng] = (inc, inc.v)

        drain = eng in self.CHAINED and USE_DRAINS

        def op(E, fn=fn, w=w, tgt=tgt, drain=drain):
            for h, v in w:
                E.wait_ge(h, v)
            r = fn(E)
            if tgt is not None:
                if drain:
                    E.drain().then_inc(*tgt)
                else:
                    if r is None:
                        r = E.nop()
                    r.then_inc(*tgt)
        self.q[eng].append(op)
        return inc.v if inc is not None else None

    def dma(self, eng, out_ap, in_ap, sem, wait=(), slow=False, serial=True):
        w = list(wait)
        if serial and sem.v > 0:
            w.append((sem, sem.v))
        return self.o(eng, lambda E, o=out_ap, i=in_ap, s=slow: E.dma_start(
            out=o, in_=i, allow_slow_non_contiguous=s),
            wait=w, inc=sem, incn=16)

    def build(self):
        with self.nc.Block() as blk:
            for e in ENGINES:
                ops = self.q[e]
                if ops:
                    getattr(blk, e)(lambda E, ops=ops: [op(E) for op in ops])
        return self.nc


class Pool:

    def __init__(self, tiles):
        self.t = tiles
        self.last = [None] * len(tiles)
        self.i = 0

    def acquire(self):
        slot = self.i % len(self.t)
        self.i += 1
        prev = self.last[slot]
        return slot, self.t[slot], ([prev] if prev is not None else [])

    def release(self, slot, sem, val):
        self.last[slot] = (sem, val)


def MM(out, lhsT, rhs, start, stop):
    return lambda E, o=out, l=lhsT, r=rhs, st=start, sp=stop: E.matmul(
        o, l, r, start=st, stop=sp, skip_group_check=True)


def _rsqrt_strip(P, ap, s_pre, v_pre, s_post):
    P.o("scalar", lambda E, a=ap: E.activation(out=a, in_=a, func=AF.Ln),
        wait=[(s_pre, v_pre)])
    P.o("scalar", lambda E, a=ap: E.activation(out=a, in_=a, func=AF.Exp, scale=-0.5),
        inc=s_post)


def emit_ln(P, pfx, pool, pbig, h_slice, ones_col_bf, ones_row_f, ones_row64,
            cw, scr_bf, scr_sq, st1, st2, st3, rstd_b, mst_b, src_dram=None,
            src_sb=None, xc=None, src_ready=(), nbuf=2, sems=None):
    if sems is None:
        sems = [P.sem(pfx + n) for n in ("cast", "sq", "mm", "str", "bc", "app", "x")]
    s_cast, s_sq, s_mm, s_str, s_bc, s_app, s_x = sems
    cast0, sq0, mm0, str0, app0 = s_cast.v, s_sq.v, s_mm.v, s_str.v, s_app.v

    def src_chunk(c, phase):
        if src_sb is not None:
            return src_sb[:, TPC * c:TPC * (c + 1)], list(src_ready)
        buf = c % 2
        ap = xc[:, TPC * buf:TPC * (buf + 1)]
        w = [(s_x, s_x.v)]
        if phase == 0:
            if c >= 2:
                w.append((s_cast, cast0 + c - 1))
        else:
            w.append((s_mm, mm0 + NCH))
            if c >= 2:
                w.append((s_app, app0 + 4 * (c - 1)))
        v = P.dma("sync", ap, src_dram[128 * c:128 * (c + 1), :], s_x, wait=w)
        return ap, [(s_x, v)]

    for c in range(NCH):
        src, waits = src_chunk(c, 0)
        buf = c % nbuf
        cb = scr_bf[:, TPC * buf:TPC * (buf + 1)]
        sq = scr_sq[:, TPC * buf:TPC * (buf + 1)]
        w = list(waits)
        if c >= nbuf:
            w.append((s_mm, mm0 + c - nbuf + 1))
        P.o("vector", lambda E, o=cb, i=src: E.tensor_copy(o, i), wait=w, inc=s_cast)
        P.o("scalar", lambda E, o=sq, i=cb: E.activation(out=o, in_=i, func=AF.Square),
            wait=[(s_cast, cast0 + c + 1)] +
            ([(s_mm, mm0 + c - nbuf + 1)] if c >= nbuf else []), inc=s_sq)
        for tg in range(NTG):
            first = (c == 0)
            last = (c == NCH - 1)
            P.o("tensor", MM(pbig[0:1, TG * tg:TG * (tg + 1)], ones_col_bf,
                             cb[:, TG * tg:TG * (tg + 1)], first, last),
                wait=([(s_sq, sq0 + c + 1)] + (list(cw) if c == 0 else []))
                if tg == 0 else [])
            P.o("tensor", MM(pbig[32:33, TG * tg:TG * (tg + 1)], ones_col_bf,
                             sq[:, TG * tg:TG * (tg + 1)], first, last),
                inc=(s_mm if tg == NTG - 1 else None))
    P.o("vector", lambda E: E.tensor_scalar_mul(st1[:, :], pbig[0:1, 0:TPC], 1.0 / D),
        wait=[(s_mm, s_mm.v)])
    P.o("vector", lambda E: E.tensor_scalar_mul(st2[:, :], pbig[32:33, 0:TPC], 1.0 / D),
        inc=s_str)
    P.o("scalar", lambda E: E.activation(out=st3, in_=st1[:, :], func=AF.Square),
        wait=[(s_str, str0 + 1)], inc=s_str)
    P.o("vector", lambda E: E.tensor_sub(st2[:, :], st2[:, :], st3),
        wait=[(s_str, str0 + 2)])
    P.o("vector", lambda E: E.tensor_scalar_add(st2[:, :], st2[:, :], EPS_LN),
        inc=s_str)
    _rsqrt_strip(P, st2[:, :], s_str, str0 + 3, s_str)
    P.o("vector", lambda E: E.tensor_mul(st1[:, :], st1[:, :], st2[:, :]),
        wait=[(s_str, str0 + 4)])
    P.o("vector", lambda E: E.tensor_scalar_mul(st1[:, :], st1[:, :], -1.0),
        inc=s_str)
    for tg in range(NTG):
        for si, (strip, orow, bdst) in enumerate(((st2, ones_row64, rstd_b),
                                                  (st1, ones_row_f, mst_b))):
            slot, pt, pw = pool.acquire()
            mmw = list(pw) + ([(s_str, str0 + 5)] if (tg == 0 and si == 0) else [])
            vmm = P.o("tensor", MM(pt[:, 0:TG], orow,
                                   strip[0:1, TG * tg:TG * (tg + 1)], True, True),
                      wait=mmw, inc=s_bc)
            P.o("scalar", lambda E, o=bdst[:, TG * tg:TG * (tg + 1)], p=pt:
                E.activation(out=o, in_=p[:, 0:TG], func=AF.Copy),
                wait=[(s_bc, vmm)], inc=s_bc)
            pool.release(slot, s_bc, s_bc.v)
    bcv = s_bc.v
    for c in range(NCH):
        src, waits = src_chunk(c, 1)
        for tg in range(NTG):
            xs = src[:, TG * tg:TG * (tg + 1)]
            tmp = pbig[:, TG * tg:TG * (tg + 1)]
            P.o("vector", lambda E, t=tmp, x=xs,
                r=rstd_b[:, TG * tg:TG * (tg + 1)]: E.tensor_mul(t, x, r),
                wait=[(s_bc, bcv)] + list(waits))
            hs = h_slice(c, tg)
            P.o("vector", lambda E, o=hs, t=tmp,
                m=mst_b[:, TG * tg:TG * (tg + 1)]: E.tensor_add(o, t, m),
                inc=s_app)
    return st2, st1, s_app


GROUPS = [[0, 1, 2, 3], [4, 5, 6, 7]]


def build_fused():
    P = Prog("mkfull")
    nc = P.nc

    def inp(name, shape, dt=F32):
        return nc.dram_tensor(name, shape, dt, kind="ExternalInput").ap()

    xT = inp("xT", [D, TPC])
    wq = inp("wq", [D, DC]); wqc = inp("wqc", [DC, 896]); wqr = inp("wqr", [D, DR])
    wkv = inp("wkv", [D, DC]); wkr = inp("wkr", [D, DR]); wkc = inp("wkc", [DC, 896])
    wv = inp("wv", [DC, D])
    bqx = inp("bqx", [128, 2]); bqcx = inp("bqcx", [128, 7]); bqrx = inp("bqrx", [128, 1])
    bkvx = inp("bkvx", [128, 2]); bkrx = inp("bkrx", [128, 1]); bkcx = inp("bkcx", [128, 7])
    bvx = inp("bvx", [1, D])
    cosT = inp("cosT", [DR, TPC]); sinT = inp("sinT", [DR, TPC])
    onesc = inp("onesc", [128, 1]); onesr = inp("onesr", [1, 128])
    tri = inp("tri", [128, 128], BF16); identb = inp("identb", [128, 128], BF16)
    identf = inp("identf", [128, 128])
    wo = inp("wo", [D, D]); box = inp("box", [128, 8])
    wrt = inp("wrt", [D, 128]); centT = inp("centT", [128, 16])
    tempx = inp("tempx", [1, 1])
    iexp = inp("iexp", [16, D, 512]); oexp = inp("oexp", [16, 512, 64])
    selc = inp("selc", [16, 8 * 128], BF16)
    wqkvT = inp("wqkvT", [D, 3 * D])
    wpT = inp("wpT", [D, D]); bpx = inp("bpx", [128, 8])
    gamx = inp("gamx", [128, 8]); betx = inp("betx", [128, 8])
    fc1q = inp("fc1q", [D, 4096], U8); sc1x = inp("sc1x", [32, D])
    b1x = inp("b1x", [128, 32])
    fc2q = inp("fc2q", [4096, D], U8); sc2x = inp("sc2x", [32, D])
    b2x = inp("b2x", [128, 8])

    x4T = nc.dram_tensor("x4T", [D, TPC], F32, kind="ExternalOutput").ap()
    if FDBG:
        dkq = nc.dram_tensor("dkq", [2 * D, TPC], BF16, kind="ExternalOutput").ap()
        dvi = nc.dram_tensor("dvi", [TPC, H * 65], BF16, kind="ExternalOutput").ap()
        dao = nc.dram_tensor("dao", [256, S], BF16, kind="ExternalOutput").ap()
        dq2 = nc.dram_tensor("dq2", [D, TPC], BF16, kind="ExternalOutput").ap()
        dctx = nc.dram_tensor("dctx", [65, H * 65], F32, kind="ExternalOutput").ap()
        dao2 = nc.dram_tensor("dao2", [256, S], BF16, kind="ExternalOutput").ap()
        dx3 = nc.dram_tensor("dx3", [D, TPC], F32, kind="ExternalOutput").ap()
        dx2 = nc.dram_tensor("dx2", [D, TPC], F32, kind="ExternalOutput").ap()

    k_in = nc.dram_tensor("k_in", [D, TPC], BF16).ap()
    q_in = nc.dram_tensor("q_in", [D, TPC], BF16).ap()
    vA_in = nc.dram_tensor("vA_in", [D, H * 65], BF16).ap()
    vB_in = nc.dram_tensor("vB_in", [D, H * 65], BF16).ap()
    kA_out = nc.dram_tensor("kA_out", [4 * D, TPC], BF16).ap()
    qB_out = nc.dram_tensor("qB_out", [4 * D, TPC], BF16).ap()
    vA_out = nc.dram_tensor("vA_out", [4 * D, H * 65], BF16).ap()
    vB_out = nc.dram_tensor("vB_out", [4 * D, H * 65], BF16).ap()
    ao_in = nc.dram_tensor("ao_in", [256, S], BF16).ap()
    ao_outA = nc.dram_tensor("ao_outA", [D // 2, S], BF16).ap()
    ao_outB = nc.dram_tensor("ao_outB", [D // 2, S], BF16).ap()
    q2_in = nc.dram_tensor("q2_in", [D, TPC], BF16).ap()
    q2_out = nc.dram_tensor("q2_out", [4 * D, TPC], BF16).ap()
    ctx_in = nc.dram_tensor("ctx_in", [65, H * 65], F32).ap()
    ctx_out = nc.dram_tensor("ctx_out", [65, H * 65], F32).ap()
    ao2_in = nc.dram_tensor("ao2_in", [256, S], BF16).ap()
    ao2_outA = nc.dram_tensor("ao2_outA", [D // 2, S], BF16).ap()
    ao2_outB = nc.dram_tensor("ao2_outB", [D // 2, S], BF16).ap()
    y1d = nc.dram_tensor("y1d", [4096, TPC], BF16).ap()
    kmy = nc.dram_tensor("kmy", [4 * 256, TPC], BF16).ap()
    qmy = nc.dram_tensor("qmy", [4 * 256, TPC], BF16).ap()
    vmy = nc.dram_tensor("vmy", [S, 260], BF16).ap()
    aomyA = nc.dram_tensor("aomyA", [D // 2, TPC], BF16).ap()
    aomyB = nc.dram_tensor("aomyB", [D // 2, TPC], BF16).ap()
    ctxmy = nc.dram_tensor("ctxmy", [65, 260], F32).ap()
    q2my = nc.dram_tensor("q2my", [4 * 256, TPC], BF16).ap()
    ao2myA = nc.dram_tensor("ao2myA", [D // 2, TPC], BF16).ap()
    ao2myB = nc.dram_tensor("ao2myB", [D // 2, TPC], BF16).ap()

    ones_col_bf = P.sb("ones_col_bf", [128, 1], BF16)
    ones_row_f = P.sb("ones_row_f", [1, 128], F32)
    ones_row64 = ones_row_f[:, :]
    idb = P.sb("idb", [128, 128], BF16)
    trib = P.sb("trib", [128, 128], BF16)
    bq_s = P.sb("bq_s", [128, 2], F32); bqc_s = P.sb("bqc_s", [128, 7], F32)
    bqr_s = P.sb("bqr_s", [128, 1], F32); bkv_s = P.sb("bkv_s", [128, 2], F32)
    bkr_s = P.sb("bkr_s", [128, 1], F32); bkc_s = P.sb("bkc_s", [128, 7], F32)
    bo_s = P.sb("bo_s", [128, 8], F32)
    cent_s = P.sb("cent_s", [128, 16], F32)
    selcs = P.sb("selcs", [16, 8 * 128], BF16)
    bp_s = P.sb("bp_s", [128, 8], F32)
    gam_s = P.sb("gam_s", [128, 8], F32); bet_s = P.sb("bet_s", [128, 8], F32)
    b1_s = P.sb("b1_s", [128, 32], F32)
    b2_s = P.sb("b2_s", [128, 8], F32); b2h_s = P.sb("b2h_s", [128, 8], F32)
    tstrip = P.sb("tstrip", [1, 16], F32)
    stp = P.sb("stp", [1, 2 * TPC], F32)
    st1 = stp[0:1, 0:TPC]
    st2 = stp[0:1, TPC:2 * TPC]
    rstd_b = P.sb("rstd_b", [128, TPC], F32)
    mst_b = P.sb("mst_b", [128, TPC], F32)

    pbig = P.ps("pbig", [128, 4 * TG], F32)
    pool = Pool([P.ps(f"pp{i}", [128, TG], F32) for i in range(4)])
    ln_sems = [P.sem("ln_" + n) for n in ("cast", "sq", "mm", "str", "bc", "app", "x")]

    s_c = P.sem("s_c")
    s_w = P.sem("s_w")
    s_cc = P.sem("s_cc")

    for dst, src in ((bq_s, bqx), (bqc_s, bqcx), (bqr_s, bqrx), (bkv_s, bkvx),
                     (bkr_s, bkrx), (bkc_s, bkcx),
                     (ones_row_f, onesr),
                     (idb, identb), (trib, tri), (bo_s, box), (cent_s, centT),
                     (selcs, selc), (bp_s, bpx), (gam_s, gamx), (bet_s, betx),
                     (b1_s, b1x), (b2_s, b2x)):
        P.dma("sync", dst[:, :], src[:, :], s_c, serial=False)
    P.dma("sync", tstrip[0:1, 0:1], tempx[:, :], s_c, serial=False)
    P.dma("gpsimd", ones_col_bf[:, :], onesc[:, :], s_w, serial=False)
    CW = [(s_c, s_c.v), (s_w, s_w.v)]

    dynv = {}

    def DV(E, key):
        eng = E.engine
        if eng not in dynv:
            pid = E.partition_id()
            hg = pid % 4
            dynv[eng] = {"hg": hg, "hg2": hg * 2}
        return dynv[eng][key]

    def dyn_dma(out_ap, mk_in, sem, wait, q="sync"):
        w = list(wait)
        if sem.v > 0:
            w.append((sem, sem.v))
        sem.v += 16
        tgt = (sem.h, 16)
        wt = tuple((x.h, int(v)) for (x, v) in w)

        def op(E, o=out_ap, mk=mk_in, wt=wt, tgt=tgt):
            for h, v in wt:
                E.wait_ge(h, v)
            E.dma_start(out=o, in_=mk(E)).then_inc(*tgt)
        P.q[q].append(op)
        return sem.v

    P.push_scope()
    xc = P.sb("a_xc", [128, 2 * TPC], F32)
    scr_bf = P.sb("a_scrbf", [128, 2 * TPC], BF16)
    scr_sq = P.sb("a_scrsq", [128, 2 * TPC], BF16)
    hbf = P.sb("a_h", [128, NCH * TPC], BF16)
    wqb = P.sb("a_wqb", [128, 8 * DC], BF16)
    wqcb = P.sb("a_wqcb", [128, 2 * 896], BF16)
    wqrb = P.sb("a_wqrb", [128, 8 * DR], BF16)
    wkvb = P.sb("a_wkvb", [128, 8 * DC], BF16)
    wkrb = P.sb("a_wkrb", [128, 8 * DR], BF16)
    wkcb = P.sb("a_wkcb", [128, 2 * 896], BF16)
    wvb = P.sb("a_wvb", [128, 2 * D], BF16)
    cos_s = P.sb("a_cos", [128, TPC], F32)
    sin_s = P.sb("a_sin", [128, TPC], F32)
    cqb = P.sb("a_cq", [128, 2 * TPC], BF16)
    ckvb = P.sb("a_ckv", [128, 2 * TPC], BF16)
    qch = P.sb("a_qch", [128, 2 * TPC], BF16)
    kch = P.sb("a_kch", [128, 2 * TPC], BF16)
    qrf = P.sb("a_qrf", [128, TPC], F32)
    krf = P.sb("a_krf", [128, TPC], F32)
    vat = P.sb("a_vat", [128, 2 * H * 65], BF16)
    bv_b = P.sb("a_bvb", [128, D], F32)
    bv_row = P.sb("a_bv_row", [1, D], F32)

    s_aw = P.sem("s_aw")
    P.dma("sync", bv_row[:, :], bvx[:, :], s_aw, serial=False)
    P.dma("gpsimd", wqb.rearrange("p (kc n) -> p kc n", n=DC),
          wq.rearrange("(kc p) n -> p kc n", p=128), s_aw, serial=False)
    P.dma("gpsimd", wqcb.rearrange("p (kc n) -> p kc n", n=896),
          wqc.rearrange("(kc p) n -> p kc n", p=128), s_aw, serial=False)
    P.dma("gpsimd", wqrb.rearrange("p (kc n) -> p kc n", n=DR),
          wqr.rearrange("(kc p) n -> p kc n", p=128), s_aw, serial=False)
    P.dma("gpsimd", wkvb.rearrange("p (kc n) -> p kc n", n=DC),
          wkv.rearrange("(kc p) n -> p kc n", p=128), s_aw, serial=False)
    P.dma("gpsimd", wkrb.rearrange("p (kc n) -> p kc n", n=DR),
          wkr.rearrange("(kc p) n -> p kc n", p=128), s_aw, serial=False)
    P.dma("gpsimd", wkcb.rearrange("p (kc n) -> p kc n", n=896),
          wkc.rearrange("(kc p) n -> p kc n", p=128), s_aw, serial=False)
    P.dma("gpsimd", wvb.rearrange("p (kc n) -> p kc n", n=D),
          wv.rearrange("(kc p) n -> p kc n", p=128), s_aw, serial=False)
    P.dma("sync", cos_s[:, :], cosT[:, :], s_aw, serial=False)
    P.dma("sync", sin_s[:, :], sinT[:, :], s_aw, serial=False)
    cw = CW + [(s_aw, s_aw.v)]

    def h_slice(c, tg):
        return hbf[:, TPC * c + TG * tg: TPC * c + TG * (tg + 1)]

    a_st, b_st, s_app = emit_ln(P, "aln", pool, pbig, h_slice,
                                ones_col_bf[:, :], ones_row_f[:, :],
                                ones_row64, cw,
                                scr_bf, scr_sq, st1, st2, pbig[64:65, 0:TPC],
                                rstd_b, mst_b, src_dram=xT, xc=xc, sems=ln_sems)
    HREADY = [(s_app, s_app.v)]

    s_pmm = P.sem("s_pmm")
    s_pev = P.sem("s_pev")

    def proj(n_oc, n_kc, lhsT_fn, rhs_fn, evac, first_wait):
        for oc in range(n_oc):
            for tg in range(NTG):
                slot, pt, pw = pool.acquire()
                w = list(pw) + (first_wait if (oc == 0 and tg == 0) else [])
                for kc in range(n_kc):
                    P.o("tensor", MM(pt[:, 0:TG], lhsT_fn(kc, oc), rhs_fn(kc, tg),
                                     kc == 0, kc == n_kc - 1),
                        wait=w if kc == 0 else [],
                        inc=(s_pmm if kc == n_kc - 1 else None))
                evac(oc, tg, pt, [(s_pmm, s_pmm.v)], s_pev)
                pool.release(slot, s_pev, s_pev.v)

    def mk_gelu_evac(dst, bias):
        def evac(oc, tg, pt, w, s_ev):
            o = dst[:, TPC * oc + TG * tg: TPC * oc + TG * (tg + 1)]
            P.o("scalar", lambda E, o=o, p=pt, b=bias[:, oc:oc + 1]: E.activation(
                out=o, in_=p[:, 0:TG], func=AF.Gelu, bias=b), wait=w, inc=s_ev)
        return evac

    proj(2, NCH,
         lambda kc, oc: wkvb[:, DC * kc + 128 * oc: DC * kc + 128 * (oc + 1)],
         lambda kc, tg: hbf[:, TPC * kc + TG * tg: TPC * kc + TG * (tg + 1)],
         mk_gelu_evac(ckvb, bkv_s), HREADY + cw)
    CKVR = [(s_pev, s_pev.v)]

    s_ag = P.sem("s_ag")
    s_ast = P.sem("s_ast")

    hc_sems = [P.sem("hc_" + n) for n in ("mm", "ev", "rmm", "rev", "rdma", "rope")]

    def emit_head_chunks(which, wsb, bsb, csrc, cready, kqd, rope_srcs,
                         rope_extra_wait=()):
        ch = qch if which == "q" else kch
        s_mm, s_ev = hc_sems[0], hc_sems[1]
        for oc in range(7):
            buf = oc % 2
            dst = ch[:, TPC * buf:TPC * (buf + 1)]
            for tg in range(NTG):
                slot, pt, pw = pool.acquire()
                w = list(pw) + (cready if (oc == 0 and tg == 0) else [])
                if tg == 0 and oc >= 2:
                    w.append((s_ast, s_ast.v - 16))
                for kc in range(2):
                    P.o("tensor", MM(pt[:, 0:TG],
                                     wsb[:, 896 * kc + 128 * oc: 896 * kc + 128 * (oc + 1)],
                                     csrc[:, TPC * kc + TG * tg: TPC * kc + TG * (tg + 1)],
                                     kc == 0, kc == 1),
                        wait=w if kc == 0 else [], inc=(s_mm if kc == 1 else None))
                P.o("vector", lambda E, o=dst[:, TG * tg:TG * (tg + 1)], p=pt,
                    b=bsb[:, oc:oc + 1]: E.tensor_single_scalar(o, p[:, 0:TG], b, ALU.add),
                    wait=[(s_mm, s_mm.v)], inc=s_ev)
                pool.release(slot, s_ev, s_ev.v)
            P.o("scalar", lambda E, a=dst: E.activation(out=a, in_=a, func=AF.Gelu),
                wait=[(s_ev, s_ev.v)], inc=s_ag)
            P.dma("sync", kqd[128 * oc:128 * (oc + 1), :], dst,
                  s_ast, wait=[(s_ag, s_ag.v)])
        wrb, brb, rf = rope_srcs
        s_rmm, s_rev = hc_sems[2], hc_sems[3]
        for tg in range(NTG):
            slot, pt, pw = pool.acquire()
            for kc in range(NCH):
                P.o("tensor", MM(pt[:, 0:TG], wrb[:, DR * kc:DR * (kc + 1)],
                                 hbf[:, TPC * kc + TG * tg: TPC * kc + TG * (tg + 1)],
                                 kc == 0, kc == NCH - 1),
                    wait=(list(pw) + HREADY) if kc == 0 else [],
                    inc=(s_rmm if kc == NCH - 1 else None))
            P.o("vector", lambda E, o=rf[:, TG * tg:TG * (tg + 1)], p=pt,
                b=brb[:, 0:1]: E.tensor_single_scalar(o, p[:, 0:TG], b, ALU.add),
                wait=[(s_rmm, s_rmm.v)], inc=s_rev)
            pool.release(slot, s_rev, s_rev.v)
        buf = 1
        dst = ch[:, TPC * buf:TPC * (buf + 1)]
        rot = xc[:, 0:TPC]
        tmp2 = xc[:, TPC:2 * TPC]
        s_rdma, s_rope = hc_sems[4], hc_sems[5]
        P.dma("sync", rot[0:64, :], rf[64:128, :], s_rdma,
              wait=[(s_rev, s_rev.v)] + list(rope_extra_wait))
        P.dma("sync", rot[64:128, :], rf[0:64, :], s_rdma)
        P.o("vector", lambda E: E.tensor_scalar_mul(rot[0:64, :], rot[0:64, :], -1.0),
            wait=[(s_rdma, s_rdma.v), (s_ast, s_ast.v - 16)])
        P.o("vector", lambda E: E.tensor_mul(tmp2[:, :], rot[:, :], sin_s[:, :]))
        P.o("vector", lambda E: E.tensor_mul(rot[:, :], rf[:, :], cos_s[:, :]))
        P.o("vector", lambda E, o=dst: E.tensor_add(o, rot[:, :], tmp2[:, :]),
            inc=s_rope)
        P.o("scalar", lambda E, a=dst: E.activation(out=a, in_=a, func=AF.Gelu),
            wait=[(s_rope, s_rope.v)], inc=s_ag)
        P.dma("sync", kqd[128 * 7:128 * 8, :], dst, s_ast,
              wait=[(s_ag, s_ag.v)])
        return s_rope

    s_vbc = P.sem("s_vbc")
    s_vbe = P.sem("s_vbe")
    for half in range(2):
        slot, pt, pw = pool.acquire()
        P.o("tensor", MM(pt[:, 0:TG], ones_row_f[:, :],
                         bv_row[0:1, TG * half:TG * (half + 1)], True, True),
            wait=list(pw) + cw, inc=s_vbc)
        P.o("vector", lambda E, o=bv_b[:, TG * half:TG * (half + 1)], p=pt:
            E.tensor_copy(o, p[:, 0:TG]), wait=[(s_vbc, s_vbc.v)], inc=s_vbe)
        pool.release(slot, s_vbe, s_vbe.v)
    s_vmm = P.sem("s_vmm")
    s_vev = P.sem("s_vev")
    NT = TPC // 128

    def emit_v_tiles(lo, hi):
        for tt in range(lo, hi):
            buf = tt % 2
            tile = vat[:, (H * 65) * buf:(H * 65) * (buf + 1)]
            t3 = tile.rearrange("p (h c) -> p h c", c=65)
            w = [(s_ast, s_ast.v - 16)] if tt >= 2 else []
            P.o("vector", lambda E, o=t3[:, :, 64:65]: E.memset(o, 1.0),
                wait=w + ([(s_vbe, s_vbe.v)] if tt == 0 else []))
            for vc in range(2):
                slot, pt, pw = pool.acquire()
                for kc in range(2):
                    P.o("tensor", MM(pt[:, 0:TG],
                                     ckvb[:, TPC * kc + 128 * tt: TPC * kc + 128 * (tt + 1)],
                                     wvb[:, D * kc + TG * vc: D * kc + TG * (vc + 1)],
                                     kc == 0, kc == 1),
                        wait=(list(pw) + CKVR) if kc == 0 else [],
                        inc=(s_vmm if kc == 1 else None))
                P.o("vector", lambda E, o=t3[:, 8 * vc:8 * (vc + 1), 0:64], p=pt,
                    b=bv_b[:, TG * vc:TG * (vc + 1)].rearrange("p (h c) -> p h c", c=64):
                    E.scalar_tensor_tensor(o, p[:, 0:TG].rearrange("p (h c) -> p h c", c=64),
                                           0.0, b, ALU.bypass, ALU.add),
                    wait=[(s_vmm, s_vmm.v)], inc=s_vev)
                pool.release(slot, s_vev, s_vev.v)
            vdst = vA_in if tt < NT // 2 else vB_in
            P.dma("sync", vdst[128 * (tt % (NT // 2)):128 * (tt % (NT // 2) + 1), :],
                  tile, s_ast, wait=[(s_vev, s_vev.v)])

    k_rope = emit_head_chunks("k", wkcb, bkc_s, ckvb, CKVR, k_in,
                              (wkrb, bkr_s, krf))
    AST_KO = s_ast.v
    P.o("gpsimd", lambda E: E.collective_compute(
        "AllGather", ALU.bypass, GROUPS, ins=[k_in[:, :]], outs=[kA_out[:, :]]),
        wait=[(s_ast, AST_KO)], inc=s_cc)
    CCK = s_cc.v
    emit_v_tiles(0, NT // 2)
    AST_VA = s_ast.v

    proj(2, NCH,
         lambda kc, oc: wqb[:, DC * kc + 128 * oc: DC * kc + 128 * (oc + 1)],
         lambda kc, tg: hbf[:, TPC * kc + TG * tg: TPC * kc + TG * (tg + 1)],
         mk_gelu_evac(cqb, bq_s), [])
    CQR = [(s_pev, s_pev.v)]
    emit_head_chunks("q", wqcb, bqc_s, cqb, CQR, q_in, (wqrb, bqr_s, qrf),
                     rope_extra_wait=[(k_rope, k_rope.v)])
    AST_QO = s_ast.v
    P.o("gpsimd", lambda E: E.collective_compute(
        "AllGather", ALU.bypass, GROUPS, ins=[q_in[:, :]], outs=[qB_out[:, :]]),
        wait=[(s_ast, AST_QO)], inc=s_cc)
    CCQ = s_cc.v
    P.o("gpsimd", lambda E: E.collective_compute(
        "AllGather", ALU.bypass, GROUPS, ins=[vA_in[:, :]], outs=[vA_out[:, :]]),
        wait=[(s_ast, AST_VA)], inc=s_cc)
    CC_VA = s_cc.v
    emit_v_tiles(NT // 2, NT)
    AST_ALL = s_ast.v
    P.pop_scope()
    P.o("gpsimd", lambda E: E.collective_compute(
        "AllGather", ALU.bypass, GROUPS, ins=[vB_in[:, :]], outs=[vB_out[:, :]]),
        wait=[(s_ast, AST_ALL)], inc=s_cc)
    CC_VB = s_cc.v

    P.push_scope()
    kp = P.sb("b_kp", [128, 2 * S], BF16)
    qp = P.sb("b_qp", [128, 2 * S], BF16)
    vh = P.sb("b_vh", [128, (S // 128) * 260], BF16)
    qsq = P.sb("b_qsq", [128, S], BF16)
    Pt = P.sb("b_Pt", [128, 4 * TG], BF16)
    aot = P.sb("b_aot", [64, 2 * TG], BF16)
    pvsbB = P.sb("b_pvsb", [65, 2 * TG], F32)
    rbB = P.sb("b_rb", [64, TG], F32)
    rstripB = P.sb("b_rstrip", [1, TG], F32)
    nstr = P.sb("b_nstr", [1, 16], F32)
    bias_col = P.sb("b_bias", [128, 1], F32)

    NKT = S // 128
    NQG = S // TG
    s_bl = P.sem("s_bl")
    kqA3 = kA_out.rearrange("(r t) c -> r t c", t=D)
    kqB3 = qB_out.rearrange("(r t) c -> r t c", t=D)
    vA3 = vA_out.rearrange("(r t) c -> r t c", t=D)
    vB3 = vB_out.rearrange("(r t) c -> r t c", t=D)
    vmy4 = vmy.rearrange("(g u) c -> g u c", u=TPC)
    dyn_dma(kmy.rearrange("(r t) c -> r t c", t=256),
            (lambda E: kqA3[:, bass.ts(DV(E, "hg"), 256), :][:, 0:256, :]),
            s_bl, [(s_cc, CCK)])
    for hp in range(2):
        for r in range(4):
            P.dma("sync", kp[:, S * hp + TPC * r:S * hp + TPC * (r + 1)],
                  kmy[256 * r + 128 * hp:256 * r + 128 * (hp + 1), :], s_bl,
                  wait=([(s_ast, AST_ALL)] if (hp == 0 and r == 0) else []))
    LWK = [(s_bl, s_bl.v)]
    dyn_dma(qmy.rearrange("(r t) c -> r t c", t=256),
            (lambda E: kqB3[:, bass.ts(DV(E, "hg"), 256), :][:, 0:256, :]),
            s_bl, [(s_cc, CCQ)])
    for hp in range(2):
        for r in range(4):
            P.dma("sync", qp[:, S * hp + TPC * r:S * hp + TPC * (r + 1)],
                  qmy[256 * r + 128 * hp:256 * r + 128 * (hp + 1), :], s_bl)
    QPL = s_bl.v
    dyn_dma(vmy4[:, 0:TPC // 2, :],
            (lambda E: vA3[:, :, bass.ts(DV(E, "hg"), 260)]), s_bl,
            [(s_cc, CC_VA)])
    dyn_dma(vmy4[:, TPC // 2:TPC, :],
            (lambda E: vB3[:, :, bass.ts(DV(E, "hg"), 260)]), s_bl,
            [(s_cc, CC_VB)])
    P.dma("sync", vh.rearrange("p (t c) -> p t c", c=260),
          vmy.rearrange("(t p) c -> p t c", p=128), s_bl)
    LWB = [(s_bl, QPL)]
    VHR = [(s_bl, s_bl.v)]

    s_nrm = P.sem("s_nrm")
    s_nmm = P.sem("s_nmm")
    s_sqs = P.sem("s_sqs")
    s_bias = P.sem("s_bias")
    s_smm = P.sem("s_smm")
    s_exp = P.sem("s_exp")
    s_pvm = P.sem("s_pvm")
    s_pvc = P.sem("s_pvc")
    s_v1 = P.sem("s_v1")
    s_bmm = P.sem("s_bmm")
    s_aoev = P.sem("s_aoev")
    s_ast2 = P.sem("s_ast2")

    slot_pv_, ppvA, pw_pv = pool.acquire()
    slot_pv2_, ppvB, pw_pv2 = pool.acquire()
    slot_pb_, pb1, pw_pb = pool.acquire()
    PV_FREE = pw_pv + pw_pv2 + pw_pb

    slot_exp = [None] * 4
    slot_pv = [None] * 4
    first_scores_extra = list(PV_FREE)
    nstore = 0

    for hp in range(2):
        kslab = kp[:, S * hp:S * (hp + 1)]
        qslab = qp[:, S * hp:S * (hp + 1)]
        for which, slab, acc in (("k", kslab, 4), ("q", qslab, 0)):
            P.o("scalar", lambda E, i=slab[:, :]: E.activation(
                out=qsq[:, :], in_=i, func=AF.Square),
                wait=(LWK if which == "k" else LWB) + [(s_nrm, s_nrm.v)],
                inc=s_sqs)
            for r in range(4):
                for j in range(4):
                    cg = 4 * r + j
                    w = [(s_sqs, s_sqs.v), (s_nrm, s_nrm.v)] if j == 0 else []
                    if r == 0 and j == 0:
                        for si in range(4):
                            if slot_exp[si] is not None:
                                w.append(slot_exp[si])
                    P.o("tensor", MM(pbig[0:1, TG * j:TG * (j + 1)],
                                     ones_col_bf[:, 0:1],
                                     qsq[:, TG * cg:TG * (cg + 1)], True, True),
                        wait=w, inc=(s_nmm if j == 3 else None))
                P.o("vector", lambda E, o=nstr[0:1, acc + r:acc + r + 1]:
                    E.tensor_reduce(o, pbig[0:1, 0:4 * TG], AX.X, ALU.max),
                    wait=[(s_nmm, s_nmm.v)], inc=s_nrm)
            P.o("vector", lambda E, o=nstr[0:1, 8 + (0 if which == "q" else 1):
                                           9 + (0 if which == "q" else 1)],
                i=nstr[0:1, acc:acc + 4]: E.tensor_reduce(o, i, AX.X, ALU.max),
                wait=[(s_nrm, s_nrm.v)], inc=s_nrm)
        P.o("vector", lambda E: E.tensor_mul(nstr[0:1, 10:11], nstr[0:1, 8:9],
                                             nstr[0:1, 9:10]),
            wait=[(s_nrm, s_nrm.v)], inc=s_nrm)
        P.o("scalar", lambda E: E.activation(out=nstr[0:1, 10:11], in_=nstr[0:1, 10:11],
                                             func=AF.Ln), wait=[(s_nrm, s_nrm.v)])
        P.o("scalar", lambda E: E.activation(out=nstr[0:1, 10:11], in_=nstr[0:1, 10:11],
                                             func=AF.Exp, scale=0.5), inc=s_sqs)
        P.o("vector", lambda E: E.tensor_scalar_mul(nstr[0:1, 10:11], nstr[0:1, 10:11],
                                                    -0.125),
            wait=[(s_sqs, s_sqs.v)], inc=s_nrm)
        P.o("tensor", MM(pb1[:, 0:1], ones_row_f[:, :], nstr[0:1, 10:11], True, True),
            wait=[(s_nrm, s_nrm.v), (s_aoev, s_aoev.v)] +
            (list(PV_FREE) if hp == 0 else []), inc=s_bmm)
        P.o("vector", lambda E: E.tensor_copy(bias_col[:, :], pb1[:, 0:1]),
            wait=[(s_bmm, s_bmm.v), (s_exp, s_exp.v)], inc=s_bias)

        for g in range(NQG):
            kts = 4 * g + 4

            def emit_scores(kt):
                sa = 2 * (kt % 2)
                diag = kt >= 4 * g
                j = kt - 4 * g if diag else 0
                for ih in range(2):
                    slot = sa + ih
                    w = []
                    if slot_exp[slot] is not None:
                        w.append(slot_exp[slot])
                        slot_exp[slot] = None
                    if kt == 0 and ih == 0:
                        w += [(s_bias, s_bias.v)]
                        if hp == 0 and g == 0:
                            w += first_scores_extra
                    P.o("tensor", MM(pbig[:, TG * slot:TG * (slot + 1)],
                                     kslab[64 * ih:64 * (ih + 1),
                                           128 * kt:128 * (kt + 1)],
                                     qslab[64 * ih:64 * (ih + 1),
                                           TG * g:TG * (g + 1)],
                                     True, not diag),
                        wait=w, inc=(None if diag else (s_smm if ih == 1 else None)))
                if diag:
                    for ih in range(2):
                        slot = sa + ih
                        P.o("tensor", MM(pbig[:, TG * slot + 128 * j:
                                              TG * slot + 128 * (j + 1)],
                                         idb[:, :], trib[:, :], False, True),
                            inc=(s_smm if ih == 1 else None))

            def emit_exp(kt):
                sa = 2 * (kt % 2)
                w = [(s_smm, s_smm.v), (s_bias, s_bias.v)]
                for ih in range(2):
                    if slot_pv[sa + ih] is not None:
                        w.append(slot_pv[sa + ih])
                        slot_pv[sa + ih] = None
                P.o("scalar", lambda E, o=Pt[:, TG * sa:TG * (sa + 2)],
                    i=pbig[:, TG * sa:TG * (sa + 2)]: E.activation(
                        out=o, in_=i, func=AF.Exp, scale=0.125, bias=bias_col[:, 0:1]),
                    wait=w, inc=s_exp)
                for ih in range(2):
                    slot_exp[sa + ih] = (s_exp, s_exp.v)

            def emit_pv(kt):
                sa = 2 * (kt % 2)
                ev = (s_exp, s_exp.v)
                diag = kt >= 4 * g
                j = kt - 4 * g if diag else 0
                jm = 128 * j
                for ih, pv in ((0, ppvA), (1, ppvB)):
                    hl = 2 * hp + ih
                    slot = sa + ih
                    w = []
                    if ih == 0:
                        w.append(ev)
                    if kt == 0:
                        w.append((s_pvc, s_pvc.v))
                        if hp == 0 and g == 0:
                            w += VHR
                    P.o("tensor", MM(pv[0:65, jm:TG],
                                     vh[:, 260 * kt + 65 * hl:260 * kt + 65 * (hl + 1)],
                                     Pt[:, TG * slot + jm:TG * (slot + 1)],
                                     kt == 0, kt == kts - 1),
                        wait=w, inc=(s_pvm if ih == 1 else None))
                for ih in range(2):
                    slot_pv[sa + ih] = (s_pvm, s_pvm.v)

            emit_scores(0)
            emit_exp(0)
            for kt in range(kts):
                if kt + 1 < kts:
                    emit_scores(kt + 1)
                emit_pv(kt)
                if kt + 1 < kts:
                    emit_exp(kt + 1)

            pvcv = {}
            for ih, pv in ((0, ppvA), (1, ppvB)):
                P.o("scalar", lambda E, o=pvsbB[0:64, TG * ih:TG * (ih + 1)], p=pv:
                    E.activation(out=o, in_=p[0:64, 0:TG], func=AF.Copy),
                    wait=[(s_pvm, s_pvm.v)] +
                    ([(s_aoev, s_aoev.v)] if ih == 0 else []))
                pvcv[ih] = P.o("scalar", lambda E,
                               o=pvsbB[64:65, TG * ih:TG * (ih + 1)], p=pv:
                               E.activation(out=o, in_=p[64:65, 0:TG],
                                            func=AF.Copy), inc=s_pvc)
            for ih in range(2):
                hl = 2 * hp + ih
                pvs = pvsbB[:, TG * ih:TG * (ih + 1)]
                P.o("vector", lambda E, p=pvs: E.reciprocal(rstripB[:, :],
                                                            p[64:65, :]),
                    wait=[(s_bmm, s_bmm.v), (s_pvc, pvcv[ih])], inc=s_v1)
                P.o("tensor", MM(pb1[0:64, 0:TG], ones_row_f[0:1, 0:64],
                                 rstripB[:, :], True, True),
                    wait=[(s_v1, s_v1.v), (s_bias, s_bias.v)], inc=s_bmm)
                P.o("vector", lambda E: E.tensor_copy(rbB[:, :], pb1[0:64, 0:TG]),
                    wait=[(s_bmm, s_bmm.v)])
                slot_o = aot[:, TG * (nstore % 2):TG * (nstore % 2 + 1)]
                w2 = [(s_ast2, s_ast2.v - 16)] if nstore >= 2 else []
                P.o("vector", lambda E, o=slot_o, p=pvs: E.tensor_mul(
                    o, p[0:64, :], rbB[:, :]), wait=w2, inc=s_aoev)
                P.dma("sync", ao_in[64 * hl:64 * (hl + 1), TG * g:TG * (g + 1)],
                      slot_o, s_ast2, wait=[(s_aoev, s_aoev.v)])
                nstore += 1
        if hp == 0:
            AST2_H0 = s_ast2.v
    pool.release(slot_pv_, s_aoev, s_aoev.v)
    pool.release(slot_pv2_, s_aoev, s_aoev.v)
    pool.release(slot_pb_, s_aoev, s_aoev.v)
    PBIG_LAST_B = [(s_exp, s_exp.v)]
    AST2_ALL = s_ast2.v
    P.pop_scope()

    P.o("gpsimd", lambda E: E.collective_compute(
        "AllGather", ALU.bypass, GROUPS, ins=[ao_in[0:128, :]],
        outs=[ao_outA[:, :]]), wait=[(s_ast2, AST2_H0)], inc=s_cc)
    CC3A = s_cc.v
    P.o("gpsimd", lambda E: E.collective_compute(
        "AllGather", ALU.bypass, GROUPS, ins=[ao_in[128:256, :]],
        outs=[ao_outB[:, :]]), wait=[(s_ast2, AST2_ALL)], inc=s_cc)
    CC3 = s_cc.v

    P.push_scope()
    x1sb = P.sb("x1sb", [128, NCH * TPC], F32)
    scrC = P.sb("c_scr", [128, TPC], BF16)
    scrC2 = P.sb("c_scr2", [128, TPC], BF16)
    P.push_scope()
    arena = P.sb("c_arena", [128, 16384], BF16)
    aob = arena[:, 0:16384]
    h2b = P.sb("c_h2", [128, NCH * TPC], BF16)
    gfm = P.sb("c_gfm", [16, TPC], BF16)
    idf = P.sb("idf", [128, 128], F32)
    gbs = P.sb("c_gbs", [128, TPC], BF16)
    oexb = P.sb("c_oexb", [128, 2 * 512], BF16)
    wrtb = P.sb("c_wrtb", [128, 8 * 128], BF16)
    csqb = P.sb("c_csqb", [128, 16], BF16)
    cnb = P.sb("c_cnb", [128, 16], BF16)
    gcol = P.sb("c_gcol", [128, 8], F32)
    gw = P.sb("c_gw", [128, 32], F32)
    tmpC = P.sb("c_tmp", [128, TG], F32)

    s_cl = P.sem("s_cl")
    s_x1 = P.sem("s_x1")
    s_wmm = P.sem("s_wmm")
    s_wol = P.sem("s_wol")
    P.dma("gpsimd", wrtb.rearrange("p (c n) -> p c n", n=128),
          wrt.rearrange("(c p) n -> p c n", p=128), s_cl, serial=False,
          wait=[(s_ast2, AST2_ALL)])
    s_idf = P.sem("s_idf")
    P.dma("gpsimd", idf[:, :], identf[:, :], s_idf, serial=False)
    dyn_dma(aomyA[:, :], (lambda E: ao_outA[:, bass.ts(DV(E, "hg"), TPC)]),
            s_cl, [(s_cc, CC3A)])
    for c in range(NCH // 2):
        P.dma("sync", aob[:, TPC * c:TPC * (c + 1)], aomyA[128 * c:128 * (c + 1), :],
              s_cl)
    dyn_dma(aomyB[:, :], (lambda E: ao_outB[:, bass.ts(DV(E, "hg"), TPC)]),
            s_cl, [(s_cc, CC3)])
    for c in range(NCH // 2):
        P.dma("sync", aob[:, TPC * (c + 4):TPC * (c + 5)],
              aomyB[128 * c:128 * (c + 1), :], s_cl)
    AOL = [(s_cl, s_cl.v)]

    P.push_scope()
    wos = P.sb("c_wos", [128, 2 * 1024], BF16)
    zf = P.sb("c_zf", [128, TPC], F32)
    for oc in range(NCH):
        wbuf = wos[:, 1024 * (oc % 2):1024 * (oc % 2 + 1)]
        wlv = P.dma("gpsimd", wbuf.rearrange("p (kc n) -> p kc n", n=128),
                    wo.rearrange("(kc p) n -> p kc n", p=128)
                    [:, :, 128 * oc:128 * (oc + 1)], s_wol,
                    wait=[(s_wmm, 4 * (oc - 1))] if oc >= 2
                    else [(s_ast2, AST2_ALL)])
        for tg in range(NTG):
            xsl = x1sb[:, TPC * oc + TG * tg:TPC * oc + TG * (tg + 1)]
            lv = P.dma("sync", xsl, xT[128 * oc:128 * (oc + 1), TG * tg:TG * (tg + 1)],
                       s_x1)
            slot, pt, pw = pool.acquire()
            for kc in range(NCH):
                P.o("tensor", MM(pt[:, 0:TG],
                                 wbuf[:, 128 * kc:128 * (kc + 1)],
                                 aob[:, TPC * kc + TG * tg:TPC * kc + TG * (tg + 1)],
                                 kc == 0, kc == NCH - 1),
                    wait=(list(pw) + CW + AOL + PBIG_LAST_B + [(s_wol, wlv)])
                    if kc == 0 else [],
                    inc=(s_wmm if kc == NCH - 1 else None))
            P.o("vector", lambda E, o=xsl, p=pt, b=bo_s[:, oc:oc + 1]:
                E.scalar_tensor_tensor(o, p[:, 0:TG], b, o, ALU.add, ALU.add),
                wait=[(s_wmm, s_wmm.v), (s_x1, lv)], inc=s_x1)
            pool.release(slot, s_x1, s_x1.v)
    X1R = [(s_x1, s_x1.v)]
    WMMF = s_wmm.v

    def h2_slice(c, tg):
        return h2b[:, TPC * c + TG * tg:TPC * c + TG * (tg + 1)]
    st3p = pbig[64:65, 0:TPC]
    _, _, s_app2 = emit_ln(P, "cln", pool, pbig, h2_slice,
                           ones_col_bf[:, :], ones_row_f[:, :], ones_row64,
                           CW + PBIG_LAST_B,
                           scrC, scrC2, st1, st2, st3p, rstd_b, mst_b,
                           src_sb=x1sb, src_ready=X1R, nbuf=1, sems=ln_sems)
    H2R = [(s_app2, s_app2.v)]

    s_zmm = P.sem("s_zmm")
    s_zev = P.sem("s_zev")
    s_zsq = P.sem("s_zsq")
    s_nmmC = P.sem("s_nmmC")
    s_strC = P.sem("s_strC")
    s_tmp = P.sem("s_tmp")
    s_zsc = P.sem("s_zsc")
    s_cn = P.sem("s_cn")
    for tg in range(NTG):
        slot, pt, pw = pool.acquire()
        for kc in range(NCH):
            P.o("tensor", MM(pt[:, 0:TG], wrtb[:, 128 * kc:128 * (kc + 1)],
                             h2b[:, TPC * kc + TG * tg:TPC * kc + TG * (tg + 1)],
                             kc == 0, kc == NCH - 1),
                wait=(list(pw) + H2R) if kc == 0 else [],
                inc=(s_zmm if kc == NCH - 1 else None))
        P.o("vector", lambda E, o=zf[:, TG * tg:TG * (tg + 1)], p=pt:
            E.tensor_copy(o, p[:, 0:TG]), wait=[(s_zmm, s_zmm.v)], inc=s_zev)
        pool.release(slot, s_zev, s_zev.v)
    P.o("scalar", lambda E: E.activation(out=scrC2[:, :], in_=zf[:, :], func=AF.Square),
        wait=[(s_zev, s_zev.v)], inc=s_zsq)
    for tg in range(NTG):
        P.o("tensor", MM(pbig[0:1, TG * tg:TG * (tg + 1)], ones_col_bf[:, :],
                         scrC2[:, TG * tg:TG * (tg + 1)], True, True),
            wait=[(s_zsq, s_zsq.v)] if tg == 0 else [],
            inc=(s_nmmC if tg == NTG - 1 else None))
    P.o("vector", lambda E: E.tensor_copy(st1[:, :], pbig[0:1, 0:TPC]),
        wait=[(s_nmmC, s_nmmC.v)], inc=s_strC)
    P.o("scalar", lambda E: E.activation(out=st1[:, :], in_=st1[:, :], func=AF.Ln),
        wait=[(s_strC, 1)])
    P.o("scalar", lambda E: E.activation(out=st1[:, :], in_=st1[:, :], func=AF.Exp,
                                         scale=0.5), inc=s_strC)
    P.o("vector", lambda E: E.tensor_scalar_max(st1[:, :], st1[:, :], 1e-12),
        wait=[(s_strC, 2)], inc=s_strC)
    P.o("vector", lambda E: E.reciprocal(st1[:, :], st1[:, :]),
        wait=[(s_strC, 3)], inc=s_strC)
    P.o("scalar", lambda E: E.activation(out=tstrip[0:1, 0:1], in_=tstrip[0:1, 0:1],
                                         func=AF.Exp), wait=CW, inc=s_tmp)
    P.o("vector", lambda E: E.tensor_scalar_mul(st1[:, :], st1[:, :],
                                                float(1.0 / np.sqrt(128.0))),
        wait=[(s_strC, 4), (s_tmp, 1)])
    P.o("vector", lambda E: E.tensor_single_scalar(st1[:, :], st1[:, :],
                                                   tstrip[0:1, 0:1], ALU.mult),
        inc=s_strC)
    for tg in range(NTG):
        slot, pt, pw = pool.acquire()
        P.o("tensor", MM(pt[:, 0:TG], ones_row_f[:, :], st1[0:1, TG * tg:TG * (tg + 1)],
                         True, True), wait=list(pw) + [(s_strC, 5)], inc=s_zmm)
        P.o("vector", lambda E, o=scrC[:, TG * tg:TG * (tg + 1)],
            z=zf[:, TG * tg:TG * (tg + 1)], p=pt: E.tensor_mul(o, z, p[:, 0:TG]),
            wait=[(s_zmm, s_zmm.v)], inc=s_zsc)
        pool.release(slot, s_zsc, s_zsc.v)
    ZSCF = s_zsc.v
    P.pop_scope()
    hidb = P.sb("c_hid", [128, 4 * TPC], BF16)
    P.o("scalar", lambda E: E.activation(out=csqb[:, :], in_=cent_s[:, :],
                                         func=AF.Square), wait=CW, inc=s_zsq)
    slot, pt, pw = pool.acquire()
    P.o("tensor", MM(pt[0:1, 0:16], ones_col_bf[:, :], csqb[:, :], True, True),
        wait=list(pw) + [(s_zsq, s_zsq.v)], inc=s_nmmC)
    P.o("vector", lambda E, p=pt: E.tensor_copy(gw[0:1, 0:16], p[0:1, 0:16]),
        wait=[(s_nmmC, s_nmmC.v)], inc=s_cn)
    P.o("scalar", lambda E: E.activation(out=gw[0:1, 0:16], in_=gw[0:1, 0:16],
                                         func=AF.Ln), wait=[(s_cn, s_cn.v)])
    P.o("scalar", lambda E: E.activation(out=gw[0:1, 0:16], in_=gw[0:1, 0:16],
                                         func=AF.Exp, scale=0.5), inc=s_zsq)
    P.o("vector", lambda E: E.tensor_scalar_max(gw[0:1, 0:16], gw[0:1, 0:16], 1e-12),
        wait=[(s_zsq, s_zsq.v)], inc=s_cn)
    P.o("vector", lambda E: E.reciprocal(gw[0:1, 0:16], gw[0:1, 0:16]),
        wait=[(s_cn, s_cn.v)], inc=s_cn)
    pool.release(slot, s_cn, s_cn.v)
    slot, pt, pw = pool.acquire()
    P.o("tensor", MM(pt[:, 0:16], ones_row_f[:, :], gw[0:1, 0:16], True, True),
        wait=list(pw) + [(s_cn, s_cn.v)], inc=s_nmmC)
    P.o("vector", lambda E, p=pt: E.tensor_mul(cnb[:, :], cent_s[:, :], p[:, 0:16]),
        wait=[(s_nmmC, s_nmmC.v)], inc=s_cn)
    pool.release(slot, s_cn, s_cn.v)
    CNR = (s_cn, s_cn.v)

    s_lg = P.sem("s_lg")
    s_g1 = P.sem("s_g1")
    s_g2 = P.sem("s_g2")
    s_g3 = P.sem("s_g3")
    s_trC = P.sem("s_trC")
    s_gf = P.sem("s_gf")
    for tt in range(TPC // 128):
        slot, pt, pw = pool.acquire()
        P.o("tensor", MM(pt[:, 0:16], scrC[:, 128 * tt:128 * (tt + 1)], cnb[:, :],
                         True, True),
            wait=list(pw) + ([CNR, (s_zsc, s_zsc.v)] if tt == 0 else []), inc=s_lg)
        P.o("vector", lambda E, p=pt: E.tensor_reduce(gcol[:, 0:1], p[:, 0:16], AX.X,
                                                      ALU.max),
            wait=[(s_lg, s_lg.v), (s_g3, s_g3.v)], inc=s_g1)
        P.o("vector", lambda E: E.tensor_scalar_mul(gcol[:, 1:2], gcol[:, 0:1], -1.0),
            wait=[(s_g1, s_g1.v)], inc=s_g1)
        P.o("scalar", lambda E, p=pt: E.activation(out=gw[:, 0:16], in_=p[:, 0:16],
                                                   func=AF.Exp, bias=gcol[:, 1:2]),
            wait=[(s_g1, s_g1.v)], inc=s_g2)
        pool.release(slot, s_g2, s_g2.v)
        P.o("vector", lambda E: E.tensor_reduce(gcol[:, 2:3], gw[:, 0:16], AX.X,
                                                ALU.add),
            wait=[(s_g2, s_g2.v)], inc=s_g1)
        P.o("vector", lambda E: E.reciprocal(gcol[:, 3:4], gcol[:, 2:3]),
            wait=[(s_g1, s_g1.v)], inc=s_g1)
        P.o("vector", lambda E: E.tensor_reduce(gcol[:, 4:5], gw[:, 0:16], AX.X,
                                                ALU.max),
            wait=[(s_g1, s_g1.v)], inc=s_g1)
        P.o("vector", lambda E: E.tensor_single_scalar(gw[:, 16:32], gw[:, 0:16],
                                                       gcol[:, 4:5], ALU.is_equal),
            wait=[(s_g1, s_g1.v)], inc=s_g1)
        P.o("vector", lambda E: E.scalar_tensor_tensor(gw[:, 16:32], gw[:, 16:32],
                                                       -2.0, gw[:, 0:16], ALU.mult,
                                                       ALU.add),
            wait=[(s_g1, s_g1.v)], inc=s_g1)
        P.o("vector", lambda E: E.tensor_reduce(gcol[:, 5:6], gw[:, 16:32], AX.X,
                                                ALU.max), wait=[(s_g1, s_g1.v)],
            inc=s_g1)
        P.o("vector", lambda E: E.tensor_single_scalar(gw[:, 16:32], gw[:, 0:16],
                                                       gcol[:, 5:6], ALU.is_ge),
            wait=[(s_g1, s_g1.v)], inc=s_g1)
        P.o("vector", lambda E: E.tensor_mul(gw[:, 16:32], gw[:, 16:32], gw[:, 0:16]),
            wait=[(s_g1, s_g1.v)], inc=s_g1)
        P.o("vector", lambda E: E.tensor_single_scalar(gw[:, 16:32], gw[:, 16:32],
                                                       gcol[:, 3:4], ALU.mult),
            wait=[(s_g1, s_g1.v)], inc=s_g3)
        slot2, pt2, pw2 = pool.acquire()
        P.o("tensor", lambda E, o=pt2[0:16, 0:128]: E.transpose(o, gw[:, 16:32],
                                                                idf[:, :]),
            wait=list(pw2) + [(s_g3, s_g3.v), (s_idf, 16)], inc=s_trC)
        P.o("vector", lambda E, o=gfm[:, 128 * tt:128 * (tt + 1)], p=pt2:
            E.tensor_copy(o, p[0:16, 0:128]), wait=[(s_trC, s_trC.v)], inc=s_gf)
        pool.release(slot2, s_gf, s_gf.v)
    GFR = [(s_gf, s_gf.v)]

    s_ie = P.sem("s_ie")
    s_oe = P.sem("s_oe")
    s_gbe = P.sem("s_gbe")
    s_hmm = P.sem("s_hmm")
    s_hev = P.sem("s_hev")
    s_omm = P.sem("s_omm")
    s_moev = P.sem("s_moev")
    gelu_val = {}
    for c in range(8):
        buf = c % 2
        iex = [arena[:, 8192 * buf + 4096 * le:8192 * buf + 4096 * (le + 1)]
               for le in range(2)]
        oex = oexb[:, 512 * buf:512 * (buf + 1)]
        wld = ([(s_hmm, 32 * (c - 1))] if c >= 2 else []) + \
            ([(s_wmm, WMMF)] if c < 2 else [])
        iev = []
        for le in range(2):
            iev.append(P.dma("gpsimd", iex[le].rearrange("p (kc n) -> p kc n", n=512),
                             iexp[2 * c + le].rearrange("(kc p) n -> p kc n", p=128),
                             s_ie, wait=wld if le == 0 else []))
        ow = [(s_omm, 8 * (c - 1))] if c >= 2 else [(s_wmm, WMMF)]
        oev = P.dma("gpsimd", oex.rearrange("p (e hc n) -> p e hc n", hc=4, n=64),
                    oexp.rearrange("e (hc p) n -> p e hc n", p=128)
                    [:, 2 * c:2 * (c + 1), :, :], s_oe, wait=ow)
        for tg in range(NTG):
            slot, pt, pw = pool.acquire()
            P.o("tensor", MM(pt[:, 0:TG], selcs[:, 128 * c:128 * (c + 1)],
                             gfm[:, TG * tg:TG * (tg + 1)], True, True),
                wait=list(pw) + (GFR if (c == 0 and tg == 0) else []), inc=s_zmm)
            P.o("vector", lambda E, o=gbs[:, TG * tg:TG * (tg + 1)], p=pt:
                E.tensor_copy(o, p[:, 0:TG]),
                wait=[(s_zmm, s_zmm.v)] + ([(s_moev, 4 * c)] if c >= 1 and tg == 0
                                           else []), inc=s_gbe)
            pool.release(slot, s_gbe, s_gbe.v)
        gbv = s_gbe.v
        for le in range(2):
            E_idx = 2 * c + le
            for hc in range(4):
                for tg in range(NTG):
                    slot, pt, pw = pool.acquire()
                    w = list(pw)
                    if hc == 0 and tg == 0:
                        w += [(s_ie, iev[le])] + (H2R if c == 0 and le == 0 else [])
                    for kc in range(NCH):
                        P.o("tensor", MM(pt[:, 0:TG],
                                         iex[le][:, 512 * kc + 128 * hc:
                                                 512 * kc + 128 * (hc + 1)],
                                         h2b[:, TPC * kc + TG * tg:
                                             TPC * kc + TG * (tg + 1)],
                                         kc == 0, kc == NCH - 1),
                            wait=w if kc == 0 else [],
                            inc=(s_hmm if kc == NCH - 1 else None))
                    hw = [(s_hmm, s_hmm.v)]
                    if hc == 0 and tg == 0:
                        hw.append((s_omm, 4 * E_idx))
                        if c == 0 and le == 0:
                            hw.append((s_zsc, ZSCF))
                    P.o("scalar", lambda E, o=hidb[:, TPC * hc + TG * tg:
                                                   TPC * hc + TG * (tg + 1)], p=pt:
                        E.activation(out=o, in_=p[:, 0:TG], func=AF.Gelu),
                        wait=hw, inc=s_hev)
                    gelu_val[(hc, tg)] = s_hev.v
                    pool.release(slot, s_hev, s_hev.v)
            for tg in range(NTG):
                for hc in range(4):
                    w = [(s_hev, gelu_val[(hc, tg)])]
                    if hc == 0:
                        w.append((s_oe, oev))
                        if le == 0 and c >= 1:
                            w.append((s_moev, 4 * (c - 1)))
                    P.o("tensor", lambda E, o=pbig[64 * le:64 * (le + 1),
                                                   TG * tg:TG * (tg + 1)],
                        l=oex.rearrange("p (e hc n) -> p e hc n", hc=4, n=64)
                        [:, le, hc, :],
                        r=hidb[:, TPC * hc + TG * tg:TPC * hc + TG * (tg + 1)],
                        st=(hc == 0), sp=(hc == 3),
                        tp=(0, 64 * le): E.matmul(o, l, r, start=st, stop=sp,
                                                  tile_position=tp,
                                                  skip_group_check=True),
                        wait=w, inc=(s_omm if hc == 3 else None))
        for tg in range(NTG):
            P.o("vector", lambda E, t=tmpC, p=pbig[:, TG * tg:TG * (tg + 1)],
                g=gbs[:, TG * tg:TG * (tg + 1)]: E.tensor_mul(t[:, :], p, g),
                wait=[(s_omm, s_omm.v), (s_gbe, gbv)])
            P.o("vector", lambda E, o=x1sb[:, TPC * c + TG * tg:
                                           TPC * c + TG * (tg + 1)], t=tmpC:
                E.tensor_add(o, o, t[:, :]), inc=s_moev)
    X2R = [(s_moev, s_moev.v)]
    PBIG_LAST_C = [(s_moev, s_moev.v)]
    if FDBG:
        s_dx2 = P.sem("s_dx2")
        for c in range(NCH):
            P.dma("sync", dx2[128 * c:128 * (c + 1), :],
                  x1sb[:, TPC * c:TPC * (c + 1)], s_dx2, wait=X2R)
    P.pop_scope()

    P.push_scope()
    h3b = P.sb("d_h3", [128, NCH * TPC], BF16)
    wqb2 = P.sb("d_wq", [128, 8 * D], BF16)
    wkvb2 = P.sb("d_wkv", [128, 8 * 2048], BF16)
    qch2 = P.sb("d_qch", [128, 2 * TPC], BF16)
    ktm = P.sb("d_ktm", [128, 2 * 1024], BF16)
    vtm = P.sb("d_vtm", [128, 2 * 1024], BF16)
    ctxsb = P.sb("d_ctx", [65, H * 65], F32)
    scrD = scrC
    scrD2 = scrC2

    s_dw = s_aw
    P.dma("gpsimd", wqb2.rearrange("p (kc n) -> p kc n", n=D),
          wqkvT.rearrange("(kc p) n -> p kc n", p=128)[:, :, 0:D], s_dw, serial=False,
          wait=X2R)
    P.dma("gpsimd", wkvb2.rearrange("p (kc n) -> p kc n", n=2048),
          wqkvT.rearrange("(kc p) n -> p kc n", p=128)[:, :, D:3 * D], s_dw,
          serial=False, wait=X2R)
    dwv = [(s_dw, s_dw.v)]

    def h3_slice(c, tg):
        return h3b[:, TPC * c + TG * tg:TPC * c + TG * (tg + 1)]
    _, _, s_app3 = emit_ln(P, "dln", pool, pbig, h3_slice,
                           ones_col_bf[:, :], ones_row_f[:, :], ones_row64,
                           CW + PBIG_LAST_C + dwv,
                           scrD, scrD2, st1, st2, st3p, rstd_b, mst_b,
                           src_sb=x1sb, src_ready=X2R, nbuf=1, sems=ln_sems)
    H3R = [(s_app3, s_app3.v)]

    s_qmm = P.sem("s_qmm")
    s_qg = P.sem("s_qg")
    s_stq2 = P.sem("s_stq2")
    for oc in range(NCH):
        buf = oc % 2
        dst = qch2[:, TPC * buf:TPC * (buf + 1)]
        for tg in range(NTG):
            slot, pt, pw = pool.acquire()
            w = list(pw) + (H3R if (oc == 0 and tg == 0) else [])
            if tg == 0 and oc >= 2:
                w.append((s_stq2, s_stq2.v - 16))
            for kc in range(NCH):
                P.o("tensor", MM(pt[:, 0:TG],
                                 wqb2[:, D * kc + 128 * oc:D * kc + 128 * (oc + 1)],
                                 h3b[:, TPC * kc + TG * tg:TPC * kc + TG * (tg + 1)],
                                 kc == 0, kc == NCH - 1),
                    wait=w if kc == 0 else [], inc=(s_qmm if kc == NCH - 1 else None))
            P.o("scalar", lambda E, o=dst[:, TG * tg:TG * (tg + 1)], p=pt:
                E.activation(out=o, in_=p[:, 0:TG], func=AF.Relu),
                wait=[(s_qmm, s_qmm.v)], inc=s_qg)
            pool.release(slot, s_qg, s_qg.v)
        P.dma("sync", q2_in[128 * oc:128 * (oc + 1), :], dst, s_stq2,
              wait=[(s_qg, s_qg.v)])

    s_kvm = P.sem("s_kvm")
    s_kve = P.sem("s_kve")
    s_cmm = P.sem("s_cmm")
    pcD = [pbig[0:64, 0:7 * 65], pbig[0:64, TG:TG + 7 * 65],
           pbig[0:64, 2 * TG:2 * TG + 2 * 65]]
    NT = TPC // 128
    for tt in range(NT):
        buf = tt % 2
        kslot = ktm[:, 1024 * buf:1024 * (buf + 1)]
        vslot = vtm[:, 1024 * buf:1024 * (buf + 1)]
        for pi, dst in ((0, kslot), (1, vslot)):
            for vc in range(2):
                slot, pt, pw = pool.acquire()
                w = list(pw)
                if pi == 0 and vc == 0:
                    if tt >= 2:
                        w.append((s_cmm, tt - 1))
                    if tt == 0:
                        w += H3R
                for kc in range(NCH):
                    P.o("tensor", MM(pt[:, 0:TG],
                                     h3b[:, TPC * kc + 128 * tt:TPC * kc + 128 * (tt + 1)],
                                     wkvb2[:, 2048 * kc + 1024 * pi + TG * vc:
                                           2048 * kc + 1024 * pi + TG * (vc + 1)],
                                     kc == 0, kc == NCH - 1),
                        wait=w if kc == 0 else [],
                        inc=(s_kvm if kc == NCH - 1 else None))
                if pi == 0:
                    P.o("vector", lambda E, o=dst[:, TG * vc:TG * (vc + 1)], p=pt:
                        E.tensor_scalar_max(o, p[:, 0:TG], 0.0),
                        wait=[(s_kvm, s_kvm.v)], inc=s_kve)
                else:
                    P.o("vector", lambda E, o=dst[:, TG * vc:TG * (vc + 1)], p=pt:
                        E.tensor_copy(o, p[:, 0:TG]),
                        wait=[(s_kvm, s_kvm.v)], inc=s_kve)
                pool.release(slot, s_kve, s_kve.v)
        kvv = s_kve.v
        for h in range(H):
            bank, off = (0, h) if h < 7 else ((1, h - 7) if h < 14 else (2, h - 14))
            P.o("tensor", lambda E, o=pcD[bank][:, 65 * off:65 * off + 64],
                l=kslot[:, 64 * h:64 * (h + 1)], r=vslot[:, 64 * h:64 * (h + 1)],
                st=(tt == 0), sp=(tt == NT - 1): E.matmul(
                    o, l, r, start=st, stop=sp, skip_group_check=True),
                wait=([(s_kve, kvv)] if h == 0 else []))
            P.o("tensor", lambda E, o=pcD[bank][:, 65 * off + 64:65 * off + 65],
                l=kslot[:, 64 * h:64 * (h + 1)],
                st=(tt == 0), sp=(tt == NT - 1): E.matmul(
                    o, l, ones_col_bf[:, 0:1], start=st, stop=sp,
                    skip_group_check=True),
                inc=(s_cmm if h == H - 1 else None))
    s_cev = P.sem("s_cev")
    s_stc = P.sem("s_stc")
    P.o("vector", lambda E: E.memset(ctxsb[64:65, :], 0.0), inc=s_cev)
    for h in range(H):
        bank, off = (0, h) if h < 7 else ((1, h - 7) if h < 14 else (2, h - 14))
        P.o("vector", lambda E, o=ctxsb[0:64, 65 * h:65 * (h + 1)],
            p=pcD[bank][:, 65 * off:65 * (off + 1)]: E.tensor_copy(o, p),
            wait=[(s_cmm, s_cmm.v)] if h == 0 else [], inc=s_cev)
    P.dma("sync", ctx_in[:, :], ctxsb[:, :], s_stc, wait=[(s_cev, s_cev.v)])
    PBIG_LAST_D = [(s_cev, s_cev.v)]
    STQ2 = s_stq2.v
    STC = s_stc.v
    P.pop_scope()

    P.o("gpsimd", lambda E: E.collective_compute(
        "AllGather", ALU.bypass, GROUPS, ins=[q2_in[:, :]], outs=[q2_out[:, :]]),
        wait=[(s_stq2, STQ2)], inc=s_cc)
    P.o("gpsimd", lambda E: E.collective_compute(
        "AllReduce", ALU.add, GROUPS, ins=[ctx_in[:, :]], outs=[ctx_out[:, :]]),
        wait=[(s_stc, STC)], inc=s_cc)
    CC45 = s_cc.v

    P.push_scope()
    qp2 = P.sb("e_qp", [128, 2 * S], BF16)
    ctxf = P.sb("e_ctxf", [65, 4 * 65], F32)
    ctxb2 = P.sb("e_ctxb", [128, 4 * 65], BF16)
    aot2 = P.sb("e_aot", [64, 2 * TG], BF16)
    pvsbE = P.sb("e_pvsb", [64, 2 * TG], F32)
    drowE = P.sb("e_drow", [1, 2 * 4 * TG], F32)
    dcolE = P.sb("e_dcol", [4, 2 * TG], F32)
    rcolE = P.sb("e_rcol", [4, 2 * TG], F32)
    rrowE = P.sb("e_rrow", [1, 2 * 4 * TG], F32)

    s_el = s_bl
    dyn_dma(ctxmy[:, :], lambda E: ctx_out[:, bass.ts(DV(E, "hg"), 4 * 65)],
            s_el, [(s_cc, CC45)])
    q23 = q2_out.rearrange("(r t) c -> r t c", t=D)
    dyn_dma(q2my.rearrange("(r t) c -> r t c", t=256),
            (lambda E: q23[:, bass.ts(DV(E, "hg"), 256), :][:, 0:256, :]),
            s_el, [])
    P.dma("sync", ctxf[:, :], ctxmy[:, :], s_el)
    for hp in range(2):
        for r in range(4):
            P.dma("sync", qp2[:, S * hp + TPC * r:S * hp + TPC * (r + 1)],
                  q2my[256 * r + 128 * hp:256 * r + 128 * (hp + 1), :], s_el)
    ELW = [(s_el, s_el.v)]
    s_sumE = s_cn
    P.o("vector", lambda E: E.tensor_copy(ctxb2[0:65, :], ctxf[:, :]),
        wait=ELW, inc=s_sumE)
    P.dma("sync", ctxb2[64:128, :], ctxb2[0:64, :], s_sumE,
          wait=[(s_sumE, s_sumE.v)])
    CTXR = [(s_sumE, s_sumE.v)]

    s_emm = P.sem("s_emm")
    s_ev1 = P.sem("s_ev1")
    s_ebm = P.sem("s_ebm")
    s_eev = P.sem("s_eev")
    s_ast3 = P.sem("s_ast3")
    s_ecp2 = P.sem("s_ecp2")
    s_edm = P.sem("s_edm")
    s_ecpd = P.sem("s_ecpd")
    s_edma = P.sem("s_edma")
    s_erc = P.sem("s_erc")
    slots_e = [pool.acquire() for _ in range(4)]
    ppv2L = [slots_e[0][1], slots_e[1][1]]
    pb2L = [slots_e[2][1], slots_e[3][1]]
    EPF = [w for sl in slots_e for w in sl[2]]
    NQ = S // TG

    def emit_dchain(m):
        par = m % 2
        pp = 64 * par
        hl_, ch = m // 4, m % 4
        hp_ = hl_ // 2
        base_ = 64 * (hl_ % 2)
        w = [(s_ecpd, m - 1)] if m >= 2 else (list(CTXR) + PBIG_LAST_D)
        for q4 in range(4):
            P.o("tensor", lambda E, o=pbig[pp:pp + 1, TG * q4:TG * (q4 + 1)],
                l=ctxb2[base_:base_ + 64, 65 * hl_ + 64:65 * hl_ + 65],
                r=qp2[base_:base_ + 64, S * hp_ + 4 * TG * ch + TG * q4:
                      S * hp_ + 4 * TG * ch + TG * (q4 + 1)]:
                E.matmul(o, l, r, start=True, stop=True, skip_group_check=True),
                wait=w if q4 == 0 else [], inc=(s_edm if q4 == 3 else None))
        P.o("scalar", lambda E, o=drowE[0:1, 4 * TG * par:4 * TG * (par + 1)],
            i=pbig[pp:pp + 1, 0:4 * TG]: E.activation(
                out=o, in_=i, func=AF.Copy, bias=EPS_F32),
            wait=[(s_edm, m + 1)] +
            ([(s_edma, 128 * (m - 2) + 64)] if m >= 2 else []),
            inc=s_ecpd)
        for j in range(4):
            P.dma("sync", dcolE[j:j + 1, TG * par:TG * (par + 1)],
                  drowE[0:1, 4 * TG * par + TG * j:4 * TG * par + TG * (j + 1)],
                  s_edma,
                  wait=([(s_ecpd, m + 1)] + ([(s_erc, m - 1)] if m >= 2 else []))
                  if j == 0 else [])
        P.o("vector", lambda E, o=rcolE[:, TG * par:TG * (par + 1)],
            i=dcolE[:, TG * par:TG * (par + 1)]: E.reciprocal(o, i),
            wait=[(s_edma, 128 * m + 64)], inc=s_erc)
        for j in range(4):
            P.dma("sync", rrowE[0:1, 4 * TG * par + TG * j:4 * TG * par + TG * (j + 1)],
                  rcolE[j:j + 1, TG * par:TG * (par + 1)], s_edma,
                  wait=([(s_erc, m + 1)] + ([(s_ebm, 4 * (m - 1))] if m >= 2 else []))
                  if j == 0 else [])

    emit_dchain(0)
    for hl in range(4):
        hp, base = hl // 2, 64 * (hl % 2)
        for g in range(NQ):
            n = NQ * hl + g
            m = n // 4
            if n % 4 == 0 and m + 1 < 16:
                emit_dchain(m + 1)
            buf = n % 2
            j = n % 4
            par = m % 2
            ppv2 = ppv2L[buf]
            pb2 = pb2L[buf]
            pvb = pvsbE[:, TG * buf:TG * (buf + 1)]
            w = (list(CTXR) + EPF) if n < 2 else [(s_ecp2, n - 1)]
            P.o("tensor", lambda E, o=ppv2[0:64, 0:TG],
                l=ctxb2[base:base + 64, 65 * hl:65 * hl + 64],
                r=qp2[base:base + 64, S * hp + TG * g:S * hp + TG * (g + 1)]:
                E.matmul(o, l, r, start=True, stop=True, skip_group_check=True),
                wait=w, inc=s_emm)
            P.o("scalar", lambda E, o=pvb, i=ppv2[0:64, 0:TG]: E.activation(
                out=o, in_=i, func=AF.Copy),
                wait=[(s_emm, n + 1)] + ([(s_eev, n - 1)] if n >= 2 else []),
                inc=s_ecp2)
            P.o("tensor", MM(pb2[0:64, 0:TG], ones_row_f[0:1, 0:64],
                             rrowE[0:1, 4 * TG * par + TG * j:
                                   4 * TG * par + TG * (j + 1)], True, True),
                wait=[(s_edma, 128 * (m + 1))] +
                ([(s_eev, n - 1)] if n >= 2 else []),
                inc=s_ebm)
            slot_o = aot2[:, TG * (n % 2):TG * (n % 2 + 1)]
            w2 = [(s_ast3, s_ast3.v - 16)] if n >= 2 else []
            P.o("vector", lambda E, o=slot_o, p=pvb, b=pb2: E.tensor_mul(
                o, p, b[0:64, 0:TG]),
                wait=[(s_ecp2, n + 1), (s_ebm, n + 1)] + w2, inc=s_eev)
            P.dma("sync", ao2_in[64 * hl:64 * (hl + 1), TG * g:TG * (g + 1)], slot_o,
                  s_ast3, wait=[(s_eev, s_eev.v)])
        if hl == 1:
            AST3_H0 = s_ast3.v
    for sl in slots_e:
        pool.release(sl[0], s_eev, s_eev.v)
    AST3 = s_ast3.v
    P.pop_scope()

    P.o("gpsimd", lambda E: E.collective_compute(
        "AllGather", ALU.bypass, GROUPS, ins=[ao2_in[0:128, :]],
        outs=[ao2_outA[:, :]]), wait=[(s_ast3, AST3_H0)], inc=s_cc)
    CC6A = s_cc.v
    P.o("gpsimd", lambda E: E.collective_compute(
        "AllGather", ALU.bypass, GROUPS, ins=[ao2_in[128:256, :]],
        outs=[ao2_outB[:, :]]), wait=[(s_ast3, AST3)], inc=s_cc)
    CC6 = s_cc.v

    x3d = nc.dram_tensor("x3d", [D, TPC], F32).ap()
    P.push_scope()
    ao2b = P.sb("f_ao2", [128, NCH * TPC], BF16)
    wos2 = P.sb("f_wos", [128, 2 * 1024], BF16)
    s_fl = s_bl
    dyn_dma(ao2myA[:, :], (lambda E: ao2_outA[:, bass.ts(DV(E, "hg"), TPC)]),
            s_fl, [(s_cc, CC6A)])
    for c in range(NCH // 2):
        P.dma("sync", ao2b[:, TPC * c:TPC * (c + 1)],
              ao2myA[128 * c:128 * (c + 1), :], s_fl)
    dyn_dma(ao2myB[:, :], (lambda E: ao2_outB[:, bass.ts(DV(E, "hg"), TPC)]),
            s_fl, [(s_cc, CC6)])
    for c in range(NCH // 2):
        P.dma("sync", ao2b[:, TPC * (c + 4):TPC * (c + 5)],
              ao2myB[128 * c:128 * (c + 1), :], s_fl)
    AO2L = [(s_fl, s_fl.v)]
    s_pm = P.sem("s_pm")
    s_pe = P.sem("s_pe")
    s_st3 = P.sem("s_st3")
    s_wol2 = s_wol
    for oc in range(NCH):
        wbuf = wos2[:, 1024 * (oc % 2):1024 * (oc % 2 + 1)]
        wlv = P.dma("gpsimd", wbuf.rearrange("p (kc n) -> p kc n", n=128),
                    wpT.rearrange("(kc p) n -> p kc n", p=128)
                    [:, :, 128 * oc:128 * (oc + 1)], s_wol2,
                    wait=[(s_pm, 4 * (oc - 1))] if oc >= 2
                    else [(s_ast3, AST3)])
        for tg in range(NTG):
            slot, pt, pw = pool.acquire()
            for kc in range(NCH):
                P.o("tensor", MM(pt[:, 0:TG],
                                 wbuf[:, 128 * kc:128 * (kc + 1)],
                                 ao2b[:, TPC * kc + TG * tg:TPC * kc + TG * (tg + 1)],
                                 kc == 0, kc == NCH - 1),
                    wait=(list(pw) + AO2L + [(s_wol2, wlv)] + PBIG_LAST_D)
                    if kc == 0 else [],
                    inc=(s_pm if kc == NCH - 1 else None))
            P.o("vector", lambda E, o=x1sb[:, TPC * oc + TG * tg:
                                           TPC * oc + TG * (tg + 1)], p=pt,
                b=bp_s[:, oc:oc + 1]: E.scalar_tensor_tensor(
                    o, p[:, 0:TG], b, o, ALU.add, ALU.add),
                wait=[(s_pm, s_pm.v)] + (X2R if oc == 0 and tg == 0 else []),
                inc=s_pe)
            pool.release(slot, s_pe, s_pe.v)
        P.dma("sync", x3d[128 * oc:128 * (oc + 1), :], x1sb[:, TPC * oc:TPC * (oc + 1)],
              s_st3, wait=[(s_pe, s_pe.v)])
    X3ALL = [(s_st3, s_st3.v)]
    P.pop_scope()
    P.pop_scope()

    P.push_scope()
    xcF = P.sb("f_xc", [128, 2 * TPC], F32)
    scrF = P.sb("f_scr", [128, TPC], BF16)
    scrF2 = P.sb("f_scr2", [128, TPC], BF16)
    str_ = P.sb("f_str", [1, TPC], F32)
    sts = P.sb("f_sts", [1, TPC], F32)
    hhy = P.sb("f_hhy", [128, NCH * TPC], BF16)
    srowF = P.sb("f_srow", [1, 2 * D], F32)
    qsb = P.sb("f_qsb", [128, 2 * D], U8)
    w1c = P.sb("f_w1c", [128, 2 * D], BF16)
    y1c = P.sb("f_y1c", [128, 2 * TPC], BF16)
    w2b = P.sb("f_w2b", [128, 32 * D], BF16)
    tmpvF = P.sb("f_tmpv", [128, TG], F32)
    tmpx3 = P.sb("f_tmpx3", [128, 2 * TG], F32)
    stm = st1
    stv = st2

    s_misc = s_tmp
    b2hv = P.o("vector", lambda E: E.tensor_scalar_mul(b2h_s[:, :], b2_s[:, :], 0.5),
               wait=CW, inc=s_misc)

    s_castF, s_sqF, s_mmF, s_strF, _, s_appF, s_xsF = ln_sems
    castF0, sqF0, mmF0, appF0 = s_castF.v, s_sqF.v, s_mmF.v, s_appF.v
    for c in range(NCH):
        buf = c % 2
        ap = xcF[:, TPC * buf:TPC * (buf + 1)]
        w = list(X3ALL) + [(s_xsF, s_xsF.v)]
        if c >= 2:
            w.append((s_castF, castF0 + c - 1))
        lv = P.dma("sync", ap, x3d[128 * c:128 * (c + 1), :], s_xsF, wait=w)
        wv = [(s_xsF, lv)]
        if c >= 1:
            wv.append((s_mmF, mmF0 + c))
        P.o("vector", lambda E, o=scrF[:, :], i=ap: E.tensor_copy(o, i),
            wait=wv, inc=s_castF)
        P.o("scalar", lambda E: E.activation(out=scrF2[:, :], in_=scrF[:, :],
                                             func=AF.Square),
            wait=[(s_castF, castF0 + c + 1)] + ([(s_mmF, mmF0 + c)] if c >= 1 else []),
            inc=s_sqF)
        for tg in range(NTG):
            P.o("tensor", MM(pbig[0:1, TG * tg:TG * (tg + 1)], ones_col_bf[:, :],
                             scrF[:, TG * tg:TG * (tg + 1)], c == 0, c == NCH - 1),
                wait=([(s_sqF, sqF0 + c + 1)] + (PBIG_LAST_D if c == 0 else []))
                if tg == 0 else [])
            P.o("tensor", MM(pbig[32:33, TG * tg:TG * (tg + 1)], ones_col_bf[:, :],
                             scrF2[:, TG * tg:TG * (tg + 1)], c == 0, c == NCH - 1),
                inc=(s_mmF if tg == NTG - 1 else None))
    st3pF = pbig[64:65, 0:TPC]
    P.o("vector", lambda E: E.tensor_scalar_mul(stm[:, :], pbig[0:1, 0:TPC], 1.0 / D),
        wait=[(s_mmF, s_mmF.v)])
    P.o("vector", lambda E: E.tensor_scalar_mul(stv[:, :], pbig[32:33, 0:TPC], 1.0 / D),
        inc=s_strF)
    P.o("scalar", lambda E: E.activation(out=st3pF, in_=stm[:, :], func=AF.Square),
        wait=[(s_strF, s_strF.v)], inc=s_strF)
    P.o("vector", lambda E: E.tensor_sub(stv[:, :], stv[:, :], st3pF),
        wait=[(s_strF, s_strF.v)])
    P.o("vector", lambda E: E.tensor_scalar_add(str_[:, :], stv[:, :], EPS_LN),
        inc=s_strF)
    P.o("scalar", lambda E: E.activation(out=str_[:, :], in_=str_[:, :], func=AF.Ln),
        wait=[(s_strF, s_strF.v)])
    P.o("scalar", lambda E: E.activation(out=str_[:, :], in_=str_[:, :], func=AF.Exp,
                                         scale=-0.5), inc=s_strF)
    P.o("vector", lambda E: E.tensor_scalar_add(sts[:, :], stv[:, :], EPS_LN),
        wait=[(s_strF, s_strF.v)], inc=s_strF)
    P.o("vector", lambda E: E.reciprocal(sts[:, :], sts[:, :]),
        wait=[(s_strF, s_strF.v)], inc=s_strF)
    P.o("vector", lambda E: E.tensor_mul(sts[:, :], sts[:, :], stv[:, :]),
        wait=[(s_strF, s_strF.v)], inc=s_strF)
    P.o("vector", lambda E: E.tensor_scalar_add(sts[:, :], sts[:, :], EPS_LN),
        inc=s_strF)
    P.o("scalar", lambda E: E.activation(out=sts[:, :], in_=sts[:, :], func=AF.Ln),
        wait=[(s_strF, s_strF.v)])
    P.o("scalar", lambda E: E.activation(out=sts[:, :], in_=sts[:, :], func=AF.Exp,
                                         scale=-0.5), inc=s_strF)
    P.o("vector", lambda E: E.tensor_mul(sts[:, :], sts[:, :], str_[:, :]),
        wait=[(s_strF, s_strF.v)])
    P.o("vector", lambda E: E.tensor_mul(stv[:, :], stm[:, :], sts[:, :]))
    P.o("vector", lambda E: E.tensor_scalar_mul(stv[:, :], stv[:, :], -1.0))
    P.o("vector", lambda E: E.tensor_mul(stm[:, :], stm[:, :], str_[:, :]))
    P.o("vector", lambda E: E.tensor_scalar_mul(stm[:, :], stm[:, :], -1.0))
    STRF = P.sem("f_strfin")
    P.o("vector", lambda E: E.tensor_scalar_add(str_[:, :], str_[:, :], 1.0),
        inc=STRF)

    s_bcF = ln_sems[4]
    for tg in range(NTG):
        for strip, orow, bdst in ((sts, ones_row_f[:, :], rstd_b),
                                  (stv, ones_row64, mst_b)):
            slot, pt, pw = pool.acquire()
            vmm = P.o("tensor", MM(pt[:, 0:TG], orow,
                                   strip[0:1, TG * tg:TG * (tg + 1)], True, True),
                      wait=list(pw) + [(STRF, 1)], inc=s_bcF)
            P.o("scalar", lambda E, o=bdst[:, TG * tg:TG * (tg + 1)], p=pt:
                E.activation(out=o, in_=p[:, 0:TG], func=AF.Copy),
                wait=[(s_bcF, vmm)], inc=s_bcF)
            pool.release(slot, s_bcF, s_bcF.v)
    bcFv = s_bcF.v
    for c in range(NCH):
        buf = c % 2
        ap = xcF[:, TPC * buf:TPC * (buf + 1)]
        w = [(s_mmF, mmF0 + NCH), (s_xsF, s_xsF.v)]
        if c >= 2:
            w.append((s_appF, appF0 + 4 * (c - 1)))
        lv = P.dma("sync", ap, x3d[128 * c:128 * (c + 1), :], s_xsF, wait=w)
        for tg in range(NTG):
            P.o("vector", lambda E, t=tmpvF, x=ap[:, TG * tg:TG * (tg + 1)],
                r=rstd_b[:, TG * tg:TG * (tg + 1)]: E.tensor_mul(t[:, :], x, r),
                wait=[(s_bcF, bcFv), (s_xsF, lv)])
            P.o("vector", lambda E, t=tmpvF, m=mst_b[:, TG * tg:TG * (tg + 1)]:
                E.tensor_add(t[:, :], t[:, :], m))
            P.o("vector", lambda E, o=hhy[:, TPC * c + TG * tg:TPC * c + TG * (tg + 1)],
                t=tmpvF, g=gam_s[:, c:c + 1], b=bet_s[:, c:c + 1]:
                E.tensor_scalar(o, t[:, :], g, b, ALU.mult, ALU.add), inc=s_appF)
    HHR = [(s_appF, s_appF.v)]

    s_q1 = P.sem("f_q1")
    s_dq = P.sem("f_dq")
    s_fmm = P.sem("f_fmm")
    s_sil = P.sem("f_sil")
    s_sty = P.sem("f_sty")
    s_sbm = P.sem("f_sbm")
    for yc in range(32):
        buf = yc % 2
        qb = qsb[:, D * buf:D * (buf + 1)]
        sr = srowF[0:1, D * buf:D * (buf + 1)]
        wc = w1c[:, D * buf:D * (buf + 1)]
        yb = y1c[:, TPC * buf:TPC * (buf + 1)]
        w = ([(s_dq, 2 * (yc - 1))] if yc >= 2 else list(X3ALL)) + [(s_q1, s_q1.v)]
        l1 = P.dma("sync", qb.rearrange("p (kc n) -> p kc n", n=128),
                   fc1q.rearrange("(kc p) n -> p kc n", p=128)
                   [:, :, 128 * yc:128 * (yc + 1)], s_q1, wait=w)
        l2 = P.dma("gpsimd", sr, sc1x[yc:yc + 1, :], s_q1)
        for hb in range(2):
            slot_s, pts, pws = pool.acquire()
            P.o("tensor", MM(pts[:, 0:TG], ones_row_f[0:1, :],
                             sr[0:1, TG * hb:TG * (hb + 1)], True, True),
                wait=list(pws) + ([(s_q1, l2)] if hb == 0 else []), inc=s_sbm)
            P.o("vector", lambda E, o=wc[:, TG * hb:TG * (hb + 1)],
                q=qb[:, TG * hb:TG * (hb + 1)], s=pts: E.scalar_tensor_tensor(
                    o, q, -128.0, s[:, 0:TG], ALU.add, ALU.mult),
                wait=[(s_sbm, s_sbm.v)] + ([(s_fmm, 4 * (yc - 1))]
                                           if (yc >= 2 and hb == 0) else []),
                inc=s_dq)
            pool.release(slot_s, s_dq, s_dq.v)
        for tg in range(NTG):
            slot, pt, pw = pool.acquire()
            for kc in range(NCH):
                P.o("tensor", MM(pt[:, 0:TG], wc[:, 128 * kc:128 * (kc + 1)],
                                 hhy[:, TPC * kc + TG * tg:TPC * kc + TG * (tg + 1)],
                                 kc == 0, kc == NCH - 1),
                    wait=(list(pw) + [(s_dq, s_dq.v)] +
                          (HHR if yc == 0 and tg == 0 else [])) if kc == 0 else [],
                    inc=(s_fmm if kc == NCH - 1 else None))
            P.o("scalar", lambda E, o=yb[:, TG * tg:TG * (tg + 1)], p=pt,
                b=b1_s[:, yc:yc + 1]: E.activation(out=o, in_=p[:, 0:TG], func=AF.Silu,
                                                   bias=b),
                wait=[(s_fmm, s_fmm.v)] + ([(s_sty, 16 * (yc - 1))] if yc >= 2 and
                                           tg == 0 else []), inc=s_sil)
            pool.release(slot, s_sil, s_sil.v)
        P.dma("sync", y1d[128 * yc:128 * (yc + 1), :], yb, s_sty,
              wait=[(s_sil, s_sil.v)])
    FC1ALL = [(s_sty, s_sty.v)]

    s_q2 = s_q1
    s_dq2 = P.sem("f_dq2")
    for yc in range(32):
        buf = yc % 2
        qb = qsb[:, D * buf:D * (buf + 1)]
        sr = srowF[0:1, D * buf:D * (buf + 1)]
        w = [(s_q2, s_q2.v)]
        if yc < 2:
            w.append((s_dq, 64))
        else:
            w.append((s_dq2, 2 * (yc - 1)))
        l1 = P.dma("sync", qb, fc2q[128 * yc:128 * (yc + 1), :], s_q2, wait=w)
        l2 = P.dma("gpsimd", sr, sc2x[yc:yc + 1, :], s_q2)
        for hb in range(2):
            slot_s, pts, pws = pool.acquire()
            P.o("tensor", MM(pts[:, 0:TG], ones_row_f[0:1, :],
                             sr[0:1, TG * hb:TG * (hb + 1)], True, True),
                wait=list(pws) + ([(s_q2, l2)] if hb == 0 else []), inc=s_sbm)
            P.o("vector", lambda E,
                o=w2b[:, D * yc + TG * hb:D * yc + TG * (hb + 1)],
                q=qb[:, TG * hb:TG * (hb + 1)], s=pts: E.scalar_tensor_tensor(
                    o, q, -128.0, s[:, 0:TG], ALU.add, ALU.mult),
                wait=[(s_sbm, s_sbm.v)], inc=s_dq2)
            pool.release(slot_s, s_dq2, s_dq2.v)
    W2R = [(s_dq2, s_dq2.v)]

    s_y1 = P.sem("f_y1")
    s_2mm = P.sem("f_2mm")
    s_x3l = P.sem("f_x3l")
    s_fin = P.sem("f_fin")
    s_st4 = P.sem("f_st4")
    for tg in range(NTG):
        for strip, orow, bdst in ((str_, ones_row_f[:, :], rstd_b),
                                  (stm, ones_row_f[:, :], mst_b)):
            slot, pt, pw = pool.acquire()
            vmm = P.o("tensor", MM(pt[:, 0:TG], orow,
                                   strip[0:1, TG * tg:TG * (tg + 1)], True, True),
                      wait=list(pw) + [(STRF, 1)] + (HHR if tg == 0 else []),
                      inc=s_bcF)
            P.o("scalar", lambda E, o=bdst[:, TG * tg:TG * (tg + 1)], p=pt:
                E.activation(out=o, in_=p[:, 0:TG], func=AF.Copy),
                wait=[(s_bcF, vmm)], inc=s_bcF)
            pool.release(slot, s_bcF, s_bcF.v)
    bcF4v = s_bcF.v
    for tg in range(NTG):
        wv = (list(W2R) + FC1ALL if tg == 0 else [(s_fin, s_fin.v)]) + \
            [(s_y1, s_y1.v)]
        l1 = P.dma("sync", hhy.rearrange("p (yc t) -> p yc t", t=TG),
                   y1d.rearrange("(yc p) t -> p yc t", p=128)
                   [:, :, TG * tg:TG * (tg + 1)], s_y1, wait=wv)
        for oc in range(NCH):
            n = NCH * tg + oc
            slot, pt, pw = pool.acquire()
            for yc in range(32):
                P.o("tensor", MM(pt[:, 0:TG],
                                 w2b[:, D * yc + 128 * oc:D * yc + 128 * (oc + 1)],
                                 hhy[:, TG * yc:TG * (yc + 1)],
                                 yc == 0, yc == 31),
                    wait=(list(pw) + [(s_y1, l1)] + (W2R if n == 0 else []))
                    if yc == 0 else [],
                    inc=(s_2mm if yc == 31 else None))
            gv = s_2mm.v
            xb3 = tmpx3[:, TG * (n % 2):TG * (n % 2 + 1)]
            lx = P.dma("sync", xb3,
                       x3d[128 * oc:128 * (oc + 1), TG * tg:TG * (tg + 1)], s_x3l,
                       wait=[(s_fin, s_fin.v - 1)] if n >= 2 else [])
            P.o("vector", lambda E, t=tmpvF, x=xb3,
                r=rstd_b[:, TG * tg:TG * (tg + 1)]: E.tensor_mul(t[:, :], x, r),
                wait=[(s_2mm, gv), (s_bcF, bcF4v), (s_x3l, lx), (s_misc, b2hv)] +
                ([(s_st4, 16 * n)] if n >= 1 else []))
            P.o("vector", lambda E, t=tmpvF,
                m=mst_b[:, TG * tg:TG * (tg + 1)]: E.tensor_add(t[:, :], t[:, :], m))
            P.o("vector", lambda E, t=tmpvF, b=b2h_s[:, oc:oc + 1]:
                E.tensor_single_scalar(t[:, :], t[:, :], b, ALU.add))
            P.o("vector", lambda E, t=tmpvF, p=pt: E.scalar_tensor_tensor(
                t[:, :], p[:, 0:TG], 0.5, t[:, :], ALU.mult, ALU.add), inc=s_fin)
            pool.release(slot, s_fin, s_fin.v)
            P.dma("sync", x4T[128 * oc:128 * (oc + 1), TG * tg:TG * (tg + 1)],
                  tmpvF[:, :], s_st4, wait=[(s_fin, s_fin.v)])
    P.pop_scope()
    if FDBG:
        s_dbg = P.sem("s_fdbg")
        for dst, src in ((dkq, kq_in), (dvi, v_in), (dao, ao_in), (dq2, q2_in),
                         (dctx, ctx_in), (dao2, ao2_in), (dx3, x3d)):
            P.dma("sync", dst[:, :], src[:, :], s_dbg, wait=[(s_st4, s_st4.v)])
        P.o("sync", lambda E: E.nop(), wait=[(s_st4, s_st4.v), (s_dbg, s_dbg.v)])
    else:
        P.o("sync", lambda E: E.nop(), wait=[(s_st4, s_st4.v)])
    P.build()
    return P


def stage_fused(inputs):
    x = inputs["L0_attn_x"]
    pos_all = np.arange(S, dtype=np.float32)
    inv = (1.0 / (10000.0 ** (np.arange(0, DR, 2, dtype=np.float32) / DR))).astype(np.float32)
    selcv = np.zeros((16, 8 * 128), BFNP)
    for c in range(8):
        selcv[2 * c, 128 * c:128 * c + 64] = 1.0
        selcv[2 * c + 1, 128 * c + 64:128 * (c + 1)] = 1.0
    sc1 = np.asarray(inputs["L1_ffn_fc1_scales"], dtype=np.float32)
    sc2 = np.asarray(inputs["L1_ffn_fc2_scales"], dtype=np.float32)
    sc1x = np.ascontiguousarray(
        np.stack([sc1[:, 128 * yc:128 * (yc + 1)].reshape(8 * 128)
                  for yc in range(32)]))
    sc2x = np.ascontiguousarray(sc2)
    common = {
        "wq": inputs["L0_attn_wq"], "wqc": inputs["L0_attn_wqc"],
        "wqr": inputs["L0_attn_wqr"], "wkv": inputs["L0_attn_wkv"],
        "wkr": inputs["L0_attn_wkr"], "wkc": inputs["L0_attn_wkc"],
        "wv": inputs["L0_attn_wv"],
        "bqx": np.ascontiguousarray(np.asarray(inputs["L0_attn_bq"]).reshape(2, 128).T),
        "bqcx": np.ascontiguousarray(np.asarray(inputs["L0_attn_bqc"]).reshape(7, 128).T),
        "bqrx": np.ascontiguousarray(np.asarray(inputs["L0_attn_bqr"]).reshape(1, 128).T),
        "bkvx": np.ascontiguousarray(np.asarray(inputs["L0_attn_bkv"]).reshape(2, 128).T),
        "bkrx": np.ascontiguousarray(np.asarray(inputs["L0_attn_bkr"]).reshape(1, 128).T),
        "bkcx": np.ascontiguousarray(np.asarray(inputs["L0_attn_bkc"]).reshape(7, 128).T),
        "bvx": np.asarray(inputs["L0_attn_bv"]).reshape(1, D),
        "onesc": np.ones((128, 1), np.float32),
        "onesr": np.ones((1, 128), np.float32),
        "tri": np.where(np.arange(128)[:, None] <= np.arange(128)[None, :],
                        0.0, -65536.0).astype(BFNP),
        "identb": np.eye(128, dtype=BFNP),
        "identf": np.eye(128, dtype=np.float32),
        "wo": np.ascontiguousarray(
            np.asarray(inputs["L0_attn_wo"]).reshape(16, 64, D)
            [[0, 1, 4, 5, 8, 9, 12, 13, 2, 3, 6, 7, 10, 11, 14, 15]]
            .reshape(D, D)),
        "box": np.ascontiguousarray(np.asarray(inputs["L0_attn_bo"]).reshape(8, 128).T),
        "wrt": np.ascontiguousarray(np.asarray(inputs["L0_ffn_w_router"]).T),
        "centT": np.ascontiguousarray(np.asarray(inputs["L0_ffn_head_centroids"]).T),
        "tempx": np.asarray(inputs["L0_ffn_temperature"]).reshape(1, 1),
        "iexp": inputs["L0_ffn_input_experts"],
        "oexp": inputs["L0_ffn_output_experts"],
        "selc": selcv,
        "wqkvT": np.ascontiguousarray(np.asarray(inputs["L1_attn_w_qkv"]).T),
        "wpT": np.ascontiguousarray(
            np.asarray(inputs["L1_attn_w_proj"]).T.reshape(16, 64, D)
            [[0, 1, 4, 5, 8, 9, 12, 13, 2, 3, 6, 7, 10, 11, 14, 15]]
            .reshape(D, D)),
        "bpx": np.ascontiguousarray(np.asarray(inputs["L1_attn_b_proj"]).reshape(8, 128).T),
        "gamx": np.ascontiguousarray(np.asarray(inputs["L1_ffn_ln_gamma"]).reshape(8, 128).T),
        "betx": np.ascontiguousarray(np.asarray(inputs["L1_ffn_ln_beta"]).reshape(8, 128).T),
        "fc1q": np.asarray(inputs["L1_ffn_fc1_q"]).astype(np.uint8),
        "sc1x": sc1x,
        "b1x": np.ascontiguousarray(np.asarray(inputs["L1_ffn_fc1_bias"]).reshape(32, 128).T),
        "fc2q": np.asarray(inputs["L1_ffn_fc2_q"]).astype(np.uint8),
        "sc2x": sc2x,
        "b2x": np.ascontiguousarray(np.asarray(inputs["L1_ffn_fc2_bias"]).reshape(8, 128).T),
    }
    maps = []
    for c in range(NCORES):
        b, i = c // 4, c % 4
        m = dict(common)
        m["xT"] = np.ascontiguousarray(np.asarray(x[b, TPC * i:TPC * (i + 1)]).T)
        pos = pos_all[TPC * i:TPC * (i + 1)]
        f = pos[:, None] * inv[None, :]
        emb = np.concatenate([f, f], axis=1)
        m["cosT"] = np.ascontiguousarray(np.cos(emb).T.astype(np.float32))
        m["sinT"] = np.ascontiguousarray(np.sin(emb).T.astype(np.float32))
        maps.append(m)
    return maps


def kernel(**inputs):
    inputs = {k: np.asarray(v) for k, v in inputs.items()}
    P = build_fused()
    maps = stage_fused(inputs)
    res = run_bass_kernel_spmd(P.nc, maps, core_ids=list(range(NCORES)))
    P.es.close()
    out = np.empty((B, S, D), np.float32)
    for c in range(NCORES):
        b, i = c // 4, c % 4
        out[b, TPC * i:TPC * (i + 1), :] = res.results[c]["x4T"].T
    return out


# version 21 (final)
# speedup vs baseline: 1.1344x; 1.0013x over previous
import numpy as np
import ml_dtypes
from contextlib import ExitStack

import concourse.bass as bass
import concourse.mybir as mybir
from concourse.bass_utils import run_bass_kernel_spmd

F32 = mybir.dt.float32
BF16 = mybir.dt.bfloat16
I32 = mybir.dt.int32
U8 = mybir.dt.uint8
AF = mybir.ActivationFunctionType
ALU = mybir.AluOpType
AX = mybir.AxisListType
BFNP = ml_dtypes.bfloat16

NCORES = 8
B, S, D = 2, 8192, 1024
TPC = 2048
NTG = 4
TG = 512
NCH = 8
H, HD = 16, 64
DC, DR = 256, 128
EPS_LN = 1e-5
EPS_F32 = float(np.finfo(np.float32).eps)

ENGINES = ("sync", "gpsimd", "scalar", "vector", "tensor")
DBG_A = False
FDBG = False
USE_DRAINS = False


class Sem:
    __slots__ = ("h", "v")

    def __init__(self, h):
        self.h = h
        self.v = 0


class Prog:
    CHAINED = ("vector", "scalar")

    def __init__(self, name):
        self.nc = bass.Bass("TRN2", target_bir_lowering=False, name=name)
        self.es = ExitStack()
        self.scopes = [self.es]
        self.q = {e: [] for e in ENGINES}
        self.chain = {e: Sem(self.es.enter_context(self.nc.semaphore(e + "_chain")))
                      for e in self.CHAINED}
        self.last = {e: None for e in ENGINES}

    def push_scope(self):
        self.scopes.append(ExitStack())

    def pop_scope(self):
        self.scopes.pop().close()

    def sem(self, name):
        return Sem(self.es.enter_context(self.nc.semaphore(name)))

    def sb(self, name, shape, dt):
        return self.scopes[-1].enter_context(self.nc.sbuf_tensor(name, shape, dt))

    def ps(self, name, shape, dt=F32):
        return self.scopes[-1].enter_context(self.nc.psum_tensor(name, shape, dt))

    def o(self, eng, fn, wait=(), inc=None, incn=1):
        w = list(wait)
        if eng in self.CHAINED:
            if self.last[eng] is not None:
                w.append(self.last[eng])
            if inc is None:
                inc = self.chain[eng]
                incn = 1
        w = tuple((s.h, int(v)) for (s, v) in w)
        tgt = None
        if inc is not None:
            inc.v += incn
            tgt = (inc.h, incn)
            if eng in self.CHAINED:
                self.last[eng] = (inc, inc.v)

        drain = eng in self.CHAINED and USE_DRAINS

        def op(E, fn=fn, w=w, tgt=tgt, drain=drain):
            for h, v in w:
                E.wait_ge(h, v)
            r = fn(E)
            if tgt is not None:
                if drain:
                    E.drain().then_inc(*tgt)
                else:
                    if r is None:
                        r = E.nop()
                    r.then_inc(*tgt)
        self.q[eng].append(op)
        return inc.v if inc is not None else None

    def dma(self, eng, out_ap, in_ap, sem, wait=(), slow=False, serial=True):
        w = list(wait)
        if serial and sem.v > 0:
            w.append((sem, sem.v))
        return self.o(eng, lambda E, o=out_ap, i=in_ap, s=slow: E.dma_start(
            out=o, in_=i, allow_slow_non_contiguous=s),
            wait=w, inc=sem, incn=16)

    def build(self):
        with self.nc.Block() as blk:
            for e in ENGINES:
                ops = self.q[e]
                if ops:
                    getattr(blk, e)(lambda E, ops=ops: [op(E) for op in ops])
        return self.nc


class Pool:

    def __init__(self, tiles):
        self.t = tiles
        self.last = [None] * len(tiles)
        self.i = 0

    def acquire(self):
        slot = self.i % len(self.t)
        self.i += 1
        prev = self.last[slot]
        return slot, self.t[slot], ([prev] if prev is not None else [])

    def release(self, slot, sem, val):
        self.last[slot] = (sem, val)


def MM(out, lhsT, rhs, start, stop):
    return lambda E, o=out, l=lhsT, r=rhs, st=start, sp=stop: E.matmul(
        o, l, r, start=st, stop=sp, skip_group_check=True)


def _rsqrt_strip(P, ap, s_pre, v_pre, s_post):
    P.o("scalar", lambda E, a=ap: E.activation(out=a, in_=a, func=AF.Ln),
        wait=[(s_pre, v_pre)])
    P.o("scalar", lambda E, a=ap: E.activation(out=a, in_=a, func=AF.Exp, scale=-0.5),
        inc=s_post)


def emit_ln(P, pfx, pool, pbig, h_slice, ones_col_bf, ones_row_f, ones_row64,
            cw, scr_bf, scr_sq, st1, st2, st3, rstd_b, mst_b, src_dram=None,
            src_sb=None, xc=None, src_ready=(), nbuf=2, sems=None):
    if sems is None:
        sems = [P.sem(pfx + n) for n in ("cast", "sq", "mm", "str", "bc", "app", "x")]
    s_cast, s_sq, s_mm, s_str, s_bc, s_app, s_x = sems
    cast0, sq0, mm0, str0, app0 = s_cast.v, s_sq.v, s_mm.v, s_str.v, s_app.v

    def src_chunk(c, phase):
        if src_sb is not None:
            return src_sb[:, TPC * c:TPC * (c + 1)], list(src_ready)
        buf = c % 2
        ap = xc[:, TPC * buf:TPC * (buf + 1)]
        w = [(s_x, s_x.v)]
        if phase == 0:
            if c >= 2:
                w.append((s_cast, cast0 + c - 1))
        else:
            w.append((s_mm, mm0 + NCH))
            if c >= 2:
                w.append((s_app, app0 + 4 * (c - 1)))
        v = P.dma("sync", ap, src_dram[128 * c:128 * (c + 1), :], s_x, wait=w)
        return ap, [(s_x, v)]

    for c in range(NCH):
        src, waits = src_chunk(c, 0)
        buf = c % nbuf
        cb = scr_bf[:, TPC * buf:TPC * (buf + 1)]
        sq = scr_sq[:, TPC * buf:TPC * (buf + 1)]
        w = list(waits)
        if c >= nbuf:
            w.append((s_mm, mm0 + c - nbuf + 1))
        P.o("vector", lambda E, o=cb, i=src: E.tensor_copy(o, i), wait=w, inc=s_cast)
        P.o("scalar", lambda E, o=sq, i=cb: E.activation(out=o, in_=i, func=AF.Square),
            wait=[(s_cast, cast0 + c + 1)] +
            ([(s_mm, mm0 + c - nbuf + 1)] if c >= nbuf else []), inc=s_sq)
        for tg in range(NTG):
            first = (c == 0)
            last = (c == NCH - 1)
            P.o("tensor", MM(pbig[0:1, TG * tg:TG * (tg + 1)], ones_col_bf,
                             cb[:, TG * tg:TG * (tg + 1)], first, last),
                wait=([(s_sq, sq0 + c + 1)] + (list(cw) if c == 0 else []))
                if tg == 0 else [])
            P.o("tensor", MM(pbig[32:33, TG * tg:TG * (tg + 1)], ones_col_bf,
                             sq[:, TG * tg:TG * (tg + 1)], first, last),
                inc=(s_mm if tg == NTG - 1 else None))
    P.o("vector", lambda E: E.tensor_scalar_mul(st1[:, :], pbig[0:1, 0:TPC], 1.0 / D),
        wait=[(s_mm, s_mm.v)])
    P.o("vector", lambda E: E.tensor_scalar_mul(st2[:, :], pbig[32:33, 0:TPC], 1.0 / D),
        inc=s_str)
    P.o("scalar", lambda E: E.activation(out=st3, in_=st1[:, :], func=AF.Square),
        wait=[(s_str, str0 + 1)], inc=s_str)
    P.o("vector", lambda E: E.tensor_sub(st2[:, :], st2[:, :], st3),
        wait=[(s_str, str0 + 2)])
    P.o("vector", lambda E: E.tensor_scalar_add(st2[:, :], st2[:, :], EPS_LN),
        inc=s_str)
    _rsqrt_strip(P, st2[:, :], s_str, str0 + 3, s_str)
    P.o("vector", lambda E: E.tensor_mul(st1[:, :], st1[:, :], st2[:, :]),
        wait=[(s_str, str0 + 4)])
    P.o("vector", lambda E: E.tensor_scalar_mul(st1[:, :], st1[:, :], -1.0),
        inc=s_str)
    for tg in range(NTG):
        for si, (strip, orow, bdst) in enumerate(((st2, ones_row64, rstd_b),
                                                  (st1, ones_row_f, mst_b))):
            slot, pt, pw = pool.acquire()
            mmw = list(pw) + ([(s_str, str0 + 5)] if (tg == 0 and si == 0) else [])
            vmm = P.o("tensor", MM(pt[:, 0:TG], orow,
                                   strip[0:1, TG * tg:TG * (tg + 1)], True, True),
                      wait=mmw, inc=s_bc)
            P.o("scalar", lambda E, o=bdst[:, TG * tg:TG * (tg + 1)], p=pt:
                E.activation(out=o, in_=p[:, 0:TG], func=AF.Copy),
                wait=[(s_bc, vmm)], inc=s_bc)
            pool.release(slot, s_bc, s_bc.v)
    bcv = s_bc.v
    for c in range(NCH):
        src, waits = src_chunk(c, 1)
        for tg in range(NTG):
            xs = src[:, TG * tg:TG * (tg + 1)]
            tmp = pbig[:, TG * tg:TG * (tg + 1)]
            P.o("vector", lambda E, t=tmp, x=xs,
                r=rstd_b[:, TG * tg:TG * (tg + 1)]: E.tensor_mul(t, x, r),
                wait=[(s_bc, bcv)] + list(waits))
            hs = h_slice(c, tg)
            P.o("vector", lambda E, o=hs, t=tmp,
                m=mst_b[:, TG * tg:TG * (tg + 1)]: E.tensor_add(o, t, m),
                inc=s_app)
    return st2, st1, s_app


GROUPS = [[0, 1, 2, 3], [4, 5, 6, 7]]


def build_fused():
    P = Prog("mkfull")
    nc = P.nc

    def inp(name, shape, dt=F32):
        return nc.dram_tensor(name, shape, dt, kind="ExternalInput").ap()

    xT = inp("xT", [D, TPC])
    wq = inp("wq", [D, DC]); wqc = inp("wqc", [DC, 896]); wqr = inp("wqr", [D, DR])
    wkv = inp("wkv", [D, DC]); wkr = inp("wkr", [D, DR]); wkc = inp("wkc", [DC, 896])
    wv = inp("wv", [DC, D])
    bqx = inp("bqx", [128, 2]); bqcx = inp("bqcx", [128, 7]); bqrx = inp("bqrx", [128, 1])
    bkvx = inp("bkvx", [128, 2]); bkrx = inp("bkrx", [128, 1]); bkcx = inp("bkcx", [128, 7])
    bvx = inp("bvx", [1, D])
    cosT = inp("cosT", [DR, TPC]); sinT = inp("sinT", [DR, TPC])
    onesc = inp("onesc", [128, 1]); onesr = inp("onesr", [1, 128])
    tri = inp("tri", [128, 128], BF16); identb = inp("identb", [128, 128], BF16)
    identf = inp("identf", [128, 128])
    wo = inp("wo", [D, D]); box = inp("box", [128, 8])
    wrt = inp("wrt", [D, 128]); centT = inp("centT", [128, 16])
    tempx = inp("tempx", [1, 1])
    iexp = inp("iexp", [16, D, 512]); oexp = inp("oexp", [16, 512, 64])
    selc = inp("selc", [16, 8 * 128], BF16)
    wqkvT = inp("wqkvT", [D, 3 * D])
    wpT = inp("wpT", [D, D]); bpx = inp("bpx", [128, 8])
    gamx = inp("gamx", [128, 8]); betx = inp("betx", [128, 8])
    fc1q = inp("fc1q", [D, 4096], U8); sc1x = inp("sc1x", [32, D])
    b1x = inp("b1x", [128, 32])
    fc2q = inp("fc2q", [4096, D], U8); sc2x = inp("sc2x", [32, D])
    b2x = inp("b2x", [128, 8])

    x4T = nc.dram_tensor("x4T", [D, TPC], F32, kind="ExternalOutput").ap()
    if FDBG:
        dkq = nc.dram_tensor("dkq", [2 * D, TPC], BF16, kind="ExternalOutput").ap()
        dvi = nc.dram_tensor("dvi", [TPC, H * 65], BF16, kind="ExternalOutput").ap()
        dao = nc.dram_tensor("dao", [256, S], BF16, kind="ExternalOutput").ap()
        dq2 = nc.dram_tensor("dq2", [D, TPC], BF16, kind="ExternalOutput").ap()
        dctx = nc.dram_tensor("dctx", [65, H * 65], F32, kind="ExternalOutput").ap()
        dao2 = nc.dram_tensor("dao2", [256, S], BF16, kind="ExternalOutput").ap()
        dx3 = nc.dram_tensor("dx3", [D, TPC], F32, kind="ExternalOutput").ap()
        dx2 = nc.dram_tensor("dx2", [D, TPC], F32, kind="ExternalOutput").ap()

    k_in = nc.dram_tensor("k_in", [D, TPC], BF16).ap()
    q_in = nc.dram_tensor("q_in", [D, TPC], BF16).ap()
    vA_in = nc.dram_tensor("vA_in", [D, H * 65], BF16).ap()
    vB_in = nc.dram_tensor("vB_in", [D, H * 65], BF16).ap()
    kA_out = nc.dram_tensor("kA_out", [4 * D, TPC], BF16).ap()
    qB_out = nc.dram_tensor("qB_out", [4 * D, TPC], BF16).ap()
    vA_out = nc.dram_tensor("vA_out", [4 * D, H * 65], BF16).ap()
    vB_out = nc.dram_tensor("vB_out", [4 * D, H * 65], BF16).ap()
    ao_in = nc.dram_tensor("ao_in", [256, S], BF16).ap()
    ao_outA = nc.dram_tensor("ao_outA", [D // 2, S], BF16).ap()
    ao_outB = nc.dram_tensor("ao_outB", [D // 2, S], BF16).ap()
    q2_in = nc.dram_tensor("q2_in", [D, TPC], BF16).ap()
    q2_out = nc.dram_tensor("q2_out", [4 * D, TPC], BF16).ap()
    ctx_in = nc.dram_tensor("ctx_in", [65, H * 65], F32).ap()
    ctx_out = nc.dram_tensor("ctx_out", [65, H * 65], F32).ap()
    ao2_in = nc.dram_tensor("ao2_in", [256, S], BF16).ap()
    ao2_outA = nc.dram_tensor("ao2_outA", [D // 2, S], BF16).ap()
    ao2_outB = nc.dram_tensor("ao2_outB", [D // 2, S], BF16).ap()
    y1d = nc.dram_tensor("y1d", [4096, TPC], BF16).ap()
    kmy = nc.dram_tensor("kmy", [4 * 256, TPC], BF16).ap()
    qmy = nc.dram_tensor("qmy", [4 * 256, TPC], BF16).ap()
    vmy = nc.dram_tensor("vmy", [S, 260], BF16).ap()
    aomyA = nc.dram_tensor("aomyA", [D // 2, TPC], BF16).ap()
    aomyB = nc.dram_tensor("aomyB", [D // 2, TPC], BF16).ap()
    ctxmy = nc.dram_tensor("ctxmy", [65, 260], F32).ap()
    q2my = nc.dram_tensor("q2my", [4 * 256, TPC], BF16).ap()
    ao2myA = nc.dram_tensor("ao2myA", [D // 2, TPC], BF16).ap()
    ao2myB = nc.dram_tensor("ao2myB", [D // 2, TPC], BF16).ap()

    ones_col_bf = P.sb("ones_col_bf", [128, 1], BF16)
    ones_row_f = P.sb("ones_row_f", [1, 128], F32)
    ones_row64 = ones_row_f[:, :]
    idb = P.sb("idb", [128, 128], BF16)
    trib = P.sb("trib", [128, 128], BF16)
    bq_s = P.sb("bq_s", [128, 2], F32); bqc_s = P.sb("bqc_s", [128, 7], F32)
    bqr_s = P.sb("bqr_s", [128, 1], F32); bkv_s = P.sb("bkv_s", [128, 2], F32)
    bkr_s = P.sb("bkr_s", [128, 1], F32); bkc_s = P.sb("bkc_s", [128, 7], F32)
    bo_s = P.sb("bo_s", [128, 8], F32)
    cent_s = P.sb("cent_s", [128, 16], F32)
    selcs = P.sb("selcs", [16, 8 * 128], BF16)
    bp_s = P.sb("bp_s", [128, 8], F32)
    gam_s = P.sb("gam_s", [128, 8], F32); bet_s = P.sb("bet_s", [128, 8], F32)
    b1_s = P.sb("b1_s", [128, 32], F32)
    b2_s = P.sb("b2_s", [128, 8], F32); b2h_s = P.sb("b2h_s", [128, 8], F32)
    tstrip = P.sb("tstrip", [1, 16], F32)
    stp = P.sb("stp", [1, 2 * TPC], F32)
    st1 = stp[0:1, 0:TPC]
    st2 = stp[0:1, TPC:2 * TPC]
    rstd_b = P.sb("rstd_b", [128, TPC], F32)
    mst_b = P.sb("mst_b", [128, TPC], F32)

    pbig = P.ps("pbig", [128, 4 * TG], F32)
    pool = Pool([P.ps(f"pp{i}", [128, TG], F32) for i in range(4)])
    ln_sems = [P.sem("ln_" + n) for n in ("cast", "sq", "mm", "str", "bc", "app", "x")]

    s_c = P.sem("s_c")
    s_w = P.sem("s_w")
    s_cc = P.sem("s_cc")

    for dst, src in ((bq_s, bqx), (bqc_s, bqcx), (bqr_s, bqrx), (bkv_s, bkvx),
                     (bkr_s, bkrx), (bkc_s, bkcx),
                     (ones_row_f, onesr),
                     (idb, identb), (trib, tri), (bo_s, box), (cent_s, centT),
                     (selcs, selc), (bp_s, bpx), (gam_s, gamx), (bet_s, betx),
                     (b1_s, b1x), (b2_s, b2x)):
        P.dma("sync", dst[:, :], src[:, :], s_c, serial=False)
    P.dma("sync", tstrip[0:1, 0:1], tempx[:, :], s_c, serial=False)
    P.dma("gpsimd", ones_col_bf[:, :], onesc[:, :], s_w, serial=False)
    CW = [(s_c, s_c.v), (s_w, s_w.v)]

    dynv = {}

    def DV(E, key):
        eng = E.engine
        if eng not in dynv:
            pid = E.partition_id()
            hg = pid % 4
            dynv[eng] = {"hg": hg, "hg2": hg * 2}
        return dynv[eng][key]

    def dyn_dma(out_ap, mk_in, sem, wait, q="sync"):
        w = list(wait)
        if sem.v > 0:
            w.append((sem, sem.v))
        sem.v += 16
        tgt = (sem.h, 16)
        wt = tuple((x.h, int(v)) for (x, v) in w)

        def op(E, o=out_ap, mk=mk_in, wt=wt, tgt=tgt):
            for h, v in wt:
                E.wait_ge(h, v)
            E.dma_start(out=o, in_=mk(E)).then_inc(*tgt)
        P.q[q].append(op)
        return sem.v

    P.push_scope()
    xc = P.sb("a_xc", [128, 2 * TPC], F32)
    scr_bf = P.sb("a_scrbf", [128, 2 * TPC], BF16)
    scr_sq = P.sb("a_scrsq", [128, 2 * TPC], BF16)
    hbf = P.sb("a_h", [128, NCH * TPC], BF16)
    wqb = P.sb("a_wqb", [128, 8 * DC], BF16)
    wqcb = P.sb("a_wqcb", [128, 2 * 896], BF16)
    wqrb = P.sb("a_wqrb", [128, 8 * DR], BF16)
    wkvb = P.sb("a_wkvb", [128, 8 * DC], BF16)
    wkrb = P.sb("a_wkrb", [128, 8 * DR], BF16)
    wkcb = P.sb("a_wkcb", [128, 2 * 896], BF16)
    wvb = P.sb("a_wvb", [128, 2 * D], BF16)
    cos_s = P.sb("a_cos", [128, TPC], F32)
    sin_s = P.sb("a_sin", [128, TPC], F32)
    cqb = P.sb("a_cq", [128, 2 * TPC], BF16)
    ckvb = P.sb("a_ckv", [128, 2 * TPC], BF16)
    qch = P.sb("a_qch", [128, 2 * TPC], BF16)
    kch = P.sb("a_kch", [128, 2 * TPC], BF16)
    qrf = P.sb("a_qrf", [128, TPC], F32)
    krf = P.sb("a_krf", [128, TPC], F32)
    vat = P.sb("a_vat", [128, 2 * H * 65], BF16)
    bv_b = P.sb("a_bvb", [128, D], F32)
    bv_row = P.sb("a_bv_row", [1, D], F32)

    s_aw = P.sem("s_aw")
    P.dma("sync", bv_row[:, :], bvx[:, :], s_aw, serial=False)
    P.dma("gpsimd", wqb.rearrange("p (kc n) -> p kc n", n=DC),
          wq.rearrange("(kc p) n -> p kc n", p=128), s_aw, serial=False)
    P.dma("gpsimd", wqcb.rearrange("p (kc n) -> p kc n", n=896),
          wqc.rearrange("(kc p) n -> p kc n", p=128), s_aw, serial=False)
    P.dma("gpsimd", wqrb.rearrange("p (kc n) -> p kc n", n=DR),
          wqr.rearrange("(kc p) n -> p kc n", p=128), s_aw, serial=False)
    P.dma("gpsimd", wkvb.rearrange("p (kc n) -> p kc n", n=DC),
          wkv.rearrange("(kc p) n -> p kc n", p=128), s_aw, serial=False)
    P.dma("gpsimd", wkrb.rearrange("p (kc n) -> p kc n", n=DR),
          wkr.rearrange("(kc p) n -> p kc n", p=128), s_aw, serial=False)
    P.dma("gpsimd", wkcb.rearrange("p (kc n) -> p kc n", n=896),
          wkc.rearrange("(kc p) n -> p kc n", p=128), s_aw, serial=False)
    P.dma("gpsimd", wvb.rearrange("p (kc n) -> p kc n", n=D),
          wv.rearrange("(kc p) n -> p kc n", p=128), s_aw, serial=False)
    P.dma("sync", cos_s[:, :], cosT[:, :], s_aw, serial=False)
    P.dma("sync", sin_s[:, :], sinT[:, :], s_aw, serial=False)
    cw = CW + [(s_aw, s_aw.v)]

    def h_slice(c, tg):
        return hbf[:, TPC * c + TG * tg: TPC * c + TG * (tg + 1)]

    a_st, b_st, s_app = emit_ln(P, "aln", pool, pbig, h_slice,
                                ones_col_bf[:, :], ones_row_f[:, :],
                                ones_row64, cw,
                                scr_bf, scr_sq, st1, st2, pbig[64:65, 0:TPC],
                                rstd_b, mst_b, src_dram=xT, xc=xc, sems=ln_sems)
    HREADY = [(s_app, s_app.v)]

    s_pmm = P.sem("s_pmm")
    s_pev = P.sem("s_pev")

    def proj(n_oc, n_kc, lhsT_fn, rhs_fn, evac, first_wait):
        for oc in range(n_oc):
            for tg in range(NTG):
                slot, pt, pw = pool.acquire()
                w = list(pw) + (first_wait if (oc == 0 and tg == 0) else [])
                for kc in range(n_kc):
                    P.o("tensor", MM(pt[:, 0:TG], lhsT_fn(kc, oc), rhs_fn(kc, tg),
                                     kc == 0, kc == n_kc - 1),
                        wait=w if kc == 0 else [],
                        inc=(s_pmm if kc == n_kc - 1 else None))
                evac(oc, tg, pt, [(s_pmm, s_pmm.v)], s_pev)
                pool.release(slot, s_pev, s_pev.v)

    def mk_gelu_evac(dst, bias):
        def evac(oc, tg, pt, w, s_ev):
            o = dst[:, TPC * oc + TG * tg: TPC * oc + TG * (tg + 1)]
            P.o("scalar", lambda E, o=o, p=pt, b=bias[:, oc:oc + 1]: E.activation(
                out=o, in_=p[:, 0:TG], func=AF.Gelu, bias=b), wait=w, inc=s_ev)
        return evac

    proj(2, NCH,
         lambda kc, oc: wkvb[:, DC * kc + 128 * oc: DC * kc + 128 * (oc + 1)],
         lambda kc, tg: hbf[:, TPC * kc + TG * tg: TPC * kc + TG * (tg + 1)],
         mk_gelu_evac(ckvb, bkv_s), HREADY + cw)
    CKVR = [(s_pev, s_pev.v)]

    s_ag = P.sem("s_ag")
    s_ast = P.sem("s_ast")

    hc_sems = [P.sem("hc_" + n) for n in ("mm", "ev", "rmm", "rev", "rdma", "rope")]

    def emit_head_chunks(which, wsb, bsb, csrc, cready, kqd, rope_srcs,
                         rope_extra_wait=()):
        ch = qch if which == "q" else kch
        s_mm, s_ev = hc_sems[0], hc_sems[1]
        for oc in range(7):
            buf = oc % 2
            dst = ch[:, TPC * buf:TPC * (buf + 1)]
            for tg in range(NTG):
                slot, pt, pw = pool.acquire()
                w = list(pw) + (cready if (oc == 0 and tg == 0) else [])
                if tg == 0 and oc >= 2:
                    w.append((s_ast, s_ast.v - 16))
                for kc in range(2):
                    P.o("tensor", MM(pt[:, 0:TG],
                                     wsb[:, 896 * kc + 128 * oc: 896 * kc + 128 * (oc + 1)],
                                     csrc[:, TPC * kc + TG * tg: TPC * kc + TG * (tg + 1)],
                                     kc == 0, kc == 1),
                        wait=w if kc == 0 else [], inc=(s_mm if kc == 1 else None))
                P.o("vector", lambda E, o=dst[:, TG * tg:TG * (tg + 1)], p=pt,
                    b=bsb[:, oc:oc + 1]: E.tensor_single_scalar(o, p[:, 0:TG], b, ALU.add),
                    wait=[(s_mm, s_mm.v)], inc=s_ev)
                pool.release(slot, s_ev, s_ev.v)
            P.o("scalar", lambda E, a=dst: E.activation(out=a, in_=a, func=AF.Gelu),
                wait=[(s_ev, s_ev.v)], inc=s_ag)
            P.dma("sync", kqd[128 * oc:128 * (oc + 1), :], dst,
                  s_ast, wait=[(s_ag, s_ag.v)])
        wrb, brb, rf = rope_srcs
        s_rmm, s_rev = hc_sems[2], hc_sems[3]
        for tg in range(NTG):
            slot, pt, pw = pool.acquire()
            for kc in range(NCH):
                P.o("tensor", MM(pt[:, 0:TG], wrb[:, DR * kc:DR * (kc + 1)],
                                 hbf[:, TPC * kc + TG * tg: TPC * kc + TG * (tg + 1)],
                                 kc == 0, kc == NCH - 1),
                    wait=(list(pw) + HREADY) if kc == 0 else [],
                    inc=(s_rmm if kc == NCH - 1 else None))
            P.o("vector", lambda E, o=rf[:, TG * tg:TG * (tg + 1)], p=pt,
                b=brb[:, 0:1]: E.tensor_single_scalar(o, p[:, 0:TG], b, ALU.add),
                wait=[(s_rmm, s_rmm.v)], inc=s_rev)
            pool.release(slot, s_rev, s_rev.v)
        buf = 1
        dst = ch[:, TPC * buf:TPC * (buf + 1)]
        rot = xc[:, 0:TPC]
        tmp2 = xc[:, TPC:2 * TPC]
        s_rdma, s_rope = hc_sems[4], hc_sems[5]
        P.dma("sync", rot[0:64, :], rf[64:128, :], s_rdma,
              wait=[(s_rev, s_rev.v)] + list(rope_extra_wait))
        P.dma("sync", rot[64:128, :], rf[0:64, :], s_rdma)
        P.o("vector", lambda E: E.tensor_scalar_mul(rot[0:64, :], rot[0:64, :], -1.0),
            wait=[(s_rdma, s_rdma.v), (s_ast, s_ast.v - 16)])
        P.o("vector", lambda E: E.tensor_mul(tmp2[:, :], rot[:, :], sin_s[:, :]))
        P.o("vector", lambda E: E.tensor_mul(rot[:, :], rf[:, :], cos_s[:, :]))
        P.o("vector", lambda E, o=dst: E.tensor_add(o, rot[:, :], tmp2[:, :]),
            inc=s_rope)
        P.o("scalar", lambda E, a=dst: E.activation(out=a, in_=a, func=AF.Gelu),
            wait=[(s_rope, s_rope.v)], inc=s_ag)
        P.dma("sync", kqd[128 * 7:128 * 8, :], dst, s_ast,
              wait=[(s_ag, s_ag.v)])
        return s_rope

    s_vbc = P.sem("s_vbc")
    s_vbe = P.sem("s_vbe")
    for half in range(2):
        slot, pt, pw = pool.acquire()
        P.o("tensor", MM(pt[:, 0:TG], ones_row_f[:, :],
                         bv_row[0:1, TG * half:TG * (half + 1)], True, True),
            wait=list(pw) + cw, inc=s_vbc)
        P.o("vector", lambda E, o=bv_b[:, TG * half:TG * (half + 1)], p=pt:
            E.tensor_copy(o, p[:, 0:TG]), wait=[(s_vbc, s_vbc.v)], inc=s_vbe)
        pool.release(slot, s_vbe, s_vbe.v)
    s_vmm = P.sem("s_vmm")
    s_vev = P.sem("s_vev")
    NT = TPC // 128

    def emit_v_tiles(lo, hi):
        for tt in range(lo, hi):
            buf = tt % 2
            tile = vat[:, (H * 65) * buf:(H * 65) * (buf + 1)]
            t3 = tile.rearrange("p (h c) -> p h c", c=65)
            w = [(s_ast, s_ast.v - 16)] if tt >= 2 else []
            P.o("vector", lambda E, o=t3[:, :, 64:65]: E.memset(o, 1.0),
                wait=w + ([(s_vbe, s_vbe.v)] if tt == 0 else []))
            for vc in range(2):
                slot, pt, pw = pool.acquire()
                for kc in range(2):
                    P.o("tensor", MM(pt[:, 0:TG],
                                     ckvb[:, TPC * kc + 128 * tt: TPC * kc + 128 * (tt + 1)],
                                     wvb[:, D * kc + TG * vc: D * kc + TG * (vc + 1)],
                                     kc == 0, kc == 1),
                        wait=(list(pw) + CKVR) if kc == 0 else [],
                        inc=(s_vmm if kc == 1 else None))
                P.o("vector", lambda E, o=t3[:, 8 * vc:8 * (vc + 1), 0:64], p=pt,
                    b=bv_b[:, TG * vc:TG * (vc + 1)].rearrange("p (h c) -> p h c", c=64):
                    E.scalar_tensor_tensor(o, p[:, 0:TG].rearrange("p (h c) -> p h c", c=64),
                                           0.0, b, ALU.bypass, ALU.add),
                    wait=[(s_vmm, s_vmm.v)], inc=s_vev)
                pool.release(slot, s_vev, s_vev.v)
            vdst = vA_in if tt < NT // 2 else vB_in
            P.dma("sync", vdst[128 * (tt % (NT // 2)):128 * (tt % (NT // 2) + 1), :],
                  tile, s_ast, wait=[(s_vev, s_vev.v)])

    k_rope = emit_head_chunks("k", wkcb, bkc_s, ckvb, CKVR, k_in,
                              (wkrb, bkr_s, krf))
    AST_KO = s_ast.v
    P.o("gpsimd", lambda E: E.collective_compute(
        "AllGather", ALU.bypass, GROUPS, ins=[k_in[:, :]], outs=[kA_out[:, :]]),
        wait=[(s_ast, AST_KO)], inc=s_cc)
    CCK = s_cc.v
    emit_v_tiles(0, NT // 2)
    AST_VA = s_ast.v

    proj(2, NCH,
         lambda kc, oc: wqb[:, DC * kc + 128 * oc: DC * kc + 128 * (oc + 1)],
         lambda kc, tg: hbf[:, TPC * kc + TG * tg: TPC * kc + TG * (tg + 1)],
         mk_gelu_evac(cqb, bq_s), [])
    CQR = [(s_pev, s_pev.v)]
    emit_head_chunks("q", wqcb, bqc_s, cqb, CQR, q_in, (wqrb, bqr_s, qrf),
                     rope_extra_wait=[(k_rope, k_rope.v)])
    AST_QO = s_ast.v
    P.o("gpsimd", lambda E: E.collective_compute(
        "AllGather", ALU.bypass, GROUPS, ins=[q_in[:, :]], outs=[qB_out[:, :]]),
        wait=[(s_ast, AST_QO)], inc=s_cc)
    CCQ = s_cc.v
    P.o("gpsimd", lambda E: E.collective_compute(
        "AllGather", ALU.bypass, GROUPS, ins=[vA_in[:, :]], outs=[vA_out[:, :]]),
        wait=[(s_ast, AST_VA)], inc=s_cc)
    CC_VA = s_cc.v
    emit_v_tiles(NT // 2, NT)
    AST_ALL = s_ast.v
    P.pop_scope()
    P.o("gpsimd", lambda E: E.collective_compute(
        "AllGather", ALU.bypass, GROUPS, ins=[vB_in[:, :]], outs=[vB_out[:, :]]),
        wait=[(s_ast, AST_ALL)], inc=s_cc)
    CC_VB = s_cc.v

    P.push_scope()
    kp = P.sb("b_kp", [128, 2 * S], BF16)
    qp = P.sb("b_qp", [128, 2 * S], BF16)
    vh = P.sb("b_vh", [128, (S // 128) * 260], BF16)
    qsq = P.sb("b_qsq", [128, S], BF16)
    Pt = P.sb("b_Pt", [128, 4 * TG], BF16)
    aot = P.sb("b_aot", [64, 2 * TG], BF16)
    pvsbB = P.sb("b_pvsb", [65, 2 * TG], F32)
    rbB = P.sb("b_rb", [64, TG], F32)
    rstripB = P.sb("b_rstrip", [1, TG], F32)
    nstr = P.sb("b_nstr", [1, 16], F32)
    bias_col = P.sb("b_bias", [128, 1], F32)

    NKT = S // 128
    NQG = S // TG
    s_bl = P.sem("s_bl")
    kqA3 = kA_out.rearrange("(r t) c -> r t c", t=D)
    kqB3 = qB_out.rearrange("(r t) c -> r t c", t=D)
    vA3 = vA_out.rearrange("(r t) c -> r t c", t=D)
    vB3 = vB_out.rearrange("(r t) c -> r t c", t=D)
    vmy4 = vmy.rearrange("(g u) c -> g u c", u=TPC)
    dyn_dma(kmy.rearrange("(r t) c -> r t c", t=256),
            (lambda E: kqA3[:, bass.ts(DV(E, "hg"), 256), :][:, 0:256, :]),
            s_bl, [(s_cc, CCK)])
    for hp in range(2):
        for r in range(4):
            P.dma("sync", kp[:, S * hp + TPC * r:S * hp + TPC * (r + 1)],
                  kmy[256 * r + 128 * hp:256 * r + 128 * (hp + 1), :], s_bl,
                  wait=([(s_ast, AST_ALL)] if (hp == 0 and r == 0) else []))
    LWK = [(s_bl, s_bl.v)]
    dyn_dma(qmy.rearrange("(r t) c -> r t c", t=256),
            (lambda E: kqB3[:, bass.ts(DV(E, "hg"), 256), :][:, 0:256, :]),
            s_bl, [(s_cc, CCQ)])
    for hp in range(2):
        for r in range(4):
            P.dma("sync", qp[:, S * hp + TPC * r:S * hp + TPC * (r + 1)],
                  qmy[256 * r + 128 * hp:256 * r + 128 * (hp + 1), :], s_bl)
    QPL = s_bl.v
    dyn_dma(vmy4[:, 0:TPC // 2, :],
            (lambda E: vA3[:, :, bass.ts(DV(E, "hg"), 260)]), s_bl,
            [(s_cc, CC_VA)])
    dyn_dma(vmy4[:, TPC // 2:TPC, :],
            (lambda E: vB3[:, :, bass.ts(DV(E, "hg"), 260)]), s_bl,
            [(s_cc, CC_VB)])
    P.dma("sync", vh.rearrange("p (t c) -> p t c", c=260),
          vmy.rearrange("(t p) c -> p t c", p=128), s_bl)
    LWB = [(s_bl, QPL)]
    VHR = [(s_bl, s_bl.v)]

    s_nrm = P.sem("s_nrm")
    s_nmm = P.sem("s_nmm")
    s_sqs = P.sem("s_sqs")
    s_bias = P.sem("s_bias")
    s_smm = P.sem("s_smm")
    s_exp = P.sem("s_exp")
    s_pvm = P.sem("s_pvm")
    s_pvc = P.sem("s_pvc")
    s_v1 = P.sem("s_v1")
    s_bmm = P.sem("s_bmm")
    s_aoev = P.sem("s_aoev")
    s_ast2 = P.sem("s_ast2")

    slot_pv_, ppvA, pw_pv = pool.acquire()
    slot_pv2_, ppvB, pw_pv2 = pool.acquire()
    slot_pb_, pb1, pw_pb = pool.acquire()
    PV_FREE = pw_pv + pw_pv2 + pw_pb

    slot_exp = [None] * 4
    slot_pv = [None] * 4
    first_scores_extra = list(PV_FREE)
    nstore = 0

    for hp in range(2):
        kslab = kp[:, S * hp:S * (hp + 1)]
        qslab = qp[:, S * hp:S * (hp + 1)]
        for which, slab, acc in (("k", kslab, 4), ("q", qslab, 0)):
            P.o("scalar", lambda E, i=slab[:, :]: E.activation(
                out=qsq[:, :], in_=i, func=AF.Square),
                wait=(LWK if which == "k" else LWB) + [(s_nrm, s_nrm.v)],
                inc=s_sqs)
            for r in range(4):
                for j in range(4):
                    cg = 4 * r + j
                    w = [(s_sqs, s_sqs.v), (s_nrm, s_nrm.v)] if j == 0 else []
                    if r == 0 and j == 0:
                        for si in range(4):
                            if slot_exp[si] is not None:
                                w.append(slot_exp[si])
                    P.o("tensor", MM(pbig[0:1, TG * j:TG * (j + 1)],
                                     ones_col_bf[:, 0:1],
                                     qsq[:, TG * cg:TG * (cg + 1)], True, True),
                        wait=w, inc=(s_nmm if j == 3 else None))
                P.o("vector", lambda E, o=nstr[0:1, acc + r:acc + r + 1]:
                    E.tensor_reduce(o, pbig[0:1, 0:4 * TG], AX.X, ALU.max),
                    wait=[(s_nmm, s_nmm.v)], inc=s_nrm)
            P.o("vector", lambda E, o=nstr[0:1, 8 + (0 if which == "q" else 1):
                                           9 + (0 if which == "q" else 1)],
                i=nstr[0:1, acc:acc + 4]: E.tensor_reduce(o, i, AX.X, ALU.max),
                wait=[(s_nrm, s_nrm.v)], inc=s_nrm)
        P.o("vector", lambda E: E.tensor_mul(nstr[0:1, 10:11], nstr[0:1, 8:9],
                                             nstr[0:1, 9:10]),
            wait=[(s_nrm, s_nrm.v)], inc=s_nrm)
        P.o("scalar", lambda E: E.activation(out=nstr[0:1, 10:11], in_=nstr[0:1, 10:11],
                                             func=AF.Ln), wait=[(s_nrm, s_nrm.v)])
        P.o("scalar", lambda E: E.activation(out=nstr[0:1, 10:11], in_=nstr[0:1, 10:11],
                                             func=AF.Exp, scale=0.5), inc=s_sqs)
        P.o("vector", lambda E: E.tensor_scalar_mul(nstr[0:1, 10:11], nstr[0:1, 10:11],
                                                    -0.125),
            wait=[(s_sqs, s_sqs.v)], inc=s_nrm)
        P.o("tensor", MM(pb1[:, 0:1], ones_row_f[:, :], nstr[0:1, 10:11], True, True),
            wait=[(s_nrm, s_nrm.v), (s_aoev, s_aoev.v)] +
            (list(PV_FREE) if hp == 0 else []), inc=s_bmm)
        P.o("vector", lambda E: E.tensor_copy(bias_col[:, :], pb1[:, 0:1]),
            wait=[(s_bmm, s_bmm.v), (s_exp, s_exp.v)], inc=s_bias)

        for g in range(NQG):
            kts = 4 * g + 4

            def emit_scores(kt):
                sa = 2 * (kt % 2)
                diag = kt >= 4 * g
                j = kt - 4 * g if diag else 0
                for ih in range(2):
                    slot = sa + ih
                    w = []
                    if slot_exp[slot] is not None:
                        w.append(slot_exp[slot])
                        slot_exp[slot] = None
                    if kt == 0 and ih == 0:
                        w += [(s_bias, s_bias.v)]
                        if hp == 0 and g == 0:
                            w += first_scores_extra
                    P.o("tensor", MM(pbig[:, TG * slot:TG * (slot + 1)],
                                     kslab[64 * ih:64 * (ih + 1),
                                           128 * kt:128 * (kt + 1)],
                                     qslab[64 * ih:64 * (ih + 1),
                                           TG * g:TG * (g + 1)],
                                     True, not diag),
                        wait=w, inc=(None if diag else (s_smm if ih == 1 else None)))
                if diag:
                    for ih in range(2):
                        slot = sa + ih
                        P.o("tensor", MM(pbig[:, TG * slot + 128 * j:
                                              TG * slot + 128 * (j + 1)],
                                         idb[:, :], trib[:, :], False, True),
                            inc=(s_smm if ih == 1 else None))

            def emit_exp(kt):
                sa = 2 * (kt % 2)
                w = [(s_smm, s_smm.v), (s_bias, s_bias.v)]
                for ih in range(2):
                    if slot_pv[sa + ih] is not None:
                        w.append(slot_pv[sa + ih])
                        slot_pv[sa + ih] = None
                P.o("scalar", lambda E, o=Pt[:, TG * sa:TG * (sa + 2)],
                    i=pbig[:, TG * sa:TG * (sa + 2)]: E.activation(
                        out=o, in_=i, func=AF.Exp, scale=0.125, bias=bias_col[:, 0:1]),
                    wait=w, inc=s_exp)
                for ih in range(2):
                    slot_exp[sa + ih] = (s_exp, s_exp.v)

            def emit_pv(kt):
                sa = 2 * (kt % 2)
                ev = (s_exp, s_exp.v)
                diag = kt >= 4 * g
                j = kt - 4 * g if diag else 0
                jm = 128 * j
                for ih, pv in ((0, ppvA), (1, ppvB)):
                    hl = 2 * hp + ih
                    slot = sa + ih
                    w = []
                    if ih == 0:
                        w.append(ev)
                    if kt == 0:
                        w.append((s_pvc, s_pvc.v))
                        if hp == 0 and g == 0:
                            w += VHR
                    P.o("tensor", MM(pv[0:65, jm:TG],
                                     vh[:, 260 * kt + 65 * hl:260 * kt + 65 * (hl + 1)],
                                     Pt[:, TG * slot + jm:TG * (slot + 1)],
                                     kt == 0, kt == kts - 1),
                        wait=w, inc=(s_pvm if ih == 1 else None))
                for ih in range(2):
                    slot_pv[sa + ih] = (s_pvm, s_pvm.v)

            emit_scores(0)
            emit_exp(0)
            for kt in range(kts):
                if kt + 1 < kts:
                    emit_scores(kt + 1)
                emit_pv(kt)
                if kt + 1 < kts:
                    emit_exp(kt + 1)

            pvcv = {}
            for ih, pv in ((0, ppvA), (1, ppvB)):
                P.o("scalar", lambda E, o=pvsbB[0:64, TG * ih:TG * (ih + 1)], p=pv:
                    E.activation(out=o, in_=p[0:64, 0:TG], func=AF.Copy),
                    wait=[(s_pvm, s_pvm.v)] +
                    ([(s_aoev, s_aoev.v)] if ih == 0 else []))
                pvcv[ih] = P.o("scalar", lambda E,
                               o=pvsbB[64:65, TG * ih:TG * (ih + 1)], p=pv:
                               E.activation(out=o, in_=p[64:65, 0:TG],
                                            func=AF.Copy), inc=s_pvc)
            for ih in range(2):
                hl = 2 * hp + ih
                pvs = pvsbB[:, TG * ih:TG * (ih + 1)]
                P.o("vector", lambda E, p=pvs: E.reciprocal(rstripB[:, :],
                                                            p[64:65, :]),
                    wait=[(s_bmm, s_bmm.v), (s_pvc, pvcv[ih])], inc=s_v1)
                P.o("tensor", MM(pb1[0:64, 0:TG], ones_row_f[0:1, 0:64],
                                 rstripB[:, :], True, True),
                    wait=[(s_v1, s_v1.v), (s_bias, s_bias.v)], inc=s_bmm)
                P.o("vector", lambda E: E.tensor_copy(rbB[:, :], pb1[0:64, 0:TG]),
                    wait=[(s_bmm, s_bmm.v)])
                slot_o = aot[:, TG * (nstore % 2):TG * (nstore % 2 + 1)]
                w2 = [(s_ast2, s_ast2.v - 16)] if nstore >= 2 else []
                P.o("vector", lambda E, o=slot_o, p=pvs: E.tensor_mul(
                    o, p[0:64, :], rbB[:, :]), wait=w2, inc=s_aoev)
                P.dma("sync", ao_in[64 * hl:64 * (hl + 1), TG * g:TG * (g + 1)],
                      slot_o, s_ast2, wait=[(s_aoev, s_aoev.v)])
                nstore += 1
        if hp == 0:
            AST2_H0 = s_ast2.v
    pool.release(slot_pv_, s_aoev, s_aoev.v)
    pool.release(slot_pv2_, s_aoev, s_aoev.v)
    pool.release(slot_pb_, s_aoev, s_aoev.v)
    PBIG_LAST_B = [(s_exp, s_exp.v)]
    AST2_ALL = s_ast2.v
    P.pop_scope()

    P.o("gpsimd", lambda E: E.collective_compute(
        "AllGather", ALU.bypass, GROUPS, ins=[ao_in[0:128, :]],
        outs=[ao_outA[:, :]]), wait=[(s_ast2, AST2_H0)], inc=s_cc)
    CC3A = s_cc.v
    P.o("gpsimd", lambda E: E.collective_compute(
        "AllGather", ALU.bypass, GROUPS, ins=[ao_in[128:256, :]],
        outs=[ao_outB[:, :]]), wait=[(s_ast2, AST2_ALL)], inc=s_cc)
    CC3 = s_cc.v

    P.push_scope()
    x1sb = P.sb("x1sb", [128, NCH * TPC], F32)
    scrC = P.sb("c_scr", [128, TPC], BF16)
    scrC2 = P.sb("c_scr2", [128, TPC], BF16)
    P.push_scope()
    arena = P.sb("c_arena", [128, 16384], BF16)
    aob = arena[:, 0:16384]
    h2b = P.sb("c_h2", [128, NCH * TPC], BF16)
    gfm = P.sb("c_gfm", [16, TPC], BF16)
    idf = P.sb("idf", [128, 128], F32)
    gbs = P.sb("c_gbs", [128, TPC], BF16)
    oexb = P.sb("c_oexb", [128, 2 * 512], BF16)
    wrtb = P.sb("c_wrtb", [128, 8 * 128], BF16)
    csqb = P.sb("c_csqb", [128, 16], BF16)
    cnb = P.sb("c_cnb", [128, 16], BF16)
    gcol = P.sb("c_gcol", [128, 8], F32)
    gw = P.sb("c_gw", [128, 32], F32)
    tmpC = P.sb("c_tmp", [128, TG], F32)

    s_cl = P.sem("s_cl")
    s_x1 = P.sem("s_x1")
    s_wmm = P.sem("s_wmm")
    s_wol = P.sem("s_wol")
    P.dma("gpsimd", wrtb.rearrange("p (c n) -> p c n", n=128),
          wrt.rearrange("(c p) n -> p c n", p=128), s_cl, serial=False,
          wait=[(s_ast2, AST2_ALL)])
    s_idf = P.sem("s_idf")
    P.dma("gpsimd", idf[:, :], identf[:, :], s_idf, serial=False)
    dyn_dma(aomyA[:, :], (lambda E: ao_outA[:, bass.ts(DV(E, "hg"), TPC)]),
            s_cl, [(s_cc, CC3A)])
    for c in range(NCH // 2):
        P.dma("sync", aob[:, TPC * c:TPC * (c + 1)], aomyA[128 * c:128 * (c + 1), :],
              s_cl)
    dyn_dma(aomyB[:, :], (lambda E: ao_outB[:, bass.ts(DV(E, "hg"), TPC)]),
            s_cl, [(s_cc, CC3)])
    for c in range(NCH // 2):
        P.dma("sync", aob[:, TPC * (c + 4):TPC * (c + 5)],
              aomyB[128 * c:128 * (c + 1), :], s_cl)
    AOL = [(s_cl, s_cl.v)]

    P.push_scope()
    wos = P.sb("c_wos", [128, 2 * 1024], BF16)
    zf = P.sb("c_zf", [128, TPC], F32)
    for oc in range(NCH):
        wbuf = wos[:, 1024 * (oc % 2):1024 * (oc % 2 + 1)]
        wlv = P.dma("gpsimd", wbuf.rearrange("p (kc n) -> p kc n", n=128),
                    wo.rearrange("(kc p) n -> p kc n", p=128)
                    [:, :, 128 * oc:128 * (oc + 1)], s_wol,
                    wait=[(s_wmm, 4 * (oc - 1))] if oc >= 2
                    else [(s_ast2, AST2_ALL)])
        for tg in range(NTG):
            xsl = x1sb[:, TPC * oc + TG * tg:TPC * oc + TG * (tg + 1)]
            lv = P.dma("sync", xsl, xT[128 * oc:128 * (oc + 1), TG * tg:TG * (tg + 1)],
                       s_x1)
            slot, pt, pw = pool.acquire()
            for kc in range(NCH):
                P.o("tensor", MM(pt[:, 0:TG],
                                 wbuf[:, 128 * kc:128 * (kc + 1)],
                                 aob[:, TPC * kc + TG * tg:TPC * kc + TG * (tg + 1)],
                                 kc == 0, kc == NCH - 1),
                    wait=(list(pw) + CW + AOL + PBIG_LAST_B + [(s_wol, wlv)])
                    if kc == 0 else [],
                    inc=(s_wmm if kc == NCH - 1 else None))
            P.o("vector", lambda E, o=xsl, p=pt, b=bo_s[:, oc:oc + 1]:
                E.scalar_tensor_tensor(o, p[:, 0:TG], b, o, ALU.add, ALU.add),
                wait=[(s_wmm, s_wmm.v), (s_x1, lv)], inc=s_x1)
            pool.release(slot, s_x1, s_x1.v)
    X1R = [(s_x1, s_x1.v)]
    WMMF = s_wmm.v

    def h2_slice(c, tg):
        return h2b[:, TPC * c + TG * tg:TPC * c + TG * (tg + 1)]
    st3p = pbig[64:65, 0:TPC]
    _, _, s_app2 = emit_ln(P, "cln", pool, pbig, h2_slice,
                           ones_col_bf[:, :], ones_row_f[:, :], ones_row64,
                           CW + PBIG_LAST_B,
                           scrC, scrC2, st1, st2, st3p, rstd_b, mst_b,
                           src_sb=x1sb, src_ready=X1R, nbuf=1, sems=ln_sems)
    H2R = [(s_app2, s_app2.v)]

    s_zmm = P.sem("s_zmm")
    s_zev = P.sem("s_zev")
    s_zsq = P.sem("s_zsq")
    s_nmmC = P.sem("s_nmmC")
    s_strC = P.sem("s_strC")
    s_tmp = P.sem("s_tmp")
    s_zsc = P.sem("s_zsc")
    s_cn = P.sem("s_cn")
    for tg in range(NTG):
        slot, pt, pw = pool.acquire()
        for kc in range(NCH):
            P.o("tensor", MM(pt[:, 0:TG], wrtb[:, 128 * kc:128 * (kc + 1)],
                             h2b[:, TPC * kc + TG * tg:TPC * kc + TG * (tg + 1)],
                             kc == 0, kc == NCH - 1),
                wait=(list(pw) + H2R) if kc == 0 else [],
                inc=(s_zmm if kc == NCH - 1 else None))
        P.o("vector", lambda E, o=zf[:, TG * tg:TG * (tg + 1)], p=pt:
            E.tensor_copy(o, p[:, 0:TG]), wait=[(s_zmm, s_zmm.v)], inc=s_zev)
        pool.release(slot, s_zev, s_zev.v)
    P.o("scalar", lambda E: E.activation(out=scrC2[:, :], in_=zf[:, :], func=AF.Square),
        wait=[(s_zev, s_zev.v)], inc=s_zsq)
    for tg in range(NTG):
        P.o("tensor", MM(pbig[0:1, TG * tg:TG * (tg + 1)], ones_col_bf[:, :],
                         scrC2[:, TG * tg:TG * (tg + 1)], True, True),
            wait=[(s_zsq, s_zsq.v)] if tg == 0 else [],
            inc=(s_nmmC if tg == NTG - 1 else None))
    P.o("vector", lambda E: E.tensor_copy(st1[:, :], pbig[0:1, 0:TPC]),
        wait=[(s_nmmC, s_nmmC.v)], inc=s_strC)
    P.o("scalar", lambda E: E.activation(out=st1[:, :], in_=st1[:, :], func=AF.Ln),
        wait=[(s_strC, 1)])
    P.o("scalar", lambda E: E.activation(out=st1[:, :], in_=st1[:, :], func=AF.Exp,
                                         scale=0.5), inc=s_strC)
    P.o("vector", lambda E: E.tensor_scalar_max(st1[:, :], st1[:, :], 1e-12),
        wait=[(s_strC, 2)], inc=s_strC)
    P.o("vector", lambda E: E.reciprocal(st1[:, :], st1[:, :]),
        wait=[(s_strC, 3)], inc=s_strC)
    P.o("scalar", lambda E: E.activation(out=tstrip[0:1, 0:1], in_=tstrip[0:1, 0:1],
                                         func=AF.Exp), wait=CW, inc=s_tmp)
    P.o("vector", lambda E: E.tensor_scalar_mul(st1[:, :], st1[:, :],
                                                float(1.0 / np.sqrt(128.0))),
        wait=[(s_strC, 4), (s_tmp, 1)])
    P.o("vector", lambda E: E.tensor_single_scalar(st1[:, :], st1[:, :],
                                                   tstrip[0:1, 0:1], ALU.mult),
        inc=s_strC)
    for tg in range(NTG):
        slot, pt, pw = pool.acquire()
        P.o("tensor", MM(pt[:, 0:TG], ones_row_f[:, :], st1[0:1, TG * tg:TG * (tg + 1)],
                         True, True), wait=list(pw) + [(s_strC, 5)], inc=s_zmm)
        P.o("vector", lambda E, o=scrC[:, TG * tg:TG * (tg + 1)],
            z=zf[:, TG * tg:TG * (tg + 1)], p=pt: E.tensor_mul(o, z, p[:, 0:TG]),
            wait=[(s_zmm, s_zmm.v)], inc=s_zsc)
        pool.release(slot, s_zsc, s_zsc.v)
    ZSCF = s_zsc.v
    P.pop_scope()
    hidb = P.sb("c_hid", [128, 4 * TPC], BF16)
    P.o("scalar", lambda E: E.activation(out=csqb[:, :], in_=cent_s[:, :],
                                         func=AF.Square), wait=CW, inc=s_zsq)
    slot, pt, pw = pool.acquire()
    P.o("tensor", MM(pt[0:1, 0:16], ones_col_bf[:, :], csqb[:, :], True, True),
        wait=list(pw) + [(s_zsq, s_zsq.v)], inc=s_nmmC)
    P.o("vector", lambda E, p=pt: E.tensor_copy(gw[0:1, 0:16], p[0:1, 0:16]),
        wait=[(s_nmmC, s_nmmC.v)], inc=s_cn)
    P.o("scalar", lambda E: E.activation(out=gw[0:1, 0:16], in_=gw[0:1, 0:16],
                                         func=AF.Ln), wait=[(s_cn, s_cn.v)])
    P.o("scalar", lambda E: E.activation(out=gw[0:1, 0:16], in_=gw[0:1, 0:16],
                                         func=AF.Exp, scale=0.5), inc=s_zsq)
    P.o("vector", lambda E: E.tensor_scalar_max(gw[0:1, 0:16], gw[0:1, 0:16], 1e-12),
        wait=[(s_zsq, s_zsq.v)], inc=s_cn)
    P.o("vector", lambda E: E.reciprocal(gw[0:1, 0:16], gw[0:1, 0:16]),
        wait=[(s_cn, s_cn.v)], inc=s_cn)
    pool.release(slot, s_cn, s_cn.v)
    slot, pt, pw = pool.acquire()
    P.o("tensor", MM(pt[:, 0:16], ones_row_f[:, :], gw[0:1, 0:16], True, True),
        wait=list(pw) + [(s_cn, s_cn.v)], inc=s_nmmC)
    P.o("vector", lambda E, p=pt: E.tensor_mul(cnb[:, :], cent_s[:, :], p[:, 0:16]),
        wait=[(s_nmmC, s_nmmC.v)], inc=s_cn)
    pool.release(slot, s_cn, s_cn.v)
    CNR = (s_cn, s_cn.v)

    s_lg = P.sem("s_lg")
    s_g1 = P.sem("s_g1")
    s_g2 = P.sem("s_g2")
    s_g3 = P.sem("s_g3")
    s_trC = P.sem("s_trC")
    s_gf = P.sem("s_gf")
    for tt in range(TPC // 128):
        slot, pt, pw = pool.acquire()
        P.o("tensor", MM(pt[:, 0:16], scrC[:, 128 * tt:128 * (tt + 1)], cnb[:, :],
                         True, True),
            wait=list(pw) + ([CNR, (s_zsc, s_zsc.v)] if tt == 0 else []), inc=s_lg)
        P.o("vector", lambda E, p=pt: E.tensor_reduce(gcol[:, 0:1], p[:, 0:16], AX.X,
                                                      ALU.max),
            wait=[(s_lg, s_lg.v), (s_g3, s_g3.v)], inc=s_g1)
        P.o("vector", lambda E: E.tensor_scalar_mul(gcol[:, 1:2], gcol[:, 0:1], -1.0),
            wait=[(s_g1, s_g1.v)], inc=s_g1)
        P.o("scalar", lambda E, p=pt: E.activation(out=gw[:, 0:16], in_=p[:, 0:16],
                                                   func=AF.Exp, bias=gcol[:, 1:2]),
            wait=[(s_g1, s_g1.v)], inc=s_g2)
        pool.release(slot, s_g2, s_g2.v)
        P.o("vector", lambda E: E.tensor_reduce(gcol[:, 2:3], gw[:, 0:16], AX.X,
                                                ALU.add),
            wait=[(s_g2, s_g2.v)], inc=s_g1)
        P.o("vector", lambda E: E.reciprocal(gcol[:, 3:4], gcol[:, 2:3]),
            wait=[(s_g1, s_g1.v)], inc=s_g1)
        P.o("vector", lambda E: E.tensor_reduce(gcol[:, 4:5], gw[:, 0:16], AX.X,
                                                ALU.max),
            wait=[(s_g1, s_g1.v)], inc=s_g1)
        P.o("vector", lambda E: E.tensor_single_scalar(gw[:, 16:32], gw[:, 0:16],
                                                       gcol[:, 4:5], ALU.is_equal),
            wait=[(s_g1, s_g1.v)], inc=s_g1)
        P.o("vector", lambda E: E.scalar_tensor_tensor(gw[:, 16:32], gw[:, 16:32],
                                                       -2.0, gw[:, 0:16], ALU.mult,
                                                       ALU.add),
            wait=[(s_g1, s_g1.v)], inc=s_g1)
        P.o("vector", lambda E: E.tensor_reduce(gcol[:, 5:6], gw[:, 16:32], AX.X,
                                                ALU.max), wait=[(s_g1, s_g1.v)],
            inc=s_g1)
        P.o("vector", lambda E: E.tensor_single_scalar(gw[:, 16:32], gw[:, 0:16],
                                                       gcol[:, 5:6], ALU.is_ge),
            wait=[(s_g1, s_g1.v)], inc=s_g1)
        P.o("vector", lambda E: E.tensor_mul(gw[:, 16:32], gw[:, 16:32], gw[:, 0:16]),
            wait=[(s_g1, s_g1.v)], inc=s_g1)
        P.o("vector", lambda E: E.tensor_single_scalar(gw[:, 16:32], gw[:, 16:32],
                                                       gcol[:, 3:4], ALU.mult),
            wait=[(s_g1, s_g1.v)], inc=s_g3)
        slot2, pt2, pw2 = pool.acquire()
        P.o("tensor", lambda E, o=pt2[0:16, 0:128]: E.transpose(o, gw[:, 16:32],
                                                                idf[:, :]),
            wait=list(pw2) + [(s_g3, s_g3.v), (s_idf, 16)], inc=s_trC)
        P.o("vector", lambda E, o=gfm[:, 128 * tt:128 * (tt + 1)], p=pt2:
            E.tensor_copy(o, p[0:16, 0:128]), wait=[(s_trC, s_trC.v)], inc=s_gf)
        pool.release(slot2, s_gf, s_gf.v)
    GFR = [(s_gf, s_gf.v)]

    s_ie = P.sem("s_ie")
    s_oe = P.sem("s_oe")
    s_gbe = P.sem("s_gbe")
    s_hmm = P.sem("s_hmm")
    s_hev = P.sem("s_hev")
    s_omm = P.sem("s_omm")
    s_moev = P.sem("s_moev")
    gelu_val = {}
    for c in range(8):
        buf = c % 2
        iex = [arena[:, 8192 * buf + 4096 * le:8192 * buf + 4096 * (le + 1)]
               for le in range(2)]
        oex = oexb[:, 512 * buf:512 * (buf + 1)]
        wld = ([(s_hmm, 32 * (c - 1))] if c >= 2 else []) + \
            ([(s_wmm, WMMF)] if c < 2 else [])
        iev = []
        for le in range(2):
            iev.append(P.dma("gpsimd", iex[le].rearrange("p (kc n) -> p kc n", n=512),
                             iexp[2 * c + le].rearrange("(kc p) n -> p kc n", p=128),
                             s_ie, wait=wld if le == 0 else []))
        ow = [(s_omm, 8 * (c - 1))] if c >= 2 else [(s_wmm, WMMF)]
        oev = P.dma("gpsimd", oex.rearrange("p (e hc n) -> p e hc n", hc=4, n=64),
                    oexp.rearrange("e (hc p) n -> p e hc n", p=128)
                    [:, 2 * c:2 * (c + 1), :, :], s_oe, wait=ow)
        for tg in range(NTG):
            slot, pt, pw = pool.acquire()
            P.o("tensor", MM(pt[:, 0:TG], selcs[:, 128 * c:128 * (c + 1)],
                             gfm[:, TG * tg:TG * (tg + 1)], True, True),
                wait=list(pw) + (GFR if (c == 0 and tg == 0) else []), inc=s_zmm)
            P.o("vector", lambda E, o=gbs[:, TG * tg:TG * (tg + 1)], p=pt:
                E.tensor_copy(o, p[:, 0:TG]),
                wait=[(s_zmm, s_zmm.v)] + ([(s_moev, 4 * c)] if c >= 1 and tg == 0
                                           else []), inc=s_gbe)
            pool.release(slot, s_gbe, s_gbe.v)
        gbv = s_gbe.v
        for le in range(2):
            E_idx = 2 * c + le
            for hc in range(4):
                for tg in range(NTG):
                    slot, pt, pw = pool.acquire()
                    w = list(pw)
                    if hc == 0 and tg == 0:
                        w += [(s_ie, iev[le])] + (H2R if c == 0 and le == 0 else [])
                    for kc in range(NCH):
                        P.o("tensor", MM(pt[:, 0:TG],
                                         iex[le][:, 512 * kc + 128 * hc:
                                                 512 * kc + 128 * (hc + 1)],
                                         h2b[:, TPC * kc + TG * tg:
                                             TPC * kc + TG * (tg + 1)],
                                         kc == 0, kc == NCH - 1),
                            wait=w if kc == 0 else [],
                            inc=(s_hmm if kc == NCH - 1 else None))
                    hw = [(s_hmm, s_hmm.v)]
                    if hc == 0 and tg == 0:
                        hw.append((s_omm, 4 * E_idx))
                        if c == 0 and le == 0:
                            hw.append((s_zsc, ZSCF))
                    P.o("scalar", lambda E, o=hidb[:, TPC * hc + TG * tg:
                                                   TPC * hc + TG * (tg + 1)], p=pt:
                        E.activation(out=o, in_=p[:, 0:TG], func=AF.Gelu),
                        wait=hw, inc=s_hev)
                    gelu_val[(hc, tg)] = s_hev.v
                    pool.release(slot, s_hev, s_hev.v)
            for tg in range(NTG):
                for hc in range(4):
                    w = [(s_hev, gelu_val[(hc, tg)])]
                    if hc == 0:
                        w.append((s_oe, oev))
                        if le == 0 and c >= 1:
                            w.append((s_moev, 4 * (c - 1)))
                    P.o("tensor", lambda E, o=pbig[64 * le:64 * (le + 1),
                                                   TG * tg:TG * (tg + 1)],
                        l=oex.rearrange("p (e hc n) -> p e hc n", hc=4, n=64)
                        [:, le, hc, :],
                        r=hidb[:, TPC * hc + TG * tg:TPC * hc + TG * (tg + 1)],
                        st=(hc == 0), sp=(hc == 3),
                        tp=(0, 64 * le): E.matmul(o, l, r, start=st, stop=sp,
                                                  tile_position=tp,
                                                  skip_group_check=True),
                        wait=w, inc=(s_omm if hc == 3 else None))
        for tg in range(NTG):
            P.o("vector", lambda E, t=tmpC, p=pbig[:, TG * tg:TG * (tg + 1)],
                g=gbs[:, TG * tg:TG * (tg + 1)]: E.tensor_mul(t[:, :], p, g),
                wait=[(s_omm, s_omm.v), (s_gbe, gbv)])
            P.o("vector", lambda E, o=x1sb[:, TPC * c + TG * tg:
                                           TPC * c + TG * (tg + 1)], t=tmpC:
                E.tensor_add(o, o, t[:, :]), inc=s_moev)
    X2R = [(s_moev, s_moev.v)]
    PBIG_LAST_C = [(s_moev, s_moev.v)]
    if FDBG:
        s_dx2 = P.sem("s_dx2")
        for c in range(NCH):
            P.dma("sync", dx2[128 * c:128 * (c + 1), :],
                  x1sb[:, TPC * c:TPC * (c + 1)], s_dx2, wait=X2R)
    P.pop_scope()

    P.push_scope()
    h3b = P.sb("d_h3", [128, NCH * TPC], BF16)
    wqb2 = P.sb("d_wq", [128, 8 * D], BF16)
    wkvb2 = P.sb("d_wkv", [128, 8 * 2048], BF16)
    qch2 = P.sb("d_qch", [128, 2 * TPC], BF16)
    ktm = P.sb("d_ktm", [128, 2 * 1024], BF16)
    vtm = P.sb("d_vtm", [128, 2 * 1024], BF16)
    ctxsb = P.sb("d_ctx", [65, H * 65], F32)
    scrD = scrC
    scrD2 = scrC2

    s_dw = s_aw
    P.dma("gpsimd", wqb2.rearrange("p (kc n) -> p kc n", n=D),
          wqkvT.rearrange("(kc p) n -> p kc n", p=128)[:, :, 0:D], s_dw, serial=False,
          wait=X2R)
    P.dma("gpsimd", wkvb2.rearrange("p (kc n) -> p kc n", n=2048),
          wqkvT.rearrange("(kc p) n -> p kc n", p=128)[:, :, D:3 * D], s_dw,
          serial=False, wait=X2R)
    dwv = [(s_dw, s_dw.v)]

    def h3_slice(c, tg):
        return h3b[:, TPC * c + TG * tg:TPC * c + TG * (tg + 1)]
    _, _, s_app3 = emit_ln(P, "dln", pool, pbig, h3_slice,
                           ones_col_bf[:, :], ones_row_f[:, :], ones_row64,
                           CW + PBIG_LAST_C + dwv,
                           scrD, scrD2, st1, st2, st3p, rstd_b, mst_b,
                           src_sb=x1sb, src_ready=X2R, nbuf=1, sems=ln_sems)
    H3R = [(s_app3, s_app3.v)]

    s_qmm = P.sem("s_qmm")
    s_qg = P.sem("s_qg")
    s_stq2 = P.sem("s_stq2")
    for oc in range(NCH):
        buf = oc % 2
        dst = qch2[:, TPC * buf:TPC * (buf + 1)]
        for tg in range(NTG):
            slot, pt, pw = pool.acquire()
            w = list(pw) + (H3R if (oc == 0 and tg == 0) else [])
            if tg == 0 and oc >= 2:
                w.append((s_stq2, s_stq2.v - 16))
            for kc in range(NCH):
                P.o("tensor", MM(pt[:, 0:TG],
                                 wqb2[:, D * kc + 128 * oc:D * kc + 128 * (oc + 1)],
                                 h3b[:, TPC * kc + TG * tg:TPC * kc + TG * (tg + 1)],
                                 kc == 0, kc == NCH - 1),
                    wait=w if kc == 0 else [], inc=(s_qmm if kc == NCH - 1 else None))
            P.o("scalar", lambda E, o=dst[:, TG * tg:TG * (tg + 1)], p=pt:
                E.activation(out=o, in_=p[:, 0:TG], func=AF.Relu),
                wait=[(s_qmm, s_qmm.v)], inc=s_qg)
            pool.release(slot, s_qg, s_qg.v)
        P.dma("sync", q2_in[128 * oc:128 * (oc + 1), :], dst, s_stq2,
              wait=[(s_qg, s_qg.v)])

    s_kvm = P.sem("s_kvm")
    s_kve = P.sem("s_kve")
    s_cmm = P.sem("s_cmm")
    pcD = [pbig[0:64, 0:7 * 65], pbig[0:64, TG:TG + 7 * 65],
           pbig[0:64, 2 * TG:2 * TG + 2 * 65]]
    NT = TPC // 128
    for tt in range(NT):
        buf = tt % 2
        kslot = ktm[:, 1024 * buf:1024 * (buf + 1)]
        vslot = vtm[:, 1024 * buf:1024 * (buf + 1)]
        for pi, dst in ((0, kslot), (1, vslot)):
            for vc in range(2):
                slot, pt, pw = pool.acquire()
                w = list(pw)
                if pi == 0 and vc == 0:
                    if tt >= 2:
                        w.append((s_cmm, tt - 1))
                    if tt == 0:
                        w += H3R
                for kc in range(NCH):
                    P.o("tensor", MM(pt[:, 0:TG],
                                     h3b[:, TPC * kc + 128 * tt:TPC * kc + 128 * (tt + 1)],
                                     wkvb2[:, 2048 * kc + 1024 * pi + TG * vc:
                                           2048 * kc + 1024 * pi + TG * (vc + 1)],
                                     kc == 0, kc == NCH - 1),
                        wait=w if kc == 0 else [],
                        inc=(s_kvm if kc == NCH - 1 else None))
                if pi == 0:
                    P.o("vector", lambda E, o=dst[:, TG * vc:TG * (vc + 1)], p=pt:
                        E.tensor_scalar_max(o, p[:, 0:TG], 0.0),
                        wait=[(s_kvm, s_kvm.v)], inc=s_kve)
                else:
                    P.o("vector", lambda E, o=dst[:, TG * vc:TG * (vc + 1)], p=pt:
                        E.tensor_copy(o, p[:, 0:TG]),
                        wait=[(s_kvm, s_kvm.v)], inc=s_kve)
                pool.release(slot, s_kve, s_kve.v)
        kvv = s_kve.v
        for h in range(H):
            bank, off = (0, h) if h < 7 else ((1, h - 7) if h < 14 else (2, h - 14))
            P.o("tensor", lambda E, o=pcD[bank][:, 65 * off:65 * off + 64],
                l=kslot[:, 64 * h:64 * (h + 1)], r=vslot[:, 64 * h:64 * (h + 1)],
                st=(tt == 0), sp=(tt == NT - 1): E.matmul(
                    o, l, r, start=st, stop=sp, skip_group_check=True),
                wait=([(s_kve, kvv)] if h == 0 else []))
            P.o("tensor", lambda E, o=pcD[bank][:, 65 * off + 64:65 * off + 65],
                l=kslot[:, 64 * h:64 * (h + 1)],
                st=(tt == 0), sp=(tt == NT - 1): E.matmul(
                    o, l, ones_col_bf[:, 0:1], start=st, stop=sp,
                    skip_group_check=True),
                inc=(s_cmm if h == H - 1 else None))
    s_cev = P.sem("s_cev")
    s_stc = P.sem("s_stc")
    P.o("vector", lambda E: E.memset(ctxsb[64:65, :], 0.0), inc=s_cev)
    for h in range(H):
        bank, off = (0, h) if h < 7 else ((1, h - 7) if h < 14 else (2, h - 14))
        P.o("vector", lambda E, o=ctxsb[0:64, 65 * h:65 * (h + 1)],
            p=pcD[bank][:, 65 * off:65 * (off + 1)]: E.tensor_copy(o, p),
            wait=[(s_cmm, s_cmm.v)] if h == 0 else [], inc=s_cev)
    P.dma("sync", ctx_in[:, :], ctxsb[:, :], s_stc, wait=[(s_cev, s_cev.v)])
    PBIG_LAST_D = [(s_cev, s_cev.v)]
    STQ2 = s_stq2.v
    STC = s_stc.v
    P.pop_scope()

    P.o("gpsimd", lambda E: E.collective_compute(
        "AllGather", ALU.bypass, GROUPS, ins=[q2_in[:, :]], outs=[q2_out[:, :]]),
        wait=[(s_stq2, STQ2)], inc=s_cc)
    P.o("gpsimd", lambda E: E.collective_compute(
        "AllReduce", ALU.add, GROUPS, ins=[ctx_in[:, :]], outs=[ctx_out[:, :]]),
        wait=[(s_stc, STC)], inc=s_cc)
    CC45 = s_cc.v

    P.push_scope()
    qp2 = P.sb("e_qp", [128, 2 * S], BF16)
    ctxf = P.sb("e_ctxf", [65, 4 * 65], F32)
    ctxb2 = P.sb("e_ctxb", [128, 4 * 65], BF16)
    aot2 = P.sb("e_aot", [64, 2 * TG], BF16)
    pvsbE = P.sb("e_pvsb", [64, 2 * TG], F32)
    drowE = P.sb("e_drow", [1, 2 * 4 * TG], F32)
    dcolE = P.sb("e_dcol", [4, 2 * TG], F32)
    rcolE = P.sb("e_rcol", [4, 2 * TG], F32)
    rrowE = P.sb("e_rrow", [1, 2 * 4 * TG], F32)

    s_el = s_bl
    dyn_dma(ctxmy[:, :], lambda E: ctx_out[:, bass.ts(DV(E, "hg"), 4 * 65)],
            s_el, [(s_cc, CC45)])
    q23 = q2_out.rearrange("(r t) c -> r t c", t=D)
    dyn_dma(q2my.rearrange("(r t) c -> r t c", t=256),
            (lambda E: q23[:, bass.ts(DV(E, "hg"), 256), :][:, 0:256, :]),
            s_el, [])
    P.dma("sync", ctxf[:, :], ctxmy[:, :], s_el)
    for hp in range(2):
        for r in range(4):
            P.dma("sync", qp2[:, S * hp + TPC * r:S * hp + TPC * (r + 1)],
                  q2my[256 * r + 128 * hp:256 * r + 128 * (hp + 1), :], s_el)
    ELW = [(s_el, s_el.v)]
    s_sumE = s_cn
    P.o("vector", lambda E: E.tensor_copy(ctxb2[0:65, :], ctxf[:, :]),
        wait=ELW, inc=s_sumE)
    P.dma("sync", ctxb2[64:128, :], ctxb2[0:64, :], s_sumE,
          wait=[(s_sumE, s_sumE.v)])
    CTXR = [(s_sumE, s_sumE.v)]

    s_emm = P.sem("s_emm")
    s_ev1 = P.sem("s_ev1")
    s_ebm = P.sem("s_ebm")
    s_eev = P.sem("s_eev")
    s_ast3 = P.sem("s_ast3")
    s_ecp2 = P.sem("s_ecp2")
    s_edm = P.sem("s_edm")
    s_ecpd = P.sem("s_ecpd")
    s_edma = P.sem("s_edma")
    s_erc = P.sem("s_erc")
    slots_e = [pool.acquire() for _ in range(4)]
    ppv2L = [slots_e[0][1], slots_e[1][1]]
    pb2L = [slots_e[2][1], slots_e[3][1]]
    EPF = [w for sl in slots_e for w in sl[2]]
    NQ = S // TG

    def emit_dchain(m):
        par = m % 2
        pp = 64 * par
        hl_, ch = m // 4, m % 4
        hp_ = hl_ // 2
        base_ = 64 * (hl_ % 2)
        w = [(s_ecpd, m - 1)] if m >= 2 else (list(CTXR) + PBIG_LAST_D)
        for q4 in range(4):
            P.o("tensor", lambda E, o=pbig[pp:pp + 1, TG * q4:TG * (q4 + 1)],
                l=ctxb2[base_:base_ + 64, 65 * hl_ + 64:65 * hl_ + 65],
                r=qp2[base_:base_ + 64, S * hp_ + 4 * TG * ch + TG * q4:
                      S * hp_ + 4 * TG * ch + TG * (q4 + 1)]:
                E.matmul(o, l, r, start=True, stop=True, skip_group_check=True),
                wait=w if q4 == 0 else [], inc=(s_edm if q4 == 3 else None))
        P.o("scalar", lambda E, o=drowE[0:1, 4 * TG * par:4 * TG * (par + 1)],
            i=pbig[pp:pp + 1, 0:4 * TG]: E.activation(
                out=o, in_=i, func=AF.Copy, bias=EPS_F32),
            wait=[(s_edm, m + 1)] +
            ([(s_edma, 128 * (m - 2) + 64)] if m >= 2 else []),
            inc=s_ecpd)
        for j in range(4):
            P.dma("sync", dcolE[j:j + 1, TG * par:TG * (par + 1)],
                  drowE[0:1, 4 * TG * par + TG * j:4 * TG * par + TG * (j + 1)],
                  s_edma,
                  wait=([(s_ecpd, m + 1)] + ([(s_erc, m - 1)] if m >= 2 else []))
                  if j == 0 else [])
        P.o("vector", lambda E, o=rcolE[:, TG * par:TG * (par + 1)],
            i=dcolE[:, TG * par:TG * (par + 1)]: E.reciprocal(o, i),
            wait=[(s_edma, 128 * m + 64)], inc=s_erc)
        for j in range(4):
            P.dma("sync", rrowE[0:1, 4 * TG * par + TG * j:4 * TG * par + TG * (j + 1)],
                  rcolE[j:j + 1, TG * par:TG * (par + 1)], s_edma,
                  wait=([(s_erc, m + 1)] + ([(s_ebm, 4 * (m - 1))] if m >= 2 else []))
                  if j == 0 else [])

    emit_dchain(0)
    for hl in range(4):
        hp, base = hl // 2, 64 * (hl % 2)
        for g in range(NQ):
            n = NQ * hl + g
            m = n // 4
            if n % 4 == 0 and m + 1 < 16:
                emit_dchain(m + 1)
            buf = n % 2
            j = n % 4
            par = m % 2
            ppv2 = ppv2L[buf]
            pb2 = pb2L[buf]
            pvb = pvsbE[:, TG * buf:TG * (buf + 1)]
            w = (list(CTXR) + EPF) if n < 2 else [(s_ecp2, n - 1)]
            P.o("tensor", lambda E, o=ppv2[0:64, 0:TG],
                l=ctxb2[base:base + 64, 65 * hl:65 * hl + 64],
                r=qp2[base:base + 64, S * hp + TG * g:S * hp + TG * (g + 1)]:
                E.matmul(o, l, r, start=True, stop=True, skip_group_check=True),
                wait=w, inc=s_emm)
            P.o("scalar", lambda E, o=pvb, i=ppv2[0:64, 0:TG]: E.activation(
                out=o, in_=i, func=AF.Copy),
                wait=[(s_emm, n + 1)] + ([(s_eev, n - 1)] if n >= 2 else []),
                inc=s_ecp2)
            P.o("tensor", MM(pb2[0:64, 0:TG], ones_row_f[0:1, 0:64],
                             rrowE[0:1, 4 * TG * par + TG * j:
                                   4 * TG * par + TG * (j + 1)], True, True),
                wait=[(s_edma, 128 * (m + 1))] +
                ([(s_eev, n - 1)] if n >= 2 else []),
                inc=s_ebm)
            slot_o = aot2[:, TG * (n % 2):TG * (n % 2 + 1)]
            w2 = [(s_ast3, s_ast3.v - 16)] if n >= 2 else []
            P.o("vector", lambda E, o=slot_o, p=pvb, b=pb2: E.tensor_mul(
                o, p, b[0:64, 0:TG]),
                wait=[(s_ecp2, n + 1), (s_ebm, n + 1)] + w2, inc=s_eev)
            P.dma("sync", ao2_in[64 * hl:64 * (hl + 1), TG * g:TG * (g + 1)], slot_o,
                  s_ast3, wait=[(s_eev, s_eev.v)])
        if hl == 1:
            AST3_H0 = s_ast3.v
    for sl in slots_e:
        pool.release(sl[0], s_eev, s_eev.v)
    AST3 = s_ast3.v
    P.pop_scope()

    P.o("gpsimd", lambda E: E.collective_compute(
        "AllGather", ALU.bypass, GROUPS, ins=[ao2_in[0:128, :]],
        outs=[ao2_outA[:, :]]), wait=[(s_ast3, AST3_H0)], inc=s_cc)
    CC6A = s_cc.v
    P.o("gpsimd", lambda E: E.collective_compute(
        "AllGather", ALU.bypass, GROUPS, ins=[ao2_in[128:256, :]],
        outs=[ao2_outB[:, :]]), wait=[(s_ast3, AST3)], inc=s_cc)
    CC6 = s_cc.v

    x3d = nc.dram_tensor("x3d", [D, TPC], F32).ap()
    P.push_scope()
    ao2b = P.sb("f_ao2", [128, NCH * TPC], BF16)
    wos2 = P.sb("f_wos", [128, 2 * 1024], BF16)
    s_fl = s_bl
    dyn_dma(ao2myA[:, :], (lambda E: ao2_outA[:, bass.ts(DV(E, "hg"), TPC)]),
            s_fl, [(s_cc, CC6A)])
    for c in range(NCH // 2):
        P.dma("sync", ao2b[:, TPC * c:TPC * (c + 1)],
              ao2myA[128 * c:128 * (c + 1), :], s_fl)
    dyn_dma(ao2myB[:, :], (lambda E: ao2_outB[:, bass.ts(DV(E, "hg"), TPC)]),
            s_fl, [(s_cc, CC6)])
    for c in range(NCH // 2):
        P.dma("sync", ao2b[:, TPC * (c + 4):TPC * (c + 5)],
              ao2myB[128 * c:128 * (c + 1), :], s_fl)
    AO2L = [(s_fl, s_fl.v)]
    s_pm = P.sem("s_pm")
    s_pe = P.sem("s_pe")
    s_st3 = P.sem("s_st3")
    s_wol2 = s_wol
    for oc in range(NCH):
        wbuf = wos2[:, 1024 * (oc % 2):1024 * (oc % 2 + 1)]
        wlv = P.dma("gpsimd", wbuf.rearrange("p (kc n) -> p kc n", n=128),
                    wpT.rearrange("(kc p) n -> p kc n", p=128)
                    [:, :, 128 * oc:128 * (oc + 1)], s_wol2,
                    wait=[(s_pm, 4 * (oc - 1))] if oc >= 2
                    else [(s_ast3, AST3)])
        for tg in range(NTG):
            slot, pt, pw = pool.acquire()
            for kc in range(NCH):
                P.o("tensor", MM(pt[:, 0:TG],
                                 wbuf[:, 128 * kc:128 * (kc + 1)],
                                 ao2b[:, TPC * kc + TG * tg:TPC * kc + TG * (tg + 1)],
                                 kc == 0, kc == NCH - 1),
                    wait=(list(pw) + AO2L + [(s_wol2, wlv)] + PBIG_LAST_D)
                    if kc == 0 else [],
                    inc=(s_pm if kc == NCH - 1 else None))
            P.o("vector", lambda E, o=x1sb[:, TPC * oc + TG * tg:
                                           TPC * oc + TG * (tg + 1)], p=pt,
                b=bp_s[:, oc:oc + 1]: E.scalar_tensor_tensor(
                    o, p[:, 0:TG], b, o, ALU.add, ALU.add),
                wait=[(s_pm, s_pm.v)] + (X2R if oc == 0 and tg == 0 else []),
                inc=s_pe)
            pool.release(slot, s_pe, s_pe.v)
        P.dma("sync", x3d[128 * oc:128 * (oc + 1), :], x1sb[:, TPC * oc:TPC * (oc + 1)],
              s_st3, wait=[(s_pe, s_pe.v)])
    X3ALL = [(s_st3, s_st3.v)]
    P.pop_scope()
    P.pop_scope()

    P.push_scope()
    xcF = P.sb("f_xc", [128, 2 * TPC], F32)
    scrF = P.sb("f_scr", [128, TPC], BF16)
    scrF2 = P.sb("f_scr2", [128, TPC], BF16)
    str_ = P.sb("f_str", [1, TPC], F32)
    sts = P.sb("f_sts", [1, TPC], F32)
    hhy = P.sb("f_hhy", [128, NCH * TPC], BF16)
    srowF = P.sb("f_srow", [1, 2 * D], F32)
    qsb = P.sb("f_qsb", [128, 2 * D], U8)
    w1c = P.sb("f_w1c", [128, 2 * D], BF16)
    y1c = P.sb("f_y1c", [128, 2 * TPC], BF16)
    w2b = P.sb("f_w2b", [128, 32 * D], BF16)
    tmpvF = P.sb("f_tmpv", [128, TG], F32)
    tmpx3 = P.sb("f_tmpx3", [128, 2 * TG], F32)
    stm = st1
    stv = st2

    s_misc = s_tmp
    b2hv = P.o("vector", lambda E: E.tensor_scalar_mul(b2h_s[:, :], b2_s[:, :], 0.5),
               wait=CW, inc=s_misc)

    s_castF, s_sqF, s_mmF, s_strF, _, s_appF, s_xsF = ln_sems
    castF0, sqF0, mmF0, appF0 = s_castF.v, s_sqF.v, s_mmF.v, s_appF.v
    for c in range(NCH):
        buf = c % 2
        ap = xcF[:, TPC * buf:TPC * (buf + 1)]
        w = list(X3ALL) + [(s_xsF, s_xsF.v)]
        if c >= 2:
            w.append((s_castF, castF0 + c - 1))
        lv = P.dma("sync", ap, x3d[128 * c:128 * (c + 1), :], s_xsF, wait=w)
        wv = [(s_xsF, lv)]
        if c >= 1:
            wv.append((s_mmF, mmF0 + c))
        P.o("vector", lambda E, o=scrF[:, :], i=ap: E.tensor_copy(o, i),
            wait=wv, inc=s_castF)
        P.o("scalar", lambda E: E.activation(out=scrF2[:, :], in_=scrF[:, :],
                                             func=AF.Square),
            wait=[(s_castF, castF0 + c + 1)] + ([(s_mmF, mmF0 + c)] if c >= 1 else []),
            inc=s_sqF)
        for tg in range(NTG):
            P.o("tensor", MM(pbig[0:1, TG * tg:TG * (tg + 1)], ones_col_bf[:, :],
                             scrF[:, TG * tg:TG * (tg + 1)], c == 0, c == NCH - 1),
                wait=([(s_sqF, sqF0 + c + 1)] + (PBIG_LAST_D if c == 0 else []))
                if tg == 0 else [])
            P.o("tensor", MM(pbig[32:33, TG * tg:TG * (tg + 1)], ones_col_bf[:, :],
                             scrF2[:, TG * tg:TG * (tg + 1)], c == 0, c == NCH - 1),
                inc=(s_mmF if tg == NTG - 1 else None))
    st3pF = pbig[64:65, 0:TPC]
    P.o("vector", lambda E: E.tensor_scalar_mul(stm[:, :], pbig[0:1, 0:TPC], 1.0 / D),
        wait=[(s_mmF, s_mmF.v)])
    P.o("vector", lambda E: E.tensor_scalar_mul(stv[:, :], pbig[32:33, 0:TPC], 1.0 / D),
        inc=s_strF)
    P.o("scalar", lambda E: E.activation(out=st3pF, in_=stm[:, :], func=AF.Square),
        wait=[(s_strF, s_strF.v)], inc=s_strF)
    P.o("vector", lambda E: E.tensor_sub(stv[:, :], stv[:, :], st3pF),
        wait=[(s_strF, s_strF.v)])
    P.o("vector", lambda E: E.tensor_scalar_add(str_[:, :], stv[:, :], EPS_LN),
        inc=s_strF)
    P.o("scalar", lambda E: E.activation(out=str_[:, :], in_=str_[:, :], func=AF.Ln),
        wait=[(s_strF, s_strF.v)])
    P.o("scalar", lambda E: E.activation(out=str_[:, :], in_=str_[:, :], func=AF.Exp,
                                         scale=-0.5), inc=s_strF)
    P.o("vector", lambda E: E.tensor_scalar_add(sts[:, :], stv[:, :], EPS_LN),
        wait=[(s_strF, s_strF.v)], inc=s_strF)
    P.o("vector", lambda E: E.reciprocal(sts[:, :], sts[:, :]),
        wait=[(s_strF, s_strF.v)], inc=s_strF)
    P.o("vector", lambda E: E.tensor_mul(sts[:, :], sts[:, :], stv[:, :]),
        wait=[(s_strF, s_strF.v)], inc=s_strF)
    P.o("vector", lambda E: E.tensor_scalar_add(sts[:, :], sts[:, :], EPS_LN),
        inc=s_strF)
    P.o("scalar", lambda E: E.activation(out=sts[:, :], in_=sts[:, :], func=AF.Ln),
        wait=[(s_strF, s_strF.v)])
    P.o("scalar", lambda E: E.activation(out=sts[:, :], in_=sts[:, :], func=AF.Exp,
                                         scale=-0.5), inc=s_strF)
    P.o("vector", lambda E: E.tensor_mul(sts[:, :], sts[:, :], str_[:, :]),
        wait=[(s_strF, s_strF.v)])
    P.o("vector", lambda E: E.tensor_mul(stv[:, :], stm[:, :], sts[:, :]))
    P.o("vector", lambda E: E.tensor_scalar_mul(stv[:, :], stv[:, :], -1.0))
    P.o("vector", lambda E: E.tensor_mul(stm[:, :], stm[:, :], str_[:, :]))
    P.o("vector", lambda E: E.tensor_scalar_mul(stm[:, :], stm[:, :], -1.0))
    STRF = P.sem("f_strfin")
    P.o("vector", lambda E: E.tensor_scalar_add(str_[:, :], str_[:, :], 1.0),
        inc=STRF)

    s_bcF = ln_sems[4]
    for tg in range(NTG):
        for strip, orow, bdst in ((sts, ones_row_f[:, :], rstd_b),
                                  (stv, ones_row64, mst_b)):
            slot, pt, pw = pool.acquire()
            vmm = P.o("tensor", MM(pt[:, 0:TG], orow,
                                   strip[0:1, TG * tg:TG * (tg + 1)], True, True),
                      wait=list(pw) + [(STRF, 1)], inc=s_bcF)
            P.o("scalar", lambda E, o=bdst[:, TG * tg:TG * (tg + 1)], p=pt:
                E.activation(out=o, in_=p[:, 0:TG], func=AF.Copy),
                wait=[(s_bcF, vmm)], inc=s_bcF)
            pool.release(slot, s_bcF, s_bcF.v)
    bcFv = s_bcF.v
    for c in range(NCH):
        buf = c % 2
        ap = xcF[:, TPC * buf:TPC * (buf + 1)]
        w = [(s_mmF, mmF0 + NCH), (s_xsF, s_xsF.v)]
        if c >= 2:
            w.append((s_appF, appF0 + 4 * (c - 1)))
        lv = P.dma("sync", ap, x3d[128 * c:128 * (c + 1), :], s_xsF, wait=w)
        for tg in range(NTG):
            P.o("vector", lambda E, t=tmpvF, x=ap[:, TG * tg:TG * (tg + 1)],
                r=rstd_b[:, TG * tg:TG * (tg + 1)]: E.tensor_mul(t[:, :], x, r),
                wait=[(s_bcF, bcFv), (s_xsF, lv)])
            P.o("vector", lambda E, t=tmpvF, m=mst_b[:, TG * tg:TG * (tg + 1)]:
                E.tensor_add(t[:, :], t[:, :], m))
            P.o("vector", lambda E, o=hhy[:, TPC * c + TG * tg:TPC * c + TG * (tg + 1)],
                t=tmpvF, g=gam_s[:, c:c + 1], b=bet_s[:, c:c + 1]:
                E.tensor_scalar(o, t[:, :], g, b, ALU.mult, ALU.add), inc=s_appF)
    HHR = [(s_appF, s_appF.v)]

    s_q1 = P.sem("f_q1")
    s_dq = P.sem("f_dq")
    s_fmm = P.sem("f_fmm")
    s_sil = P.sem("f_sil")
    s_sty = P.sem("f_sty")
    s_sbm = P.sem("f_sbm")
    for yc in range(32):
        buf = yc % 2
        qb = qsb[:, D * buf:D * (buf + 1)]
        sr = srowF[0:1, D * buf:D * (buf + 1)]
        wc = w1c[:, D * buf:D * (buf + 1)]
        yb = y1c[:, TPC * buf:TPC * (buf + 1)]
        w = ([(s_dq, 2 * (yc - 1))] if yc >= 2 else list(X3ALL)) + [(s_q1, s_q1.v)]
        l1 = P.dma("sync", qb.rearrange("p (kc n) -> p kc n", n=128),
                   fc1q.rearrange("(kc p) n -> p kc n", p=128)
                   [:, :, 128 * yc:128 * (yc + 1)], s_q1, wait=w)
        l2 = P.dma("gpsimd", sr, sc1x[yc:yc + 1, :], s_q1)
        for hb in range(2):
            slot_s, pts, pws = pool.acquire()
            P.o("tensor", MM(pts[:, 0:TG], ones_row_f[0:1, :],
                             sr[0:1, TG * hb:TG * (hb + 1)], True, True),
                wait=list(pws) + ([(s_q1, l2)] if hb == 0 else []), inc=s_sbm)
            P.o("vector", lambda E, o=wc[:, TG * hb:TG * (hb + 1)],
                q=qb[:, TG * hb:TG * (hb + 1)], s=pts: E.scalar_tensor_tensor(
                    o, q, -128.0, s[:, 0:TG], ALU.add, ALU.mult),
                wait=[(s_sbm, s_sbm.v)] + ([(s_fmm, 4 * (yc - 1))]
                                           if (yc >= 2 and hb == 0) else []),
                inc=s_dq)
            pool.release(slot_s, s_dq, s_dq.v)
        for tg in range(NTG):
            slot, pt, pw = pool.acquire()
            for kc in range(NCH):
                P.o("tensor", MM(pt[:, 0:TG], wc[:, 128 * kc:128 * (kc + 1)],
                                 hhy[:, TPC * kc + TG * tg:TPC * kc + TG * (tg + 1)],
                                 kc == 0, kc == NCH - 1),
                    wait=(list(pw) + [(s_dq, s_dq.v)] +
                          (HHR if yc == 0 and tg == 0 else [])) if kc == 0 else [],
                    inc=(s_fmm if kc == NCH - 1 else None))
            P.o("scalar", lambda E, o=yb[:, TG * tg:TG * (tg + 1)], p=pt,
                b=b1_s[:, yc:yc + 1]: E.activation(out=o, in_=p[:, 0:TG], func=AF.Silu,
                                                   bias=b),
                wait=[(s_fmm, s_fmm.v)] + ([(s_sty, 16 * (yc - 1))] if yc >= 2 and
                                           tg == 0 else []), inc=s_sil)
            pool.release(slot, s_sil, s_sil.v)
        P.dma("sync", y1d[128 * yc:128 * (yc + 1), :], yb, s_sty,
              wait=[(s_sil, s_sil.v)])
    FC1ALL = [(s_sty, s_sty.v)]

    s_q2 = s_q1
    s_dq2 = P.sem("f_dq2")
    for yc in range(32):
        buf = yc % 2
        qb = qsb[:, D * buf:D * (buf + 1)]
        sr = srowF[0:1, D * buf:D * (buf + 1)]
        w = [(s_q2, s_q2.v)]
        if yc < 2:
            w.append((s_dq, 64))
        else:
            w.append((s_dq2, 2 * (yc - 1)))
        l1 = P.dma("sync", qb, fc2q[128 * yc:128 * (yc + 1), :], s_q2, wait=w)
        l2 = P.dma("gpsimd", sr, sc2x[yc:yc + 1, :], s_q2)
        for hb in range(2):
            slot_s, pts, pws = pool.acquire()
            P.o("tensor", MM(pts[:, 0:TG], ones_row_f[0:1, :],
                             sr[0:1, TG * hb:TG * (hb + 1)], True, True),
                wait=list(pws) + ([(s_q2, l2)] if hb == 0 else []), inc=s_sbm)
            P.o("vector", lambda E,
                o=w2b[:, D * yc + TG * hb:D * yc + TG * (hb + 1)],
                q=qb[:, TG * hb:TG * (hb + 1)], s=pts: E.scalar_tensor_tensor(
                    o, q, -128.0, s[:, 0:TG], ALU.add, ALU.mult),
                wait=[(s_sbm, s_sbm.v)], inc=s_dq2)
            pool.release(slot_s, s_dq2, s_dq2.v)
    W2R = [(s_dq2, s_dq2.v)]

    s_y1 = P.sem("f_y1")
    s_2mm = P.sem("f_2mm")
    s_x3l = P.sem("f_x3l")
    s_fin = P.sem("f_fin")
    s_st4 = P.sem("f_st4")
    for tg in range(NTG):
        for strip, orow, bdst in ((str_, ones_row_f[:, :], rstd_b),
                                  (stm, ones_row_f[:, :], mst_b)):
            slot, pt, pw = pool.acquire()
            vmm = P.o("tensor", MM(pt[:, 0:TG], orow,
                                   strip[0:1, TG * tg:TG * (tg + 1)], True, True),
                      wait=list(pw) + [(STRF, 1)] + (HHR if tg == 0 else []),
                      inc=s_bcF)
            P.o("scalar", lambda E, o=bdst[:, TG * tg:TG * (tg + 1)], p=pt:
                E.activation(out=o, in_=p[:, 0:TG], func=AF.Copy),
                wait=[(s_bcF, vmm)], inc=s_bcF)
            pool.release(slot, s_bcF, s_bcF.v)
    bcF4v = s_bcF.v
    for tg in range(NTG):
        wv = (list(W2R) + FC1ALL if tg == 0 else [(s_fin, s_fin.v)]) + \
            [(s_y1, s_y1.v)]
        l1 = P.dma("sync", hhy.rearrange("p (yc t) -> p yc t", t=TG),
                   y1d.rearrange("(yc p) t -> p yc t", p=128)
                   [:, :, TG * tg:TG * (tg + 1)], s_y1, wait=wv)
        for oc in range(NCH):
            n = NCH * tg + oc
            slot, pt, pw = pool.acquire()
            for yc in range(32):
                P.o("tensor", MM(pt[:, 0:TG],
                                 w2b[:, D * yc + 128 * oc:D * yc + 128 * (oc + 1)],
                                 hhy[:, TG * yc:TG * (yc + 1)],
                                 yc == 0, yc == 31),
                    wait=(list(pw) + [(s_y1, l1)] + (W2R if n == 0 else []))
                    if yc == 0 else [],
                    inc=(s_2mm if yc == 31 else None))
            gv = s_2mm.v
            xb3 = tmpx3[:, TG * (n % 2):TG * (n % 2 + 1)]
            lxw = [(s_fin, s_fin.v - 1)] if n >= 2 else []
            if n >= 2:
                lxw.append((s_st4, 16 * (n - 1)))
            lx = P.dma("sync", xb3,
                       x3d[128 * oc:128 * (oc + 1), TG * tg:TG * (tg + 1)], s_x3l,
                       wait=lxw)
            P.o("vector", lambda E, x=xb3,
                r=rstd_b[:, TG * tg:TG * (tg + 1)]: E.tensor_mul(x, x, r),
                wait=[(s_2mm, gv), (s_bcF, bcF4v), (s_x3l, lx), (s_misc, b2hv)])
            P.o("vector", lambda E, x=xb3,
                m=mst_b[:, TG * tg:TG * (tg + 1)]: E.tensor_add(x, x, m))
            P.o("vector", lambda E, x=xb3, b=b2h_s[:, oc:oc + 1]:
                E.tensor_single_scalar(x, x, b, ALU.add))
            P.o("vector", lambda E, x=xb3, p=pt: E.scalar_tensor_tensor(
                x, p[:, 0:TG], 0.5, x, ALU.mult, ALU.add), inc=s_fin)
            pool.release(slot, s_fin, s_fin.v)
            P.dma("sync", x4T[128 * oc:128 * (oc + 1), TG * tg:TG * (tg + 1)],
                  xb3, s_st4, wait=[(s_fin, s_fin.v)])
    P.pop_scope()
    if FDBG:
        s_dbg = P.sem("s_fdbg")
        for dst, src in ((dkq, kq_in), (dvi, v_in), (dao, ao_in), (dq2, q2_in),
                         (dctx, ctx_in), (dao2, ao2_in), (dx3, x3d)):
            P.dma("sync", dst[:, :], src[:, :], s_dbg, wait=[(s_st4, s_st4.v)])
        P.o("sync", lambda E: E.nop(), wait=[(s_st4, s_st4.v), (s_dbg, s_dbg.v)])
    else:
        P.o("sync", lambda E: E.nop(), wait=[(s_st4, s_st4.v)])
    P.build()
    return P


def stage_fused(inputs):
    x = inputs["L0_attn_x"]
    pos_all = np.arange(S, dtype=np.float32)
    inv = (1.0 / (10000.0 ** (np.arange(0, DR, 2, dtype=np.float32) / DR))).astype(np.float32)
    selcv = np.zeros((16, 8 * 128), BFNP)
    for c in range(8):
        selcv[2 * c, 128 * c:128 * c + 64] = 1.0
        selcv[2 * c + 1, 128 * c + 64:128 * (c + 1)] = 1.0
    sc1 = np.asarray(inputs["L1_ffn_fc1_scales"], dtype=np.float32)
    sc2 = np.asarray(inputs["L1_ffn_fc2_scales"], dtype=np.float32)
    sc1x = np.ascontiguousarray(
        np.stack([sc1[:, 128 * yc:128 * (yc + 1)].reshape(8 * 128)
                  for yc in range(32)]))
    sc2x = np.ascontiguousarray(sc2)
    common = {
        "wq": inputs["L0_attn_wq"], "wqc": inputs["L0_attn_wqc"],
        "wqr": inputs["L0_attn_wqr"], "wkv": inputs["L0_attn_wkv"],
        "wkr": inputs["L0_attn_wkr"], "wkc": inputs["L0_attn_wkc"],
        "wv": inputs["L0_attn_wv"],
        "bqx": np.ascontiguousarray(np.asarray(inputs["L0_attn_bq"]).reshape(2, 128).T),
        "bqcx": np.ascontiguousarray(np.asarray(inputs["L0_attn_bqc"]).reshape(7, 128).T),
        "bqrx": np.ascontiguousarray(np.asarray(inputs["L0_attn_bqr"]).reshape(1, 128).T),
        "bkvx": np.ascontiguousarray(np.asarray(inputs["L0_attn_bkv"]).reshape(2, 128).T),
        "bkrx": np.ascontiguousarray(np.asarray(inputs["L0_attn_bkr"]).reshape(1, 128).T),
        "bkcx": np.ascontiguousarray(np.asarray(inputs["L0_attn_bkc"]).reshape(7, 128).T),
        "bvx": np.asarray(inputs["L0_attn_bv"]).reshape(1, D),
        "onesc": np.ones((128, 1), np.float32),
        "onesr": np.ones((1, 128), np.float32),
        "tri": np.where(np.arange(128)[:, None] <= np.arange(128)[None, :],
                        0.0, -65536.0).astype(BFNP),
        "identb": np.eye(128, dtype=BFNP),
        "identf": np.eye(128, dtype=np.float32),
        "wo": np.ascontiguousarray(
            np.asarray(inputs["L0_attn_wo"]).reshape(16, 64, D)
            [[0, 1, 4, 5, 8, 9, 12, 13, 2, 3, 6, 7, 10, 11, 14, 15]]
            .reshape(D, D)),
        "box": np.ascontiguousarray(np.asarray(inputs["L0_attn_bo"]).reshape(8, 128).T),
        "wrt": np.ascontiguousarray(np.asarray(inputs["L0_ffn_w_router"]).T),
        "centT": np.ascontiguousarray(np.asarray(inputs["L0_ffn_head_centroids"]).T),
        "tempx": np.asarray(inputs["L0_ffn_temperature"]).reshape(1, 1),
        "iexp": inputs["L0_ffn_input_experts"],
        "oexp": inputs["L0_ffn_output_experts"],
        "selc": selcv,
        "wqkvT": np.ascontiguousarray(np.asarray(inputs["L1_attn_w_qkv"]).T),
        "wpT": np.ascontiguousarray(
            np.asarray(inputs["L1_attn_w_proj"]).T.reshape(16, 64, D)
            [[0, 1, 4, 5, 8, 9, 12, 13, 2, 3, 6, 7, 10, 11, 14, 15]]
            .reshape(D, D)),
        "bpx": np.ascontiguousarray(np.asarray(inputs["L1_attn_b_proj"]).reshape(8, 128).T),
        "gamx": np.ascontiguousarray(np.asarray(inputs["L1_ffn_ln_gamma"]).reshape(8, 128).T),
        "betx": np.ascontiguousarray(np.asarray(inputs["L1_ffn_ln_beta"]).reshape(8, 128).T),
        "fc1q": np.asarray(inputs["L1_ffn_fc1_q"]).astype(np.uint8),
        "sc1x": sc1x,
        "b1x": np.ascontiguousarray(np.asarray(inputs["L1_ffn_fc1_bias"]).reshape(32, 128).T),
        "fc2q": np.asarray(inputs["L1_ffn_fc2_q"]).astype(np.uint8),
        "sc2x": sc2x,
        "b2x": np.ascontiguousarray(np.asarray(inputs["L1_ffn_fc2_bias"]).reshape(8, 128).T),
    }
    maps = []
    for c in range(NCORES):
        b, i = c // 4, c % 4
        m = dict(common)
        m["xT"] = np.ascontiguousarray(np.asarray(x[b, TPC * i:TPC * (i + 1)]).T)
        pos = pos_all[TPC * i:TPC * (i + 1)]
        f = pos[:, None] * inv[None, :]
        emb = np.concatenate([f, f], axis=1)
        m["cosT"] = np.ascontiguousarray(np.cos(emb).T.astype(np.float32))
        m["sinT"] = np.ascontiguousarray(np.sin(emb).T.astype(np.float32))
        maps.append(m)
    return maps


def kernel(**inputs):
    inputs = {k: np.asarray(v) for k, v in inputs.items()}
    P = build_fused()
    maps = stage_fused(inputs)
    res = run_bass_kernel_spmd(P.nc, maps, core_ids=list(range(NCORES)))
    P.es.close()
    out = np.empty((B, S, D), np.float32)
    for c in range(NCORES):
        b, i = c // 4, c % 4
        out[b, TPC * i:TPC * (i + 1), :] = res.results[c]["x4T"].T
    return out
